# version 13
# baseline (speedup 1.0000x reference)
import numpy as np
from contextlib import ExitStack
import concourse.bass as bass
import concourse.mybir as mybir
from concourse.bass_utils import run_bass_kernel_spmd

F32 = mybir.dt.float32
BF16 = mybir.dt.bfloat16
AF = mybir.ActivationFunctionType
ALU = mybir.AluOpType
AX = mybir.AxisListType

T = 2048
D = 1024
NT = T // 128
ND = D // 128
EPS = 1e-5

DEBUG = ()

REG1 = 40960
REG2 = 73728
REG3 = 106496
REG4 = 139264
REG5 = 172032
WS = 204800

INPUT_SPECS = [
    ("L0_attn_x", [T, D]),
    ("L0_attn_W_qkv", [D, 3 * D]),
    ("L0_attn_b_qkv", [1, 3 * D]),
    ("L0_attn_W_vconv", [D, 3]),
    ("L0_ffn_W1", [4 * D, D]),
    ("L0_ffn_b1", [1, 4 * D]),
    ("L0_ffn_W2", [D, 4 * D]),
    ("L0_ffn_b2", [1, D]),
    ("L1_attn_basis", [D, 256]),
    ("L1_attn_q_coeffs", [256, 256]),
    ("L1_attn_k_coeffs", [256, 256]),
    ("L1_attn_v_coeffs", [256, 256]),
    ("L1_attn_o_coeffs", [256, 256]),
    ("L1_attn_decay_logit", [1, 1]),
    ("L1_attn_out_scale", [1, 1]),
    ("L1_ffn_ln1_w", [1, D]),
    ("L1_ffn_ln1_b", [1, D]),
    ("L1_ffn_ln2_w", [1, D]),
    ("L1_ffn_ln2_b", [1, D]),
    ("L1_ffn_smha_sim", [D, 8]),
    ("L1_ffn_smha_gates", [1, 8]),
    ("L1_ffn_q_proj", [8 * D, 128]),
    ("L1_ffn_k_proj", [8 * D, 128]),
    ("L1_ffn_v_proj", [8 * D, 128]),
    ("L1_ffn_o_proj", [8 * 128, D]),
    ("L1_ffn_moe_sim", [D, 8]),
    ("L1_ffn_moe_gates", [1, 8]),
    ("L1_ffn_w1", [8 * D, D]),
    ("L1_ffn_w2", [8 * D, D]),
]


class Sem:
    def __init__(self, ctx, nc, name, step=1):
        self.h = ctx.enter_context(nc.semaphore(name))
        self.val = 0
        self.step = step

    def inc(self, ins, n=None):
        n = self.step if n is None else n
        ins.then_inc(self.h, n)
        self.val += n
        return (self.h, self.val)

    def tok(self):
        return (self.h, self.val)


def wait(eng, *toks):
    for tk in toks:
        if tk is None:
            continue
        h, v = tk
        if v > 0:
            eng.wait_ge(h, v)


class KB:
    def __init__(self, nc, ctx):
        self.nc = nc
        self.ctx = ctx
        self.blk = ctx.enter_context(nc.Block())
        self.dq = Sem(ctx, nc, "dq", 16)
        self.dg = Sem(ctx, nc, "dg", 16)
        self._dsems = {}
        self.ct = Sem(ctx, nc, "ct")
        self.cv = Sem(ctx, nc, "cv")
        self.cs = Sem(ctx, nc, "cs")
        self.cg = Sem(ctx, nc, "cg")
        self.dbg_outs = {}
        self.dbg_toks = []

    def dsem(self, name, step=16):
        if name not in self._dsems:
            self._dsems[name] = Sem(self.ctx, self.nc, "d_" + name, step)
        return self._dsems[name]

    def q(self, fn):
        self.blk.sync(fn)

    def g(self, fn):
        self.blk.gpsimd(fn)

    def t(self, fn):
        self.blk.tensor(fn)

    def v(self, fn):
        self.blk.vector(fn)

    def s(self, fn):
        self.blk.scalar(fn)

    def barrier(self, dma=True):
        toks = [x.tok() for x in (self.ct, self.cv, self.cs, self.cg)]
        if dma:
            toks += [x.tok() for x in (self.dq, self.dg)]
            toks += [s.tok() for s in self._dsems.values()]

        def _w(eng):
            wait(eng, *toks)
        self.q(_w)
        self.g(_w)
        self.t(_w)
        self.v(_w)
        self.s(_w)

    def sbuf(self, name, shape, dtype, scope=None):
        return (scope or self.ctx).enter_context(self.nc.sbuf_tensor(name, shape, dtype))

    def at(self, name, shape, dtype, offset):
        return self.nc.alloc_sbuf_tensor_at(name, shape, dtype, offset=offset)

    def psum(self, name, shape, dtype, scope=None):
        return (scope or self.ctx).enter_context(self.nc.psum_tensor(name, shape, dtype))

    def dbg(self, name, ap, tok):
        if name in DEBUG:
            o = self.nc.dram_tensor("dbg_" + name, list(ap.shape), ap.dtype,
                                    kind="ExternalOutput").ap()
            self.dbg_outs[name] = (o, ap)
            self.dbg_toks.append(tok)

    def flush_dbg(self):
        if not self.dbg_outs:
            return
        outs, toks, dq = self.dbg_outs, self.dbg_toks, self.dq

        def _(sync):
            wait(sync, *toks)
            for name, (o, ap) in outs.items():
                dq.inc(sync.dma_start(out=o, in_=ap))
            wait(sync, dq.tok())
        self.q(_)
        self.dbg_outs = {}
        self.dbg_toks = []


class LNB:

    def __init__(self, kb, st, sq_scr, eps_ap):
        self.kb = kb
        self.st = st
        self.sq = sq_scr
        self.eps = eps_ap

    def stats(self, src_ap, col, toks, f_ap=None, prev=None):
        kb, st = self.kb, self.st
        cv, cg = kb.cv, kb.cg
        invD = 1.0 / D
        m = st[:, col:col + 1]
        va = st[:, 16 + col:17 + col]
        r = st[:, 32 + col:33 + col]
        fcol = st[:, 48 + col:49 + col]
        nfree = len(src_ap.ap) - 1
        ax = AX.X if nfree == 1 else AX.XY
        if nfree == 1:
            sq = self.sq
        else:
            counts = [p[1] for p in src_ap.ap[1:]]
            dims = [list(self.sq.ap[0])]
            stride = 1
            rev = []
            for cdim in reversed(counts):
                rev.append([stride, cdim])
                stride *= cdim
            dims += rev[::-1]
            sq = bass.AP(self.sq.tensor, 0, dims)

        def _v1(v):
            wait(v, prev, *toks)
            ins = v.tensor_reduce(m, src_ap, ax, ALU.add)
            if f_ap is not None:
                ins = v.tensor_copy(fcol, f_ap)
            cv.inc(ins)
        kb.v(_v1)
        vt1 = cv.tok()

        def _g1(g):
            wait(g, prev, *toks)
            cg.inc(g.tensor_tensor(sq, src_ap, src_ap, ALU.mult))
        kb.g(_g1)
        gt1 = cg.tok()

        def _v1b(v):
            wait(v, gt1, vt1)
            cv.inc(v.tensor_reduce(va, sq, ax, ALU.add))
        kb.v(_v1b)
        gt2 = cv.tok()

        def _v2(v):
            wait(v, vt1, gt2)
            if f_ap is not None:
                v.tensor_scalar(m, m, fcol, invD, ALU.mult, ALU.mult)
                ins = v.tensor_mul(r, fcol, fcol)
            else:
                ins = v.tensor_scalar_mul(m, m, invD)
            cv.inc(ins)
        kb.v(_v2)
        vt2 = cv.tok()

        def _v3(v):
            wait(v, vt2)
            if f_ap is not None:
                cv.inc(v.tensor_scalar(va, va, r, invD, ALU.mult, ALU.mult))
            else:
                cv.inc(v.tensor_scalar_mul(va, va, invD))
        kb.v(_v3)
        vt3 = cv.tok()

        def _v4(v):
            wait(v, vt3)
            cv.inc(v.tensor_mul(r, m, m))
        kb.v(_v4)
        vt4 = cv.tok()

        def _v5(v):
            wait(v, vt4)
            cv.inc(v.tensor_sub(va, va, r))
        kb.v(_v5)
        return cv.tok()

    def rstd_batch(self, c0, c1, toks):
        kb, st = self.kb, self.st
        cs = kb.cs

        def _s1(s):
            wait(s, *toks)
            cs.inc(s.activation(st[:, 32 + c0:32 + c1], st[:, 16 + c0:16 + c1],
                                AF.Ln, bias=self.eps))
        kb.s(_s1)
        t1 = cs.tok()

        def _s2(s):
            wait(s, t1)
            cs.inc(s.activation(st[:, 32 + c0:32 + c1], st[:, 32 + c0:32 + c1],
                                AF.Exp, scale=-0.5))
        kb.s(_s2)
        return cs.tok()

    def apply(self, src_ap, dst_ap, col, toks, scaled=False):
        kb, st = self.kb, self.st
        cv = kb.cv
        m = st[:, col:col + 1]
        r = st[:, 32 + col:33 + col]

        def _v5(v):
            wait(v, *toks)
            cv.inc(v.scalar_tensor_tensor(m, m, -1.0, r, ALU.mult, ALU.mult))
        kb.v(_v5)
        vt5 = cv.tok()
        if scaled:
            def _v6(v):
                wait(v, vt5)
                cv.inc(v.tensor_mul(r, r, st[:, 48 + col:49 + col]))
            kb.v(_v6)
            vt5 = cv.tok()

        def _v7(v):
            wait(v, vt5)
            cv.inc(v.tensor_scalar(dst_ap, src_ap, r, m, ALU.mult, ALU.add))
        kb.v(_v7)
        return cv.tok()


def transpose_tile(kb, src_cols_fn, nchunks, ps_bf, dst_cols_fn, idbf,
                   waits, ps_free=None):
    ct, cv = kb.ct, kb.cv

    def _t(t):
        wait(t, ps_free, *waits)
        for c in range(nchunks):
            ins = t.transpose(ps_bf[:, c * 128:(c + 1) * 128], src_cols_fn(c), idbf)
        ct.inc(ins)
    kb.t(_t)
    tt = ct.tok()

    def _v(v):
        wait(v, tt)
        for c in range(nchunks):
            ins = v.tensor_copy(dst_cols_fn(c), ps_bf[:, c * 128:(c + 1) * 128])
        cv.inc(ins)
    kb.v(_v)
    return cv.tok(), tt


def build(debug=(), phases="ABCD"):
    global DEBUG
    DEBUG = tuple(debug)
    ctx = ExitStack()
    nc = bass.Bass("TRN2", target_bir_lowering=False, debug=False)
    inp = {}
    for name, shape in INPUT_SPECS:
        inp[name] = nc.dram_tensor(name, shape, F32, kind="ExternalInput").ap()
    out = nc.dram_tensor("out", [T, D], F32, kind="ExternalOutput").ap()
    w1t_sc = nc.dram_tensor("w1t_sc", [4 * D, D], BF16, kind="Internal").ap()
    w2t_sc = nc.dram_tensor("w2t_sc", [D, 4 * D], BF16, kind="Internal").ap()
    x4_sc = nc.dram_tensor("x4_sc", [T, D], F32, kind="Internal").ap()

    kb = KB(nc, ctx)
    dq, dg, ct, cv, cs, cg = kb.dq, kb.dg, kb.ct, kb.cv, kb.cs, kb.cg

    idbf = kb.sbuf("idbf", [128, 128], BF16)
    nidbf = kb.sbuf("nidbf", [128, 128], BF16)
    ones_b = kb.sbuf("ones_b", [1, 512], BF16)
    eps_t = kb.sbuf("eps_t", [128, 1], F32)
    sm = kb.sbuf("sm", [128, 16], F32)
    ssum = kb.sbuf("ssum", [128, 2], F32)
    sq_scr = kb.sbuf("sq_scr", [128, D], F32)
    bqkv_row = kb.sbuf("bqkv_row", [1, 3 * D], BF16)
    wv_pp = kb.sbuf("wv_pp", [128, 24], F32)

    dpro = kb.dsem("pro")

    def _(g):
        dpro.inc(g.dma_start(out=bqkv_row[:], in_=inp["L0_attn_b_qkv"]))
        dpro.inc(g.dma_start(out=wv_pp[:].rearrange("p (c k) -> p c k", k=3),
                           in_=inp["L0_attn_W_vconv"].rearrange("(c p) k -> p c k", p=128)))
        cg.inc(g.iota(idbf[:], [[1, 128]], base=0, channel_multiplier=-1,
                      allow_small_or_imprecise_dtypes=True))
    kb.g(_)
    g0 = cg.tok()

    def _(g):
        wait(g, g0)
        cg.inc(g.tensor_single_scalar(idbf[:], idbf[:], 0.0, ALU.is_equal))
    kb.g(_)
    g1 = cg.tok()

    def _(g):
        wait(g, g1)
        cg.inc(g.tensor_scalar_mul(nidbf[:], idbf[:], -1.0))
    kb.g(_)
    gid = cg.tok()

    def _(v):
        v.memset(ones_b[:], 1.0)
        cv.inc(v.memset(eps_t[:], 1.0 / EPS))
    kb.v(_)
    vc0 = cv.tok()

    def _(v):
        wait(v, vc0)
        cv.inc(v.reciprocal(eps_t[:], eps_t[:]))
    kb.v(_)
    vconst = cv.tok()
    pro_dg = dpro.tok()

    b0_steps = []
    b0_state = {}
    if "B" in phases:
        NW = 4096 * D
        asums = kb.sbuf("asums", [128, 64], F32)
        taus = kb.sbuf("taus", [128, 4], F32)
        tot_sb = kb.sbuf("tot_sb", [1, 2], F32)
        ones_f = kb.sbuf("ones_f", [1, 128], F32)
        ones_c = kb.sbuf("ones_c", [128, 1], F32)

        def _(v):
            v.memset(ones_f[:], 1.0)
            cv.inc(v.memset(ones_c[:], 1.0))
        kb.v(_)
        vof = cv.tok()

        W1r = inp["L0_ffn_W1"].rearrange("(c p) d -> p c d", p=128)
        W2r = inp["L0_ffn_W2"].rearrange("(c p) d -> p c d", p=128)
        NSLOT = 5
        tw = kb.at("tw", [128, NSLOT * 1024], F32, REG1)
        NSLOT1 = 2
        tw1 = kb.at("tw1", [128, NSLOT1 * 1024], F32, REG5)
        absd = kb.at("absd", [128, 1024], BF16, REG5 + NSLOT1 * 4096)
        thr = kb.at("thr", [128, 1024], F32, REG1 + 20480)
        wtb = kb.at("wtb", [128, 2 * 1024], F32, REG1 + 24576)
        dts = [kb.dsem("t%d" % k) for k in range(NSLOT)]
        dts1 = [kb.dsem("u%d" % k) for k in range(NSLOT1)]
        dst_sem = kb.dsem("ts")
        scope_b0ps = ExitStack()
        ps_b0 = kb.psum("ps_b0", [128, 4], F32, scope_b0ps)
        batches = []
        for b in range(32):
            batches.append((W1r[:, b:b + 1, :], b))
        for b in range(32):
            c, h = b // 4, b % 4
            batches.append((W2r[:, c:c + 1, h * 1024:(h + 1) * 1024], 32 + b))
        b0_state.update(slot_free={k: None for k in range(NSLOT)},
                        p1_free={k: None for k in range(NSLOT1)},
                        abs_prev=None, g_prev=None, st_tok=None, st_tok2=None,
                        tau_tok=None)

        def emit_load(src_ap, col):
            slot = col % NSLOT
            twa = tw[:, slot * 1024:(slot + 1) * 1024]

            def _(sync, w=b0_state["slot_free"][slot], ds=dts[slot],
                  w2=(b0_state.get("reg1_free") if col < NSLOT else None)):
                wait(sync, w, w2)
                ds.inc(sync.dma_start(out=twa.rearrange("p (c d) -> p c d",
                                                        c=src_ap.shape[1]),
                                      in_=src_ap))
            kb.q(_)
            return twa, dts[slot].tok()

        def mk_pass1(src_ap, col):
            load = {}
            slot = col % NSLOT1

            def prefetch():
                twa = tw1[:, slot * 1024:(slot + 1) * 1024]

                def _(sync, w=b0_state["p1_free"][slot], ds=dts1[slot]):
                    wait(sync, w)
                    ds.inc(sync.dma_start(
                        out=twa.rearrange("p (c d) -> p c d", c=src_ap.shape[1]),
                        in_=src_ap))
                kb.q(_)
                load["twa"], load["tok"] = twa, dts1[slot].tok()

            def step():
                def _(s, ltok=load["tok"], twa=load["twa"],
                      pv=b0_state["abs_prev"]):
                    wait(s, ltok, pv)
                    cs.inc(s.activation(absd[:], twa, AF.Abs,
                                        accum_out=asums[:, col:col + 1]))
                kb.s(_)
                b0_state["abs_prev"] = cs.tok()
                b0_state["p1_free"][slot] = cs.tok()
            return prefetch, step

        def tau_step():
            red_all = cs.tok()

            def _(v, w=red_all):
                wait(v, w)
                v.tensor_reduce(sm[:, 9:10], asums[:, 0:32], AX.X, ALU.add)
                cv.inc(v.tensor_reduce(sm[:, 10:11], asums[:, 32:64], AX.X, ALU.add))
            kb.v(_)
            tsum_tok = cv.tok()

            def _(t, w=tsum_tok, w2=vof):
                wait(t, w, w2)
                t.matmul(ps_b0[0:1, 0:1], sm[:, 9:10], ones_c[:], start=True, stop=True)
                ct.inc(t.matmul(ps_b0[0:1, 1:2], sm[:, 10:11], ones_c[:],
                                start=True, stop=True))
            kb.t(_)
            ttot = ct.tok()

            def _(v, w=ttot):
                wait(v, w)
                cv.inc(v.tensor_scalar_mul(tot_sb[0:1, 0:2], ps_b0[0:1, 0:2], 1.0 / NW))
            kb.v(_)
            vv1 = cv.tok()

            def _(v, w=vv1):
                wait(v, w)
                cv.inc(v.tensor_scalar(tot_sb[0:1, 0:2], tot_sb[0:1, 0:2], 1e-8, 0.5,
                                       ALU.max, ALU.mult))
            kb.v(_)
            vv2 = cv.tok()

            def _(t, w=vv2):
                wait(t, w)
                ct.inc(t.matmul(ps_b0[:, 2:4], ones_f[:], tot_sb[0:1, 0:2],
                                start=True, stop=True))
            kb.t(_)
            tbc = ct.tok()

            def _(v, w=tbc):
                wait(v, w)
                v.tensor_copy(taus[:, 0:1], ps_b0[:, 2:3])
                v.tensor_copy(taus[:, 2:3], ps_b0[:, 3:4])
                cv.inc(v.tensor_scalar_mul(taus[:, 1:2], ps_b0[:, 2:3], -1.0))
            kb.v(_)
            vt_ = cv.tok()

            def _(v, w=vt_):
                wait(v, w)
                cv.inc(v.tensor_scalar_mul(taus[:, 3:4], ps_b0[:, 3:4], -1.0))
            kb.v(_)
            b0_state["tau_tok"] = cv.tok()

        def tau_switch_step():
            pass

        W1s = w1t_sc.rearrange("(c p) d -> p c d", p=128)
        W2s = w2t_sc.rearrange("(c p) d -> p c d", p=128)

        def mk_pass2(src_ap, col):
            load = {}

            def prefetch():
                load["twa"], load["tok"] = emit_load(src_ap, 64 + col)

            def step():
                m = 0 if col < 32 else 1
                slot = (64 + col) % NSLOT
                ws = col % 2
                wslice = wtb[:, ws * 1024:(ws + 1) * 1024]
                if col < 32:
                    dst = W1s[:, col:col + 1, :]
                else:
                    c, h = (col - 32) // 4, (col - 32) % 4
                    dst = W2s[:, c:c + 1, h * 1024:(h + 1) * 1024]
                twa, ltok = load["twa"], load["tok"]

                def _(v, m=m, w=ltok, pv=b0_state["g_prev"],
                      tt=b0_state["tau_tok"]):
                    wait(v, w, pv, tt)
                    cv.inc(v.tensor_scalar(thr[:], twa, taus[:, 2 * m + 1:2 * m + 2],
                                           1.0, ALU.is_ge, ALU.subtract))
                kb.v(_)
                g1 = cv.tok()

                def _(v, m=m, w=g1, st=b0_state["st_tok2"] if ws == 0 else
                      b0_state["st_tok"], wslice=wslice):
                    wait(v, w, st)
                    cv.inc(v.scalar_tensor_tensor(wslice, twa,
                                                  taus[:, 2 * m:2 * m + 1],
                                                  thr[:], ALU.is_gt, ALU.add))
                kb.v(_)
                g4 = cv.tok()
                b0_state["g_prev"] = g4
                b0_state["slot_free"][slot] = g4

                def _(g, dst=dst, w=g4, wslice=wslice):
                    wait(g, w)
                    dst_sem.inc(g.dma_start(
                        out=dst, in_=wslice.rearrange("p (c d) -> p c d",
                                                      c=dst.shape[1])))
                kb.g(_)
                b0_state["st_tok2"] = b0_state["st_tok"]
                b0_state["st_tok"] = dst_sem.tok()
            return prefetch, step

        _p1 = [mk_pass1(s_, c_) for s_, c_ in batches]
        _p2 = [mk_pass2(s_, c_) for s_, c_ in batches]
        b0p1_steps = []
        for k in range(NSLOT1):
            _p1[k][0]()
        for k in range(64):
            def mk(k=k):
                def step():
                    _p1[k][1]()
                    if k + NSLOT1 < 64:
                        _p1[k + NSLOT1][0]()
                return step
            b0p1_steps.append(mk())
        b0_steps.append(tau_step)

        def _pf2_initial():
            for k in range(NSLOT):
                _p2[k][0]()
        b0_steps.append(_pf2_initial)
        for k in range(64):
            def mk(k=k):
                def step():
                    _p2[k][1]()
                    if k + NSLOT < 64:
                        _p2[k + NSLOT][0]()
                return step
            b0_steps.append(mk())


    q_fm = kb.at("q_fm", [128, ND * T], BF16, REG2)
    k_fm = kb.at("k_fm", [128, ND * T], BF16, REG3)
    v_fm = kb.at("v_fm", [128, ND * T], BF16, REG4)
    C_tm = kb.at("C_tm", [128, NT * D], BF16, REG5)
    n1_fm = kb.at("n1_fm", [128, ND * T], BF16, REG1)

    lnst = kb.sbuf("lnst", [128, 64], F32)
    lnb = LNB(kb, lnst, sq_scr[:], eps_t[:, 0:1])
    xstage = q_fm
    dxa = kb.dsem("xa")

    def _(g):
        dxa.inc(g.dma_start(
            out=xstage[:].rearrange("p (i d) -> p i d", i=NT),
            in_=inp["L0_attn_x"].rearrange("(i p) d -> p i d", p=128)))
    kb.g(_)
    xload = dxa.tok()

    with ExitStack() as pha:
        n1_tm = kb.at("n1_tm", [128, D], BF16, WS + 8192)
        ps_tr = kb.psum("ps_tr", [128, 1024], BF16, pha)
        stok = None
        for i in range(NT):
            for _k in range(2):
                if b0p1_steps:
                    b0p1_steps.pop(0)()
            stok = lnb.stats(xstage[:, i * D:(i + 1) * D], i, [xload], prev=stok)
        rtok = lnb.rstd_batch(0, 16, [stok])
        evtok = None
        trtok = None
        aptok = None
        for i in range(NT):
            aptok = lnb.apply(xstage[:, i * D:(i + 1) * D], n1_tm[:], i,
                              [rtok, trtok])
            evtok, trtok = transpose_tile(
                kb, lambda c: n1_tm[:, c * 128:(c + 1) * 128], ND, ps_tr[:],
                lambda c, i=i: n1_fm[:, c * T + i * 128: c * T + (i + 1) * 128],
                idbf[:], [aptok, gid], ps_free=evtok)
        n1_done = evtok
        kb.dbg("n1_fm", n1_fm[:], n1_done)
    kb.flush_dbg()
    kb.barrier(dma=False)

    with ExitStack() as pha:
        wbuf = kb.at("wbuf", [128, 2 * 8 * 512], BF16, WS)
        ps_q = [kb.psum("ps_qkv0", [128, 512], F32, pha),
                kb.psum("ps_qkv1", [128, 512], F32, pha)]
        Wq = inp["L0_attn_W_qkv"].rearrange("(c p) f -> p c f", p=128)
        dws = [kb.dsem("w0"), kb.dsem("w1")]
        slot_tok = {0: None, 1: None}
        bank_tok = {0: None, 1: None}
        for grp in range(6):
            slot = grp % 2

            def _(g, grp=grp, slot=slot, w=slot_tok[slot], ds=dws[slot]):
                wait(g, w)
                ds.inc(g.dma_start(
                    out=wbuf[:, slot * 4096:(slot + 1) * 4096].rearrange(
                        "p (c f) -> p c f", c=8),
                    in_=Wq[:, :, grp * 512:(grp + 1) * 512]))
            kb.g(_)
            wtok = dws[slot].tok()

            for fc4 in range(4):
                if b0p1_steps:
                    b0p1_steps.pop(0)()
                if grp >= 4 and b0p1_steps:
                    b0p1_steps.pop(0)()
                fc = grp * 4 + fc4
                mat, lc = fc // 8, fc % 8
                dst = (q_fm, k_fm, v_fm)[mat]
                bk = fc4 % 2
                pb = ps_q[bk]
                for tg in range(4):
                    first = (grp == 0 and fc4 == 0 and tg == 0)
                    def _(t, pb=pb, slot=slot, fc4=fc4, tg=tg, wtok=wtok,
                          fc=fc, w=bank_tok[bk], first=first):
                        wait(t, wtok, w)
                        if first:
                            wait(t, pro_dg)
                        for kc in range(8):
                            t.matmul(pb[:],
                                     wbuf[:, slot * 4096 + kc * 512 + fc4 * 128:
                                          slot * 4096 + kc * 512 + (fc4 + 1) * 128],
                                     n1_fm[:, kc * T + tg * 512: kc * T + (tg + 1) * 512],
                                     start=(kc == 0), stop=False)
                        ct.inc(t.matmul(pb[:], bqkv_row[0:1, fc * 128:(fc + 1) * 128],
                                        ones_b[0:1, :], start=False, stop=True))
                    kb.t(_)
                    mtok = ct.tok()

                    def _(v, pb=pb, mat=mat, lc=lc, dst=dst, tg=tg, mtok=mtok):
                        wait(v, mtok)
                        dd = dst[:, lc * T + tg * 512: lc * T + (tg + 1) * 512]
                        if mat == 0:
                            cv.inc(v.tensor_scalar_mul(dd, pb[:], 1.0 / 32.0))
                        else:
                            cv.inc(v.tensor_copy(dd, pb[:]))
                    kb.v(_)
                    bank_tok[bk] = cv.tok()
            slot_tok[slot] = cv.tok()
        qkv_done = cv.tok()
        b0_state["reg1_free"] = qkv_done
        kb.dbg("q_fm", q_fm[:], qkv_done)
        kb.dbg("k_fm", k_fm[:], qkv_done)
        kb.dbg("v_fm", v_fm[:], qkv_done)
    kb.flush_dbg()
    kb.barrier(dma=False)

    with ExitStack() as pha:
        cbuf = kb.at("cbuf", [128, T], BF16, WS)
        ps_trc = kb.psum("ps_trc", [128, 1024], BF16, pha)
        ev = None
        ttk = None
        for c in range(ND):
            vchunk = v_fm[:, c * T:(c + 1) * T]
            w0 = wv_pp[:, c * 3 + 0:c * 3 + 1]
            w1 = wv_pp[:, c * 3 + 1:c * 3 + 2]
            w2 = wv_pp[:, c * 3 + 2:c * 3 + 3]

            def _(v, vchunk=vchunk, w1=w1, ev=ev):
                wait(v, ev, pro_dg, qkv_done)
                cv.inc(v.tensor_scalar_mul(cbuf[:], vchunk, w1))
            kb.v(_)
            t1 = cv.tok()

            def _(v, vchunk=vchunk, w0=w0, t1=t1):
                wait(v, t1)
                cv.inc(v.scalar_tensor_tensor(cbuf[:, 1:T], vchunk[:, 0:T - 1], w0,
                                              cbuf[:, 1:T], ALU.mult, ALU.add))
            kb.v(_)
            t2 = cv.tok()

            def _(v, vchunk=vchunk, w2=w2, t2=t2):
                wait(v, t2)
                cv.inc(v.scalar_tensor_tensor(cbuf[:, 0:T - 1], vchunk[:, 1:T], w2,
                                              cbuf[:, 0:T - 1], ALU.mult, ALU.add))
            kb.v(_)
            t3 = cv.tok()

            for half in range(2):
                ev, ttk = transpose_tile(
                    kb, lambda j, half=half: cbuf[:, half * 1024 + j * 128:
                                                  half * 1024 + (j + 1) * 128],
                    8, ps_trc[:],
                    lambda j, c=c, half=half: C_tm[:, (half * 8 + j) * D + c * 128:
                                                   (half * 8 + j) * D + (c + 1) * 128],
                    idbf[:], [t3, gid], ps_free=ev)
        c_done = ev
        kb.dbg("C_tm", C_tm[:], c_done)
    kb.flush_dbg()
    kb.barrier(dma=False)

    def outln_ap(i):
        return bass.AP(q_fm, 128 * i, [[q_fm.shape[1], 128], [T, ND], [1, 128]])

    with ExitStack() as pha:
        S_sb = kb.at("S_sb", [128, T], F32, WS)
        P_sb = kb.at("P_sb", [128, T], BF16, WS + 8192)
        AT_sb = kb.at("AT_sb", [128, T], BF16, WS + 12288)
        Dw_sb = kb.at("Dw_sb", [128, D], BF16, WS + 16384)
        D_sb = kb.at("D_sb", [128, D], BF16, WS + 18432)
        B_sb = kb.sbuf("B_sb", [128, 128], BF16, pha)
        ps_s = [kb.psum("ps_s%d" % k, [128, 512], F32, pha) for k in range(4)]
        ps_at = kb.psum("ps_at", [128, 1024], BF16, pha)
        ps_o = kb.psum("ps_o", [128, 1024], F32, pha)

        sbank_tok = {k: None for k in range(4)}
        at_ps_tok = None
        p_free = None
        s_free = None
        dw_free = None
        out_ps_free = None
        at_free = None
        stok = None
        sexp_prev = None

        def emit_scores(i):
            W = 128 * (i + 1)
            out = []
            for b in range((W + 511) // 512):
                w0c = b * 512
                w1c = min(W, w0c + 512)
                bk = b % 4

                def _(t, w0c=w0c, w1c=w1c, bk=bk, i=i, w=sbank_tok[bk]):
                    wait(t, w)
                    for kc in range(ND):
                        ins = t.matmul(
                            ps_s[bk][:, 0:w1c - w0c],
                            q_fm[:, kc * T + i * 128: kc * T + (i + 1) * 128],
                            k_fm[:, kc * T + w0c: kc * T + w1c],
                            start=(kc == 0), stop=(kc == ND - 1))
                    ct.inc(ins)
                kb.t(_)
                out.append((bk, w0c, w1c, ct.tok()))
            return out

        pend = emit_scores(0)
        for i in range(NT):
            W = 128 * (i + 1)
            ev_toks = []
            for n, (bk, w0c, w1c, mtok) in enumerate(pend):
                def _(v, bk=bk, w0c=w0c, w1c=w1c, mtok=mtok,
                      sf=(s_free if n == 0 else None)):
                    wait(v, mtok, sf)
                    cv.inc(v.tensor_copy(S_sb[:, w0c:w1c], ps_s[bk][:, 0:w1c - w0c]))
                kb.v(_)
                sbank_tok[bk] = cv.tok()
                ev_toks.append(cv.tok())

            def _(g, i=i, ev=ev_toks[-1]):
                wait(g, ev)
                cg.inc(g.affine_select(S_sb[:, i * 128:(i + 1) * 128],
                                       S_sb[:, i * 128:(i + 1) * 128],
                                       [[-1, 128]], ALU.is_ge, -1e30,
                                       base=0, channel_multiplier=1))
            kb.g(_)
            gmask = cg.tok()

            def _(v, W=W, gmask=gmask, ev=list(ev_toks), sp=sexp_prev):
                wait(v, gmask, sp, *ev)
                cv.inc(v.reduce_max(sm[:, 8:9], S_sb[:, 0:W], AX.X, negate=True))
            kb.v(_)
            vmax = cv.tok()

            def _(s, W=W, vmax=vmax, pf=p_free):
                wait(s, vmax, pf)
                cs.inc(s.activation(P_sb[:, 0:W], S_sb[:, 0:W], AF.Exp,
                                    bias=sm[:, 8:9], accum_out=ssum[:, 0:1]))
            kb.s(_)
            sexp = cs.tok()
            sexp_prev = sexp
            s_free = sexp

            if i + 1 < NT:
                pend = emit_scores(i + 1)
            while b0p1_steps:
                b0p1_steps.pop(0)()
            for _k in range(6 if i else 3):
                if b0_steps:
                    b0_steps.pop(0)()

            at_toks = []
            for bg in range((i + 8) // 8):
                j0, j1 = bg * 8, min(i + 1, bg * 8 + 8)

                def _(t, j0=j0, j1=j1, sexp=sexp, w=at_ps_tok,
                      af=(at_free if bg == 0 else None)):
                    wait(t, sexp, w, af)
                    for j in range(j0, j1):
                        ins = t.transpose(ps_at[:, (j - j0) * 128:(j - j0 + 1) * 128],
                                          P_sb[:, j * 128:(j + 1) * 128], idbf[:])
                    ct.inc(ins)
                kb.t(_)
                ttok = ct.tok()
                p_free = ttok

                def _(v, j0=j0, j1=j1, ttok=ttok):
                    wait(v, ttok)
                    cv.inc(v.tensor_copy(AT_sb[:, j0 * 128:j1 * 128],
                                         ps_at[:, 0:(j1 - j0) * 128]))
                kb.v(_)
                at_ps_tok = cv.tok()
                at_toks.append(cv.tok())

            def _(v, i=i, at=at_toks[-1]):
                wait(v, at)
                cv.inc(v.tensor_mul(B_sb[:], AT_sb[:, i * 128:(i + 1) * 128], nidbf[:]))
            kb.v(_)
            btok = cv.tok()

            def _(v, i=i, w=dw_free):
                wait(v, w, qkv_done)
                t0 = i * 128 + 1
                wlen = min(T, (i + 1) * 128 + 1) - t0
                for c in range(ND):
                    if wlen < 128:
                        v.memset(Dw_sb[:, c * 128 + wlen:(c + 1) * 128], 0.0)
                    ins = v.tensor_scalar_mul(Dw_sb[:, c * 128:c * 128 + wlen],
                                              v_fm[:, c * T + t0:c * T + t0 + wlen],
                                              wv_pp[:, c * 3 + 2:c * 3 + 3])
                cv.inc(ins)
            kb.v(_)
            dwtok = cv.tok()

            dtr, dw_free = transpose_tile(
                kb, lambda c: Dw_sb[:, c * 128:(c + 1) * 128], ND, ps_at[:],
                lambda c: D_sb[:, c * 128:(c + 1) * 128],
                idbf[:], [dwtok, at_free], ps_free=at_ps_tok)
            at_ps_tok = dtr

            def _(t, i=i, at=list(at_toks), btok=btok, dtr=dtr, w=out_ps_free):
                wait(t, w, btok, dtr, *at)
                for half in range(2):
                    for j in range(i + 1):
                        t.matmul(ps_o[:, half * 512:(half + 1) * 512],
                                 AT_sb[:, j * 128:(j + 1) * 128],
                                 C_tm[:, j * D + half * 512: j * D + (half + 1) * 512],
                                 start=(j == 0), stop=False)
                    ins = t.matmul(ps_o[:, half * 512:(half + 1) * 512], B_sb[:],
                                   D_sb[:, half * 512:(half + 1) * 512],
                                   start=False, stop=True)
                ct.inc(ins)
            kb.t(_)
            otok = ct.tok()
            at_free = otok

            def _(v, sexp=sexp):
                wait(v, sexp)
                cv.inc(v.reciprocal(ssum[:, 1:2], ssum[:, 0:1]))
            kb.v(_)
            rtok = cv.tok()

            def _(v, i=i, w=otok):
                wait(v, w)
                cv.inc(v.tensor_copy(outln_ap(i), ps_o[:]))
            kb.v(_)
            out_ps_free = cv.tok()

            stok = lnb.stats(outln_ap(i), i, [out_ps_free, rtok],
                             f_ap=ssum[:, 1:2], prev=stok)
            if i == NT - 1 and "a3_oraw" in DEBUG:
                oraw_sc = kb.sbuf("oraw_sc", [128, ND * 128], F32, pha)
                p_sc = kb.sbuf("p_sc", [128, T], BF16, pha)

                def _(v, i=i, w=out_ps_free, se=sexp):
                    wait(v, w, se)
                    cv.inc(v.tensor_copy(oraw_sc[:], outln_ap(i)))
                    cv.inc(v.tensor_copy(p_sc[:], P_sb[:]))
                kb.v(_)
                a3dbg_tok = cv.tok()
                kb.dbg("a3_oraw", oraw_sc[:], a3dbg_tok)
                kb.dbg("a3_p", p_sc[:], a3dbg_tok)
                kb.dbg("a3_ssum", ssum[:], a3dbg_tok)
            else:
                a3dbg_tok = None
        while b0_steps:
            b0_steps.pop(0)()
        rtok_b = lnb.rstd_batch(0, 16, [stok])
        aptok = None
        for i in range(NT):
            aptok = lnb.apply(outln_ap(i), outln_ap(i), i,
                              [rtok_b, aptok, a3dbg_tok if i == 0 else None],
                              scaled=True)
        a3_done = aptok
        kb.dbg("outln_q", q_fm[:], a3_done)
        kb.dbg("a3_lnst", lnst[:], a3_done)
        kb.flush_dbg()

    kb.barrier(dma=False)
    if "B" in phases:
        scope_b0ps.close()

    if "B" in phases:
        scope_b1 = ExitStack()
        scope_b1tr = ExitStack()
        X2 = kb.at("X2", [128, NT * D], F32, REG4)
        act_fm = kb.at("act_fm", [128, ND * T], BF16, REG3)
        xt2 = kb.at("xt2", [128, 2 * D], F32, WS)
        tmb = kb.at("tmb", [128, D], BF16, WS + 16384)
        ps_tr2 = kb.psum("ps_tr2", [128, 1024], BF16, scope_b1tr)
        dx2 = [kb.dsem("x0"), kb.dsem("x1")]
        slot_done = {0: None, 1: None}
        stok = None
        for i in range(NT):
            slot = i % 2
            xa = xt2[:, slot * D:(slot + 1) * D]

            def _(sync, i=i, xa=xa, w=slot_done[slot], ds=dx2[slot]):
                wait(sync, w)
                ds.inc(sync.dma_start(out=xa, in_=inp["L0_attn_x"][i * 128:(i + 1) * 128, :]))
            kb.q(_)
            ldtok = dx2[slot].tok()

            def _(v, i=i, xa=xa, ldtok=ldtok):
                wait(v, ldtok)
                cv.inc(v.tensor_add(X2[:, i * D:(i + 1) * D], xa, outln_ap(i)))
            kb.v(_)
            addtok = cv.tok()
            slot_done[slot] = addtok

            stok = lnb.stats(X2[:, i * D:(i + 1) * D], i, [addtok], prev=stok)
        rtok_b = lnb.rstd_batch(0, 16, [stok])
        aptok = None
        evtok = None
        trtok = None
        for i in range(NT):
            aptok = lnb.apply(X2[:, i * D:(i + 1) * D], tmb[:], i, [rtok_b, trtok])
            evtok, trtok = transpose_tile(
                kb, lambda c: tmb[:, c * 128:(c + 1) * 128], ND, ps_tr2[:],
                lambda c, i=i: act_fm[:, c * T + i * 128: c * T + (i + 1) * 128],
                idbf[:], [aptok], ps_free=evtok)
        b1_done = evtok
        kb.dbg("x2", X2[:], aptok)
        kb.dbg("n2_fm", act_fm[:], b1_done)
        kb.flush_dbg()
        kb.barrier()
        scope_b1tr.close()

        h_g = kb.at("h_g", [128, 32 * 512], BF16, REG1)
        w1tb = kb.at("w1tb", [128, 2 * 8 * 512], BF16, REG2)
        b1row = kb.at("b1row", [1, 4096], BF16, REG2 + 16384)
        b2row = kb.at("b2row", [1, 1024], BF16, REG2 + 24576)
        w2tb = kb.at("w2tb", [128, 4 * 1024], BF16, WS)
        out2g = kb.at("out2g", [128, 4 * 1024], BF16, WS + 8192)
        smB = kb.sbuf("smB", [128, 12], F32)
        drow = kb.dsem("rows")

        def _(g):
            drow.inc(g.dma_start(out=b1row[:], in_=inp["L0_ffn_b1"]))
            drow.inc(g.dma_start(out=b2row[:], in_=inp["L0_ffn_b2"]))
        kb.g(_)
        rows_tok = drow.tok()

        dw1 = [kb.dsem("tw1a"), kb.dsem("tw1b")]
        dw2 = [kb.dsem("w2a"), kb.dsem("w2b")]
        ps_h = [kb.psum("ps_h0", [128, 1024], F32, scope_b1),
                kb.psum("ps_h1", [128, 1024], F32, scope_b1)]
        ps_o2 = [kb.psum("ps_o2%d" % u, [128, 1024], F32, scope_b1) for u in range(2)]

        w1slot_tok = {0: None, 1: None}
        w2slot_tok = {0: None, 1: None}
        hbank_tok = {0: None, 1: None}
        o2_tok = {0: None, 1: None}
        hg_free = None
        out2g_free = None
        for grp in range(4):
            for p in range(8):
                slot = p % 2

                def _(sync, p=p, slot=slot, w=w1slot_tok[slot], ds=dw1[slot],
                      w2=(b0_state.get("st_tok") if (grp == 0 and p < 2) else None)):
                    wait(sync, w, w2)
                    for dc in range(8):
                        ds.inc(sync.dma_start(
                            out=w1tb[:, slot * 4096 + dc * 512:slot * 4096 + (dc + 1) * 512],
                            in_=w1t_sc[p * 512:(p + 1) * 512, dc * 128:(dc + 1) * 128],
                            transpose=True))
                kb.q(_)
                wtok = dw1[slot].tok()

                for fpr in range(2):
                    bk = fpr % 2
                    pb = ps_h[bk]

                    def _(t, pb=pb, slot=slot, fpr=fpr, p=p, grp=grp, wtok=wtok,
                          w=hbank_tok[bk]):
                        wait(t, wtok, w, rows_tok, b1_done)
                        for half in range(2):
                            fl = 2 * fpr + half
                            fc = 4 * p + fl
                            for kc in range(8):
                                t.matmul(pb[:, half * 512:(half + 1) * 512],
                                         w1tb[:, slot * 4096 + kc * 512 + fl * 128:
                                              slot * 4096 + kc * 512 + (fl + 1) * 128],
                                         act_fm[:, kc * T + grp * 512:
                                                kc * T + (grp + 1) * 512],
                                         start=(kc == 0), stop=False)
                            ins = t.matmul(pb[:, half * 512:(half + 1) * 512],
                                           b1row[0:1, fc * 128:(fc + 1) * 128],
                                           ones_b[0:1, :], start=False, stop=True)
                        ct.inc(ins)
                    kb.t(_)
                    mtok = ct.tok()

                    def _(s, pb=pb, p=p, fpr=fpr, mtok=mtok, hf=hg_free,
                          bf=(b0_state.get("st_tok") if grp == 0 and p < 2 else None),
                          bv=(b0_state.get("vthr_tok") if grp == 0 and p < 2 else None)):
                        wait(s, mtok, hf, bf, bv)
                        fc0 = 4 * p + 2 * fpr
                        cs.inc(s.activation(h_g[:, fc0 * 512:(fc0 + 2) * 512], pb[:],
                                            AF.Gelu))
                    kb.s(_)
                    hbank_tok[bk] = cs.tok()
                w1slot_tok[slot] = ct.tok()
            h_done = cs.tok()
            if grp == 3:
                kb.dbg("h_g", h_g[:], h_done)

            ev_toks = [None] * 4
            for pair in range(2):
                for fc in range(32):
                    slot = fc % 2
                    wslice = w2tb[:, slot * 1024:(slot + 1) * 1024]
                    def _(sync, fc=fc, wslice=wslice, w=w2slot_tok[slot], ds=dw2[slot],
                          w2=(b0_state.get("st_tok") if (grp == 0 and pair == 0 and fc < 2) else None)):
                        wait(sync, w, w2)
                        ds.inc(sync.dma_start(out=wslice,
                                              in_=w2t_sc[:, fc * 128:(fc + 1) * 128],
                                              transpose=True))
                    kb.q(_)
                    w2tok = dw2[slot].tok()

                    def _(t, fc=fc, pair=pair, wslice=wslice, w2tok=w2tok,
                          h_done=h_done,
                          ot=[o2_tok[u] for u in range(2)] if fc == 0 else []):
                        wait(t, w2tok, h_done, *ot)
                        for u in range(2):
                            sub = pair * 2 + u
                            for half in range(2):
                                ins = t.matmul(
                                    ps_o2[u][:, half * 512:(half + 1) * 512],
                                    h_g[:, fc * 512 + sub * 128:fc * 512 + (sub + 1) * 128],
                                    wslice[:, half * 512:(half + 1) * 512],
                                    start=(fc == 0), stop=False)
                        if fc == 31:
                            for u in range(2):
                                for half in range(2):
                                    ins = t.matmul(
                                        ps_o2[u][:, half * 512:(half + 1) * 512],
                                        ones_b[0:1, 0:128],
                                        b2row[0:1, half * 512:(half + 1) * 512],
                                        start=False, stop=True)
                        ct.inc(ins)
                    kb.t(_)
                    w2slot_tok[slot] = ct.tok()
                o2done = ct.tok()

                for u in range(2):
                    sub = pair * 2 + u

                    def _(s, u=u, sub=sub, w=o2done, of=out2g_free):
                        wait(s, w, of)
                        cs.inc(s.activation(out2g[:, sub * D:(sub + 1) * D],
                                            ps_o2[u][:], AF.Identity))
                    kb.s(_)
                    ev_toks[sub] = cs.tok()
                    o2_tok[u] = cs.tok()

            stat_toks = []
            for sub in range(4):
                evt = ev_toks[sub]

                def _(v, sub=sub, w=evt):
                    wait(v, w)
                    cv.inc(v.tensor_reduce(smB[:, sub:sub + 1],
                                           out2g[:, sub * D:(sub + 1) * D],
                                           AX.X, ALU.add))
                kb.v(_)
                vt = cv.tok()

                def _(s, sub=sub, w=vt):
                    wait(s, w)
                    cs.inc(s.activation(sq_scr[:], out2g[:, sub * D:(sub + 1) * D],
                                        AF.Square, accum_out=smB[:, 4 + sub:5 + sub]))
                kb.s(_)
                sqt = cs.tok()

                def _(v, sub=sub, w=sqt):
                    wait(v, w)
                    v.tensor_scalar_mul(smB[:, sub:sub + 1], smB[:, sub:sub + 1], 1.0 / D)
                    cv.inc(v.tensor_scalar_mul(smB[:, 4 + sub:5 + sub],
                                               smB[:, 4 + sub:5 + sub], 1.0 / D))
                kb.v(_)
                vt2_ = cv.tok()

                def _(v, sub=sub, w=vt2_):
                    wait(v, w)
                    cv.inc(v.tensor_tensor(smB[:, 8 + sub:9 + sub],
                                           smB[:, sub:sub + 1], smB[:, sub:sub + 1],
                                           ALU.mult))
                kb.v(_)
                vt3_ = cv.tok()

                def _(v, sub=sub, w=vt3_):
                    wait(v, w)
                    cv.inc(v.tensor_sub(smB[:, 4 + sub:5 + sub],
                                        smB[:, 4 + sub:5 + sub], smB[:, 8 + sub:9 + sub]))
                kb.v(_)
                stat_toks.append(cv.tok())

            def _(s, w=list(stat_toks)):
                wait(s, *w)
                cs.inc(s.activation(smB[:, 8:12], smB[:, 4:8], AF.Ln,
                                    bias=eps_t[:, 0:1]))
            kb.s(_)
            r1 = cs.tok()

            def _(s, w=r1):
                wait(s, w)
                cs.inc(s.activation(smB[:, 8:12], smB[:, 8:12], AF.Exp, scale=-0.5))
            kb.s(_)
            r2 = cs.tok()

            ap_toks = []
            for sub in range(4):
                i = grp * 4 + sub

                def _(v, sub=sub, w=r2):
                    wait(v, w)
                    cv.inc(v.scalar_tensor_tensor(smB[:, sub:sub + 1],
                                                  smB[:, sub:sub + 1], -1.0,
                                                  smB[:, 8 + sub:9 + sub],
                                                  ALU.mult, ALU.mult))
                kb.v(_)
                bt = cv.tok()

                def _(s, sub=sub, i=i, w=bt):
                    wait(s, w)
                    cs.inc(s.activation(tmb[:], out2g[:, sub * D:(sub + 1) * D],
                                        AF.Identity, bias=smB[:, sub:sub + 1],
                                        scale=smB[:, 8 + sub:9 + sub]))
                kb.s(_)
                apt = cs.tok()

                def _(v, i=i, w=apt):
                    wait(v, w)
                    cv.inc(v.tensor_add(X2[:, i * D:(i + 1) * D],
                                        X2[:, i * D:(i + 1) * D], tmb[:]))
                kb.v(_)
                ap_toks.append(cv.tok())
            out2g_free = cv.tok()
            hg_free = o2done
        b2_done = cv.tok()
        kb.dbg("x3", X2[:], b2_done)
        kb.flush_dbg()
        kb.barrier()
        scope_b1.close()


    if "C" in phases:
        scl = kb.sbuf("scl", [1, 16], F32)
        g8s = kb.sbuf("g8s", [1, 16], F32)
        lnd_pp = kb.sbuf("lnd_pp", [128, 1], F32)
        osc_pp = kb.sbuf("osc_pp", [128, 1], F32)
        ones_f2 = kb.sbuf("ones_f2", [1, 128], F32)
        basis_b = kb.at("basis_b", [128, 8 * 256], BF16, REG1)
        cf_b = kb.at("cf_b", [128, 4 * 2 * 256], BF16, REG1 + 4096)
        basisT = kb.at("basisT", [128, 2 * 1024], BF16, REG1 + 8192)
        cfT = kb.at("cfT", [128, 4 * 2 * 256], BF16, REG1 + 12288)
        qw_b = kb.at("qw_b", [128, 8 * 256], BF16, REG1 + 16384)
        kw_b = kb.at("kw_b", [128, 8 * 256], BF16, REG1 + 20480)
        vw_b = kb.at("vw_b", [128, 8 * 256], BF16, REG1 + 24576)
        owT_b = kb.at("owT_b", [128, 2 * 1024], BF16, REG1 + 28672)
        dl1 = kb.dsem("l1a")

        def _(g):
            dl1.inc(g.dma_start(out=scl[0:1, 0:1], in_=inp["L1_attn_decay_logit"]))
            dl1.inc(g.dma_start(out=scl[0:1, 1:2], in_=inp["L1_attn_out_scale"]))
            dl1.inc(g.dma_start(out=g8s[0:1, 0:8], in_=inp["L1_ffn_smha_gates"]))
            dl1.inc(g.dma_start(out=g8s[0:1, 8:16], in_=inp["L1_ffn_moe_gates"]))
            dl1.inc(g.dma_start(
                out=basis_b[:].rearrange("p (c b) -> p c b", c=8),
                in_=inp["L1_attn_basis"].rearrange("(c p) b -> p c b", p=128)))
            for x, name in enumerate(("q", "k", "v", "o")):
                dl1.inc(g.dma_start(
                    out=cf_b[:, x * 512:(x + 1) * 512].rearrange("p (c b) -> p c b", c=2),
                    in_=inp["L1_attn_%s_coeffs" % name].rearrange("(c p) b -> p c b", p=128)))
        kb.g(_)
        l1tok = dl1.tok()

        def _(v):
            cv.inc(v.memset(ones_f2[:], 1.0))
        kb.v(_)
        vo2 = cv.tok()

        def _(s, w=l1tok):
            wait(s, w)
            s.activation(scl[0:1, 2:3], scl[0:1, 0:1], AF.Sigmoid)
            s.activation(g8s[0:1, 0:8], g8s[0:1, 0:8], AF.Sigmoid)
            cs.inc(s.activation(g8s[0:1, 8:16], g8s[0:1, 8:16], AF.Sigmoid))
        kb.s(_)
        sig_tok = cs.tok()

        def _(v, w=sig_tok):
            wait(v, w)
            cv.inc(v.tensor_scalar_max(scl[0:1, 2:3], scl[0:1, 2:3], 1e-38))
        kb.v(_)
        cl_tok = cv.tok()

        def _(s, w=cl_tok):
            wait(s, w)
            cs.inc(s.activation(scl[0:1, 3:4], scl[0:1, 2:3], AF.Ln))
        kb.s(_)
        lnd_tok = cs.tok()

        scope_c0 = ExitStack()
        ps_c0 = kb.psum("ps_c0", [128, 1024], F32, scope_c0)
        ps_c1 = kb.psum("ps_c1", [128, 1024], F32, scope_c0)
        ps_trC = kb.psum("ps_trC", [128, 1024], BF16, scope_c0)

        def _(t, w=lnd_tok, w2=vo2):
            wait(t, w, w2)
            t.matmul(ps_c0[:, 0:1], ones_f2[:], scl[0:1, 3:4], start=True, stop=True)
            ct.inc(t.matmul(ps_c0[:, 1:2], ones_f2[:], scl[0:1, 1:2], start=True, stop=True))
        kb.t(_)
        tb = ct.tok()

        def _(v, w=tb):
            wait(v, w)
            v.tensor_copy(lnd_pp[:], ps_c0[:, 0:1])
            cv.inc(v.tensor_copy(osc_pp[:], ps_c0[:, 1:2]))
        kb.v(_)
        bc_tok = cv.tok()

        tr_toks = []
        for c in range(8):
            for h in range(2):
                ev, tt = transpose_tile(
                    kb, lambda _c, c=c, h=h: basis_b[:, c * 256 + h * 128:
                                                     c * 256 + (h + 1) * 128],
                    1, ps_trC[:],
                    lambda _c, c=c, h=h: basisT[:, h * 1024 + c * 128:
                                                h * 1024 + (c + 1) * 128],
                    idbf[:], [l1tok, (tr_toks[-1] if tr_toks else None)],
                    ps_free=(tr_toks[-1] if tr_toks else None))
                tr_toks.append(ev)
        for x in range(4):
            for c in range(2):
                for h in range(2):
                    ev, tt = transpose_tile(
                        kb, lambda _c, x=x, c=c, h=h: cf_b[:, x * 512 + c * 256 +
                                                           h * 128: x * 512 + c * 256 +
                                                           (h + 1) * 128],
                        1, ps_trC[:],
                        lambda _c, x=x, c=c, h=h: cfT[:, x * 512 + h * 256 + c * 128:
                                                      x * 512 + h * 256 + (c + 1) * 128],
                        idbf[:], [l1tok, tr_toks[-1]], ps_free=tr_toks[-1])
                    tr_toks.append(ev)
        trC_done = tr_toks[-1]

        wsmall_ev = None
        for x, dstb in enumerate((qw_b, kw_b, vw_b)):
            for vc in range(8):
                def _(t, x=x, vc=vc, w=trC_done, w2=wsmall_ev):
                    wait(t, w, w2)
                    for bc in range(2):
                        ins = t.matmul(ps_c0[:, 0:256],
                                       basisT[:, bc * 1024 + vc * 128:bc * 1024 + (vc + 1) * 128],
                                       cfT[:, x * 512 + bc * 256: x * 512 + (bc + 1) * 256],
                                       start=(bc == 0), stop=(bc == 1))
                    ct.inc(ins)
                kb.t(_)
                mt = ct.tok()

                def _(v, dstb=dstb, vc=vc, mt=mt):
                    wait(v, mt)
                    cv.inc(v.tensor_copy(dstb[:, vc * 256:(vc + 1) * 256], ps_c0[:, 0:256]))
                kb.v(_)
                wsmall_ev = cv.tok()
        for chc in range(2):
            def _(t, chc=chc, w=trC_done, w2=wsmall_ev):
                wait(t, w, w2)
                for bc in range(2):
                    for half in range(2):
                        ins = t.matmul(ps_c1[:, half * 512:(half + 1) * 512],
                                       cfT[:, 3 * 512 + bc * 256 + chc * 128:
                                           3 * 512 + bc * 256 + (chc + 1) * 128],
                                       basisT[:, bc * 1024 + half * 512:
                                              bc * 1024 + (half + 1) * 512],
                                       start=(bc == 0), stop=(bc == 1))
                ct.inc(ins)
            kb.t(_)
            mt = ct.tok()

            def _(v, chc=chc, mt=mt):
                wait(v, mt)
                cv.inc(v.tensor_copy(owT_b[:, chc * 1024:(chc + 1) * 1024], ps_c1[:]))
            kb.v(_)
            wsmall_ev = cv.tok()

        G_sb = kb.at("G_sb", [128, 4096], BF16, REG1)
        m01 = kb.at("m01", [128, 4096], BF16, REG1 + 8192)
        dio = kb.at("dio", [128, 4096], F32, WS)

        def _(g, w=wsmall_ev):
            wait(g, w, trC_done)
            cg.inc(g.iota(dio[:], [[1, 4096]], base=-1920, channel_multiplier=-1,
                          allow_small_or_imprecise_dtypes=True))
        kb.g(_)
        gio = cg.tok()

        def _(v, w=gio):
            wait(v, w)
            cv.inc(v.tensor_single_scalar(m01[:], dio[:], 0.0, ALU.is_gt))
        kb.v(_)
        vm01 = cv.tok()

        def _(v, w=vm01):
            wait(v, w)
            cv.inc(v.tensor_scalar(dio[:], dio[:], 1.0, 0.0, ALU.subtract, ALU.max))
        kb.v(_)
        vexp = cv.tok()

        def _(s, w=vexp, w2=bc_tok):
            wait(s, w, w2)
            cs.inc(s.activation(G_sb[:], dio[:], AF.Exp, scale=lnd_pp[:, 0:1]))
        kb.s(_)
        sG = cs.tok()

        def _(v, w=sG):
            wait(v, w)
            cv.inc(v.tensor_mul(G_sb[:], G_sb[:], m01[:]))
        kb.v(_)
        G_done = cv.tok()
        kb.barrier()
        scope_c0.close()

        scope_c1 = ExitStack()
        tmc = kb.at("tmc", [128, D], BF16, WS + 8192)
        ps_trD = kb.psum("ps_trD", [128, 1024], BF16, scope_c1)
        stok = None
        for i in range(NT):
            stok = lnb.stats(X2[:, i * D:(i + 1) * D], i, [], prev=stok)
        rtok_b = lnb.rstd_batch(0, 16, [stok])
        aptok = None
        evtok = None
        trtok = None
        for i in range(NT):
            aptok = lnb.apply(X2[:, i * D:(i + 1) * D], tmc[:], i, [rtok_b, trtok])
            evtok, trtok = transpose_tile(
                kb, lambda c: tmc[:, c * 128:(c + 1) * 128], ND, ps_trD[:],
                lambda c, i=i: act_fm[:, c * T + i * 128: c * T + (i + 1) * 128],
                idbf[:], [aptok], ps_free=evtok)
        c1_done = evtok
        kb.dbg("n3_fm", act_fm[:], c1_done)
        kb.flush_dbg()
        kb.barrier()
        scope_c1.close()

        scope_c2 = ExitStack()
        queries_b = kb.at("queries_b", [128, 2 * T], BF16, REG2)
        keys_b = kb.at("keys_b", [128, 2 * T], BF16, REG2 + 8192)
        values_b = kb.at("values_b", [128, 2 * T], BF16, REG2 + 16384)
        values_tm = kb.at("values_tm", [128, NT * 256], BF16, REG2 + 24576)
        ps_p = [kb.psum("ps_p0", [128, 512], F32, scope_c2),
                kb.psum("ps_p1", [128, 512], F32, scope_c2)]
        pbank = {0: None, 1: None}
        for x, dstb in enumerate((qw_b, kw_b, vw_b)):
            outb = (queries_b, keys_b, values_b)[x]
            for chc in range(2):
                for tg in range(4):
                    bk = tg % 2

                    def _(t, x=x, chc=chc, tg=tg, bk=bk, dstb=dstb, w=pbank[bk]):
                        wait(t, w)
                        for vc in range(8):
                            ins = t.matmul(
                                ps_p[bk][:],
                                dstb[:, vc * 256 + chc * 128:vc * 256 + (chc + 1) * 128],
                                act_fm[:, vc * T + tg * 512: vc * T + (tg + 1) * 512],
                                start=(vc == 0), stop=(vc == 7))
                        ct.inc(ins)
                    kb.t(_)
                    mt = ct.tok()

                    def _(v, outb=outb, chc=chc, tg=tg, bk=bk, mt=mt):
                        wait(v, mt)
                        cv.inc(v.tensor_copy(
                            outb[:, chc * T + tg * 512: chc * T + (tg + 1) * 512],
                            ps_p[bk][:]))
                    kb.v(_)
                    pbank[bk] = cv.tok()
        qkv1_done = cv.tok()
        kb.dbg("queries_b", queries_b[:], qkv1_done)

        ps_trE = kb.psum("ps_trE", [128, 1024], BF16, scope_c2)
        vt_ev = None
        for j in range(NT):
            for chc in range(2):
                vt_ev, _tt = transpose_tile(
                    kb, lambda _c, j=j, chc=chc: values_b[:, chc * T + j * 128:
                                                          chc * T + (j + 1) * 128],
                    1, ps_trE[:],
                    lambda _c, j=j, chc=chc: values_tm[:, j * 256 + chc * 128:
                                                       j * 256 + (chc + 1) * 128],
                    idbf[:], [qkv1_done, vt_ev], ps_free=vt_ev)
        vtm_done = vt_ev
        kb.barrier()

        Sp_sb = kb.at("Sp_sb", [128, T], BF16, WS)
        SpT = kb.at("SpT", [128, 16 * 512], BF16, WS + 4096)
        retr_b = kb.at("retr_b", [128, 2 * 512], BF16, WS + 20480)
        a1raw = kb.at("a1raw", [128, 8 * D], BF16, REG1 + 8192)
        a1ln = kb.sbuf("a1ln", [128, D], BF16)
        ps_at2 = kb.psum("ps_at2", [128, 1024], BF16, scope_c2)
        ps_o1 = kb.psum("ps_o1", [128, 1024], F32, scope_c2)
        dx4 = kb.dsem("x4")
        sbk = {0: None, 1: None}
        at_tok = None
        spt_free = None
        sp_free = None
        retr_free = None
        o1_free = None
        a1_free = None
        stok = None
        addtok = None
        for bg in range(4):
            ts0 = 512 * bg
            retr_toks = []
            sp_done = []
            for u in range(4):
                i = 4 * bg + u
                Wd = T - ts0
                nbk = Wd // 512
                ev_toks = []
                for b in range(nbk):
                    w0c = ts0 + b * 512
                    bk = b % 2

                    def _(t, i=i, w0c=w0c, bk=bk, w=sbk[bk]):
                        wait(t, w, qkv1_done)
                        for chc in range(2):
                            ins = t.matmul(
                                ps_p[bk][:],
                                queries_b[:, chc * T + i * 128: chc * T + (i + 1) * 128],
                                keys_b[:, chc * T + w0c: chc * T + w0c + 512],
                                start=(chc == 0), stop=(chc == 1))
                        ct.inc(ins)
                    kb.t(_)
                    mt = ct.tok()

                    def _(v, i=i, w0c=w0c, bk=bk, mt=mt, G_done=G_done,
                          spf=(sp_free if b == 0 else None)):
                        wait(v, mt, G_done, spf)
                        cv.inc(v.scalar_tensor_tensor(
                            Sp_sb[:, w0c - ts0: w0c - ts0 + 512], ps_p[bk][:], 1.0,
                            G_sb[:, 1920 - 128 * i + w0c: 1920 - 128 * i + w0c + 512],
                            ALU.mult, ALU.mult))
                    kb.v(_)
                    sbk[bk] = cv.tok()
                    ev_toks.append(cv.tok())

                for grp8 in range((16 - 4 * bg + 7) // 8):
                    j0 = 4 * bg + grp8 * 8
                    j1 = min(16, j0 + 8)

                    def _(t, i=i, j0=j0, j1=j1, ts0=ts0, w=list(ev_toks),
                          pf=at_tok, sf=(spt_free if (u == 0 and grp8 == 0) else None)):
                        wait(t, pf, sf, *w)
                        for j in range(j0, j1):
                            ins = t.transpose(ps_at2[:, (j - j0) * 128:(j - j0 + 1) * 128],
                                              Sp_sb[:, j * 128 - ts0: (j + 1) * 128 - ts0],
                                              idbf[:])
                        ct.inc(ins)
                    kb.t(_)
                    tt = ct.tok()
                    sp_free = tt

                    def _(v, i=i, u=u, j0=j0, j1=j1, bg=bg, tt=tt):
                        wait(v, tt)
                        for j in range(j0, j1):
                            ins = v.tensor_copy(
                                SpT[:, (j - 4 * bg) * 512 + u * 128:
                                    (j - 4 * bg) * 512 + (u + 1) * 128],
                                ps_at2[:, (j - j0) * 128:(j - j0 + 1) * 128])
                        cv.inc(ins)
                    kb.v(_)
                    at_tok = cv.tok()
                sp_done.append(at_tok)

            for chc in range(2):
                def _(t, chc=chc, bg=bg, w=list(sp_done), rf=retr_free):
                    wait(t, vtm_done, rf, *w)
                    njc = 16 - 4 * bg
                    for jj in range(njc):
                        j = 4 * bg + jj
                        ins = t.matmul(ps_p[chc][:],
                                       values_tm[:, j * 256 + chc * 128:
                                                 j * 256 + (chc + 1) * 128],
                                       SpT[:, jj * 512:(jj + 1) * 512],
                                       start=(jj == 0), stop=(jj == njc - 1))
                    ct.inc(ins)
                kb.t(_)
                mt = ct.tok()

                def _(v, chc=chc, mt=mt):
                    wait(v, mt)
                    cv.inc(v.tensor_copy(retr_b[:, chc * 512:(chc + 1) * 512],
                                         ps_p[chc][:]))
                kb.v(_)
                retr_toks.append(cv.tok())
            spt_free = ct.tok()
            retr_done = cv.tok()
            sbk[0] = retr_done
            sbk[1] = retr_done

            for u in range(4):
                i = 4 * bg + u

                def _(t, u=u, w=retr_done, of=o1_free):
                    wait(t, w, of)
                    for chc in range(2):
                        for half in range(2):
                            ins = t.matmul(
                                ps_o1[:, half * 512:(half + 1) * 512],
                                retr_b[:, chc * 512 + u * 128: chc * 512 + (u + 1) * 128],
                                owT_b[:, chc * 1024 + half * 512:
                                      chc * 1024 + (half + 1) * 512],
                                start=(chc == 0), stop=(chc == 1))
                    ct.inc(ins)
                kb.t(_)
                otok = ct.tok()
                retr_free = otok

                def _(v, i=i, w=otok, af=a1_free):
                    wait(v, w, af)
                    cv.inc(v.tensor_copy(a1raw[:, (i % 8) * D:(i % 8 + 1) * D],
                                         ps_o1[:]))
                kb.v(_)
                o1_free = cv.tok()

                stok = lnb.stats(a1raw[:, (i % 8) * D:(i % 8 + 1) * D], i % 8,
                                 [o1_free, bc_tok], f_ap=osc_pp[:, 0:1],
                                 prev=stok)

            if bg % 2 == 1:
                rtok_b = lnb.rstd_batch(0, 8, [stok, o1_free])
                aptok = None
                for u8 in range(8):
                    i = (bg - 1) * 4 + u8
                    aptok = lnb.apply(a1raw[:, u8 * D:(u8 + 1) * D],
                                      a1ln[:], u8, [rtok_b, addtok], scaled=True)

                    def _(v, i=i, w=aptok):
                        wait(v, w)
                        cv.inc(v.tensor_add(X2[:, i * D:(i + 1) * D],
                                            X2[:, i * D:(i + 1) * D], a1ln[:]))
                    kb.v(_)
                    addtok = cv.tok()

                    def _(sync, i=i, w=addtok):
                        wait(sync, w)
                        dx4.inc(sync.dma_start(out=x4_sc[i * 128:(i + 1) * 128, :],
                                               in_=X2[:, i * D:(i + 1) * D]))
                    kb.q(_)
                a1_free = cv.tok()
                stok = None
        c4_done = cv.tok()
        kb.dbg("x4", X2[:], c4_done)
        kb.flush_dbg()
        kb.barrier()
        scope_c2.close()


    if "D" in phases:
        H128 = 1.0 / float(np.sqrt(128.0))
        sim1_b = kb.sbuf("sim1_b", [128, 64], BF16)
        sim2_b = kb.sbuf("sim2_b", [128, 64], BF16)
        simsq = kb.sbuf("simsq", [128, 64], F32)
        rn_pp = kb.sbuf("rn_pp", [128, 16], F32)
        sg_pp = kb.sbuf("sg_pp", [128, 16], F32)
        lnw_pp = kb.sbuf("lnw_pp", [128, 32], F32)
        rs_all = kb.sbuf("rs_all", [128, 16], F32)
        lgb = kb.sbuf("lgb", [128, 8], F32)
        t8b = kb.sbuf("t8b", [128, 8], F32)
        mkb_ = kb.sbuf("mkb_", [128, 24], F32)
        rw_t = kb.sbuf("rw_t", [128, 8], BF16)
        tmd = kb.sbuf("tmd", [128, D], BF16)
        idf32 = kb.sbuf("idf32", [128, 128], F32)
        sel8 = kb.sbuf("sel8", [8, 8 * 128], BF16)
        nrm8 = kb.sbuf("nrm8", [1, 16], F32)
        dl2 = kb.dsem("l1f")

        def _(g):
            cg.inc(g.iota(idf32[:], [[1, 128]], base=0, channel_multiplier=-1,
                          allow_small_or_imprecise_dtypes=True))
        kb.g(_)
        gi1 = cg.tok()

        def _(g, w=gi1):
            wait(g, w)
            cg.inc(g.tensor_single_scalar(idf32[:], idf32[:], 0.0, ALU.is_equal))
        kb.g(_)
        gid32 = cg.tok()

        def _(g, w=gid32):
            wait(g, w)
            cg.inc(g.iota(sel8[:], [[1, 8], [0, 128]], base=0, channel_multiplier=-1,
                          allow_small_or_imprecise_dtypes=True))
        kb.g(_)
        gs1 = cg.tok()

        def _(g, w=gs1):
            wait(g, w)
            cg.inc(g.tensor_single_scalar(sel8[:], sel8[:], 0.0, ALU.is_equal))
        kb.g(_)
        sel_tok = cg.tok()

        def _(g):
            dl2.inc(g.dma_start(out=sim1_b[:].rearrange("p (c e) -> p c e", c=8),
                                in_=inp["L1_ffn_smha_sim"].rearrange("(c p) e -> p c e", p=128)))
            dl2.inc(g.dma_start(out=sim2_b[:].rearrange("p (c e) -> p c e", c=8),
                                in_=inp["L1_ffn_moe_sim"].rearrange("(c p) e -> p c e", p=128)))
            for k, name in enumerate(("ln1_w", "ln1_b", "ln2_w", "ln2_b")):
                dl2.inc(g.dma_start(
                    out=lnw_pp[:, k * 8:(k + 1) * 8],
                    in_=inp["L1_ffn_" + name].rearrange("one (c p) -> p (one c)", p=128),
                    allow_slow_non_contiguous=True))
        kb.g(_)
        d0tok = dl2.tok()

        scope_d0 = ExitStack()
        ps_d0 = kb.psum("ps_d0", [128, 16], F32, scope_d0)
        nrm_toks = []
        for mi, simb in enumerate((sim1_b, sim2_b)):
            def _(v, simb=simb, w=d0tok, w2=(nrm_toks[-1] if nrm_toks else None)):
                wait(v, w, w2)
                cv.inc(v.tensor_mul(simsq[:], simb[:], simb[:]))
            kb.v(_)
            sqt = cv.tok()

            def _(v, mi=mi, w=sqt):
                wait(v, w)
                cv.inc(v.tensor_reduce(
                    rn_pp[:, 8 * mi:8 * mi + 8],
                    bass.AP(simsq, 0, [[64, 128], [1, 8], [8, 8]]),
                    AX.X, ALU.add))
            kb.v(_)
            nrm_toks.append(cv.tok())

        def _(t, w=nrm_toks[-1], w2=vo2):
            wait(t, w, w2)
            t.matmul(ps_d0[0:1, 0:8], ones_c[:], rn_pp[:, 0:8], start=True, stop=True)
            ct.inc(t.matmul(ps_d0[0:1, 8:16], ones_c[:], rn_pp[:, 8:16],
                            start=True, stop=True))
        kb.t(_)
        tnorm = ct.tok()

        def _(v, w=tnorm):
            wait(v, w)
            cv.inc(v.tensor_scalar_max(nrm8[0:1, :], ps_d0[0:1, 0:16], 1e-24))
        kb.v(_)
        vnm = cv.tok()

        def _(s, w=vnm):
            wait(s, w)
            cs.inc(s.activation(nrm8[0:1, :], nrm8[0:1, :], AF.Ln))
        kb.s(_)
        snm = cs.tok()

        def _(s, w=snm):
            wait(s, w)
            cs.inc(s.activation(nrm8[0:1, :], nrm8[0:1, :], AF.Exp, scale=-0.5))
        kb.s(_)
        snm2 = cs.tok()

        def _(t, w=snm2, w2=sig_tok):
            wait(t, w, w2)
            t.matmul(ps_d0[:, 0:16], ones_f2[:], nrm8[0:1, 0:16], start=True, stop=True)
            ct.inc(t.matmul(ps_d0[:, 0:16], ones_f2[:], nrm8[0:1, 0:16],
                            start=True, stop=True))
        kb.t(_)
        tb1 = ct.tok()

        def _(v, w=tb1):
            wait(v, w)
            cv.inc(v.tensor_copy(rn_pp[:], ps_d0[:, 0:16]))
        kb.v(_)
        vb1 = cv.tok()

        def _(t, w=vb1):
            wait(t, w)
            ct.inc(t.matmul(ps_d0[:, 0:16], ones_f2[:], g8s[0:1, 0:16],
                            start=True, stop=True))
        kb.t(_)
        tb2 = ct.tok()

        def _(v, w=tb2):
            wait(v, w)
            cv.inc(v.tensor_copy(sg_pp[:], ps_d0[:, 0:16]))
        kb.v(_)
        d0_done = cv.tok()
        kb.barrier()
        scope_d0.close()

        def d_lnpass(wcol, bcol, dbgname=None):
            scope = ExitStack()
            ps_trF = kb.psum("ps_trF_%s" % wcol, [128, 1024], BF16, scope)
            stok = None
            for i in range(NT):
                stok = lnb.stats(X2[:, i * D:(i + 1) * D], i, [], prev=stok)
            rtok_b = lnb.rstd_batch(0, 16, [stok])
            aptok = None
            evtok = None
            trtok = None
            for i in range(NT):
                aptok = lnb.apply(X2[:, i * D:(i + 1) * D], tmd[:], i,
                                  [rtok_b, trtok])
                evtok, trtok = transpose_tile(
                    kb, lambda c: tmd[:, c * 128:(c + 1) * 128], ND, ps_trF[:],
                    lambda c, i=i: act_fm[:, c * T + i * 128: c * T + (i + 1) * 128],
                    idbf[:], [aptok], ps_free=evtok)
            aff_toks = []
            for c in range(ND):
                def _(v, c=c, w=evtok, w2=d0tok, w3=(aff_toks[-1] if aff_toks else None)):
                    wait(v, w, w2, w3)
                    cv.inc(v.tensor_scalar(act_fm[:, c * T:(c + 1) * T],
                                           act_fm[:, c * T:(c + 1) * T],
                                           lnw_pp[:, wcol * 8 + c:wcol * 8 + c + 1],
                                           lnw_pp[:, bcol * 8 + c:bcol * 8 + c + 1],
                                           ALU.mult, ALU.add))
                kb.v(_)
                aff_toks.append(cv.tok())
            scope.close()
            return cv.tok()

        stok = None
        for i in range(NT):
            stok = lnb.stats(X2[:, i * D:(i + 1) * D], i, [], prev=stok)
        rtok_b = lnb.rstd_batch(0, 16, [stok])
        aptok = None
        for i in range(NT):
            aptok = lnb.apply(X2[:, i * D:(i + 1) * D], X2[:, i * D:(i + 1) * D],
                              i, [rtok_b, aptok])
        kb.barrier()
        n1f_done = d_lnpass(0, 1)
        kb.dbg("n1f_fm", act_fm[:], n1f_done)
        kb.flush_dbg()
        kb.barrier()

        rw_emaj = kb.at("rw_emaj", [8, NT * 128], BF16, WS)
        rw2_emaj = kb.at("rw2_emaj", [8, NT * 128], BF16, WS + 4096)

        lgall = kb.sbuf("lgall", [128, 128], F32)
        glst = kb.sbuf("glst", [128, 16], F32)

        def d_gating(mi, dst_emaj):
            scope = ExitStack()
            ps_nn = kb.psum("ps_nn_%d" % mi, [128, 128], F32, scope)
            ps_lg = kb.psum("ps_lg_%d" % mi, [128, 8], F32, scope)
            ps_rt = kb.psum("ps_rt_%d" % mi, [128, 128], BF16, scope)
            simb = (sim1_b, sim2_b)[mi]
            prev_ap = None
            for i in range(NT):
                def _(t, i=i, w=prev_ap):
                    wait(t, w)
                    for cc in range(ND):
                        ins = t.matmul(ps_nn[:],
                                       act_fm[:, cc * T + i * 128: cc * T + (i + 1) * 128],
                                       act_fm[:, cc * T + i * 128: cc * T + (i + 1) * 128],
                                       start=(cc == 0), stop=(cc == ND - 1))
                    ct.inc(ins)
                    for cc in range(ND):
                        ins = t.matmul(ps_lg[:],
                                       act_fm[:, cc * T + i * 128: cc * T + (i + 1) * 128],
                                       simb[:, cc * 8:(cc + 1) * 8],
                                       start=(cc == 0), stop=(cc == ND - 1))
                    ct.inc(ins)
                kb.t(_)
                mt = ct.tok()

                def _(v, i=i, w=mt, w2=gid32):
                    wait(v, w, w2)
                    v.tensor_mul(sq_scr[:, 0:128], ps_nn[:], idf32[:])
                    cv.inc(v.tensor_copy(lgall[:, i * 8:(i + 1) * 8], ps_lg[:]))
                kb.v(_)
                v1 = cv.tok()

                def _(v, i=i, w=v1):
                    wait(v, w)
                    cv.inc(v.tensor_reduce(glst[:, i:i + 1], sq_scr[:, 0:128],
                                           AX.X, ALU.add))
                kb.v(_)
                prev_ap = cv.tok()

            def _(v, w=prev_ap):
                wait(v, w)
                cv.inc(v.tensor_scalar_max(glst[:], glst[:], 1e-24))
            kb.v(_)
            g1_ = cv.tok()

            def _(s, w=g1_):
                wait(s, w)
                cs.inc(s.activation(glst[:], glst[:], AF.Ln))
            kb.s(_)
            g2_ = cs.tok()

            def _(s, w=g2_):
                wait(s, w)
                cs.inc(s.activation(glst[:], glst[:], AF.Exp, scale=-0.5))
            kb.s(_)
            rln_tok = cs.tok()

            prev_tr = None
            prev_chain = None
            for i in range(NT):
                def _(v, i=i, w=rln_tok, pc=prev_chain, mi=mi):
                    wait(v, w, pc)
                    cv.inc(v.tensor_scalar_mul(lgb[:], lgall[:, i * 8:(i + 1) * 8],
                                               glst[:, i:i + 1]))
                kb.v(_)
                v4 = cv.tok()

                def _(v, w=v4, mi=mi):
                    wait(v, w)
                    cv.inc(v.tensor_mul(lgb[:], lgb[:], rn_pp[:, 8 * mi:8 * mi + 8]))
                kb.v(_)
                v5 = cv.tok()

                def _(v, w=v5, mi=mi):
                    wait(v, w)
                    cv.inc(v.tensor_sub(lgb[:], lgb[:], sg_pp[:, 8 * mi:8 * mi + 8]))
                kb.v(_)
                v6 = cv.tok()

                def _(v, w=v6):
                    wait(v, w)
                    cv.inc(v.max(t8b[:], lgb[:]))
                kb.v(_)
                v7 = cv.tok()

                def _(v, w=v7):
                    wait(v, w)
                    v.tensor_single_scalar(mkb_[:, 16:17], t8b[:, 0:1], 0.0, ALU.is_gt)
                    cv.inc(v.tensor_single_scalar(mkb_[:, 0:8], lgb[:], 0.0, ALU.is_gt))
                kb.v(_)
                v8 = cv.tok()

                def _(v, w=v8):
                    wait(v, w)
                    cv.inc(v.tensor_scalar(mkb_[:, 8:16], lgb[:], t8b[:, 1:2], None,
                                           ALU.is_ge))
                kb.v(_)
                v9 = cv.tok()

                def _(v, w=v9):
                    wait(v, w)
                    cv.inc(v.tensor_sub(mkb_[:, 0:8], mkb_[:, 0:8], mkb_[:, 8:16]))
                kb.v(_)
                v10 = cv.tok()

                def _(v, w=v10):
                    wait(v, w)
                    cv.inc(v.scalar_tensor_tensor(mkb_[:, 0:8], mkb_[:, 0:8],
                                                  mkb_[:, 16:17], mkb_[:, 8:16],
                                                  ALU.mult, ALU.add))
                kb.v(_)
                v11 = cv.tok()

                def _(v, w=v11):
                    wait(v, w)
                    cv.inc(v.tensor_relu(lgb[:], lgb[:]))
                kb.v(_)
                v12 = cv.tok()

                def _(s, w=v12):
                    wait(s, w)
                    cs.inc(s.activation(t8b[:], lgb[:], AF.Exp))
                kb.s(_)
                s3 = cs.tok()

                def _(v, w=s3):
                    wait(v, w)
                    cv.inc(v.tensor_mul(t8b[:], t8b[:], mkb_[:, 0:8]))
                kb.v(_)
                v13 = cv.tok()

                def _(v, w=v13):
                    wait(v, w)
                    cv.inc(v.tensor_reduce(sm[:, 12:13], t8b[:], AX.X, ALU.add))
                kb.v(_)
                v14 = cv.tok()

                def _(v, w=v14):
                    wait(v, w)
                    cv.inc(v.reciprocal(sm[:, 12:13], sm[:, 12:13]))
                kb.v(_)
                v15 = cv.tok()

                def _(v, w=v15, w2=prev_tr):
                    wait(v, w, w2)
                    cv.inc(v.tensor_scalar_mul(rw_t[:], t8b[:], sm[:, 12:13]))
                kb.v(_)
                v16 = cv.tok()
                prev_chain = v16

                def _(t, i=i, w=v16, w2=prev_tr):
                    wait(t, w, w2)
                    ct.inc(t.transpose(ps_rt[0:8, 0:128], rw_t[:, 0:8], idbf[:]))
                kb.t(_)
                tt = ct.tok()

                def _(v, i=i, w=tt, dst_emaj=dst_emaj):
                    wait(v, w)
                    cv.inc(v.tensor_copy(dst_emaj[0:8, i * 128:(i + 1) * 128],
                                         ps_rt[0:8, 0:128]))
                kb.v(_)
                prev_tr = cv.tok()
            scope.close()
            return prev_tr

        rw_done = d_gating(0, rw_emaj)
        kb.dbg("rw_emaj", rw_emaj[0:8, :], rw_done)
        kb.flush_dbg()
        kb.barrier()

        scope_d3a = ExitStack()
        qp_b = kb.at("qp_b", [128, 64 * 128], BF16, REG1)
        kp_b = kb.at("kp_b", [128, 64 * 128], BF16, REG1 + 16384)
        vp_b = kb.at("vp_b", [128, 64 * 128], BF16, REG2)
        n1e = kb.at("n1e", [128, 8 * 512], BF16, REG2 + 16384)
        rwb_eg = kb.at("rwb_eg", [128, 512], BF16, REG2 + 24576)
        q_fm_d = kb.at("q_fm_d", [128, T], BF16, WS + 8192)
        k_fm_d = kb.at("k_fm_d", [128, T], BF16, WS + 12288)
        v_fm_d = kb.at("v_fm_d", [128, T], BF16, WS + 16384)
        dpq = kb.dsem("pq")

        def _(g):
            for nm, dst in (("q_proj", qp_b), ("k_proj", kp_b), ("v_proj", vp_b)):
                dpq.inc(g.dma_start(
                    out=dst[:].rearrange("p (ec h) -> p ec h", h=128),
                    in_=inp["L1_ffn_" + nm].rearrange("(ec p) h -> p ec h", p=128)))
        kb.g(_)
        pq_tok = dpq.tok()

        ps_q3 = kb.psum("ps_q3", [128, 512], F32, scope_d3a)
        ps_k3 = kb.psum("ps_k3", [128, 512], F32, scope_d3a)
        ps_v3 = kb.psum("ps_v3", [128, 512], F32, scope_d3a)
        ps_bc3 = kb.psum("ps_bc3", [128, 512], F32, scope_d3a)
        n1e_free = None
        qkv_ev = {0: None, 1: None, 2: None}
        for g_ in range(4):
            for e in range(8):
                def _(t, g_=g_, e=e, w=rw_done, w2=sel_tok):
                    wait(t, w, w2)
                    ct.inc(t.matmul(ps_bc3[:], sel8[0:8, e * 128:(e + 1) * 128],
                                    rw_emaj[0:8, g_ * 512:(g_ + 1) * 512],
                                    start=True, stop=True))
                kb.t(_)
                bct = ct.tok()

                def _(v, w=bct, nf=n1e_free):
                    wait(v, w, nf)
                    cv.inc(v.tensor_copy(rwb_eg[:], ps_bc3[:]))
                kb.v(_)
                bce = cv.tok()

                def _(v, g_=g_, w=bce):
                    wait(v, w)
                    for cc in range(ND):
                        ins = v.tensor_mul(n1e[:, cc * 512:(cc + 1) * 512],
                                           act_fm[:, cc * T + g_ * 512:
                                                  cc * T + (g_ + 1) * 512],
                                           rwb_eg[:])
                    cv.inc(ins)
                kb.v(_)
                n1etok = cv.tok()

                def _(t, g_=g_, e=e, w=n1etok, w2=pq_tok,
                      ev=[qkv_ev[x] for x in range(3)] if (g_ > 0 and e == 0) else []):
                    wait(t, w, w2, *ev)
                    for cc in range(ND):
                        fl = (e * 8 + cc) * 128
                        st_ = (e == 0 and cc == 0)
                        sp = (e == 7 and cc == ND - 1)
                        t.matmul(ps_q3[:], qp_b[:, fl:fl + 128],
                                 n1e[:, cc * 512:(cc + 1) * 512], start=st_, stop=sp)
                        t.matmul(ps_k3[:], kp_b[:, fl:fl + 128],
                                 n1e[:, cc * 512:(cc + 1) * 512], start=st_, stop=sp)
                        ins = t.matmul(ps_v3[:], vp_b[:, fl:fl + 128],
                                       n1e[:, cc * 512:(cc + 1) * 512],
                                       start=st_, stop=sp)
                    ct.inc(ins)
                kb.t(_)
                n1e_free = ct.tok()
            mtok = ct.tok()

            def _(v, g_=g_, w=mtok):
                wait(v, w)
                v.tensor_scalar_mul(q_fm_d[:, g_ * 512:(g_ + 1) * 512], ps_q3[:], H128)
                v.tensor_copy(k_fm_d[:, g_ * 512:(g_ + 1) * 512], ps_k3[:])
                cv.inc(v.tensor_copy(v_fm_d[:, g_ * 512:(g_ + 1) * 512], ps_v3[:]))
            kb.v(_)
            for x in range(3):
                qkv_ev[x] = cv.tok()
        d3a_done = cv.tok()
        kb.dbg("qf_d", q_fm_d[:], d3a_done)
        kb.flush_dbg()
        kb.barrier()
        scope_d3a.close()

        scope_d3b = ExitStack()
        v_tm_d = kb.at("v_tm_d", [128, NT * 128], BF16, REG1)
        attn_fm_d = kb.at("attn_fm_d", [128, T], BF16, REG1 + 4096)
        S2_sb = kb.at("S2_sb", [128, T], F32, REG2)
        P2_sb = kb.at("P2_sb", [128, T], BF16, REG2 + 8192)
        PT_sb = kb.at("PT_sb", [128, T], BF16, REG1 + 8192)
        ps_s3 = [kb.psum("ps_s3a", [128, 512], F32, scope_d3b),
                 kb.psum("ps_s3b", [128, 512], F32, scope_d3b)]
        ps_at3 = kb.psum("ps_at3", [128, 1024], BF16, scope_d3b)
        ps_pv = kb.psum("ps_pv", [128, 128], F32, scope_d3b)

        vt_ev = None
        for j in range(NT):
            vt_ev, _tt = transpose_tile(
                kb, lambda _c, j=j: v_fm_d[:, j * 128:(j + 1) * 128],
                1, ps_at3[:],
                lambda _c, j=j: v_tm_d[:, j * 128:(j + 1) * 128],
                idbf[:], [d3a_done, vt_ev], ps_free=vt_ev)
        vtm2_done = vt_ev

        sbk3 = {0: None, 1: None}
        at3_tok = vtm2_done
        p2_free = None
        s2_free = None
        pv_free = None
        attn_ev = None
        sexp_prev = None
        for i in range(NT):
            W = 128 * (i + 1)
            nb = (W + 511) // 512
            ev_toks = []
            for b in range(nb):
                w0c = b * 512
                w1c = min(W, w0c + 512)
                bk = b % 2

                def _(t, i=i, w0c=w0c, w1c=w1c, bk=bk, w=sbk3[bk]):
                    wait(t, w)
                    ct.inc(t.matmul(ps_s3[bk][:, 0:w1c - w0c],
                                    q_fm_d[:, i * 128:(i + 1) * 128],
                                    k_fm_d[:, w0c:w1c], start=True, stop=True))
                kb.t(_)
                mt = ct.tok()

                def _(v, w0c=w0c, w1c=w1c, bk=bk, mt=mt,
                      sf=(s2_free if b == 0 else None)):
                    wait(v, mt, sf)
                    cv.inc(v.tensor_copy(S2_sb[:, w0c:w1c], ps_s3[bk][:, 0:w1c - w0c]))
                kb.v(_)
                sbk3[bk] = cv.tok()
                ev_toks.append(cv.tok())

            def _(g, i=i, ev=ev_toks[-1]):
                wait(g, ev)
                cg.inc(g.affine_select(S2_sb[:, i * 128:(i + 1) * 128],
                                       S2_sb[:, i * 128:(i + 1) * 128],
                                       [[-1, 128]], ALU.is_ge, -1e30,
                                       base=0, channel_multiplier=1))
            kb.g(_)
            gmask = cg.tok()

            def _(v, W=W, gmask=gmask, ev=list(ev_toks), sp=sexp_prev):
                wait(v, gmask, sp, *ev)
                cv.inc(v.reduce_max(sm[:, 8:9], S2_sb[:, 0:W], AX.X, negate=True))
            kb.v(_)
            vmax = cv.tok()

            def _(s, W=W, vmax=vmax, pf=p2_free):
                wait(s, vmax, pf)
                cs.inc(s.activation(P2_sb[:, 0:W], S2_sb[:, 0:W], AF.Exp,
                                    bias=sm[:, 8:9], accum_out=ssum[:, 0:1]))
            kb.s(_)
            sexp = cs.tok()
            sexp_prev = sexp
            s2_free = sexp

            def _(v, i=i, w=sexp):
                wait(v, w)
                cv.inc(v.reciprocal(rs_all[:, i:i + 1], ssum[:, 0:1]))
            kb.v(_)

            at_toks = []
            for bg2 in range((i + 8) // 8):
                j0, j1 = bg2 * 8, min(i + 1, bg2 * 8 + 8)

                def _(t, j0=j0, j1=j1, sexp=sexp, w=at3_tok):
                    wait(t, sexp, w)
                    for j in range(j0, j1):
                        ins = t.transpose(ps_at3[:, (j - j0) * 128:(j - j0 + 1) * 128],
                                          P2_sb[:, j * 128:(j + 1) * 128], idbf[:])
                    ct.inc(ins)
                kb.t(_)
                ttok = ct.tok()
                p2_free = ttok

                def _(v, j0=j0, j1=j1, ttok=ttok):
                    wait(v, ttok)
                    cv.inc(v.tensor_copy(PT_sb[:, j0 * 128:j1 * 128],
                                         ps_at3[:, 0:(j1 - j0) * 128]))
                kb.v(_)
                at3_tok = cv.tok()
                at_toks.append(cv.tok())

            def _(t, i=i, w=list(at_toks), pvf=pv_free):
                wait(t, vtm2_done, pvf, *w)
                for j in range(i + 1):
                    ins = t.matmul(ps_pv[:], v_tm_d[:, j * 128:(j + 1) * 128],
                                   PT_sb[:, j * 128:(j + 1) * 128],
                                   start=(j == 0), stop=(j == i))
                ct.inc(ins)
            kb.t(_)
            pvt = ct.tok()

            def _(v, i=i, w=pvt, ae=attn_ev):
                wait(v, w)
                cv.inc(v.tensor_copy(attn_fm_d[:, i * 128:(i + 1) * 128], ps_pv[:]))
            kb.v(_)
            attn_ev = cv.tok()
            pv_free = attn_ev
        d3b_done = attn_ev
        kb.barrier()
        scope_d3b.close()

        op_b = kb.at("op_b", [128, 8 * 1024], BF16, REG1 + 16384)
        rwb_all = kb.at("rwb_all", [128, 8 * T], BF16, REG2)
        attn_e = kb.at("attn_e", [128, 512], BF16, WS + 20480)
        dop = kb.dsem("op")

        def _(g):
            dop.inc(g.dma_start(
                out=op_b[:].rearrange("p (e c) -> p e c", e=8),
                in_=inp["L1_ffn_o_proj"].rearrange("(e p) c -> p e c", p=128)))
        kb.g(_)
        op_tok = dop.tok()

        scope_bcc = ExitStack()
        ps_bc3 = kb.psum("ps_bcc", [128, 512], F32, scope_bcc)
        rwb_toks = []
        for e in range(8):
            for g_ in range(4):
                def _(t, e=e, g_=g_, w=rw_done,
                      w2=(rwb_toks[-1] if rwb_toks else None)):
                    wait(t, w, w2, sel_tok)
                    ct.inc(t.matmul(ps_bc3[:], sel8[0:8, e * 128:(e + 1) * 128],
                                    rw_emaj[0:8, g_ * 512:(g_ + 1) * 512],
                                    start=True, stop=True))
                kb.t(_)
                bt_ = ct.tok()

                def _(v, e=e, g_=g_, w=bt_):
                    wait(v, w)
                    cv.inc(v.tensor_copy(rwb_all[:, e * T + g_ * 512:
                                                 e * T + (g_ + 1) * 512], ps_bc3[:]))
                kb.v(_)
                rwb_toks.append(cv.tok())
        rwb_done = cv.tok()
        kb.barrier()
        scope_bcc.close()

        scope_d3c = ExitStack()
        ps_om = [kb.psum("ps_om%d" % u, [128, 1024], F32, scope_d3c) for u in range(4)]
        om_free = {u: None for u in range(4)}
        ae_free = None
        for g_ in range(4):
            add_toks = []
            for e in range(8):
                def _(v, g_=g_, e=e, w=rwb_done, w2=d3b_done, af=ae_free):
                    wait(v, w, w2, af)
                    cv.inc(v.tensor_mul(attn_e[:],
                                        attn_fm_d[:, g_ * 512:(g_ + 1) * 512],
                                        rwb_all[:, e * T + g_ * 512:e * T + (g_ + 1) * 512]))
                kb.v(_)
                aet = cv.tok()

                def _(t, g_=g_, e=e, w=aet, w2=op_tok,
                      omf=[om_free[u] for u in range(4)] if e == 0 else []):
                    wait(t, w, w2, *omf)
                    for u in range(4):
                        for half in range(2):
                            ins = t.matmul(
                                ps_om[u][:, half * 512:(half + 1) * 512],
                                attn_e[:, u * 128:(u + 1) * 128],
                                op_b[:, e * 1024 + half * 512:e * 1024 + (half + 1) * 512],
                                start=(e == 0), stop=(e == 7))
                    ct.inc(ins)
                kb.t(_)
                ae_free = ct.tok()
            omt = ct.tok()

            for u in range(4):
                i = g_ * 4 + u

                def _(v, u=u, i=i, w=omt):
                    wait(v, w)
                    cv.inc(v.scalar_tensor_tensor(X2[:, i * D:(i + 1) * D],
                                                  ps_om[u][:], rs_all[:, i:i + 1],
                                                  X2[:, i * D:(i + 1) * D],
                                                  ALU.mult, ALU.add))
                kb.v(_)
                om_free[u] = cv.tok()
                add_toks.append(cv.tok())
        d3c_done = cv.tok()
        kb.dbg("hs", X2[:], d3c_done)
        kb.flush_dbg()
        kb.barrier()
        scope_d3c.close()

        n2f_done = d_lnpass(2, 3)
        kb.dbg("n2f_fm", act_fm[:], n2f_done)
        kb.barrier()
        rw2_done = d_gating(1, rw2_emaj)
        kb.dbg("rw2_emaj", rw2_emaj[0:8, :], rw2_done)
        kb.flush_dbg()
        kb.barrier()

        scope_d5 = ExitStack()
        w1b = kb.at("w1b", [128, 2 * 8 * 1024], BF16, REG1)
        w2b = kb.at("w2b", [128, 2 * 8 * 1024], BF16, REG2)
        scaled = kb.at("scaled", [128, 8 * 512], BF16, WS + 8192)
        rw2b_eg = kb.at("rw2b_eg", [128, 512], BF16, WS + 16384)
        dmw = [kb.dsem("mw0"), kb.dsem("mw1")]
        ps_i5 = [kb.psum("ps_i5a", [128, 1024], F32, scope_d5),
                 kb.psum("ps_i5b", [128, 1024], F32, scope_d5)]
        ps_m5 = [kb.psum("ps_m5a", [128, 512], F32, scope_d5),
                 kb.psum("ps_m5b", [128, 512], F32, scope_d5)]
        ps_bc5 = kb.psum("ps_bc5", [128, 512], F32, scope_d5)
        W1f = inp["L1_ffn_w1"].rearrange("(ec p) i -> p ec i", p=128)
        W2f = inp["L1_ffn_w2"].rearrange("(ec p) i -> p ec i", p=128)
        wslot_tok = {0: None, 1: None}
        ibank_tok = {0: None, 1: None}
        mbank_tok = {0: None, 1: None}
        sc_free = None
        rwb5_free = None
        for e in range(8):
            slot = e % 2

            def _(g, e=e, slot=slot, w=wslot_tok[slot], ds=dmw[slot]):
                wait(g, w)
                ds.inc(g.dma_start(
                    out=w1b[:, slot * 8192:(slot + 1) * 8192].rearrange(
                        "p (c i) -> p c i", c=8),
                    in_=W1f[:, e * 8:(e + 1) * 8, :]))
                ds.inc(g.dma_start(
                    out=w2b[:, slot * 8192:(slot + 1) * 8192].rearrange(
                        "p (c i) -> p c i", c=8),
                    in_=W2f[:, e * 8:(e + 1) * 8, :]))
            kb.g(_)
            wtok = dmw[slot].tok()

            for g_ in range(4):
                def _(t, e=e, g_=g_, w=rw2_done, rf=rwb5_free):
                    wait(t, w, rf, sel_tok)
                    ct.inc(t.matmul(ps_bc5[:], sel8[0:8, e * 128:(e + 1) * 128],
                                    rw2_emaj[0:8, g_ * 512:(g_ + 1) * 512],
                                    start=True, stop=True))
                kb.t(_)
                bct = ct.tok()

                def _(v, w=bct, sf=sc_free):
                    wait(v, w, sf)
                    cv.inc(v.tensor_copy(rw2b_eg[:], ps_bc5[:]))
                kb.v(_)
                rwb5 = cv.tok()

                sc_toks = []
                for ipr in range(4):
                    bk = ipr % 2

                    def _(t, e=e, g_=g_, ipr=ipr, bk=bk, slot=slot, wtok=wtok,
                          w=ibank_tok[bk], w2=n2f_done):
                        wait(t, wtok, w, w2)
                        for half in range(2):
                            ic = 2 * ipr + half
                            for cc in range(ND):
                                ins = t.matmul(
                                    ps_i5[bk][:, half * 512:(half + 1) * 512],
                                    w1b[:, slot * 8192 + cc * 1024 + ic * 128:
                                        slot * 8192 + cc * 1024 + (ic + 1) * 128],
                                    act_fm[:, cc * T + g_ * 512: cc * T + (g_ + 1) * 512],
                                    start=(cc == 0), stop=(cc == ND - 1))
                        ct.inc(ins)
                    kb.t(_)
                    imt = ct.tok()

                    def _(s, ipr=ipr, bk=bk, w=imt, sf=sc_free if ipr < 2 else None):
                        wait(s, w, sf)
                        cs.inc(s.activation(scaled[:, ipr * 1024:(ipr + 1) * 1024],
                                            ps_i5[bk][:], AF.Gelu))
                    kb.s(_)
                    gt = cs.tok()
                    ibank_tok[bk] = gt

                    def _(v, ipr=ipr, w=gt, w2=rwb5):
                        wait(v, w, w2)
                        for half in range(2):
                            ic = 2 * ipr + half
                            ins = v.tensor_mul(scaled[:, ic * 512:(ic + 1) * 512],
                                               scaled[:, ic * 512:(ic + 1) * 512],
                                               rw2b_eg[:])
                        cv.inc(ins)
                    kb.v(_)
                    sc_toks.append(cv.tok())
                rwb5_free = ct.tok()

                for u in range(4):
                    i = g_ * 4 + u
                    for half in range(2):
                        bk = (2 * u + half) % 2

                        def _(t, u=u, half=half, e=e, slot=slot, bk=bk,
                              w=list(sc_toks), mf=mbank_tok[bk]):
                            wait(t, mf, *w)
                            for ic in range(ND):
                                ins = t.matmul(
                                    ps_m5[bk][:],
                                    scaled[:, ic * 512 + u * 128:ic * 512 + (u + 1) * 128],
                                    w2b[:, slot * 8192 + ic * 1024 + half * 512:
                                        slot * 8192 + ic * 1024 + (half + 1) * 512],
                                    start=(ic == 0), stop=(ic == ND - 1))
                            ct.inc(ins)
                        kb.t(_)
                        omt = ct.tok()
                        sc_free = omt

                        def _(v, i=i, half=half, bk=bk, w=omt):
                            wait(v, w)
                            dd = X2[:, i * D + half * 512: i * D + (half + 1) * 512]
                            cv.inc(v.tensor_add(dd, dd, ps_m5[bk][:]))
                        kb.v(_)
                        mbank_tok[bk] = cv.tok()
            wslot_tok[slot] = ct.tok()
        d5_done = cv.tok()
        kb.dbg("f1", X2[:], d5_done)
        kb.flush_dbg()
        kb.barrier()
        scope_d5.close()

        xt4 = kb.at("xt4", [128, 2 * D], F32, WS)
        outt = kb.at("outt", [128, 2 * D], F32, WS + 8192)
        dx4r = [kb.dsem("x4r0"), kb.dsem("x4r1")]
        douts = [kb.dsem("out0"), kb.dsem("out1")]
        stok = None
        for i in range(NT):
            stok = lnb.stats(X2[:, i * D:(i + 1) * D], i, [], prev=stok)
        rtok_b = lnb.rstd_batch(0, 16, [stok])
        slot_done6 = {0: None, 1: None}
        aptok = None
        addtok = None
        for i in range(NT):
            slot = i % 2
            xa = xt4[:, slot * D:(slot + 1) * D]
            oa = outt[:, slot * D:(slot + 1) * D]

            def _(sync, i=i, xa=xa, w=slot_done6[slot], ds=dx4r[slot]):
                wait(sync, w)
                ds.inc(sync.dma_start(out=xa, in_=x4_sc[i * 128:(i + 1) * 128, :]))
            kb.q(_)
            ldtok = dx4r[slot].tok()

            aptok = lnb.apply(X2[:, i * D:(i + 1) * D], tmd[:], i,
                              [rtok_b, aptok, addtok])

            def _(v, i=i, xa=xa, oa=oa, w=aptok, w2=ldtok, w3=slot_done6[slot]):
                wait(v, w, w2, w3)
                cv.inc(v.tensor_add(oa, xa, tmd[:]))
            kb.v(_)
            addtok = cv.tok()

            def _(sync, i=i, oa=oa, w=addtok, ds=douts[slot]):
                wait(sync, w)
                ds.inc(sync.dma_start(out=out[i * 128:(i + 1) * 128, :], in_=oa))
            kb.q(_)
            slot_done6[slot] = douts[slot].tok()

        def _(sync):
            wait(sync, douts[0].tok(), douts[1].tok())
        kb.q(_)

    if "final_stub" in DEBUG:
        def _(sync):
            for i in range(NT):
                dq.inc(sync.dma_start(out=out[i * 128:(i + 1) * 128, :],
                                      in_=outln_ap(i)))
            wait(sync, dq.tok())
        kb.q(_)

    rem = nc.sbuf_bytes_remaining
    assert rem >= 229376 - REG1, f"stack smalls overflow into manual arena: remaining={rem}"
    ctx.close()
    return nc


def shard_inputs(inputs, core):
    m = {}
    for name, shape in INPUT_SPECS:
        v = np.asarray(inputs[name])
        if name == "L0_attn_x":
            v = np.ascontiguousarray(v[core])
        m[name] = np.ascontiguousarray(
            v.astype(np.float32, copy=False).reshape(shape))
    return m


def kernel(**inputs):
    nc = build()
    in_maps = [shard_inputs(inputs, c) for c in range(8)]
    res = run_bass_kernel_spmd(nc, in_maps, core_ids=list(range(8)))
    return np.stack([res.results[c]["out"] for c in range(8)], axis=0)


# version 15
# speedup vs baseline: 1.0063x; 1.0063x over previous
import numpy as np
from contextlib import ExitStack
import concourse.bass as bass
import concourse.mybir as mybir
from concourse.bass_utils import run_bass_kernel_spmd

F32 = mybir.dt.float32
BF16 = mybir.dt.bfloat16
AF = mybir.ActivationFunctionType
ALU = mybir.AluOpType
AX = mybir.AxisListType

T = 2048
D = 1024
NT = T // 128
ND = D // 128
EPS = 1e-5

DEBUG = ()

REG1 = 40960
REG2 = 73728
REG3 = 106496
REG4 = 139264
REG5 = 172032
WS = 204800

INPUT_SPECS = [
    ("L0_attn_x", [T, D]),
    ("L0_attn_W_qkv", [D, 3 * D]),
    ("L0_attn_b_qkv", [1, 3 * D]),
    ("L0_attn_W_vconv", [D, 3]),
    ("L0_ffn_W1", [4 * D, D]),
    ("L0_ffn_b1", [1, 4 * D]),
    ("L0_ffn_W2", [D, 4 * D]),
    ("L0_ffn_b2", [1, D]),
    ("L1_attn_basis", [D, 256]),
    ("L1_attn_q_coeffs", [256, 256]),
    ("L1_attn_k_coeffs", [256, 256]),
    ("L1_attn_v_coeffs", [256, 256]),
    ("L1_attn_o_coeffs", [256, 256]),
    ("L1_attn_decay_logit", [1, 1]),
    ("L1_attn_out_scale", [1, 1]),
    ("L1_ffn_ln1_w", [1, D]),
    ("L1_ffn_ln1_b", [1, D]),
    ("L1_ffn_ln2_w", [1, D]),
    ("L1_ffn_ln2_b", [1, D]),
    ("L1_ffn_smha_sim", [D, 8]),
    ("L1_ffn_smha_gates", [1, 8]),
    ("L1_ffn_q_proj", [8 * D, 128]),
    ("L1_ffn_k_proj", [8 * D, 128]),
    ("L1_ffn_v_proj", [8 * D, 128]),
    ("L1_ffn_o_proj", [8 * 128, D]),
    ("L1_ffn_moe_sim", [D, 8]),
    ("L1_ffn_moe_gates", [1, 8]),
    ("L1_ffn_w1", [8 * D, D]),
    ("L1_ffn_w2", [8 * D, D]),
]


class Sem:
    def __init__(self, ctx, nc, name, step=1):
        self.h = ctx.enter_context(nc.semaphore(name))
        self.val = 0
        self.step = step

    def inc(self, ins, n=None):
        n = self.step if n is None else n
        ins.then_inc(self.h, n)
        self.val += n
        return (self.h, self.val)

    def tok(self):
        return (self.h, self.val)


def wait(eng, *toks):
    for tk in toks:
        if tk is None:
            continue
        h, v = tk
        if v > 0:
            eng.wait_ge(h, v)


class KB:
    def __init__(self, nc, ctx):
        self.nc = nc
        self.ctx = ctx
        self.blk = ctx.enter_context(nc.Block())
        self.dq = Sem(ctx, nc, "dq", 16)
        self.dg = Sem(ctx, nc, "dg", 16)
        self._dsems = {}
        self.ct = Sem(ctx, nc, "ct")
        self.cv = Sem(ctx, nc, "cv")
        self.cs = Sem(ctx, nc, "cs")
        self.cg = Sem(ctx, nc, "cg")
        self.dbg_outs = {}
        self.dbg_toks = []

    def dsem(self, name, step=16):
        if name not in self._dsems:
            self._dsems[name] = Sem(self.ctx, self.nc, "d_" + name, step)
        return self._dsems[name]

    def q(self, fn):
        self.blk.sync(fn)

    def g(self, fn):
        self.blk.gpsimd(fn)

    def t(self, fn):
        self.blk.tensor(fn)

    def v(self, fn):
        self.blk.vector(fn)

    def s(self, fn):
        self.blk.scalar(fn)

    def barrier(self, dma=True):
        toks = [x.tok() for x in (self.ct, self.cv, self.cs, self.cg)]
        if dma:
            toks += [x.tok() for x in (self.dq, self.dg)]
            toks += [s.tok() for s in self._dsems.values()]

        def _w(eng):
            wait(eng, *toks)
        self.q(_w)
        self.g(_w)
        self.t(_w)
        self.v(_w)
        self.s(_w)

    def sbuf(self, name, shape, dtype, scope=None):
        return (scope or self.ctx).enter_context(self.nc.sbuf_tensor(name, shape, dtype))

    def at(self, name, shape, dtype, offset):
        return self.nc.alloc_sbuf_tensor_at(name, shape, dtype, offset=offset)

    def psum(self, name, shape, dtype, scope=None):
        return (scope or self.ctx).enter_context(self.nc.psum_tensor(name, shape, dtype))

    def dbg(self, name, ap, tok):
        if name in DEBUG:
            o = self.nc.dram_tensor("dbg_" + name, list(ap.shape), ap.dtype,
                                    kind="ExternalOutput").ap()
            self.dbg_outs[name] = (o, ap)
            self.dbg_toks.append(tok)

    def flush_dbg(self):
        if not self.dbg_outs:
            return
        outs, toks, dq = self.dbg_outs, self.dbg_toks, self.dq

        def _(sync):
            wait(sync, *toks)
            for name, (o, ap) in outs.items():
                dq.inc(sync.dma_start(out=o, in_=ap))
            wait(sync, dq.tok())
        self.q(_)
        self.dbg_outs = {}
        self.dbg_toks = []


class LNB:

    def __init__(self, kb, st, sq_scr, eps_ap):
        self.kb = kb
        self.st = st
        self.sq = sq_scr
        self.eps = eps_ap

    def stats(self, src_ap, col, toks, f_ap=None, prev=None):
        kb, st = self.kb, self.st
        cv, cg = kb.cv, kb.cg
        invD = 1.0 / D
        m = st[:, col:col + 1]
        va = st[:, 16 + col:17 + col]
        r = st[:, 32 + col:33 + col]
        fcol = st[:, 48 + col:49 + col]
        nfree = len(src_ap.ap) - 1
        ax = AX.X if nfree == 1 else AX.XY
        if nfree == 1:
            sq = self.sq
        else:
            counts = [p[1] for p in src_ap.ap[1:]]
            dims = [list(self.sq.ap[0])]
            stride = 1
            rev = []
            for cdim in reversed(counts):
                rev.append([stride, cdim])
                stride *= cdim
            dims += rev[::-1]
            sq = bass.AP(self.sq.tensor, 0, dims)

        def _v1(v):
            wait(v, prev, *toks)
            ins = v.tensor_reduce(m, src_ap, ax, ALU.add)
            if f_ap is not None:
                ins = v.tensor_copy(fcol, f_ap)
            cv.inc(ins)
        kb.v(_v1)
        vt1 = cv.tok()

        def _g1(g):
            wait(g, prev, *toks)
            cg.inc(g.tensor_tensor(sq, src_ap, src_ap, ALU.mult))
        kb.g(_g1)
        gt1 = cg.tok()

        def _v1b(v):
            wait(v, gt1, vt1)
            cv.inc(v.tensor_reduce(va, sq, ax, ALU.add))
        kb.v(_v1b)
        gt2 = cv.tok()

        def _v2(v):
            wait(v, vt1, gt2)
            if f_ap is not None:
                v.tensor_scalar(m, m, fcol, invD, ALU.mult, ALU.mult)
                ins = v.tensor_mul(r, fcol, fcol)
            else:
                ins = v.tensor_scalar_mul(m, m, invD)
            cv.inc(ins)
        kb.v(_v2)
        vt2 = cv.tok()

        def _v3(v):
            wait(v, vt2)
            if f_ap is not None:
                cv.inc(v.tensor_scalar(va, va, r, invD, ALU.mult, ALU.mult))
            else:
                cv.inc(v.tensor_scalar_mul(va, va, invD))
        kb.v(_v3)
        vt3 = cv.tok()

        def _v4(v):
            wait(v, vt3)
            cv.inc(v.tensor_mul(r, m, m))
        kb.v(_v4)
        vt4 = cv.tok()

        def _v5(v):
            wait(v, vt4)
            cv.inc(v.tensor_sub(va, va, r))
        kb.v(_v5)
        return cv.tok()

    def rstd_batch(self, c0, c1, toks):
        kb, st = self.kb, self.st
        cs = kb.cs

        def _s1(s):
            wait(s, *toks)
            cs.inc(s.activation(st[:, 32 + c0:32 + c1], st[:, 16 + c0:16 + c1],
                                AF.Ln, bias=self.eps))
        kb.s(_s1)
        t1 = cs.tok()

        def _s2(s):
            wait(s, t1)
            cs.inc(s.activation(st[:, 32 + c0:32 + c1], st[:, 32 + c0:32 + c1],
                                AF.Exp, scale=-0.5))
        kb.s(_s2)
        return cs.tok()

    def apply(self, src_ap, dst_ap, col, toks, scaled=False):
        kb, st = self.kb, self.st
        cv = kb.cv
        m = st[:, col:col + 1]
        r = st[:, 32 + col:33 + col]

        def _v5(v):
            wait(v, *toks)
            cv.inc(v.scalar_tensor_tensor(m, m, -1.0, r, ALU.mult, ALU.mult))
        kb.v(_v5)
        vt5 = cv.tok()
        if scaled:
            def _v6(v):
                wait(v, vt5)
                cv.inc(v.tensor_mul(r, r, st[:, 48 + col:49 + col]))
            kb.v(_v6)
            vt5 = cv.tok()

        def _v7(v):
            wait(v, vt5)
            cv.inc(v.tensor_scalar(dst_ap, src_ap, r, m, ALU.mult, ALU.add))
        kb.v(_v7)
        return cv.tok()


def transpose_tile(kb, src_cols_fn, nchunks, ps_bf, dst_cols_fn, idbf,
                   waits, ps_free=None):
    ct, cv = kb.ct, kb.cv

    def _t(t):
        wait(t, ps_free, *waits)
        for c in range(nchunks):
            ins = t.transpose(ps_bf[:, c * 128:(c + 1) * 128], src_cols_fn(c), idbf)
        ct.inc(ins)
    kb.t(_t)
    tt = ct.tok()

    def _v(v):
        wait(v, tt)
        for c in range(nchunks):
            ins = v.tensor_copy(dst_cols_fn(c), ps_bf[:, c * 128:(c + 1) * 128])
        cv.inc(ins)
    kb.v(_v)
    return cv.tok(), tt


def build(debug=(), phases="ABCD"):
    global DEBUG
    DEBUG = tuple(debug)
    ctx = ExitStack()
    nc = bass.Bass("TRN2", target_bir_lowering=False, debug=False)
    inp = {}
    for name, shape in INPUT_SPECS:
        inp[name] = nc.dram_tensor(name, shape, F32, kind="ExternalInput").ap()
    out = nc.dram_tensor("out", [T, D], F32, kind="ExternalOutput").ap()
    w1t_sc = nc.dram_tensor("w1t_sc", [4 * D, D], BF16, kind="Internal").ap()
    w2t_sc = nc.dram_tensor("w2t_sc", [D, 4 * D], BF16, kind="Internal").ap()
    x4_sc = nc.dram_tensor("x4_sc", [T, D], F32, kind="Internal").ap()

    kb = KB(nc, ctx)
    dq, dg, ct, cv, cs, cg = kb.dq, kb.dg, kb.ct, kb.cv, kb.cs, kb.cg

    idbf = kb.sbuf("idbf", [128, 128], BF16)
    nidbf = kb.sbuf("nidbf", [128, 128], BF16)
    ones_b = kb.sbuf("ones_b", [1, 512], BF16)
    eps_t = kb.sbuf("eps_t", [128, 1], F32)
    sm = kb.sbuf("sm", [128, 16], F32)
    ssum = kb.sbuf("ssum", [128, 2], F32)
    sq_scr = kb.sbuf("sq_scr", [128, D], F32)
    bqkv_row = kb.sbuf("bqkv_row", [1, 3 * D], BF16)
    wv_pp = kb.sbuf("wv_pp", [128, 24], F32)

    dpro = kb.dsem("pro")

    def _(g):
        dpro.inc(g.dma_start(out=bqkv_row[:], in_=inp["L0_attn_b_qkv"]))
        dpro.inc(g.dma_start(out=wv_pp[:].rearrange("p (c k) -> p c k", k=3),
                           in_=inp["L0_attn_W_vconv"].rearrange("(c p) k -> p c k", p=128)))
        cg.inc(g.iota(idbf[:], [[1, 128]], base=0, channel_multiplier=-1,
                      allow_small_or_imprecise_dtypes=True))
    kb.g(_)
    g0 = cg.tok()

    def _(g):
        wait(g, g0)
        cg.inc(g.tensor_single_scalar(idbf[:], idbf[:], 0.0, ALU.is_equal))
    kb.g(_)
    g1 = cg.tok()

    def _(g):
        wait(g, g1)
        cg.inc(g.tensor_scalar_mul(nidbf[:], idbf[:], -1.0))
    kb.g(_)
    gid = cg.tok()

    def _(v):
        v.memset(ones_b[:], 1.0)
        cv.inc(v.memset(eps_t[:], 1.0 / EPS))
    kb.v(_)
    vc0 = cv.tok()

    def _(v):
        wait(v, vc0)
        cv.inc(v.reciprocal(eps_t[:], eps_t[:]))
    kb.v(_)
    vconst = cv.tok()
    pro_dg = dpro.tok()

    b0_steps = []
    b0_state = {}
    if "B" in phases:
        NW = 4096 * D
        asums = kb.sbuf("asums", [128, 64], F32)
        taus = kb.sbuf("taus", [128, 4], F32)
        tot_sb = kb.sbuf("tot_sb", [1, 2], F32)
        ones_f = kb.sbuf("ones_f", [1, 128], F32)
        ones_c = kb.sbuf("ones_c", [128, 1], F32)

        def _(v):
            v.memset(ones_f[:], 1.0)
            cv.inc(v.memset(ones_c[:], 1.0))
        kb.v(_)
        vof = cv.tok()

        W1r = inp["L0_ffn_W1"].rearrange("(c p) d -> p c d", p=128)
        W2r = inp["L0_ffn_W2"].rearrange("(c p) d -> p c d", p=128)
        NSLOT = 5
        tw = kb.at("tw", [128, NSLOT * 1024], F32, REG1)
        NSLOT1 = 2
        tw1 = kb.at("tw1", [128, NSLOT1 * 1024], F32, REG5)
        absd = kb.at("absd", [128, 1024], BF16, REG5 + NSLOT1 * 4096)
        thr = kb.at("thr", [128, 1024], F32, REG1 + 20480)
        wtb = kb.at("wtb", [128, 2 * 1024], F32, REG1 + 24576)
        dts = [kb.dsem("t%d" % k) for k in range(NSLOT)]
        dts1 = [kb.dsem("u%d" % k) for k in range(NSLOT1)]
        dst_sem = kb.dsem("ts")
        scope_b0ps = ExitStack()
        ps_b0 = kb.psum("ps_b0", [128, 4], F32, scope_b0ps)
        batches = []
        for b in range(32):
            batches.append((W1r[:, b:b + 1, :], b))
        for b in range(32):
            c, h = b // 4, b % 4
            batches.append((W2r[:, c:c + 1, h * 1024:(h + 1) * 1024], 32 + b))
        b0_state.update(slot_free={k: None for k in range(NSLOT)},
                        p1_free={k: None for k in range(NSLOT1)},
                        abs_prev=None, g_prev=None, st_tok=None, st_tok2=None,
                        tau_tok=None)

        def emit_load(src_ap, col):
            slot = col % NSLOT
            twa = tw[:, slot * 1024:(slot + 1) * 1024]

            def _(sync, w=b0_state["slot_free"][slot], ds=dts[slot],
                  w2=(b0_state.get("reg1_free") if col < NSLOT else None)):
                wait(sync, w, w2)
                ds.inc(sync.dma_start(out=twa.rearrange("p (c d) -> p c d",
                                                        c=src_ap.shape[1]),
                                      in_=src_ap))
            kb.q(_)
            return twa, dts[slot].tok()

        def mk_pass1(src_ap, col):
            load = {}
            slot = col % NSLOT1

            def prefetch():
                twa = tw1[:, slot * 1024:(slot + 1) * 1024]

                def _(sync, w=b0_state["p1_free"][slot], ds=dts1[slot]):
                    wait(sync, w)
                    ds.inc(sync.dma_start(
                        out=twa.rearrange("p (c d) -> p c d", c=src_ap.shape[1]),
                        in_=src_ap))
                kb.q(_)
                load["twa"], load["tok"] = twa, dts1[slot].tok()

            def step():
                def _(s, ltok=load["tok"], twa=load["twa"],
                      pv=b0_state["abs_prev"]):
                    wait(s, ltok, pv)
                    cs.inc(s.activation(absd[:], twa, AF.Abs,
                                        accum_out=asums[:, col:col + 1]))
                kb.s(_)
                b0_state["abs_prev"] = cs.tok()
                b0_state["p1_free"][slot] = cs.tok()
            return prefetch, step

        def tau_step():
            red_all = cs.tok()

            def _(v, w=red_all):
                wait(v, w)
                v.tensor_reduce(sm[:, 9:10], asums[:, 0:32], AX.X, ALU.add)
                cv.inc(v.tensor_reduce(sm[:, 10:11], asums[:, 32:64], AX.X, ALU.add))
            kb.v(_)
            tsum_tok = cv.tok()

            def _(t, w=tsum_tok, w2=vof):
                wait(t, w, w2)
                t.matmul(ps_b0[0:1, 0:1], sm[:, 9:10], ones_c[:], start=True, stop=True)
                ct.inc(t.matmul(ps_b0[0:1, 1:2], sm[:, 10:11], ones_c[:],
                                start=True, stop=True))
            kb.t(_)
            ttot = ct.tok()

            def _(v, w=ttot):
                wait(v, w)
                cv.inc(v.tensor_scalar_mul(tot_sb[0:1, 0:2], ps_b0[0:1, 0:2], 1.0 / NW))
            kb.v(_)
            vv1 = cv.tok()

            def _(v, w=vv1):
                wait(v, w)
                cv.inc(v.tensor_scalar(tot_sb[0:1, 0:2], tot_sb[0:1, 0:2], 1e-8, 0.5,
                                       ALU.max, ALU.mult))
            kb.v(_)
            vv2 = cv.tok()

            def _(t, w=vv2):
                wait(t, w)
                ct.inc(t.matmul(ps_b0[:, 2:4], ones_f[:], tot_sb[0:1, 0:2],
                                start=True, stop=True))
            kb.t(_)
            tbc = ct.tok()

            def _(v, w=tbc):
                wait(v, w)
                v.tensor_copy(taus[:, 0:1], ps_b0[:, 2:3])
                v.tensor_copy(taus[:, 2:3], ps_b0[:, 3:4])
                cv.inc(v.tensor_scalar_mul(taus[:, 1:2], ps_b0[:, 2:3], -1.0))
            kb.v(_)
            vt_ = cv.tok()

            def _(v, w=vt_):
                wait(v, w)
                cv.inc(v.tensor_scalar_mul(taus[:, 3:4], ps_b0[:, 3:4], -1.0))
            kb.v(_)
            b0_state["tau_tok"] = cv.tok()

        def tau_switch_step():
            pass

        W1s = w1t_sc.rearrange("(c p) d -> p c d", p=128)
        W2s = w2t_sc.rearrange("(c p) d -> p c d", p=128)

        def mk_pass2(src_ap, col):
            load = {}

            def prefetch():
                load["twa"], load["tok"] = emit_load(src_ap, 64 + col)

            def step():
                m = 0 if col < 32 else 1
                slot = (64 + col) % NSLOT
                ws = col % 2
                wslice = wtb[:, ws * 1024:(ws + 1) * 1024]
                if col < 32:
                    dst = W1s[:, col:col + 1, :]
                else:
                    c, h = (col - 32) // 4, (col - 32) % 4
                    dst = W2s[:, c:c + 1, h * 1024:(h + 1) * 1024]
                twa, ltok = load["twa"], load["tok"]

                def _(v, m=m, w=ltok, pv=b0_state["g_prev"],
                      tt=b0_state["tau_tok"]):
                    wait(v, w, pv, tt)
                    cv.inc(v.tensor_scalar(thr[:], twa, taus[:, 2 * m + 1:2 * m + 2],
                                           1.0, ALU.is_ge, ALU.subtract))
                kb.v(_)
                g1 = cv.tok()

                def _(v, m=m, w=g1, st=b0_state["st_tok2"], wslice=wslice):
                    wait(v, w, st)
                    cv.inc(v.scalar_tensor_tensor(wslice, twa,
                                                  taus[:, 2 * m:2 * m + 1],
                                                  thr[:], ALU.is_gt, ALU.add))
                kb.v(_)
                g4 = cv.tok()
                b0_state["g_prev"] = g4
                b0_state["slot_free"][slot] = g4

                def _(g, dst=dst, w=g4, wslice=wslice):
                    wait(g, w)
                    dst_sem.inc(g.dma_start(
                        out=dst, in_=wslice.rearrange("p (c d) -> p c d",
                                                      c=dst.shape[1])))
                kb.g(_)
                b0_state["st_tok2"] = b0_state["st_tok"]
                b0_state["st_tok"] = dst_sem.tok()
            return prefetch, step

        _p1 = [mk_pass1(s_, c_) for s_, c_ in batches]
        _p2 = [mk_pass2(s_, c_) for s_, c_ in batches]
        b0p1_steps = []
        for k in range(NSLOT1):
            _p1[k][0]()
        for k in range(64):
            def mk(k=k):
                def step():
                    _p1[k][1]()
                    if k + NSLOT1 < 64:
                        _p1[k + NSLOT1][0]()
                return step
            b0p1_steps.append(mk())
        b0_steps.append(tau_step)

        def _pf2_initial():
            for k in range(NSLOT):
                _p2[k][0]()
        b0_steps.append(_pf2_initial)
        for k in range(64):
            def mk(k=k):
                def step():
                    _p2[k][1]()
                    if k + NSLOT < 64:
                        _p2[k + NSLOT][0]()
                return step
            b0_steps.append(mk())


    q_fm = kb.at("q_fm", [128, ND * T], BF16, REG2)
    k_fm = kb.at("k_fm", [128, ND * T], BF16, REG3)
    v_fm = kb.at("v_fm", [128, ND * T], BF16, REG4)
    C_tm = kb.at("C_tm", [128, NT * D], BF16, REG5)
    n1_fm = kb.at("n1_fm", [128, ND * T], BF16, REG1)

    lnst = kb.sbuf("lnst", [128, 64], F32)
    lnb = LNB(kb, lnst, sq_scr[:], eps_t[:, 0:1])
    xstage = q_fm
    dxa = kb.dsem("xa")

    def _(g):
        dxa.inc(g.dma_start(
            out=xstage[:].rearrange("p (i d) -> p i d", i=NT),
            in_=inp["L0_attn_x"].rearrange("(i p) d -> p i d", p=128)))
    kb.g(_)
    xload = dxa.tok()

    with ExitStack() as pha:
        n1_tm = kb.at("n1_tm", [128, D], BF16, WS + 8192)
        ps_tr = kb.psum("ps_tr", [128, 1024], BF16, pha)
        stok = None
        for i in range(NT):
            for _k in range(2):
                if b0p1_steps:
                    b0p1_steps.pop(0)()
            stok = lnb.stats(xstage[:, i * D:(i + 1) * D], i, [xload], prev=stok)
        rtok = lnb.rstd_batch(0, 16, [stok])
        evtok = None
        trtok = None
        aptok = None
        for i in range(NT):
            aptok = lnb.apply(xstage[:, i * D:(i + 1) * D], n1_tm[:], i,
                              [rtok, trtok])
            evtok, trtok = transpose_tile(
                kb, lambda c: n1_tm[:, c * 128:(c + 1) * 128], ND, ps_tr[:],
                lambda c, i=i: n1_fm[:, c * T + i * 128: c * T + (i + 1) * 128],
                idbf[:], [aptok, gid], ps_free=evtok)
        n1_done = evtok
        kb.dbg("n1_fm", n1_fm[:], n1_done)
    kb.flush_dbg()
    kb.barrier(dma=False)

    with ExitStack() as pha:
        wbuf = kb.at("wbuf", [128, 2 * 8 * 512], BF16, WS)
        ps_q = [kb.psum("ps_qkv0", [128, 512], F32, pha),
                kb.psum("ps_qkv1", [128, 512], F32, pha)]
        Wq = inp["L0_attn_W_qkv"].rearrange("(c p) f -> p c f", p=128)
        dws = [kb.dsem("w0"), kb.dsem("w1")]
        slot_tok = {0: None, 1: None}
        bank_tok = {0: None, 1: None}
        for grp in range(6):
            slot = grp % 2

            def _(g, grp=grp, slot=slot, w=slot_tok[slot], ds=dws[slot]):
                wait(g, w)
                ds.inc(g.dma_start(
                    out=wbuf[:, slot * 4096:(slot + 1) * 4096].rearrange(
                        "p (c f) -> p c f", c=8),
                    in_=Wq[:, :, grp * 512:(grp + 1) * 512]))
            kb.g(_)
            wtok = dws[slot].tok()

            for fc4 in range(4):
                if b0p1_steps:
                    b0p1_steps.pop(0)()
                if grp >= 4 and b0p1_steps:
                    b0p1_steps.pop(0)()
                fc = grp * 4 + fc4
                mat, lc = fc // 8, fc % 8
                dst = (q_fm, k_fm, v_fm)[mat]
                bk = fc4 % 2
                pb = ps_q[bk]
                for tg in range(4):
                    first = (grp == 0 and fc4 == 0 and tg == 0)
                    def _(t, pb=pb, slot=slot, fc4=fc4, tg=tg, wtok=wtok,
                          fc=fc, w=bank_tok[bk], first=first):
                        wait(t, wtok, w)
                        if first:
                            wait(t, pro_dg)
                        for kc in range(8):
                            t.matmul(pb[:],
                                     wbuf[:, slot * 4096 + kc * 512 + fc4 * 128:
                                          slot * 4096 + kc * 512 + (fc4 + 1) * 128],
                                     n1_fm[:, kc * T + tg * 512: kc * T + (tg + 1) * 512],
                                     start=(kc == 0), stop=False)
                        ct.inc(t.matmul(pb[:], bqkv_row[0:1, fc * 128:(fc + 1) * 128],
                                        ones_b[0:1, :], start=False, stop=True))
                    kb.t(_)
                    mtok = ct.tok()

                    def _(v, pb=pb, mat=mat, lc=lc, dst=dst, tg=tg, mtok=mtok):
                        wait(v, mtok)
                        dd = dst[:, lc * T + tg * 512: lc * T + (tg + 1) * 512]
                        if mat == 0:
                            cv.inc(v.tensor_scalar_mul(dd, pb[:], 1.0 / 32.0))
                        else:
                            cv.inc(v.tensor_copy(dd, pb[:]))
                    kb.v(_)
                    bank_tok[bk] = cv.tok()
            slot_tok[slot] = cv.tok()
        qkv_done = cv.tok()
        b0_state["reg1_free"] = qkv_done
        kb.dbg("q_fm", q_fm[:], qkv_done)
        kb.dbg("k_fm", k_fm[:], qkv_done)
        kb.dbg("v_fm", v_fm[:], qkv_done)
    kb.flush_dbg()
    kb.barrier(dma=False)

    with ExitStack() as pha:
        cbuf = kb.at("cbuf", [128, T], BF16, WS)
        ps_trc = kb.psum("ps_trc", [128, 1024], BF16, pha)
        ev = None
        ttk = None
        for c in range(ND):
            vchunk = v_fm[:, c * T:(c + 1) * T]
            w0 = wv_pp[:, c * 3 + 0:c * 3 + 1]
            w1 = wv_pp[:, c * 3 + 1:c * 3 + 2]
            w2 = wv_pp[:, c * 3 + 2:c * 3 + 3]

            def _(v, vchunk=vchunk, w1=w1, ev=ev):
                wait(v, ev, pro_dg, qkv_done)
                cv.inc(v.tensor_scalar_mul(cbuf[:], vchunk, w1))
            kb.v(_)
            t1 = cv.tok()

            def _(v, vchunk=vchunk, w0=w0, t1=t1):
                wait(v, t1)
                cv.inc(v.scalar_tensor_tensor(cbuf[:, 1:T], vchunk[:, 0:T - 1], w0,
                                              cbuf[:, 1:T], ALU.mult, ALU.add))
            kb.v(_)
            t2 = cv.tok()

            def _(v, vchunk=vchunk, w2=w2, t2=t2):
                wait(v, t2)
                cv.inc(v.scalar_tensor_tensor(cbuf[:, 0:T - 1], vchunk[:, 1:T], w2,
                                              cbuf[:, 0:T - 1], ALU.mult, ALU.add))
            kb.v(_)
            t3 = cv.tok()

            for half in range(2):
                ev, ttk = transpose_tile(
                    kb, lambda j, half=half: cbuf[:, half * 1024 + j * 128:
                                                  half * 1024 + (j + 1) * 128],
                    8, ps_trc[:],
                    lambda j, c=c, half=half: C_tm[:, (half * 8 + j) * D + c * 128:
                                                   (half * 8 + j) * D + (c + 1) * 128],
                    idbf[:], [t3, gid], ps_free=ev)
        c_done = ev
        kb.dbg("C_tm", C_tm[:], c_done)
    kb.flush_dbg()
    kb.barrier(dma=False)

    def outln_ap(i):
        return bass.AP(q_fm, 128 * i, [[q_fm.shape[1], 128], [T, ND], [1, 128]])

    with ExitStack() as pha:
        S_sb = kb.at("S_sb", [128, T], F32, WS)
        P_sb = kb.at("P_sb", [128, T], BF16, WS + 8192)
        AT_sb = kb.at("AT_sb", [128, T], BF16, WS + 12288)
        Dw_sb = kb.at("Dw_sb", [128, D], BF16, WS + 16384)
        D_sb = kb.at("D_sb", [128, D], BF16, WS + 18432)
        B_sb = kb.sbuf("B_sb", [128, 128], BF16, pha)
        ps_s = [kb.psum("ps_s%d" % k, [128, 512], F32, pha) for k in range(4)]
        ps_at = kb.psum("ps_at", [128, 1024], BF16, pha)
        ps_o = kb.psum("ps_o", [128, 1024], F32, pha)

        sbank_tok = {k: None for k in range(4)}
        at_ps_tok = None
        p_free = None
        s_free = None
        dw_free = None
        out_ps_free = None
        at_free = None
        stok = None
        sexp_prev = None

        def emit_scores(i):
            W = 128 * (i + 1)
            out = []
            for b in range((W + 511) // 512):
                w0c = b * 512
                w1c = min(W, w0c + 512)
                bk = b % 4

                def _(t, w0c=w0c, w1c=w1c, bk=bk, i=i, w=sbank_tok[bk]):
                    wait(t, w)
                    for kc in range(ND):
                        ins = t.matmul(
                            ps_s[bk][:, 0:w1c - w0c],
                            q_fm[:, kc * T + i * 128: kc * T + (i + 1) * 128],
                            k_fm[:, kc * T + w0c: kc * T + w1c],
                            start=(kc == 0), stop=(kc == ND - 1))
                    ct.inc(ins)
                kb.t(_)
                out.append((bk, w0c, w1c, ct.tok()))
            return out

        pend = emit_scores(0)
        for i in range(NT):
            W = 128 * (i + 1)
            ev_toks = []
            for n, (bk, w0c, w1c, mtok) in enumerate(pend):
                def _(v, bk=bk, w0c=w0c, w1c=w1c, mtok=mtok,
                      sf=(s_free if n == 0 else None)):
                    wait(v, mtok, sf)
                    cv.inc(v.tensor_copy(S_sb[:, w0c:w1c], ps_s[bk][:, 0:w1c - w0c]))
                kb.v(_)
                sbank_tok[bk] = cv.tok()
                ev_toks.append(cv.tok())

            def _(g, i=i, ev=ev_toks[-1]):
                wait(g, ev)
                cg.inc(g.affine_select(S_sb[:, i * 128:(i + 1) * 128],
                                       S_sb[:, i * 128:(i + 1) * 128],
                                       [[-1, 128]], ALU.is_ge, -1e30,
                                       base=0, channel_multiplier=1))
            kb.g(_)
            gmask = cg.tok()

            def _(v, W=W, gmask=gmask, ev=list(ev_toks), sp=sexp_prev):
                wait(v, gmask, sp, *ev)
                cv.inc(v.reduce_max(sm[:, 8:9], S_sb[:, 0:W], AX.X, negate=True))
            kb.v(_)
            vmax = cv.tok()

            def _(s, W=W, vmax=vmax, pf=p_free):
                wait(s, vmax, pf)
                cs.inc(s.activation(P_sb[:, 0:W], S_sb[:, 0:W], AF.Exp,
                                    bias=sm[:, 8:9], accum_out=ssum[:, 0:1]))
            kb.s(_)
            sexp = cs.tok()
            sexp_prev = sexp
            s_free = sexp

            if i + 1 < NT:
                pend = emit_scores(i + 1)
            while b0p1_steps:
                b0p1_steps.pop(0)()
            for _k in range(6 if i else 3):
                if b0_steps:
                    b0_steps.pop(0)()

            at_toks = []
            for bg in range((i + 8) // 8):
                j0, j1 = bg * 8, min(i + 1, bg * 8 + 8)

                def _(t, j0=j0, j1=j1, sexp=sexp, w=at_ps_tok,
                      af=(at_free if bg == 0 else None)):
                    wait(t, sexp, w, af)
                    for j in range(j0, j1):
                        ins = t.transpose(ps_at[:, (j - j0) * 128:(j - j0 + 1) * 128],
                                          P_sb[:, j * 128:(j + 1) * 128], idbf[:])
                    ct.inc(ins)
                kb.t(_)
                ttok = ct.tok()
                p_free = ttok

                def _(v, j0=j0, j1=j1, ttok=ttok):
                    wait(v, ttok)
                    cv.inc(v.tensor_copy(AT_sb[:, j0 * 128:j1 * 128],
                                         ps_at[:, 0:(j1 - j0) * 128]))
                kb.v(_)
                at_ps_tok = cv.tok()
                at_toks.append(cv.tok())

            def _(v, i=i, at=at_toks[-1]):
                wait(v, at)
                cv.inc(v.tensor_mul(B_sb[:], AT_sb[:, i * 128:(i + 1) * 128], nidbf[:]))
            kb.v(_)
            btok = cv.tok()

            def _(v, i=i, w=dw_free):
                wait(v, w, qkv_done)
                t0 = i * 128 + 1
                wlen = min(T, (i + 1) * 128 + 1) - t0
                for c in range(ND):
                    if wlen < 128:
                        v.memset(Dw_sb[:, c * 128 + wlen:(c + 1) * 128], 0.0)
                    ins = v.tensor_scalar_mul(Dw_sb[:, c * 128:c * 128 + wlen],
                                              v_fm[:, c * T + t0:c * T + t0 + wlen],
                                              wv_pp[:, c * 3 + 2:c * 3 + 3])
                cv.inc(ins)
            kb.v(_)
            dwtok = cv.tok()

            dtr, dw_free = transpose_tile(
                kb, lambda c: Dw_sb[:, c * 128:(c + 1) * 128], ND, ps_at[:],
                lambda c: D_sb[:, c * 128:(c + 1) * 128],
                idbf[:], [dwtok, at_free], ps_free=at_ps_tok)
            at_ps_tok = dtr

            def _(t, i=i, at=list(at_toks), btok=btok, dtr=dtr, w=out_ps_free):
                wait(t, w, btok, dtr, *at)
                for half in range(2):
                    for j in range(i + 1):
                        t.matmul(ps_o[:, half * 512:(half + 1) * 512],
                                 AT_sb[:, j * 128:(j + 1) * 128],
                                 C_tm[:, j * D + half * 512: j * D + (half + 1) * 512],
                                 start=(j == 0), stop=False)
                    ins = t.matmul(ps_o[:, half * 512:(half + 1) * 512], B_sb[:],
                                   D_sb[:, half * 512:(half + 1) * 512],
                                   start=False, stop=True)
                ct.inc(ins)
            kb.t(_)
            otok = ct.tok()
            at_free = otok

            def _(v, sexp=sexp):
                wait(v, sexp)
                cv.inc(v.reciprocal(ssum[:, 1:2], ssum[:, 0:1]))
            kb.v(_)
            rtok = cv.tok()

            def _(v, i=i, w=otok):
                wait(v, w)
                cv.inc(v.tensor_copy(outln_ap(i), ps_o[:]))
            kb.v(_)
            out_ps_free = cv.tok()

            stok = lnb.stats(outln_ap(i), i, [out_ps_free, rtok],
                             f_ap=ssum[:, 1:2], prev=stok)
            if i == NT - 1 and "a3_oraw" in DEBUG:
                oraw_sc = kb.sbuf("oraw_sc", [128, ND * 128], F32, pha)
                p_sc = kb.sbuf("p_sc", [128, T], BF16, pha)

                def _(v, i=i, w=out_ps_free, se=sexp):
                    wait(v, w, se)
                    cv.inc(v.tensor_copy(oraw_sc[:], outln_ap(i)))
                    cv.inc(v.tensor_copy(p_sc[:], P_sb[:]))
                kb.v(_)
                a3dbg_tok = cv.tok()
                kb.dbg("a3_oraw", oraw_sc[:], a3dbg_tok)
                kb.dbg("a3_p", p_sc[:], a3dbg_tok)
                kb.dbg("a3_ssum", ssum[:], a3dbg_tok)
            else:
                a3dbg_tok = None
        while b0_steps:
            b0_steps.pop(0)()
        rtok_b = lnb.rstd_batch(0, 16, [stok])
        aptok = None
        for i in range(NT):
            aptok = lnb.apply(outln_ap(i), outln_ap(i), i,
                              [rtok_b, aptok, a3dbg_tok if i == 0 else None],
                              scaled=True)
        a3_done = aptok
        kb.dbg("outln_q", q_fm[:], a3_done)
        kb.dbg("a3_lnst", lnst[:], a3_done)
        kb.flush_dbg()

    kb.barrier(dma=False)
    if "B" in phases:
        scope_b0ps.close()

    if "B" in phases:
        scope_b1 = ExitStack()
        scope_b1tr = ExitStack()
        X2 = kb.at("X2", [128, NT * D], F32, REG4)
        act_fm = kb.at("act_fm", [128, ND * T], BF16, REG3)
        xt2 = kb.at("xt2", [128, 2 * D], F32, WS)
        tmb = kb.at("tmb", [128, D], BF16, WS + 16384)
        ps_tr2 = kb.psum("ps_tr2", [128, 1024], BF16, scope_b1tr)
        dx2 = [kb.dsem("x0"), kb.dsem("x1")]
        slot_done = {0: None, 1: None}
        stok = None
        for i in range(NT):
            slot = i % 2
            xa = xt2[:, slot * D:(slot + 1) * D]

            def _(sync, i=i, xa=xa, w=slot_done[slot], ds=dx2[slot]):
                wait(sync, w)
                ds.inc(sync.dma_start(out=xa, in_=inp["L0_attn_x"][i * 128:(i + 1) * 128, :]))
            kb.q(_)
            ldtok = dx2[slot].tok()

            def _(v, i=i, xa=xa, ldtok=ldtok):
                wait(v, ldtok)
                cv.inc(v.tensor_add(X2[:, i * D:(i + 1) * D], xa, outln_ap(i)))
            kb.v(_)
            addtok = cv.tok()
            slot_done[slot] = addtok

            stok = lnb.stats(X2[:, i * D:(i + 1) * D], i, [addtok], prev=stok)
        rtok_b = lnb.rstd_batch(0, 16, [stok])
        aptok = None
        evtok = None
        trtok = None
        for i in range(NT):
            aptok = lnb.apply(X2[:, i * D:(i + 1) * D], tmb[:], i, [rtok_b, trtok])
            evtok, trtok = transpose_tile(
                kb, lambda c: tmb[:, c * 128:(c + 1) * 128], ND, ps_tr2[:],
                lambda c, i=i: act_fm[:, c * T + i * 128: c * T + (i + 1) * 128],
                idbf[:], [aptok], ps_free=evtok)
        b1_done = evtok
        kb.dbg("x2", X2[:], aptok)
        kb.dbg("n2_fm", act_fm[:], b1_done)
        kb.flush_dbg()
        kb.barrier()
        scope_b1tr.close()

        h_g = kb.at("h_g", [128, 32 * 512], BF16, REG1)
        w1tb = kb.at("w1tb", [128, 2 * 8 * 512], BF16, REG2)
        b1row = kb.at("b1row", [1, 4096], BF16, REG2 + 16384)
        b2row = kb.at("b2row", [1, 1024], BF16, REG2 + 24576)
        w2tb = kb.at("w2tb", [128, 4 * 1024], BF16, WS)
        out2g = kb.at("out2g", [128, 4 * 1024], BF16, WS + 8192)
        smB = kb.sbuf("smB", [128, 12], F32)
        drow = kb.dsem("rows")

        def _(g):
            drow.inc(g.dma_start(out=b1row[:], in_=inp["L0_ffn_b1"]))
            drow.inc(g.dma_start(out=b2row[:], in_=inp["L0_ffn_b2"]))
        kb.g(_)
        rows_tok = drow.tok()

        dw1 = [kb.dsem("tw1a"), kb.dsem("tw1b")]
        dw2 = [kb.dsem("w2a"), kb.dsem("w2b")]
        ps_h = [kb.psum("ps_h0", [128, 1024], F32, scope_b1),
                kb.psum("ps_h1", [128, 1024], F32, scope_b1)]
        ps_o2 = [kb.psum("ps_o2%d" % u, [128, 1024], F32, scope_b1) for u in range(2)]

        w1slot_tok = {0: None, 1: None}
        w2slot_tok = {0: None, 1: None}
        hbank_tok = {0: None, 1: None}
        o2_tok = {0: None, 1: None}
        hg_free = None
        out2g_free = None
        for grp in range(4):
            for p in range(8):
                slot = p % 2

                def _(sync, p=p, slot=slot, w=w1slot_tok[slot], ds=dw1[slot],
                      w2=(b0_state.get("st_tok") if (grp == 0 and p < 2) else None)):
                    wait(sync, w, w2)
                    for dc in range(8):
                        ds.inc(sync.dma_start(
                            out=w1tb[:, slot * 4096 + dc * 512:slot * 4096 + (dc + 1) * 512],
                            in_=w1t_sc[p * 512:(p + 1) * 512, dc * 128:(dc + 1) * 128],
                            transpose=True))
                kb.q(_)
                wtok = dw1[slot].tok()

                for fpr in range(2):
                    bk = fpr % 2
                    pb = ps_h[bk]

                    def _(t, pb=pb, slot=slot, fpr=fpr, p=p, grp=grp, wtok=wtok,
                          w=hbank_tok[bk]):
                        wait(t, wtok, w, rows_tok, b1_done)
                        for half in range(2):
                            fl = 2 * fpr + half
                            fc = 4 * p + fl
                            for kc in range(8):
                                t.matmul(pb[:, half * 512:(half + 1) * 512],
                                         w1tb[:, slot * 4096 + kc * 512 + fl * 128:
                                              slot * 4096 + kc * 512 + (fl + 1) * 128],
                                         act_fm[:, kc * T + grp * 512:
                                                kc * T + (grp + 1) * 512],
                                         start=(kc == 0), stop=False)
                            ins = t.matmul(pb[:, half * 512:(half + 1) * 512],
                                           b1row[0:1, fc * 128:(fc + 1) * 128],
                                           ones_b[0:1, :], start=False, stop=True)
                        ct.inc(ins)
                    kb.t(_)
                    mtok = ct.tok()

                    def _(s, pb=pb, p=p, fpr=fpr, mtok=mtok, hf=hg_free,
                          bf=(b0_state.get("st_tok") if grp == 0 and p < 2 else None),
                          bv=(b0_state.get("vthr_tok") if grp == 0 and p < 2 else None)):
                        wait(s, mtok, hf, bf, bv)
                        fc0 = 4 * p + 2 * fpr
                        cs.inc(s.activation(h_g[:, fc0 * 512:(fc0 + 2) * 512], pb[:],
                                            AF.Gelu))
                    kb.s(_)
                    hbank_tok[bk] = cs.tok()
                w1slot_tok[slot] = ct.tok()
            h_done = cs.tok()
            if grp == 3:
                kb.dbg("h_g", h_g[:], h_done)

            ev_toks = [None] * 4
            for pair in range(2):
                for fc in range(32):
                    slot = fc % 2
                    wslice = w2tb[:, slot * 1024:(slot + 1) * 1024]
                    def _(sync, fc=fc, wslice=wslice, w=w2slot_tok[slot], ds=dw2[slot],
                          w2=(b0_state.get("st_tok") if (grp == 0 and pair == 0 and fc < 2) else None)):
                        wait(sync, w, w2)
                        ds.inc(sync.dma_start(out=wslice,
                                              in_=w2t_sc[:, fc * 128:(fc + 1) * 128],
                                              transpose=True))
                    kb.q(_)
                    w2tok = dw2[slot].tok()

                    def _(t, fc=fc, pair=pair, wslice=wslice, w2tok=w2tok,
                          h_done=h_done,
                          ot=[o2_tok[u] for u in range(2)] if fc == 0 else []):
                        wait(t, w2tok, h_done, *ot)
                        for u in range(2):
                            sub = pair * 2 + u
                            for half in range(2):
                                ins = t.matmul(
                                    ps_o2[u][:, half * 512:(half + 1) * 512],
                                    h_g[:, fc * 512 + sub * 128:fc * 512 + (sub + 1) * 128],
                                    wslice[:, half * 512:(half + 1) * 512],
                                    start=(fc == 0), stop=False)
                        if fc == 31:
                            for u in range(2):
                                for half in range(2):
                                    ins = t.matmul(
                                        ps_o2[u][:, half * 512:(half + 1) * 512],
                                        ones_b[0:1, 0:128],
                                        b2row[0:1, half * 512:(half + 1) * 512],
                                        start=False, stop=True)
                        ct.inc(ins)
                    kb.t(_)
                    w2slot_tok[slot] = ct.tok()
                o2done = ct.tok()

                for u in range(2):
                    sub = pair * 2 + u

                    def _(s, u=u, sub=sub, w=o2done, of=out2g_free):
                        wait(s, w, of)
                        cs.inc(s.activation(out2g[:, sub * D:(sub + 1) * D],
                                            ps_o2[u][:], AF.Identity))
                    kb.s(_)
                    ev_toks[sub] = cs.tok()
                    o2_tok[u] = cs.tok()

            stat_toks = []
            for sub in range(4):
                evt = ev_toks[sub]

                def _(v, sub=sub, w=evt):
                    wait(v, w)
                    cv.inc(v.tensor_reduce(smB[:, sub:sub + 1],
                                           out2g[:, sub * D:(sub + 1) * D],
                                           AX.X, ALU.add))
                kb.v(_)
                vt = cv.tok()

                def _(s, sub=sub, w=vt):
                    wait(s, w)
                    cs.inc(s.activation(sq_scr[:], out2g[:, sub * D:(sub + 1) * D],
                                        AF.Square, accum_out=smB[:, 4 + sub:5 + sub]))
                kb.s(_)
                sqt = cs.tok()

                def _(v, sub=sub, w=sqt):
                    wait(v, w)
                    v.tensor_scalar_mul(smB[:, sub:sub + 1], smB[:, sub:sub + 1], 1.0 / D)
                    cv.inc(v.tensor_scalar_mul(smB[:, 4 + sub:5 + sub],
                                               smB[:, 4 + sub:5 + sub], 1.0 / D))
                kb.v(_)
                vt2_ = cv.tok()

                def _(v, sub=sub, w=vt2_):
                    wait(v, w)
                    cv.inc(v.tensor_tensor(smB[:, 8 + sub:9 + sub],
                                           smB[:, sub:sub + 1], smB[:, sub:sub + 1],
                                           ALU.mult))
                kb.v(_)
                vt3_ = cv.tok()

                def _(v, sub=sub, w=vt3_):
                    wait(v, w)
                    cv.inc(v.tensor_sub(smB[:, 4 + sub:5 + sub],
                                        smB[:, 4 + sub:5 + sub], smB[:, 8 + sub:9 + sub]))
                kb.v(_)
                stat_toks.append(cv.tok())

            def _(s, w=list(stat_toks)):
                wait(s, *w)
                cs.inc(s.activation(smB[:, 8:12], smB[:, 4:8], AF.Ln,
                                    bias=eps_t[:, 0:1]))
            kb.s(_)
            r1 = cs.tok()

            def _(s, w=r1):
                wait(s, w)
                cs.inc(s.activation(smB[:, 8:12], smB[:, 8:12], AF.Exp, scale=-0.5))
            kb.s(_)
            r2 = cs.tok()

            ap_toks = []
            for sub in range(4):
                i = grp * 4 + sub

                def _(v, sub=sub, w=r2):
                    wait(v, w)
                    cv.inc(v.scalar_tensor_tensor(smB[:, sub:sub + 1],
                                                  smB[:, sub:sub + 1], -1.0,
                                                  smB[:, 8 + sub:9 + sub],
                                                  ALU.mult, ALU.mult))
                kb.v(_)
                bt = cv.tok()

                def _(s, sub=sub, i=i, w=bt):
                    wait(s, w)
                    cs.inc(s.activation(tmb[:], out2g[:, sub * D:(sub + 1) * D],
                                        AF.Identity, bias=smB[:, sub:sub + 1],
                                        scale=smB[:, 8 + sub:9 + sub]))
                kb.s(_)
                apt = cs.tok()

                def _(v, i=i, w=apt):
                    wait(v, w)
                    cv.inc(v.tensor_add(X2[:, i * D:(i + 1) * D],
                                        X2[:, i * D:(i + 1) * D], tmb[:]))
                kb.v(_)
                ap_toks.append(cv.tok())
            out2g_free = cv.tok()
            hg_free = o2done
        b2_done = cv.tok()
        kb.dbg("x3", X2[:], b2_done)
        kb.flush_dbg()
        kb.barrier()
        scope_b1.close()


    if "C" in phases:
        scl = kb.sbuf("scl", [1, 16], F32)
        g8s = kb.sbuf("g8s", [1, 16], F32)
        lnd_pp = kb.sbuf("lnd_pp", [128, 1], F32)
        osc_pp = kb.sbuf("osc_pp", [128, 1], F32)
        ones_f2 = kb.sbuf("ones_f2", [1, 128], F32)
        basis_b = kb.at("basis_b", [128, 8 * 256], BF16, REG1)
        cf_b = kb.at("cf_b", [128, 4 * 2 * 256], BF16, REG1 + 4096)
        basisT = kb.at("basisT", [128, 2 * 1024], BF16, REG1 + 8192)
        cfT = kb.at("cfT", [128, 4 * 2 * 256], BF16, REG1 + 12288)
        qw_b = kb.at("qw_b", [128, 8 * 256], BF16, REG1 + 16384)
        kw_b = kb.at("kw_b", [128, 8 * 256], BF16, REG1 + 20480)
        vw_b = kb.at("vw_b", [128, 8 * 256], BF16, REG1 + 24576)
        owT_b = kb.at("owT_b", [128, 2 * 1024], BF16, REG1 + 28672)
        dl1 = kb.dsem("l1a")

        def _(g):
            dl1.inc(g.dma_start(out=scl[0:1, 0:1], in_=inp["L1_attn_decay_logit"]))
            dl1.inc(g.dma_start(out=scl[0:1, 1:2], in_=inp["L1_attn_out_scale"]))
            dl1.inc(g.dma_start(out=g8s[0:1, 0:8], in_=inp["L1_ffn_smha_gates"]))
            dl1.inc(g.dma_start(out=g8s[0:1, 8:16], in_=inp["L1_ffn_moe_gates"]))
            dl1.inc(g.dma_start(
                out=basis_b[:].rearrange("p (c b) -> p c b", c=8),
                in_=inp["L1_attn_basis"].rearrange("(c p) b -> p c b", p=128)))
            for x, name in enumerate(("q", "k", "v", "o")):
                dl1.inc(g.dma_start(
                    out=cf_b[:, x * 512:(x + 1) * 512].rearrange("p (c b) -> p c b", c=2),
                    in_=inp["L1_attn_%s_coeffs" % name].rearrange("(c p) b -> p c b", p=128)))
        kb.g(_)
        l1tok = dl1.tok()

        def _(v):
            cv.inc(v.memset(ones_f2[:], 1.0))
        kb.v(_)
        vo2 = cv.tok()

        def _(s, w=l1tok):
            wait(s, w)
            s.activation(scl[0:1, 2:3], scl[0:1, 0:1], AF.Sigmoid)
            s.activation(g8s[0:1, 0:8], g8s[0:1, 0:8], AF.Sigmoid)
            cs.inc(s.activation(g8s[0:1, 8:16], g8s[0:1, 8:16], AF.Sigmoid))
        kb.s(_)
        sig_tok = cs.tok()

        def _(v, w=sig_tok):
            wait(v, w)
            cv.inc(v.tensor_scalar_max(scl[0:1, 2:3], scl[0:1, 2:3], 1e-38))
        kb.v(_)
        cl_tok = cv.tok()

        def _(s, w=cl_tok):
            wait(s, w)
            cs.inc(s.activation(scl[0:1, 3:4], scl[0:1, 2:3], AF.Ln))
        kb.s(_)
        lnd_tok = cs.tok()

        scope_c0 = ExitStack()
        ps_c0 = kb.psum("ps_c0", [128, 1024], F32, scope_c0)
        ps_c1 = kb.psum("ps_c1", [128, 1024], F32, scope_c0)
        ps_trC = kb.psum("ps_trC", [128, 1024], BF16, scope_c0)

        def _(t, w=lnd_tok, w2=vo2):
            wait(t, w, w2)
            t.matmul(ps_c0[:, 0:1], ones_f2[:], scl[0:1, 3:4], start=True, stop=True)
            ct.inc(t.matmul(ps_c0[:, 1:2], ones_f2[:], scl[0:1, 1:2], start=True, stop=True))
        kb.t(_)
        tb = ct.tok()

        def _(v, w=tb):
            wait(v, w)
            v.tensor_copy(lnd_pp[:], ps_c0[:, 0:1])
            cv.inc(v.tensor_copy(osc_pp[:], ps_c0[:, 1:2]))
        kb.v(_)
        bc_tok = cv.tok()

        tr_toks = []
        for c in range(8):
            for h in range(2):
                ev, tt = transpose_tile(
                    kb, lambda _c, c=c, h=h: basis_b[:, c * 256 + h * 128:
                                                     c * 256 + (h + 1) * 128],
                    1, ps_trC[:],
                    lambda _c, c=c, h=h: basisT[:, h * 1024 + c * 128:
                                                h * 1024 + (c + 1) * 128],
                    idbf[:], [l1tok, (tr_toks[-1] if tr_toks else None)],
                    ps_free=(tr_toks[-1] if tr_toks else None))
                tr_toks.append(ev)
        for x in range(4):
            for c in range(2):
                for h in range(2):
                    ev, tt = transpose_tile(
                        kb, lambda _c, x=x, c=c, h=h: cf_b[:, x * 512 + c * 256 +
                                                           h * 128: x * 512 + c * 256 +
                                                           (h + 1) * 128],
                        1, ps_trC[:],
                        lambda _c, x=x, c=c, h=h: cfT[:, x * 512 + h * 256 + c * 128:
                                                      x * 512 + h * 256 + (c + 1) * 128],
                        idbf[:], [l1tok, tr_toks[-1]], ps_free=tr_toks[-1])
                    tr_toks.append(ev)
        trC_done = tr_toks[-1]

        wsmall_ev = None
        for x, dstb in enumerate((qw_b, kw_b, vw_b)):
            for vc in range(8):
                def _(t, x=x, vc=vc, w=trC_done, w2=wsmall_ev):
                    wait(t, w, w2)
                    for bc in range(2):
                        ins = t.matmul(ps_c0[:, 0:256],
                                       basisT[:, bc * 1024 + vc * 128:bc * 1024 + (vc + 1) * 128],
                                       cfT[:, x * 512 + bc * 256: x * 512 + (bc + 1) * 256],
                                       start=(bc == 0), stop=(bc == 1))
                    ct.inc(ins)
                kb.t(_)
                mt = ct.tok()

                def _(v, dstb=dstb, vc=vc, mt=mt):
                    wait(v, mt)
                    cv.inc(v.tensor_copy(dstb[:, vc * 256:(vc + 1) * 256], ps_c0[:, 0:256]))
                kb.v(_)
                wsmall_ev = cv.tok()
        for chc in range(2):
            def _(t, chc=chc, w=trC_done, w2=wsmall_ev):
                wait(t, w, w2)
                for bc in range(2):
                    for half in range(2):
                        ins = t.matmul(ps_c1[:, half * 512:(half + 1) * 512],
                                       cfT[:, 3 * 512 + bc * 256 + chc * 128:
                                           3 * 512 + bc * 256 + (chc + 1) * 128],
                                       basisT[:, bc * 1024 + half * 512:
                                              bc * 1024 + (half + 1) * 512],
                                       start=(bc == 0), stop=(bc == 1))
                ct.inc(ins)
            kb.t(_)
            mt = ct.tok()

            def _(v, chc=chc, mt=mt):
                wait(v, mt)
                cv.inc(v.tensor_copy(owT_b[:, chc * 1024:(chc + 1) * 1024], ps_c1[:]))
            kb.v(_)
            wsmall_ev = cv.tok()

        G_sb = kb.at("G_sb", [128, 4096], BF16, REG1)
        m01 = kb.at("m01", [128, 4096], BF16, REG1 + 8192)
        dio = kb.at("dio", [128, 4096], F32, WS)

        def _(g, w=wsmall_ev):
            wait(g, w, trC_done)
            cg.inc(g.iota(dio[:], [[1, 4096]], base=-1920, channel_multiplier=-1,
                          allow_small_or_imprecise_dtypes=True))
        kb.g(_)
        gio = cg.tok()

        def _(v, w=gio):
            wait(v, w)
            cv.inc(v.tensor_single_scalar(m01[:], dio[:], 0.0, ALU.is_gt))
        kb.v(_)
        vm01 = cv.tok()

        def _(v, w=vm01):
            wait(v, w)
            cv.inc(v.tensor_scalar(dio[:], dio[:], 1.0, 0.0, ALU.subtract, ALU.max))
        kb.v(_)
        vexp = cv.tok()

        def _(s, w=vexp, w2=bc_tok):
            wait(s, w, w2)
            cs.inc(s.activation(G_sb[:], dio[:], AF.Exp, scale=lnd_pp[:, 0:1]))
        kb.s(_)
        sG = cs.tok()

        def _(v, w=sG):
            wait(v, w)
            cv.inc(v.tensor_mul(G_sb[:], G_sb[:], m01[:]))
        kb.v(_)
        G_done = cv.tok()
        kb.barrier()
        scope_c0.close()

        scope_c1 = ExitStack()
        tmc = kb.at("tmc", [128, D], BF16, WS + 8192)
        ps_trD = kb.psum("ps_trD", [128, 1024], BF16, scope_c1)
        stok = None
        for i in range(NT):
            stok = lnb.stats(X2[:, i * D:(i + 1) * D], i, [], prev=stok)
        rtok_b = lnb.rstd_batch(0, 16, [stok])
        aptok = None
        evtok = None
        trtok = None
        for i in range(NT):
            aptok = lnb.apply(X2[:, i * D:(i + 1) * D], tmc[:], i, [rtok_b, trtok])
            evtok, trtok = transpose_tile(
                kb, lambda c: tmc[:, c * 128:(c + 1) * 128], ND, ps_trD[:],
                lambda c, i=i: act_fm[:, c * T + i * 128: c * T + (i + 1) * 128],
                idbf[:], [aptok], ps_free=evtok)
        c1_done = evtok
        kb.dbg("n3_fm", act_fm[:], c1_done)
        kb.flush_dbg()
        kb.barrier()
        scope_c1.close()

        scope_c2 = ExitStack()
        queries_b = kb.at("queries_b", [128, 2 * T], BF16, REG2)
        keys_b = kb.at("keys_b", [128, 2 * T], BF16, REG2 + 8192)
        values_b = kb.at("values_b", [128, 2 * T], BF16, REG2 + 16384)
        values_tm = kb.at("values_tm", [128, NT * 256], BF16, REG2 + 24576)
        ps_p = [kb.psum("ps_p0", [128, 512], F32, scope_c2),
                kb.psum("ps_p1", [128, 512], F32, scope_c2)]
        pbank = {0: None, 1: None}
        for x, dstb in enumerate((qw_b, kw_b, vw_b)):
            outb = (queries_b, keys_b, values_b)[x]
            for chc in range(2):
                for tg in range(4):
                    bk = tg % 2

                    def _(t, x=x, chc=chc, tg=tg, bk=bk, dstb=dstb, w=pbank[bk]):
                        wait(t, w)
                        for vc in range(8):
                            ins = t.matmul(
                                ps_p[bk][:],
                                dstb[:, vc * 256 + chc * 128:vc * 256 + (chc + 1) * 128],
                                act_fm[:, vc * T + tg * 512: vc * T + (tg + 1) * 512],
                                start=(vc == 0), stop=(vc == 7))
                        ct.inc(ins)
                    kb.t(_)
                    mt = ct.tok()

                    def _(v, outb=outb, chc=chc, tg=tg, bk=bk, mt=mt):
                        wait(v, mt)
                        cv.inc(v.tensor_copy(
                            outb[:, chc * T + tg * 512: chc * T + (tg + 1) * 512],
                            ps_p[bk][:]))
                    kb.v(_)
                    pbank[bk] = cv.tok()
        qkv1_done = cv.tok()
        kb.dbg("queries_b", queries_b[:], qkv1_done)

        ps_trE = kb.psum("ps_trE", [128, 1024], BF16, scope_c2)
        vt_ev = None
        for j in range(NT):
            for chc in range(2):
                vt_ev, _tt = transpose_tile(
                    kb, lambda _c, j=j, chc=chc: values_b[:, chc * T + j * 128:
                                                          chc * T + (j + 1) * 128],
                    1, ps_trE[:],
                    lambda _c, j=j, chc=chc: values_tm[:, j * 256 + chc * 128:
                                                       j * 256 + (chc + 1) * 128],
                    idbf[:], [qkv1_done, vt_ev], ps_free=vt_ev)
        vtm_done = vt_ev
        kb.barrier()

        Sp_sb = kb.at("Sp_sb", [128, T], BF16, WS)
        SpT = kb.at("SpT", [128, 16 * 512], BF16, WS + 4096)
        retr_b = kb.at("retr_b", [128, 2 * 512], BF16, WS + 20480)
        a1raw = kb.at("a1raw", [128, 8 * D], BF16, REG1 + 8192)
        a1ln = kb.sbuf("a1ln", [128, D], BF16)
        ps_at2 = kb.psum("ps_at2", [128, 1024], BF16, scope_c2)
        ps_o1 = kb.psum("ps_o1", [128, 1024], F32, scope_c2)
        dx4 = kb.dsem("x4")
        sbk = {0: None, 1: None}
        at_tok = None
        spt_free = None
        sp_free = None
        retr_free = None
        o1_free = None
        a1_free = None
        stok = None
        addtok = None
        for bg in range(4):
            ts0 = 512 * bg
            retr_toks = []
            sp_done = []
            for u in range(4):
                i = 4 * bg + u
                Wd = T - ts0
                nbk = Wd // 512
                ev_toks = []
                for b in range(nbk):
                    w0c = ts0 + b * 512
                    bk = b % 2

                    def _(t, i=i, w0c=w0c, bk=bk, w=sbk[bk]):
                        wait(t, w, qkv1_done)
                        for chc in range(2):
                            ins = t.matmul(
                                ps_p[bk][:],
                                queries_b[:, chc * T + i * 128: chc * T + (i + 1) * 128],
                                keys_b[:, chc * T + w0c: chc * T + w0c + 512],
                                start=(chc == 0), stop=(chc == 1))
                        ct.inc(ins)
                    kb.t(_)
                    mt = ct.tok()

                    def _(v, i=i, w0c=w0c, bk=bk, mt=mt, G_done=G_done,
                          spf=(sp_free if b == 0 else None)):
                        wait(v, mt, G_done, spf)
                        cv.inc(v.scalar_tensor_tensor(
                            Sp_sb[:, w0c - ts0: w0c - ts0 + 512], ps_p[bk][:], 1.0,
                            G_sb[:, 1920 - 128 * i + w0c: 1920 - 128 * i + w0c + 512],
                            ALU.mult, ALU.mult))
                    kb.v(_)
                    sbk[bk] = cv.tok()
                    ev_toks.append(cv.tok())

                for grp8 in range((16 - 4 * bg + 7) // 8):
                    j0 = 4 * bg + grp8 * 8
                    j1 = min(16, j0 + 8)

                    def _(t, i=i, j0=j0, j1=j1, ts0=ts0, w=list(ev_toks),
                          pf=at_tok, sf=(spt_free if (u == 0 and grp8 == 0) else None)):
                        wait(t, pf, sf, *w)
                        for j in range(j0, j1):
                            ins = t.transpose(ps_at2[:, (j - j0) * 128:(j - j0 + 1) * 128],
                                              Sp_sb[:, j * 128 - ts0: (j + 1) * 128 - ts0],
                                              idbf[:])
                        ct.inc(ins)
                    kb.t(_)
                    tt = ct.tok()
                    sp_free = tt

                    def _(v, i=i, u=u, j0=j0, j1=j1, bg=bg, tt=tt):
                        wait(v, tt)
                        for j in range(j0, j1):
                            ins = v.tensor_copy(
                                SpT[:, (j - 4 * bg) * 512 + u * 128:
                                    (j - 4 * bg) * 512 + (u + 1) * 128],
                                ps_at2[:, (j - j0) * 128:(j - j0 + 1) * 128])
                        cv.inc(ins)
                    kb.v(_)
                    at_tok = cv.tok()
                sp_done.append(at_tok)

            for chc in range(2):
                def _(t, chc=chc, bg=bg, w=list(sp_done), rf=retr_free):
                    wait(t, vtm_done, rf, *w)
                    njc = 16 - 4 * bg
                    for jj in range(njc):
                        j = 4 * bg + jj
                        ins = t.matmul(ps_p[chc][:],
                                       values_tm[:, j * 256 + chc * 128:
                                                 j * 256 + (chc + 1) * 128],
                                       SpT[:, jj * 512:(jj + 1) * 512],
                                       start=(jj == 0), stop=(jj == njc - 1))
                    ct.inc(ins)
                kb.t(_)
                mt = ct.tok()

                def _(v, chc=chc, mt=mt):
                    wait(v, mt)
                    cv.inc(v.tensor_copy(retr_b[:, chc * 512:(chc + 1) * 512],
                                         ps_p[chc][:]))
                kb.v(_)
                retr_toks.append(cv.tok())
            spt_free = ct.tok()
            retr_done = cv.tok()
            sbk[0] = retr_done
            sbk[1] = retr_done

            for u in range(4):
                i = 4 * bg + u

                def _(t, u=u, w=retr_done, of=o1_free):
                    wait(t, w, of)
                    for chc in range(2):
                        for half in range(2):
                            ins = t.matmul(
                                ps_o1[:, half * 512:(half + 1) * 512],
                                retr_b[:, chc * 512 + u * 128: chc * 512 + (u + 1) * 128],
                                owT_b[:, chc * 1024 + half * 512:
                                      chc * 1024 + (half + 1) * 512],
                                start=(chc == 0), stop=(chc == 1))
                    ct.inc(ins)
                kb.t(_)
                otok = ct.tok()
                retr_free = otok

                def _(v, i=i, w=otok, af=a1_free):
                    wait(v, w, af)
                    cv.inc(v.tensor_copy(a1raw[:, (i % 8) * D:(i % 8 + 1) * D],
                                         ps_o1[:]))
                kb.v(_)
                o1_free = cv.tok()

                stok = lnb.stats(a1raw[:, (i % 8) * D:(i % 8 + 1) * D], i % 8,
                                 [o1_free, bc_tok], f_ap=osc_pp[:, 0:1],
                                 prev=stok)

            if bg % 2 == 1:
                rtok_b = lnb.rstd_batch(0, 8, [stok, o1_free])
                aptok = None
                for u8 in range(8):
                    i = (bg - 1) * 4 + u8
                    aptok = lnb.apply(a1raw[:, u8 * D:(u8 + 1) * D],
                                      a1ln[:], u8, [rtok_b, addtok], scaled=True)

                    def _(v, i=i, w=aptok):
                        wait(v, w)
                        cv.inc(v.tensor_add(X2[:, i * D:(i + 1) * D],
                                            X2[:, i * D:(i + 1) * D], a1ln[:]))
                    kb.v(_)
                    addtok = cv.tok()

                    def _(sync, i=i, w=addtok):
                        wait(sync, w)
                        dx4.inc(sync.dma_start(out=x4_sc[i * 128:(i + 1) * 128, :],
                                               in_=X2[:, i * D:(i + 1) * D]))
                    kb.q(_)
                a1_free = cv.tok()
                stok = None
        c4_done = cv.tok()
        kb.dbg("x4", X2[:], c4_done)
        kb.flush_dbg()
        kb.barrier()
        scope_c2.close()


    if "D" in phases:
        H128 = 1.0 / float(np.sqrt(128.0))
        sim1_b = kb.sbuf("sim1_b", [128, 64], BF16)
        sim2_b = kb.sbuf("sim2_b", [128, 64], BF16)
        simsq = kb.sbuf("simsq", [128, 64], F32)
        rn_pp = kb.sbuf("rn_pp", [128, 16], F32)
        sg_pp = kb.sbuf("sg_pp", [128, 16], F32)
        lnw_pp = kb.sbuf("lnw_pp", [128, 32], F32)
        rs_all = kb.sbuf("rs_all", [128, 16], F32)
        lgb = kb.sbuf("lgb", [128, 8], F32)
        t8b = kb.sbuf("t8b", [128, 8], F32)
        mkb_ = kb.sbuf("mkb_", [128, 24], F32)
        rw_t = kb.sbuf("rw_t", [128, 8], BF16)
        tmd = kb.sbuf("tmd", [128, D], BF16)
        idf32 = kb.sbuf("idf32", [128, 128], F32)
        sel8 = kb.sbuf("sel8", [8, 8 * 128], BF16)
        nrm8 = kb.sbuf("nrm8", [1, 16], F32)
        dl2 = kb.dsem("l1f")

        def _(g):
            cg.inc(g.iota(idf32[:], [[1, 128]], base=0, channel_multiplier=-1,
                          allow_small_or_imprecise_dtypes=True))
        kb.g(_)
        gi1 = cg.tok()

        def _(g, w=gi1):
            wait(g, w)
            cg.inc(g.tensor_single_scalar(idf32[:], idf32[:], 0.0, ALU.is_equal))
        kb.g(_)
        gid32 = cg.tok()

        def _(g, w=gid32):
            wait(g, w)
            cg.inc(g.iota(sel8[:], [[1, 8], [0, 128]], base=0, channel_multiplier=-1,
                          allow_small_or_imprecise_dtypes=True))
        kb.g(_)
        gs1 = cg.tok()

        def _(g, w=gs1):
            wait(g, w)
            cg.inc(g.tensor_single_scalar(sel8[:], sel8[:], 0.0, ALU.is_equal))
        kb.g(_)
        sel_tok = cg.tok()

        def _(g):
            dl2.inc(g.dma_start(out=sim1_b[:].rearrange("p (c e) -> p c e", c=8),
                                in_=inp["L1_ffn_smha_sim"].rearrange("(c p) e -> p c e", p=128)))
            dl2.inc(g.dma_start(out=sim2_b[:].rearrange("p (c e) -> p c e", c=8),
                                in_=inp["L1_ffn_moe_sim"].rearrange("(c p) e -> p c e", p=128)))
            for k, name in enumerate(("ln1_w", "ln1_b", "ln2_w", "ln2_b")):
                dl2.inc(g.dma_start(
                    out=lnw_pp[:, k * 8:(k + 1) * 8],
                    in_=inp["L1_ffn_" + name].rearrange("one (c p) -> p (one c)", p=128),
                    allow_slow_non_contiguous=True))
        kb.g(_)
        d0tok = dl2.tok()

        scope_d0 = ExitStack()
        ps_d0 = kb.psum("ps_d0", [128, 16], F32, scope_d0)
        nrm_toks = []
        for mi, simb in enumerate((sim1_b, sim2_b)):
            def _(v, simb=simb, w=d0tok, w2=(nrm_toks[-1] if nrm_toks else None)):
                wait(v, w, w2)
                cv.inc(v.tensor_mul(simsq[:], simb[:], simb[:]))
            kb.v(_)
            sqt = cv.tok()

            def _(v, mi=mi, w=sqt):
                wait(v, w)
                cv.inc(v.tensor_reduce(
                    rn_pp[:, 8 * mi:8 * mi + 8],
                    bass.AP(simsq, 0, [[64, 128], [1, 8], [8, 8]]),
                    AX.X, ALU.add))
            kb.v(_)
            nrm_toks.append(cv.tok())

        def _(t, w=nrm_toks[-1], w2=vo2):
            wait(t, w, w2)
            t.matmul(ps_d0[0:1, 0:8], ones_c[:], rn_pp[:, 0:8], start=True, stop=True)
            ct.inc(t.matmul(ps_d0[0:1, 8:16], ones_c[:], rn_pp[:, 8:16],
                            start=True, stop=True))
        kb.t(_)
        tnorm = ct.tok()

        def _(v, w=tnorm):
            wait(v, w)
            cv.inc(v.tensor_scalar_max(nrm8[0:1, :], ps_d0[0:1, 0:16], 1e-24))
        kb.v(_)
        vnm = cv.tok()

        def _(s, w=vnm):
            wait(s, w)
            cs.inc(s.activation(nrm8[0:1, :], nrm8[0:1, :], AF.Ln))
        kb.s(_)
        snm = cs.tok()

        def _(s, w=snm):
            wait(s, w)
            cs.inc(s.activation(nrm8[0:1, :], nrm8[0:1, :], AF.Exp, scale=-0.5))
        kb.s(_)
        snm2 = cs.tok()

        def _(t, w=snm2, w2=sig_tok):
            wait(t, w, w2)
            t.matmul(ps_d0[:, 0:16], ones_f2[:], nrm8[0:1, 0:16], start=True, stop=True)
            ct.inc(t.matmul(ps_d0[:, 0:16], ones_f2[:], nrm8[0:1, 0:16],
                            start=True, stop=True))
        kb.t(_)
        tb1 = ct.tok()

        def _(v, w=tb1):
            wait(v, w)
            cv.inc(v.tensor_copy(rn_pp[:], ps_d0[:, 0:16]))
        kb.v(_)
        vb1 = cv.tok()

        def _(t, w=vb1):
            wait(t, w)
            ct.inc(t.matmul(ps_d0[:, 0:16], ones_f2[:], g8s[0:1, 0:16],
                            start=True, stop=True))
        kb.t(_)
        tb2 = ct.tok()

        def _(v, w=tb2):
            wait(v, w)
            cv.inc(v.tensor_copy(sg_pp[:], ps_d0[:, 0:16]))
        kb.v(_)
        d0_done = cv.tok()
        kb.barrier()
        scope_d0.close()

        def d_lnpass(wcol, bcol, dbgname=None):
            scope = ExitStack()
            ps_trF = kb.psum("ps_trF_%s" % wcol, [128, 1024], BF16, scope)
            stok = None
            for i in range(NT):
                stok = lnb.stats(X2[:, i * D:(i + 1) * D], i, [], prev=stok)
            rtok_b = lnb.rstd_batch(0, 16, [stok])
            aptok = None
            evtok = None
            trtok = None
            for i in range(NT):
                aptok = lnb.apply(X2[:, i * D:(i + 1) * D], tmd[:], i,
                                  [rtok_b, trtok])
                evtok, trtok = transpose_tile(
                    kb, lambda c: tmd[:, c * 128:(c + 1) * 128], ND, ps_trF[:],
                    lambda c, i=i: act_fm[:, c * T + i * 128: c * T + (i + 1) * 128],
                    idbf[:], [aptok], ps_free=evtok)
            aff_toks = []
            for c in range(ND):
                def _(v, c=c, w=evtok, w2=d0tok, w3=(aff_toks[-1] if aff_toks else None)):
                    wait(v, w, w2, w3)
                    cv.inc(v.tensor_scalar(act_fm[:, c * T:(c + 1) * T],
                                           act_fm[:, c * T:(c + 1) * T],
                                           lnw_pp[:, wcol * 8 + c:wcol * 8 + c + 1],
                                           lnw_pp[:, bcol * 8 + c:bcol * 8 + c + 1],
                                           ALU.mult, ALU.add))
                kb.v(_)
                aff_toks.append(cv.tok())
            scope.close()
            return cv.tok()

        stok = None
        for i in range(NT):
            stok = lnb.stats(X2[:, i * D:(i + 1) * D], i, [], prev=stok)
        rtok_b = lnb.rstd_batch(0, 16, [stok])
        aptok = None
        for i in range(NT):
            aptok = lnb.apply(X2[:, i * D:(i + 1) * D], X2[:, i * D:(i + 1) * D],
                              i, [rtok_b, aptok])
        kb.barrier()
        n1f_done = d_lnpass(0, 1)
        kb.dbg("n1f_fm", act_fm[:], n1f_done)
        kb.flush_dbg()
        kb.barrier()

        rw_emaj = kb.at("rw_emaj", [8, NT * 128], BF16, WS)
        rw2_emaj = kb.at("rw2_emaj", [8, NT * 128], BF16, WS + 4096)

        lgall = kb.sbuf("lgall", [128, 128], F32)
        glst = kb.sbuf("glst", [128, 16], F32)

        def d_gating(mi, dst_emaj):
            scope = ExitStack()
            ps_nn = kb.psum("ps_nn_%d" % mi, [128, 128], F32, scope)
            ps_lg = kb.psum("ps_lg_%d" % mi, [128, 8], F32, scope)
            ps_rt = kb.psum("ps_rt_%d" % mi, [128, 128], BF16, scope)
            simb = (sim1_b, sim2_b)[mi]
            prev_ap = None
            for i in range(NT):
                def _(t, i=i, w=prev_ap):
                    wait(t, w)
                    for cc in range(ND):
                        ins = t.matmul(ps_nn[:],
                                       act_fm[:, cc * T + i * 128: cc * T + (i + 1) * 128],
                                       act_fm[:, cc * T + i * 128: cc * T + (i + 1) * 128],
                                       start=(cc == 0), stop=(cc == ND - 1))
                    ct.inc(ins)
                    for cc in range(ND):
                        ins = t.matmul(ps_lg[:],
                                       act_fm[:, cc * T + i * 128: cc * T + (i + 1) * 128],
                                       simb[:, cc * 8:(cc + 1) * 8],
                                       start=(cc == 0), stop=(cc == ND - 1))
                    ct.inc(ins)
                kb.t(_)
                mt = ct.tok()

                def _(v, i=i, w=mt, w2=gid32):
                    wait(v, w, w2)
                    v.tensor_mul(sq_scr[:, 0:128], ps_nn[:], idf32[:])
                    cv.inc(v.tensor_copy(lgall[:, i * 8:(i + 1) * 8], ps_lg[:]))
                kb.v(_)
                v1 = cv.tok()

                def _(v, i=i, w=v1):
                    wait(v, w)
                    cv.inc(v.tensor_reduce(glst[:, i:i + 1], sq_scr[:, 0:128],
                                           AX.X, ALU.add))
                kb.v(_)
                prev_ap = cv.tok()

            def _(v, w=prev_ap):
                wait(v, w)
                cv.inc(v.tensor_scalar_max(glst[:], glst[:], 1e-24))
            kb.v(_)
            g1_ = cv.tok()

            def _(s, w=g1_):
                wait(s, w)
                cs.inc(s.activation(glst[:], glst[:], AF.Ln))
            kb.s(_)
            g2_ = cs.tok()

            def _(s, w=g2_):
                wait(s, w)
                cs.inc(s.activation(glst[:], glst[:], AF.Exp, scale=-0.5))
            kb.s(_)
            rln_tok = cs.tok()

            prev_tr = None
            prev_chain = None
            for i in range(NT):
                def _(v, i=i, w=rln_tok, pc=prev_chain, mi=mi):
                    wait(v, w, pc)
                    cv.inc(v.tensor_scalar_mul(lgb[:], lgall[:, i * 8:(i + 1) * 8],
                                               glst[:, i:i + 1]))
                kb.v(_)
                v4 = cv.tok()

                def _(v, w=v4, mi=mi):
                    wait(v, w)
                    cv.inc(v.tensor_mul(lgb[:], lgb[:], rn_pp[:, 8 * mi:8 * mi + 8]))
                kb.v(_)
                v5 = cv.tok()

                def _(v, w=v5, mi=mi):
                    wait(v, w)
                    cv.inc(v.tensor_sub(lgb[:], lgb[:], sg_pp[:, 8 * mi:8 * mi + 8]))
                kb.v(_)
                v6 = cv.tok()

                def _(v, w=v6):
                    wait(v, w)
                    cv.inc(v.max(t8b[:], lgb[:]))
                kb.v(_)
                v7 = cv.tok()

                def _(v, w=v7):
                    wait(v, w)
                    v.tensor_single_scalar(mkb_[:, 16:17], t8b[:, 0:1], 0.0, ALU.is_gt)
                    cv.inc(v.tensor_single_scalar(mkb_[:, 0:8], lgb[:], 0.0, ALU.is_gt))
                kb.v(_)
                v8 = cv.tok()

                def _(v, w=v8):
                    wait(v, w)
                    cv.inc(v.tensor_scalar(mkb_[:, 8:16], lgb[:], t8b[:, 1:2], None,
                                           ALU.is_ge))
                kb.v(_)
                v9 = cv.tok()

                def _(v, w=v9):
                    wait(v, w)
                    cv.inc(v.tensor_sub(mkb_[:, 0:8], mkb_[:, 0:8], mkb_[:, 8:16]))
                kb.v(_)
                v10 = cv.tok()

                def _(v, w=v10):
                    wait(v, w)
                    cv.inc(v.scalar_tensor_tensor(mkb_[:, 0:8], mkb_[:, 0:8],
                                                  mkb_[:, 16:17], mkb_[:, 8:16],
                                                  ALU.mult, ALU.add))
                kb.v(_)
                v11 = cv.tok()

                def _(v, w=v11):
                    wait(v, w)
                    cv.inc(v.tensor_relu(lgb[:], lgb[:]))
                kb.v(_)
                v12 = cv.tok()

                def _(s, w=v12):
                    wait(s, w)
                    cs.inc(s.activation(t8b[:], lgb[:], AF.Exp))
                kb.s(_)
                s3 = cs.tok()

                def _(v, w=s3):
                    wait(v, w)
                    cv.inc(v.tensor_mul(t8b[:], t8b[:], mkb_[:, 0:8]))
                kb.v(_)
                v13 = cv.tok()

                def _(v, w=v13):
                    wait(v, w)
                    cv.inc(v.tensor_reduce(sm[:, 12:13], t8b[:], AX.X, ALU.add))
                kb.v(_)
                v14 = cv.tok()

                def _(v, w=v14):
                    wait(v, w)
                    cv.inc(v.reciprocal(sm[:, 12:13], sm[:, 12:13]))
                kb.v(_)
                v15 = cv.tok()

                def _(v, w=v15, w2=prev_tr):
                    wait(v, w, w2)
                    cv.inc(v.tensor_scalar_mul(rw_t[:], t8b[:], sm[:, 12:13]))
                kb.v(_)
                v16 = cv.tok()
                prev_chain = v16

                def _(t, i=i, w=v16, w2=prev_tr):
                    wait(t, w, w2)
                    ct.inc(t.transpose(ps_rt[0:8, 0:128], rw_t[:, 0:8], idbf[:]))
                kb.t(_)
                tt = ct.tok()

                def _(v, i=i, w=tt, dst_emaj=dst_emaj):
                    wait(v, w)
                    cv.inc(v.tensor_copy(dst_emaj[0:8, i * 128:(i + 1) * 128],
                                         ps_rt[0:8, 0:128]))
                kb.v(_)
                prev_tr = cv.tok()
            scope.close()
            return prev_tr

        rw_done = d_gating(0, rw_emaj)
        kb.dbg("rw_emaj", rw_emaj[0:8, :], rw_done)
        kb.flush_dbg()
        kb.barrier()

        scope_d3a = ExitStack()
        qp_b = kb.at("qp_b", [128, 64 * 128], BF16, REG1)
        kp_b = kb.at("kp_b", [128, 64 * 128], BF16, REG1 + 16384)
        vp_b = kb.at("vp_b", [128, 64 * 128], BF16, REG2)
        n1e = kb.at("n1e", [128, 8 * 512], BF16, REG2 + 16384)
        rwb_eg = kb.at("rwb_eg", [128, 512], BF16, REG2 + 24576)
        q_fm_d = kb.at("q_fm_d", [128, T], BF16, WS + 8192)
        k_fm_d = kb.at("k_fm_d", [128, T], BF16, WS + 12288)
        v_fm_d = kb.at("v_fm_d", [128, T], BF16, WS + 16384)
        dpq = kb.dsem("pq")

        def _(g):
            for nm, dst in (("q_proj", qp_b), ("k_proj", kp_b), ("v_proj", vp_b)):
                dpq.inc(g.dma_start(
                    out=dst[:].rearrange("p (ec h) -> p ec h", h=128),
                    in_=inp["L1_ffn_" + nm].rearrange("(ec p) h -> p ec h", p=128)))
        kb.g(_)
        pq_tok = dpq.tok()

        ps_q3 = kb.psum("ps_q3", [128, 512], F32, scope_d3a)
        ps_k3 = kb.psum("ps_k3", [128, 512], F32, scope_d3a)
        ps_v3 = kb.psum("ps_v3", [128, 512], F32, scope_d3a)
        ps_bc3 = kb.psum("ps_bc3", [128, 512], F32, scope_d3a)
        n1e_free = None
        qkv_ev = {0: None, 1: None, 2: None}
        for g_ in range(4):
            for e in range(8):
                def _(t, g_=g_, e=e, w=rw_done, w2=sel_tok):
                    wait(t, w, w2)
                    ct.inc(t.matmul(ps_bc3[:], sel8[0:8, e * 128:(e + 1) * 128],
                                    rw_emaj[0:8, g_ * 512:(g_ + 1) * 512],
                                    start=True, stop=True))
                kb.t(_)
                bct = ct.tok()

                def _(v, w=bct, nf=n1e_free):
                    wait(v, w, nf)
                    cv.inc(v.tensor_copy(rwb_eg[:], ps_bc3[:]))
                kb.v(_)
                bce = cv.tok()

                def _(v, g_=g_, w=bce):
                    wait(v, w)
                    for cc in range(ND):
                        ins = v.tensor_mul(n1e[:, cc * 512:(cc + 1) * 512],
                                           act_fm[:, cc * T + g_ * 512:
                                                  cc * T + (g_ + 1) * 512],
                                           rwb_eg[:])
                    cv.inc(ins)
                kb.v(_)
                n1etok = cv.tok()

                def _(t, g_=g_, e=e, w=n1etok, w2=pq_tok,
                      ev=[qkv_ev[x] for x in range(3)] if (g_ > 0 and e == 0) else []):
                    wait(t, w, w2, *ev)
                    for cc in range(ND):
                        fl = (e * 8 + cc) * 128
                        st_ = (e == 0 and cc == 0)
                        sp = (e == 7 and cc == ND - 1)
                        t.matmul(ps_q3[:], qp_b[:, fl:fl + 128],
                                 n1e[:, cc * 512:(cc + 1) * 512], start=st_, stop=sp)
                        t.matmul(ps_k3[:], kp_b[:, fl:fl + 128],
                                 n1e[:, cc * 512:(cc + 1) * 512], start=st_, stop=sp)
                        ins = t.matmul(ps_v3[:], vp_b[:, fl:fl + 128],
                                       n1e[:, cc * 512:(cc + 1) * 512],
                                       start=st_, stop=sp)
                    ct.inc(ins)
                kb.t(_)
                n1e_free = ct.tok()
            mtok = ct.tok()

            def _(v, g_=g_, w=mtok):
                wait(v, w)
                v.tensor_scalar_mul(q_fm_d[:, g_ * 512:(g_ + 1) * 512], ps_q3[:], H128)
                v.tensor_copy(k_fm_d[:, g_ * 512:(g_ + 1) * 512], ps_k3[:])
                cv.inc(v.tensor_copy(v_fm_d[:, g_ * 512:(g_ + 1) * 512], ps_v3[:]))
            kb.v(_)
            for x in range(3):
                qkv_ev[x] = cv.tok()
        d3a_done = cv.tok()
        kb.dbg("qf_d", q_fm_d[:], d3a_done)
        kb.flush_dbg()
        kb.barrier()
        scope_d3a.close()

        scope_d3b = ExitStack()
        v_tm_d = kb.at("v_tm_d", [128, NT * 128], BF16, REG1)
        attn_fm_d = kb.at("attn_fm_d", [128, T], BF16, REG1 + 4096)
        S2_sb = kb.at("S2_sb", [128, T], F32, REG2)
        P2_sb = kb.at("P2_sb", [128, T], BF16, REG2 + 8192)
        PT_sb = kb.at("PT_sb", [128, T], BF16, REG1 + 8192)
        ps_s3 = [kb.psum("ps_s3a", [128, 512], F32, scope_d3b),
                 kb.psum("ps_s3b", [128, 512], F32, scope_d3b)]
        ps_at3 = kb.psum("ps_at3", [128, 1024], BF16, scope_d3b)
        ps_pv = kb.psum("ps_pv", [128, 128], F32, scope_d3b)

        vt_ev = None
        for j in range(NT):
            vt_ev, _tt = transpose_tile(
                kb, lambda _c, j=j: v_fm_d[:, j * 128:(j + 1) * 128],
                1, ps_at3[:],
                lambda _c, j=j: v_tm_d[:, j * 128:(j + 1) * 128],
                idbf[:], [d3a_done, vt_ev], ps_free=vt_ev)
        vtm2_done = vt_ev

        sbk3 = {0: None, 1: None}
        at3_tok = vtm2_done
        p2_free = None
        s2_free = None
        pv_free = None
        attn_ev = None
        sexp_prev = None
        for i in range(NT):
            W = 128 * (i + 1)
            nb = (W + 511) // 512
            ev_toks = []
            for b in range(nb):
                w0c = b * 512
                w1c = min(W, w0c + 512)
                bk = b % 2

                def _(t, i=i, w0c=w0c, w1c=w1c, bk=bk, w=sbk3[bk]):
                    wait(t, w)
                    ct.inc(t.matmul(ps_s3[bk][:, 0:w1c - w0c],
                                    q_fm_d[:, i * 128:(i + 1) * 128],
                                    k_fm_d[:, w0c:w1c], start=True, stop=True))
                kb.t(_)
                mt = ct.tok()

                def _(v, w0c=w0c, w1c=w1c, bk=bk, mt=mt,
                      sf=(s2_free if b == 0 else None)):
                    wait(v, mt, sf)
                    cv.inc(v.tensor_copy(S2_sb[:, w0c:w1c], ps_s3[bk][:, 0:w1c - w0c]))
                kb.v(_)
                sbk3[bk] = cv.tok()
                ev_toks.append(cv.tok())

            def _(g, i=i, ev=ev_toks[-1]):
                wait(g, ev)
                cg.inc(g.affine_select(S2_sb[:, i * 128:(i + 1) * 128],
                                       S2_sb[:, i * 128:(i + 1) * 128],
                                       [[-1, 128]], ALU.is_ge, -1e30,
                                       base=0, channel_multiplier=1))
            kb.g(_)
            gmask = cg.tok()

            def _(v, W=W, gmask=gmask, ev=list(ev_toks), sp=sexp_prev):
                wait(v, gmask, sp, *ev)
                cv.inc(v.reduce_max(sm[:, 8:9], S2_sb[:, 0:W], AX.X, negate=True))
            kb.v(_)
            vmax = cv.tok()

            def _(s, W=W, vmax=vmax, pf=p2_free):
                wait(s, vmax, pf)
                cs.inc(s.activation(P2_sb[:, 0:W], S2_sb[:, 0:W], AF.Exp,
                                    bias=sm[:, 8:9], accum_out=ssum[:, 0:1]))
            kb.s(_)
            sexp = cs.tok()
            sexp_prev = sexp
            s2_free = sexp

            def _(v, i=i, w=sexp):
                wait(v, w)
                cv.inc(v.reciprocal(rs_all[:, i:i + 1], ssum[:, 0:1]))
            kb.v(_)

            at_toks = []
            for bg2 in range((i + 8) // 8):
                j0, j1 = bg2 * 8, min(i + 1, bg2 * 8 + 8)

                def _(t, j0=j0, j1=j1, sexp=sexp, w=at3_tok):
                    wait(t, sexp, w)
                    for j in range(j0, j1):
                        ins = t.transpose(ps_at3[:, (j - j0) * 128:(j - j0 + 1) * 128],
                                          P2_sb[:, j * 128:(j + 1) * 128], idbf[:])
                    ct.inc(ins)
                kb.t(_)
                ttok = ct.tok()
                p2_free = ttok

                def _(v, j0=j0, j1=j1, ttok=ttok):
                    wait(v, ttok)
                    cv.inc(v.tensor_copy(PT_sb[:, j0 * 128:j1 * 128],
                                         ps_at3[:, 0:(j1 - j0) * 128]))
                kb.v(_)
                at3_tok = cv.tok()
                at_toks.append(cv.tok())

            def _(t, i=i, w=list(at_toks), pvf=pv_free):
                wait(t, vtm2_done, pvf, *w)
                for j in range(i + 1):
                    ins = t.matmul(ps_pv[:], v_tm_d[:, j * 128:(j + 1) * 128],
                                   PT_sb[:, j * 128:(j + 1) * 128],
                                   start=(j == 0), stop=(j == i))
                ct.inc(ins)
            kb.t(_)
            pvt = ct.tok()

            def _(v, i=i, w=pvt, ae=attn_ev):
                wait(v, w)
                cv.inc(v.tensor_copy(attn_fm_d[:, i * 128:(i + 1) * 128], ps_pv[:]))
            kb.v(_)
            attn_ev = cv.tok()
            pv_free = attn_ev
        d3b_done = attn_ev
        kb.barrier()
        scope_d3b.close()

        op_b = kb.at("op_b", [128, 8 * 1024], BF16, REG1 + 16384)
        rwb_all = kb.at("rwb_all", [128, 8 * T], BF16, REG2)
        attn_e = kb.at("attn_e", [128, 512], BF16, WS + 20480)
        dop = kb.dsem("op")

        def _(g):
            dop.inc(g.dma_start(
                out=op_b[:].rearrange("p (e c) -> p e c", e=8),
                in_=inp["L1_ffn_o_proj"].rearrange("(e p) c -> p e c", p=128)))
        kb.g(_)
        op_tok = dop.tok()

        scope_bcc = ExitStack()
        ps_bc3 = kb.psum("ps_bcc", [128, 512], F32, scope_bcc)
        rwb_toks = []
        for e in range(8):
            for g_ in range(4):
                def _(t, e=e, g_=g_, w=rw_done,
                      w2=(rwb_toks[-1] if rwb_toks else None)):
                    wait(t, w, w2, sel_tok)
                    ct.inc(t.matmul(ps_bc3[:], sel8[0:8, e * 128:(e + 1) * 128],
                                    rw_emaj[0:8, g_ * 512:(g_ + 1) * 512],
                                    start=True, stop=True))
                kb.t(_)
                bt_ = ct.tok()

                def _(v, e=e, g_=g_, w=bt_):
                    wait(v, w)
                    cv.inc(v.tensor_copy(rwb_all[:, e * T + g_ * 512:
                                                 e * T + (g_ + 1) * 512], ps_bc3[:]))
                kb.v(_)
                rwb_toks.append(cv.tok())
        rwb_done = cv.tok()
        kb.barrier()
        scope_bcc.close()

        scope_d3c = ExitStack()
        ps_om = [kb.psum("ps_om%d" % u, [128, 1024], F32, scope_d3c) for u in range(4)]
        om_free = {u: None for u in range(4)}
        ae_free = None
        for g_ in range(4):
            add_toks = []
            for e in range(8):
                def _(v, g_=g_, e=e, w=rwb_done, w2=d3b_done, af=ae_free):
                    wait(v, w, w2, af)
                    cv.inc(v.tensor_mul(attn_e[:],
                                        attn_fm_d[:, g_ * 512:(g_ + 1) * 512],
                                        rwb_all[:, e * T + g_ * 512:e * T + (g_ + 1) * 512]))
                kb.v(_)
                aet = cv.tok()

                def _(t, g_=g_, e=e, w=aet, w2=op_tok,
                      omf=[om_free[u] for u in range(4)] if e == 0 else []):
                    wait(t, w, w2, *omf)
                    for u in range(4):
                        for half in range(2):
                            ins = t.matmul(
                                ps_om[u][:, half * 512:(half + 1) * 512],
                                attn_e[:, u * 128:(u + 1) * 128],
                                op_b[:, e * 1024 + half * 512:e * 1024 + (half + 1) * 512],
                                start=(e == 0), stop=(e == 7))
                    ct.inc(ins)
                kb.t(_)
                ae_free = ct.tok()
            omt = ct.tok()

            for u in range(4):
                i = g_ * 4 + u

                def _(v, u=u, i=i, w=omt):
                    wait(v, w)
                    cv.inc(v.scalar_tensor_tensor(X2[:, i * D:(i + 1) * D],
                                                  ps_om[u][:], rs_all[:, i:i + 1],
                                                  X2[:, i * D:(i + 1) * D],
                                                  ALU.mult, ALU.add))
                kb.v(_)
                om_free[u] = cv.tok()
                add_toks.append(cv.tok())
        d3c_done = cv.tok()
        kb.dbg("hs", X2[:], d3c_done)
        kb.flush_dbg()
        kb.barrier()
        scope_d3c.close()

        n2f_done = d_lnpass(2, 3)
        kb.dbg("n2f_fm", act_fm[:], n2f_done)
        kb.barrier()
        rw2_done = d_gating(1, rw2_emaj)
        kb.dbg("rw2_emaj", rw2_emaj[0:8, :], rw2_done)
        kb.flush_dbg()
        kb.barrier()

        scope_d5 = ExitStack()
        w1b = kb.at("w1b", [128, 2 * 8 * 1024], BF16, REG1)
        w2b = kb.at("w2b", [128, 2 * 8 * 1024], BF16, REG2)
        scaled = kb.at("scaled", [128, 8 * 512], BF16, WS + 8192)
        rw2b_eg = kb.at("rw2b_eg", [128, 512], BF16, WS + 16384)
        dmw = [kb.dsem("mw0"), kb.dsem("mw1")]
        ps_i5 = [kb.psum("ps_i5a", [128, 1024], F32, scope_d5),
                 kb.psum("ps_i5b", [128, 1024], F32, scope_d5)]
        ps_m5 = [kb.psum("ps_m5a", [128, 512], F32, scope_d5),
                 kb.psum("ps_m5b", [128, 512], F32, scope_d5)]
        ps_bc5 = kb.psum("ps_bc5", [128, 512], F32, scope_d5)
        W1f = inp["L1_ffn_w1"].rearrange("(ec p) i -> p ec i", p=128)
        W2f = inp["L1_ffn_w2"].rearrange("(ec p) i -> p ec i", p=128)
        wslot_tok = {0: None, 1: None}
        ibank_tok = {0: None, 1: None}
        mbank_tok = {0: None, 1: None}
        sc_free = None
        rwb5_free = None
        for e in range(8):
            slot = e % 2

            def _(g, e=e, slot=slot, w=wslot_tok[slot], ds=dmw[slot]):
                wait(g, w)
                ds.inc(g.dma_start(
                    out=w1b[:, slot * 8192:(slot + 1) * 8192].rearrange(
                        "p (c i) -> p c i", c=8),
                    in_=W1f[:, e * 8:(e + 1) * 8, :]))
                ds.inc(g.dma_start(
                    out=w2b[:, slot * 8192:(slot + 1) * 8192].rearrange(
                        "p (c i) -> p c i", c=8),
                    in_=W2f[:, e * 8:(e + 1) * 8, :]))
            kb.g(_)
            wtok = dmw[slot].tok()

            for g_ in range(4):
                def _(t, e=e, g_=g_, w=rw2_done, rf=rwb5_free):
                    wait(t, w, rf, sel_tok)
                    ct.inc(t.matmul(ps_bc5[:], sel8[0:8, e * 128:(e + 1) * 128],
                                    rw2_emaj[0:8, g_ * 512:(g_ + 1) * 512],
                                    start=True, stop=True))
                kb.t(_)
                bct = ct.tok()

                def _(v, w=bct, sf=sc_free):
                    wait(v, w, sf)
                    cv.inc(v.tensor_copy(rw2b_eg[:], ps_bc5[:]))
                kb.v(_)
                rwb5 = cv.tok()

                sc_toks = []
                for ipr in range(4):
                    bk = ipr % 2

                    def _(t, e=e, g_=g_, ipr=ipr, bk=bk, slot=slot, wtok=wtok,
                          w=ibank_tok[bk], w2=n2f_done):
                        wait(t, wtok, w, w2)
                        for half in range(2):
                            ic = 2 * ipr + half
                            for cc in range(ND):
                                ins = t.matmul(
                                    ps_i5[bk][:, half * 512:(half + 1) * 512],
                                    w1b[:, slot * 8192 + cc * 1024 + ic * 128:
                                        slot * 8192 + cc * 1024 + (ic + 1) * 128],
                                    act_fm[:, cc * T + g_ * 512: cc * T + (g_ + 1) * 512],
                                    start=(cc == 0), stop=(cc == ND - 1))
                        ct.inc(ins)
                    kb.t(_)
                    imt = ct.tok()

                    def _(s, ipr=ipr, bk=bk, w=imt, sf=sc_free if ipr < 2 else None):
                        wait(s, w, sf)
                        cs.inc(s.activation(scaled[:, ipr * 1024:(ipr + 1) * 1024],
                                            ps_i5[bk][:], AF.Gelu))
                    kb.s(_)
                    gt = cs.tok()
                    ibank_tok[bk] = gt

                    def _(v, ipr=ipr, w=gt, w2=rwb5):
                        wait(v, w, w2)
                        for half in range(2):
                            ic = 2 * ipr + half
                            ins = v.tensor_mul(scaled[:, ic * 512:(ic + 1) * 512],
                                               scaled[:, ic * 512:(ic + 1) * 512],
                                               rw2b_eg[:])
                        cv.inc(ins)
                    kb.v(_)
                    sc_toks.append(cv.tok())
                rwb5_free = ct.tok()

                for u in range(4):
                    i = g_ * 4 + u
                    for half in range(2):
                        bk = (2 * u + half) % 2

                        def _(t, u=u, half=half, e=e, slot=slot, bk=bk,
                              w=list(sc_toks), mf=mbank_tok[bk]):
                            wait(t, mf, *w)
                            for ic in range(ND):
                                ins = t.matmul(
                                    ps_m5[bk][:],
                                    scaled[:, ic * 512 + u * 128:ic * 512 + (u + 1) * 128],
                                    w2b[:, slot * 8192 + ic * 1024 + half * 512:
                                        slot * 8192 + ic * 1024 + (half + 1) * 512],
                                    start=(ic == 0), stop=(ic == ND - 1))
                            ct.inc(ins)
                        kb.t(_)
                        omt = ct.tok()
                        sc_free = omt

                        def _(v, i=i, half=half, bk=bk, w=omt):
                            wait(v, w)
                            dd = X2[:, i * D + half * 512: i * D + (half + 1) * 512]
                            cv.inc(v.tensor_add(dd, dd, ps_m5[bk][:]))
                        kb.v(_)
                        mbank_tok[bk] = cv.tok()
            wslot_tok[slot] = ct.tok()
        d5_done = cv.tok()
        kb.dbg("f1", X2[:], d5_done)
        kb.flush_dbg()
        kb.barrier()
        scope_d5.close()

        xt4 = kb.at("xt4", [128, 2 * D], F32, WS)
        outt = kb.at("outt", [128, 2 * D], F32, WS + 8192)
        dx4r = [kb.dsem("x4r0"), kb.dsem("x4r1")]
        douts = [kb.dsem("out0"), kb.dsem("out1")]
        stok = None
        for i in range(NT):
            stok = lnb.stats(X2[:, i * D:(i + 1) * D], i, [], prev=stok)
        rtok_b = lnb.rstd_batch(0, 16, [stok])
        slot_done6 = {0: None, 1: None}
        aptok = None
        addtok = None
        for i in range(NT):
            slot = i % 2
            xa = xt4[:, slot * D:(slot + 1) * D]
            oa = outt[:, slot * D:(slot + 1) * D]

            def _(sync, i=i, xa=xa, w=slot_done6[slot], ds=dx4r[slot]):
                wait(sync, w)
                ds.inc(sync.dma_start(out=xa, in_=x4_sc[i * 128:(i + 1) * 128, :]))
            kb.q(_)
            ldtok = dx4r[slot].tok()

            aptok = lnb.apply(X2[:, i * D:(i + 1) * D], tmd[:], i,
                              [rtok_b, aptok, addtok])

            def _(v, i=i, xa=xa, oa=oa, w=aptok, w2=ldtok, w3=slot_done6[slot]):
                wait(v, w, w2, w3)
                cv.inc(v.tensor_add(oa, xa, tmd[:]))
            kb.v(_)
            addtok = cv.tok()

            def _(sync, i=i, oa=oa, w=addtok, ds=douts[slot]):
                wait(sync, w)
                ds.inc(sync.dma_start(out=out[i * 128:(i + 1) * 128, :], in_=oa))
            kb.q(_)
            slot_done6[slot] = douts[slot].tok()

        def _(sync):
            wait(sync, douts[0].tok(), douts[1].tok())
        kb.q(_)

    if "final_stub" in DEBUG:
        def _(sync):
            for i in range(NT):
                dq.inc(sync.dma_start(out=out[i * 128:(i + 1) * 128, :],
                                      in_=outln_ap(i)))
            wait(sync, dq.tok())
        kb.q(_)

    rem = nc.sbuf_bytes_remaining
    assert rem >= 229376 - REG1, f"stack smalls overflow into manual arena: remaining={rem}"
    ctx.close()
    return nc


def shard_inputs(inputs, core):
    m = {}
    for name, shape in INPUT_SPECS:
        v = np.asarray(inputs[name])
        if name == "L0_attn_x":
            v = np.ascontiguousarray(v[core])
        m[name] = np.ascontiguousarray(
            v.astype(np.float32, copy=False).reshape(shape))
    return m


def kernel(**inputs):
    nc = build()
    in_maps = [shard_inputs(inputs, c) for c in range(8)]
    res = run_bass_kernel_spmd(nc, in_maps, core_ids=list(range(8)))
    return np.stack([res.results[c]["out"] for c in range(8)], axis=0)


# version 16 (final)
# speedup vs baseline: 1.0276x; 1.0212x over previous
import numpy as np
from contextlib import ExitStack
import concourse.bass as bass
import concourse.mybir as mybir
from concourse.bass_utils import run_bass_kernel_spmd

F32 = mybir.dt.float32
BF16 = mybir.dt.bfloat16
AF = mybir.ActivationFunctionType
ALU = mybir.AluOpType
AX = mybir.AxisListType

T = 2048
D = 1024
NT = T // 128
ND = D // 128
EPS = 1e-5

DEBUG = ()

REG1 = 40960
REG2 = 73728
REG3 = 106496
REG4 = 139264
REG5 = 172032
WS = 204800

INPUT_SPECS = [
    ("L0_attn_x", [T, D]),
    ("L0_attn_W_qkv", [D, 3 * D]),
    ("L0_attn_b_qkv", [1, 3 * D]),
    ("L0_attn_W_vconv", [D, 3]),
    ("L0_ffn_W1", [4 * D, D]),
    ("L0_ffn_b1", [1, 4 * D]),
    ("L0_ffn_W2", [D, 4 * D]),
    ("L0_ffn_b2", [1, D]),
    ("L1_attn_basis", [D, 256]),
    ("L1_attn_q_coeffs", [256, 256]),
    ("L1_attn_k_coeffs", [256, 256]),
    ("L1_attn_v_coeffs", [256, 256]),
    ("L1_attn_o_coeffs", [256, 256]),
    ("L1_attn_decay_logit", [1, 1]),
    ("L1_attn_out_scale", [1, 1]),
    ("L1_ffn_ln1_w", [1, D]),
    ("L1_ffn_ln1_b", [1, D]),
    ("L1_ffn_ln2_w", [1, D]),
    ("L1_ffn_ln2_b", [1, D]),
    ("L1_ffn_smha_sim", [D, 8]),
    ("L1_ffn_smha_gates", [1, 8]),
    ("L1_ffn_q_proj", [8 * D, 128]),
    ("L1_ffn_k_proj", [8 * D, 128]),
    ("L1_ffn_v_proj", [8 * D, 128]),
    ("L1_ffn_o_proj", [8 * 128, D]),
    ("L1_ffn_moe_sim", [D, 8]),
    ("L1_ffn_moe_gates", [1, 8]),
    ("L1_ffn_w1", [8 * D, D]),
    ("L1_ffn_w2", [8 * D, D]),
]


class Sem:
    def __init__(self, ctx, nc, name, step=1):
        self.h = ctx.enter_context(nc.semaphore(name))
        self.val = 0
        self.step = step

    def inc(self, ins, n=None):
        n = self.step if n is None else n
        ins.then_inc(self.h, n)
        self.val += n
        return (self.h, self.val)

    def tok(self):
        return (self.h, self.val)


def wait(eng, *toks):
    for tk in toks:
        if tk is None:
            continue
        h, v = tk
        if v > 0:
            eng.wait_ge(h, v)


class KB:
    def __init__(self, nc, ctx):
        self.nc = nc
        self.ctx = ctx
        self.blk = ctx.enter_context(nc.Block())
        self.dq = Sem(ctx, nc, "dq", 16)
        self.dg = Sem(ctx, nc, "dg", 16)
        self._dsems = {}
        self.ct = Sem(ctx, nc, "ct")
        self.cv = Sem(ctx, nc, "cv")
        self.cs = Sem(ctx, nc, "cs")
        self.cg = Sem(ctx, nc, "cg")
        self.dbg_outs = {}
        self.dbg_toks = []

    def dsem(self, name, step=16):
        if name not in self._dsems:
            self._dsems[name] = Sem(self.ctx, self.nc, "d_" + name, step)
        return self._dsems[name]

    def q(self, fn):
        self.blk.sync(fn)

    def g(self, fn):
        self.blk.gpsimd(fn)

    def t(self, fn):
        self.blk.tensor(fn)

    def v(self, fn):
        self.blk.vector(fn)

    def s(self, fn):
        self.blk.scalar(fn)

    def barrier(self, dma=True):
        toks = [x.tok() for x in (self.ct, self.cv, self.cs, self.cg)]
        if dma:
            toks += [x.tok() for x in (self.dq, self.dg)]
            toks += [s.tok() for s in self._dsems.values()]

        def _w(eng):
            wait(eng, *toks)
        self.q(_w)
        self.g(_w)
        self.t(_w)
        self.v(_w)
        self.s(_w)

    def sbuf(self, name, shape, dtype, scope=None):
        return (scope or self.ctx).enter_context(self.nc.sbuf_tensor(name, shape, dtype))

    def at(self, name, shape, dtype, offset):
        return self.nc.alloc_sbuf_tensor_at(name, shape, dtype, offset=offset)

    def psum(self, name, shape, dtype, scope=None):
        return (scope or self.ctx).enter_context(self.nc.psum_tensor(name, shape, dtype))

    def dbg(self, name, ap, tok):
        if name in DEBUG:
            o = self.nc.dram_tensor("dbg_" + name, list(ap.shape), ap.dtype,
                                    kind="ExternalOutput").ap()
            self.dbg_outs[name] = (o, ap)
            self.dbg_toks.append(tok)

    def flush_dbg(self):
        if not self.dbg_outs:
            return
        outs, toks, dq = self.dbg_outs, self.dbg_toks, self.dq

        def _(sync):
            wait(sync, *toks)
            for name, (o, ap) in outs.items():
                dq.inc(sync.dma_start(out=o, in_=ap))
            wait(sync, dq.tok())
        self.q(_)
        self.dbg_outs = {}
        self.dbg_toks = []


class LNB:

    def __init__(self, kb, st, sq_scr, eps_ap):
        self.kb = kb
        self.st = st
        self.sq = sq_scr
        self.eps = eps_ap

    def stats(self, src_ap, col, toks, f_ap=None, prev=None):
        kb, st = self.kb, self.st
        cv, cg = kb.cv, kb.cg
        invD = 1.0 / D
        m = st[:, col:col + 1]
        va = st[:, 16 + col:17 + col]
        r = st[:, 32 + col:33 + col]
        fcol = st[:, 48 + col:49 + col]
        nfree = len(src_ap.ap) - 1
        ax = AX.X if nfree == 1 else AX.XY
        if nfree == 1:
            sq = self.sq
        else:
            counts = [p[1] for p in src_ap.ap[1:]]
            dims = [list(self.sq.ap[0])]
            stride = 1
            rev = []
            for cdim in reversed(counts):
                rev.append([stride, cdim])
                stride *= cdim
            dims += rev[::-1]
            sq = bass.AP(self.sq.tensor, 0, dims)

        def _v1(v):
            wait(v, prev, *toks)
            ins = v.tensor_reduce(m, src_ap, ax, ALU.add)
            if f_ap is not None:
                ins = v.tensor_copy(fcol, f_ap)
            cv.inc(ins)
        kb.v(_v1)
        vt1 = cv.tok()

        def _g1(g):
            wait(g, prev, *toks)
            cg.inc(g.tensor_tensor(sq, src_ap, src_ap, ALU.mult))
        kb.g(_g1)
        gt1 = cg.tok()

        def _v1b(v):
            wait(v, gt1, vt1)
            cv.inc(v.tensor_reduce(va, sq, ax, ALU.add))
        kb.v(_v1b)
        gt2 = cv.tok()

        def _v2(v):
            wait(v, vt1, gt2)
            if f_ap is not None:
                v.tensor_scalar(m, m, fcol, invD, ALU.mult, ALU.mult)
                ins = v.tensor_mul(r, fcol, fcol)
            else:
                ins = v.tensor_scalar_mul(m, m, invD)
            cv.inc(ins)
        kb.v(_v2)
        vt2 = cv.tok()

        def _v3(v):
            wait(v, vt2)
            if f_ap is not None:
                cv.inc(v.tensor_scalar(va, va, r, invD, ALU.mult, ALU.mult))
            else:
                cv.inc(v.tensor_scalar_mul(va, va, invD))
        kb.v(_v3)
        vt3 = cv.tok()

        def _v4(v):
            wait(v, vt3)
            cv.inc(v.tensor_mul(r, m, m))
        kb.v(_v4)
        vt4 = cv.tok()

        def _v5(v):
            wait(v, vt4)
            cv.inc(v.tensor_sub(va, va, r))
        kb.v(_v5)
        return cv.tok()

    def rstd_batch(self, c0, c1, toks):
        kb, st = self.kb, self.st
        cs = kb.cs

        def _s1(s):
            wait(s, *toks)
            cs.inc(s.activation(st[:, 32 + c0:32 + c1], st[:, 16 + c0:16 + c1],
                                AF.Ln, bias=self.eps))
        kb.s(_s1)
        t1 = cs.tok()

        def _s2(s):
            wait(s, t1)
            cs.inc(s.activation(st[:, 32 + c0:32 + c1], st[:, 32 + c0:32 + c1],
                                AF.Exp, scale=-0.5))
        kb.s(_s2)
        return cs.tok()

    def apply(self, src_ap, dst_ap, col, toks, scaled=False):
        kb, st = self.kb, self.st
        cv = kb.cv
        m = st[:, col:col + 1]
        r = st[:, 32 + col:33 + col]

        def _v5(v):
            wait(v, *toks)
            cv.inc(v.scalar_tensor_tensor(m, m, -1.0, r, ALU.mult, ALU.mult))
        kb.v(_v5)
        vt5 = cv.tok()
        if scaled:
            def _v6(v):
                wait(v, vt5)
                cv.inc(v.tensor_mul(r, r, st[:, 48 + col:49 + col]))
            kb.v(_v6)
            vt5 = cv.tok()

        def _v7(v):
            wait(v, vt5)
            cv.inc(v.tensor_scalar(dst_ap, src_ap, r, m, ALU.mult, ALU.add))
        kb.v(_v7)
        return cv.tok()


def transpose_tile(kb, src_cols_fn, nchunks, ps_bf, dst_cols_fn, idbf,
                   waits, ps_free=None):
    ct, cv = kb.ct, kb.cv

    def _t(t):
        wait(t, ps_free, *waits)
        for c in range(nchunks):
            ins = t.transpose(ps_bf[:, c * 128:(c + 1) * 128], src_cols_fn(c), idbf)
        ct.inc(ins)
    kb.t(_t)
    tt = ct.tok()

    def _v(v):
        wait(v, tt)
        for c in range(nchunks):
            ins = v.tensor_copy(dst_cols_fn(c), ps_bf[:, c * 128:(c + 1) * 128])
        cv.inc(ins)
    kb.v(_v)
    return cv.tok(), tt


def build(debug=(), phases="ABCD"):
    global DEBUG
    DEBUG = tuple(debug)
    ctx = ExitStack()
    nc = bass.Bass("TRN2", target_bir_lowering=False, debug=False)
    inp = {}
    for name, shape in INPUT_SPECS:
        inp[name] = nc.dram_tensor(name, shape, F32, kind="ExternalInput").ap()
    out = nc.dram_tensor("out", [T, D], F32, kind="ExternalOutput").ap()
    w1t_sc = nc.dram_tensor("w1t_sc", [4 * D, D], BF16, kind="Internal").ap()
    w2t_sc = nc.dram_tensor("w2t_sc", [D, 4 * D], BF16, kind="Internal").ap()
    x4_sc = nc.dram_tensor("x4_sc", [T, D], F32, kind="Internal").ap()

    kb = KB(nc, ctx)
    dq, dg, ct, cv, cs, cg = kb.dq, kb.dg, kb.ct, kb.cv, kb.cs, kb.cg

    idbf = kb.sbuf("idbf", [128, 128], BF16)
    nidbf = kb.sbuf("nidbf", [128, 128], BF16)
    ones_b = kb.sbuf("ones_b", [1, 512], BF16)
    eps_t = kb.sbuf("eps_t", [128, 1], F32)
    sm = kb.sbuf("sm", [128, 16], F32)
    ssum = kb.sbuf("ssum", [128, 2], F32)
    sq_scr = kb.sbuf("sq_scr", [128, D], F32)
    bqkv_row = kb.sbuf("bqkv_row", [1, 3 * D], BF16)
    wv_pp = kb.sbuf("wv_pp", [128, 24], F32)

    dpro = kb.dsem("pro")

    def _(g):
        dpro.inc(g.dma_start(out=bqkv_row[:], in_=inp["L0_attn_b_qkv"]))
        dpro.inc(g.dma_start(out=wv_pp[:].rearrange("p (c k) -> p c k", k=3),
                           in_=inp["L0_attn_W_vconv"].rearrange("(c p) k -> p c k", p=128)))
        cg.inc(g.iota(idbf[:], [[1, 128]], base=0, channel_multiplier=-1,
                      allow_small_or_imprecise_dtypes=True))
    kb.g(_)
    g0 = cg.tok()

    def _(g):
        wait(g, g0)
        cg.inc(g.tensor_single_scalar(idbf[:], idbf[:], 0.0, ALU.is_equal))
    kb.g(_)
    g1 = cg.tok()

    def _(g):
        wait(g, g1)
        cg.inc(g.tensor_scalar_mul(nidbf[:], idbf[:], -1.0))
    kb.g(_)
    gid = cg.tok()

    def _(v):
        v.memset(ones_b[:], 1.0)
        cv.inc(v.memset(eps_t[:], 1.0 / EPS))
    kb.v(_)
    vc0 = cv.tok()

    def _(v):
        wait(v, vc0)
        cv.inc(v.reciprocal(eps_t[:], eps_t[:]))
    kb.v(_)
    vconst = cv.tok()
    pro_dg = dpro.tok()

    b0_steps = []
    b0_state = {}
    if "B" in phases:
        NW = 4096 * D
        asums = kb.sbuf("asums", [128, 64], F32)
        taus = kb.sbuf("taus", [128, 4], F32)
        tot_sb = kb.sbuf("tot_sb", [1, 2], F32)
        ones_f = kb.sbuf("ones_f", [1, 128], F32)
        ones_c = kb.sbuf("ones_c", [128, 1], F32)

        def _(v):
            v.memset(ones_f[:], 1.0)
            cv.inc(v.memset(ones_c[:], 1.0))
        kb.v(_)
        vof = cv.tok()

        W1r = inp["L0_ffn_W1"].rearrange("(c p) d -> p c d", p=128)
        W2r = inp["L0_ffn_W2"].rearrange("(c p) d -> p c d", p=128)
        NSLOT = 5
        tw = kb.at("tw", [128, NSLOT * 1024], F32, REG1)
        NSLOT1 = 2
        tw1 = kb.at("tw1", [128, NSLOT1 * 1024], F32, REG5)
        absd = kb.at("absd", [128, 1024], BF16, REG5 + NSLOT1 * 4096)
        thr = kb.at("thr", [128, 1024], F32, REG1 + 20480)
        wtb = kb.at("wtb", [128, 2 * 1024], F32, REG1 + 24576)
        dts = [kb.dsem("t%d" % k) for k in range(NSLOT)]
        dts1 = [kb.dsem("u%d" % k) for k in range(NSLOT1)]
        dst_sems = [kb.dsem("ts"), kb.dsem("ts2")]
        scope_b0ps = ExitStack()
        ps_b0 = kb.psum("ps_b0", [128, 4], F32, scope_b0ps)
        batches = []
        for b in range(32):
            batches.append((W1r[:, b:b + 1, :], b))
        for b in range(32):
            c, h = b // 4, b % 4
            batches.append((W2r[:, c:c + 1, h * 1024:(h + 1) * 1024], 32 + b))
        b0_state.update(slot_free={k: None for k in range(NSLOT)},
                        p1_free={k: None for k in range(NSLOT1)},
                        abs_prev=None, g_prev=None, st_tok=None, st_tok2=None,
                        st_par=[None, None], tau_tok=None)

        def emit_load(src_ap, col):
            slot = col % NSLOT
            twa = tw[:, slot * 1024:(slot + 1) * 1024]

            def _(sync, w=b0_state["slot_free"][slot], ds=dts[slot],
                  w2=(b0_state.get("reg1_free") if col < NSLOT else None)):
                wait(sync, w, w2)
                ds.inc(sync.dma_start(out=twa.rearrange("p (c d) -> p c d",
                                                        c=src_ap.shape[1]),
                                      in_=src_ap))
            kb.q(_)
            return twa, dts[slot].tok()

        def mk_pass1(src_ap, col):
            load = {}
            slot = col % NSLOT1

            def prefetch():
                twa = tw1[:, slot * 1024:(slot + 1) * 1024]

                def _(sync, w=b0_state["p1_free"][slot], ds=dts1[slot]):
                    wait(sync, w)
                    ds.inc(sync.dma_start(
                        out=twa.rearrange("p (c d) -> p c d", c=src_ap.shape[1]),
                        in_=src_ap))
                kb.q(_)
                load["twa"], load["tok"] = twa, dts1[slot].tok()

            def step():
                def _(s, ltok=load["tok"], twa=load["twa"],
                      pv=b0_state["abs_prev"]):
                    wait(s, ltok, pv)
                    cs.inc(s.activation(absd[:], twa, AF.Abs,
                                        accum_out=asums[:, col:col + 1]))
                kb.s(_)
                b0_state["abs_prev"] = cs.tok()
                b0_state["p1_free"][slot] = cs.tok()
            return prefetch, step

        def tau_step():
            red_all = cs.tok()

            def _(v, w=red_all):
                wait(v, w)
                v.tensor_reduce(sm[:, 9:10], asums[:, 0:32], AX.X, ALU.add)
                cv.inc(v.tensor_reduce(sm[:, 10:11], asums[:, 32:64], AX.X, ALU.add))
            kb.v(_)
            tsum_tok = cv.tok()

            def _(t, w=tsum_tok, w2=vof):
                wait(t, w, w2)
                t.matmul(ps_b0[0:1, 0:1], sm[:, 9:10], ones_c[:], start=True, stop=True)
                ct.inc(t.matmul(ps_b0[0:1, 1:2], sm[:, 10:11], ones_c[:],
                                start=True, stop=True))
            kb.t(_)
            ttot = ct.tok()

            def _(v, w=ttot):
                wait(v, w)
                cv.inc(v.tensor_scalar_mul(tot_sb[0:1, 0:2], ps_b0[0:1, 0:2], 1.0 / NW))
            kb.v(_)
            vv1 = cv.tok()

            def _(v, w=vv1):
                wait(v, w)
                cv.inc(v.tensor_scalar(tot_sb[0:1, 0:2], tot_sb[0:1, 0:2], 1e-8, 0.5,
                                       ALU.max, ALU.mult))
            kb.v(_)
            vv2 = cv.tok()

            def _(t, w=vv2):
                wait(t, w)
                ct.inc(t.matmul(ps_b0[:, 2:4], ones_f[:], tot_sb[0:1, 0:2],
                                start=True, stop=True))
            kb.t(_)
            tbc = ct.tok()

            def _(v, w=tbc):
                wait(v, w)
                v.tensor_copy(taus[:, 0:1], ps_b0[:, 2:3])
                v.tensor_copy(taus[:, 2:3], ps_b0[:, 3:4])
                cv.inc(v.tensor_scalar_mul(taus[:, 1:2], ps_b0[:, 2:3], -1.0))
            kb.v(_)
            vt_ = cv.tok()

            def _(v, w=vt_):
                wait(v, w)
                cv.inc(v.tensor_scalar_mul(taus[:, 3:4], ps_b0[:, 3:4], -1.0))
            kb.v(_)
            b0_state["tau_tok"] = cv.tok()

        def tau_switch_step():
            pass

        W1s = w1t_sc.rearrange("(c p) d -> p c d", p=128)
        W2s = w2t_sc.rearrange("(c p) d -> p c d", p=128)

        def mk_pass2(src_ap, col):
            load = {}

            def prefetch():
                load["twa"], load["tok"] = emit_load(src_ap, 64 + col)

            def step():
                m = 0 if col < 32 else 1
                slot = (64 + col) % NSLOT
                ws = col % 2
                wslice = wtb[:, ws * 1024:(ws + 1) * 1024]
                if col < 32:
                    dst = W1s[:, col:col + 1, :]
                else:
                    c, h = (col - 32) // 4, (col - 32) % 4
                    dst = W2s[:, c:c + 1, h * 1024:(h + 1) * 1024]
                twa, ltok = load["twa"], load["tok"]

                def _(v, m=m, w=ltok, pv=b0_state["g_prev"],
                      tt=b0_state["tau_tok"]):
                    wait(v, w, pv, tt)
                    cv.inc(v.tensor_scalar(thr[:], twa, taus[:, 2 * m + 1:2 * m + 2],
                                           1.0, ALU.is_ge, ALU.subtract))
                kb.v(_)
                g1 = cv.tok()

                def _(v, m=m, w=g1, st=b0_state["st_par"][ws], wslice=wslice):
                    wait(v, w, st)
                    cv.inc(v.scalar_tensor_tensor(wslice, twa,
                                                  taus[:, 2 * m:2 * m + 1],
                                                  thr[:], ALU.is_gt, ALU.add))
                kb.v(_)
                g4 = cv.tok()
                b0_state["g_prev"] = g4
                b0_state["slot_free"][slot] = g4

                def _(g, dst=dst, w=g4, wslice=wslice, ds=dst_sems[ws]):
                    wait(g, w)
                    ds.inc(g.dma_start(
                        out=dst, in_=wslice.rearrange("p (c d) -> p c d",
                                                      c=dst.shape[1])))
                kb.g(_)
                b0_state["st_par"][ws] = dst_sems[ws].tok()
                b0_state["st_tok2"] = b0_state["st_par"][1 - ws]
                b0_state["st_tok"] = b0_state["st_par"][ws]
            return prefetch, step

        _p1 = [mk_pass1(s_, c_) for s_, c_ in batches]
        _p2 = [mk_pass2(s_, c_) for s_, c_ in batches]
        b0p1_steps = []
        for k in range(NSLOT1):
            _p1[k][0]()
        for k in range(64):
            def mk(k=k):
                def step():
                    _p1[k][1]()
                    if k + NSLOT1 < 64:
                        _p1[k + NSLOT1][0]()
                return step
            b0p1_steps.append(mk())
        b0_steps.append(tau_step)

        def _pf2_initial():
            for k in range(NSLOT):
                _p2[k][0]()
        b0_steps.append(_pf2_initial)
        for k in range(64):
            def mk(k=k):
                def step():
                    _p2[k][1]()
                    if k + NSLOT < 64:
                        _p2[k + NSLOT][0]()
                return step
            b0_steps.append(mk())


    q_fm = kb.at("q_fm", [128, ND * T], BF16, REG2)
    k_fm = kb.at("k_fm", [128, ND * T], BF16, REG3)
    v_fm = kb.at("v_fm", [128, ND * T], BF16, REG4)
    C_tm = kb.at("C_tm", [128, NT * D], BF16, REG5)
    n1_fm = kb.at("n1_fm", [128, ND * T], BF16, REG1)

    lnst = kb.sbuf("lnst", [128, 64], F32)
    lnb = LNB(kb, lnst, sq_scr[:], eps_t[:, 0:1])
    xstage = q_fm
    dxa = kb.dsem("xa")

    def _(g):
        dxa.inc(g.dma_start(
            out=xstage[:].rearrange("p (i d) -> p i d", i=NT),
            in_=inp["L0_attn_x"].rearrange("(i p) d -> p i d", p=128)))
    kb.g(_)
    xload = dxa.tok()

    with ExitStack() as pha:
        n1_tm = kb.at("n1_tm", [128, D], BF16, WS + 8192)
        ps_tr = kb.psum("ps_tr", [128, 1024], BF16, pha)
        stok = None
        for i in range(NT):
            for _k in range(2):
                if b0p1_steps:
                    b0p1_steps.pop(0)()
            stok = lnb.stats(xstage[:, i * D:(i + 1) * D], i, [xload], prev=stok)
        rtok = lnb.rstd_batch(0, 16, [stok])
        evtok = None
        trtok = None
        aptok = None
        for i in range(NT):
            aptok = lnb.apply(xstage[:, i * D:(i + 1) * D], n1_tm[:], i,
                              [rtok, trtok])
            evtok, trtok = transpose_tile(
                kb, lambda c: n1_tm[:, c * 128:(c + 1) * 128], ND, ps_tr[:],
                lambda c, i=i: n1_fm[:, c * T + i * 128: c * T + (i + 1) * 128],
                idbf[:], [aptok, gid], ps_free=evtok)
        n1_done = evtok
        kb.dbg("n1_fm", n1_fm[:], n1_done)
    kb.flush_dbg()
    kb.barrier(dma=False)

    with ExitStack() as pha:
        wbuf = kb.at("wbuf", [128, 2 * 8 * 512], BF16, WS)
        ps_q = [kb.psum("ps_qkv0", [128, 512], F32, pha),
                kb.psum("ps_qkv1", [128, 512], F32, pha)]
        Wq = inp["L0_attn_W_qkv"].rearrange("(c p) f -> p c f", p=128)
        dws = [kb.dsem("w0"), kb.dsem("w1")]
        slot_tok = {0: None, 1: None}
        bank_tok = {0: None, 1: None}
        for grp in range(6):
            slot = grp % 2

            def _(g, grp=grp, slot=slot, w=slot_tok[slot], ds=dws[slot]):
                wait(g, w)
                ds.inc(g.dma_start(
                    out=wbuf[:, slot * 4096:(slot + 1) * 4096].rearrange(
                        "p (c f) -> p c f", c=8),
                    in_=Wq[:, :, grp * 512:(grp + 1) * 512]))
            kb.g(_)
            wtok = dws[slot].tok()

            for fc4 in range(4):
                if b0p1_steps:
                    b0p1_steps.pop(0)()
                if grp >= 4 and b0p1_steps:
                    b0p1_steps.pop(0)()
                fc = grp * 4 + fc4
                mat, lc = fc // 8, fc % 8
                dst = (q_fm, k_fm, v_fm)[mat]
                bk = fc4 % 2
                pb = ps_q[bk]
                for tg in range(4):
                    first = (grp == 0 and fc4 == 0 and tg == 0)
                    def _(t, pb=pb, slot=slot, fc4=fc4, tg=tg, wtok=wtok,
                          fc=fc, w=bank_tok[bk], first=first):
                        wait(t, wtok, w)
                        if first:
                            wait(t, pro_dg)
                        for kc in range(8):
                            t.matmul(pb[:],
                                     wbuf[:, slot * 4096 + kc * 512 + fc4 * 128:
                                          slot * 4096 + kc * 512 + (fc4 + 1) * 128],
                                     n1_fm[:, kc * T + tg * 512: kc * T + (tg + 1) * 512],
                                     start=(kc == 0), stop=False)
                        ct.inc(t.matmul(pb[:], bqkv_row[0:1, fc * 128:(fc + 1) * 128],
                                        ones_b[0:1, :], start=False, stop=True))
                    kb.t(_)
                    mtok = ct.tok()

                    def _(v, pb=pb, mat=mat, lc=lc, dst=dst, tg=tg, mtok=mtok):
                        wait(v, mtok)
                        dd = dst[:, lc * T + tg * 512: lc * T + (tg + 1) * 512]
                        if mat == 0:
                            cv.inc(v.tensor_scalar_mul(dd, pb[:], 1.0 / 32.0))
                        else:
                            cv.inc(v.tensor_copy(dd, pb[:]))
                    kb.v(_)
                    bank_tok[bk] = cv.tok()
            slot_tok[slot] = cv.tok()
        qkv_done = cv.tok()
        b0_state["reg1_free"] = qkv_done
        kb.dbg("q_fm", q_fm[:], qkv_done)
        kb.dbg("k_fm", k_fm[:], qkv_done)
        kb.dbg("v_fm", v_fm[:], qkv_done)
    kb.flush_dbg()
    kb.barrier(dma=False)

    with ExitStack() as pha:
        cbuf = kb.at("cbuf", [128, T], BF16, WS)
        ps_trc = kb.psum("ps_trc", [128, 1024], BF16, pha)
        ev = None
        ttk = None
        for c in range(ND):
            vchunk = v_fm[:, c * T:(c + 1) * T]
            w0 = wv_pp[:, c * 3 + 0:c * 3 + 1]
            w1 = wv_pp[:, c * 3 + 1:c * 3 + 2]
            w2 = wv_pp[:, c * 3 + 2:c * 3 + 3]

            def _(v, vchunk=vchunk, w1=w1, ev=ev):
                wait(v, ev, pro_dg, qkv_done)
                cv.inc(v.tensor_scalar_mul(cbuf[:], vchunk, w1))
            kb.v(_)
            t1 = cv.tok()

            def _(v, vchunk=vchunk, w0=w0, t1=t1):
                wait(v, t1)
                cv.inc(v.scalar_tensor_tensor(cbuf[:, 1:T], vchunk[:, 0:T - 1], w0,
                                              cbuf[:, 1:T], ALU.mult, ALU.add))
            kb.v(_)
            t2 = cv.tok()

            def _(v, vchunk=vchunk, w2=w2, t2=t2):
                wait(v, t2)
                cv.inc(v.scalar_tensor_tensor(cbuf[:, 0:T - 1], vchunk[:, 1:T], w2,
                                              cbuf[:, 0:T - 1], ALU.mult, ALU.add))
            kb.v(_)
            t3 = cv.tok()

            for half in range(2):
                ev, ttk = transpose_tile(
                    kb, lambda j, half=half: cbuf[:, half * 1024 + j * 128:
                                                  half * 1024 + (j + 1) * 128],
                    8, ps_trc[:],
                    lambda j, c=c, half=half: C_tm[:, (half * 8 + j) * D + c * 128:
                                                   (half * 8 + j) * D + (c + 1) * 128],
                    idbf[:], [t3, gid], ps_free=ev)
        c_done = ev
        kb.dbg("C_tm", C_tm[:], c_done)
    kb.flush_dbg()
    kb.barrier(dma=False)

    def outln_ap(i):
        return bass.AP(q_fm, 128 * i, [[q_fm.shape[1], 128], [T, ND], [1, 128]])

    with ExitStack() as pha:
        S_sb = kb.at("S_sb", [128, T], F32, WS)
        P_sb = kb.at("P_sb", [128, T], BF16, WS + 8192)
        AT_sb = kb.at("AT_sb", [128, T], BF16, WS + 12288)
        Dw_sb = kb.at("Dw_sb", [128, D], BF16, WS + 16384)
        D_sb = kb.at("D_sb", [128, D], BF16, WS + 18432)
        B_sb = kb.sbuf("B_sb", [128, 128], BF16, pha)
        ps_s = [kb.psum("ps_s%d" % k, [128, 512], F32, pha) for k in range(4)]
        ps_at = kb.psum("ps_at", [128, 1024], BF16, pha)
        ps_o = kb.psum("ps_o", [128, 1024], F32, pha)

        sbank_tok = {k: None for k in range(4)}
        at_ps_tok = None
        p_free = None
        s_free = None
        dw_free = None
        out_ps_free = None
        at_free = None
        stok = None
        sexp_prev = None

        def emit_scores(i):
            W = 128 * (i + 1)
            out = []
            for b in range((W + 511) // 512):
                w0c = b * 512
                w1c = min(W, w0c + 512)
                bk = b % 4

                def _(t, w0c=w0c, w1c=w1c, bk=bk, i=i, w=sbank_tok[bk]):
                    wait(t, w)
                    for kc in range(ND):
                        ins = t.matmul(
                            ps_s[bk][:, 0:w1c - w0c],
                            q_fm[:, kc * T + i * 128: kc * T + (i + 1) * 128],
                            k_fm[:, kc * T + w0c: kc * T + w1c],
                            start=(kc == 0), stop=(kc == ND - 1))
                    ct.inc(ins)
                kb.t(_)
                out.append((bk, w0c, w1c, ct.tok()))
            return out

        pend = emit_scores(0)
        for i in range(NT):
            W = 128 * (i + 1)
            ev_toks = []
            for n, (bk, w0c, w1c, mtok) in enumerate(pend):
                def _(v, bk=bk, w0c=w0c, w1c=w1c, mtok=mtok,
                      sf=(s_free if n == 0 else None)):
                    wait(v, mtok, sf)
                    cv.inc(v.tensor_copy(S_sb[:, w0c:w1c], ps_s[bk][:, 0:w1c - w0c]))
                kb.v(_)
                sbank_tok[bk] = cv.tok()
                ev_toks.append(cv.tok())

            def _(g, i=i, ev=ev_toks[-1]):
                wait(g, ev)
                cg.inc(g.affine_select(S_sb[:, i * 128:(i + 1) * 128],
                                       S_sb[:, i * 128:(i + 1) * 128],
                                       [[-1, 128]], ALU.is_ge, -1e30,
                                       base=0, channel_multiplier=1))
            kb.g(_)
            gmask = cg.tok()

            def _(v, W=W, gmask=gmask, ev=list(ev_toks), sp=sexp_prev):
                wait(v, gmask, sp, *ev)
                cv.inc(v.reduce_max(sm[:, 8:9], S_sb[:, 0:W], AX.X, negate=True))
            kb.v(_)
            vmax = cv.tok()

            def _(s, W=W, vmax=vmax, pf=p_free):
                wait(s, vmax, pf)
                cs.inc(s.activation(P_sb[:, 0:W], S_sb[:, 0:W], AF.Exp,
                                    bias=sm[:, 8:9], accum_out=ssum[:, 0:1]))
            kb.s(_)
            sexp = cs.tok()
            sexp_prev = sexp
            s_free = sexp

            if i + 1 < NT:
                pend = emit_scores(i + 1)
            while b0p1_steps:
                b0p1_steps.pop(0)()
            for _k in range(6 if i else 3):
                if b0_steps:
                    b0_steps.pop(0)()

            at_toks = []
            for bg in range((i + 8) // 8):
                j0, j1 = bg * 8, min(i + 1, bg * 8 + 8)

                def _(t, j0=j0, j1=j1, sexp=sexp, w=at_ps_tok,
                      af=(at_free if bg == 0 else None)):
                    wait(t, sexp, w, af)
                    for j in range(j0, j1):
                        ins = t.transpose(ps_at[:, (j - j0) * 128:(j - j0 + 1) * 128],
                                          P_sb[:, j * 128:(j + 1) * 128], idbf[:])
                    ct.inc(ins)
                kb.t(_)
                ttok = ct.tok()
                p_free = ttok

                def _(v, j0=j0, j1=j1, ttok=ttok):
                    wait(v, ttok)
                    cv.inc(v.tensor_copy(AT_sb[:, j0 * 128:j1 * 128],
                                         ps_at[:, 0:(j1 - j0) * 128]))
                kb.v(_)
                at_ps_tok = cv.tok()
                at_toks.append(cv.tok())

            def _(v, i=i, at=at_toks[-1]):
                wait(v, at)
                cv.inc(v.tensor_mul(B_sb[:], AT_sb[:, i * 128:(i + 1) * 128], nidbf[:]))
            kb.v(_)
            btok = cv.tok()

            def _(v, i=i, w=dw_free):
                wait(v, w, qkv_done)
                t0 = i * 128 + 1
                wlen = min(T, (i + 1) * 128 + 1) - t0
                for c in range(ND):
                    if wlen < 128:
                        v.memset(Dw_sb[:, c * 128 + wlen:(c + 1) * 128], 0.0)
                    ins = v.tensor_scalar_mul(Dw_sb[:, c * 128:c * 128 + wlen],
                                              v_fm[:, c * T + t0:c * T + t0 + wlen],
                                              wv_pp[:, c * 3 + 2:c * 3 + 3])
                cv.inc(ins)
            kb.v(_)
            dwtok = cv.tok()

            dtr, dw_free = transpose_tile(
                kb, lambda c: Dw_sb[:, c * 128:(c + 1) * 128], ND, ps_at[:],
                lambda c: D_sb[:, c * 128:(c + 1) * 128],
                idbf[:], [dwtok, at_free], ps_free=at_ps_tok)
            at_ps_tok = dtr

            def _(t, i=i, at=list(at_toks), btok=btok, dtr=dtr, w=out_ps_free):
                wait(t, w, btok, dtr, *at)
                for half in range(2):
                    for j in range(i + 1):
                        t.matmul(ps_o[:, half * 512:(half + 1) * 512],
                                 AT_sb[:, j * 128:(j + 1) * 128],
                                 C_tm[:, j * D + half * 512: j * D + (half + 1) * 512],
                                 start=(j == 0), stop=False)
                    ins = t.matmul(ps_o[:, half * 512:(half + 1) * 512], B_sb[:],
                                   D_sb[:, half * 512:(half + 1) * 512],
                                   start=False, stop=True)
                ct.inc(ins)
            kb.t(_)
            otok = ct.tok()
            at_free = otok

            def _(v, sexp=sexp):
                wait(v, sexp)
                cv.inc(v.reciprocal(ssum[:, 1:2], ssum[:, 0:1]))
            kb.v(_)
            rtok = cv.tok()

            def _(v, i=i, w=otok):
                wait(v, w)
                cv.inc(v.tensor_copy(outln_ap(i), ps_o[:]))
            kb.v(_)
            out_ps_free = cv.tok()

            stok = lnb.stats(outln_ap(i), i, [out_ps_free, rtok],
                             f_ap=ssum[:, 1:2], prev=stok)
            if i == NT - 1 and "a3_oraw" in DEBUG:
                oraw_sc = kb.sbuf("oraw_sc", [128, ND * 128], F32, pha)
                p_sc = kb.sbuf("p_sc", [128, T], BF16, pha)

                def _(v, i=i, w=out_ps_free, se=sexp):
                    wait(v, w, se)
                    cv.inc(v.tensor_copy(oraw_sc[:], outln_ap(i)))
                    cv.inc(v.tensor_copy(p_sc[:], P_sb[:]))
                kb.v(_)
                a3dbg_tok = cv.tok()
                kb.dbg("a3_oraw", oraw_sc[:], a3dbg_tok)
                kb.dbg("a3_p", p_sc[:], a3dbg_tok)
                kb.dbg("a3_ssum", ssum[:], a3dbg_tok)
            else:
                a3dbg_tok = None
        while b0_steps:
            b0_steps.pop(0)()
        rtok_b = lnb.rstd_batch(0, 16, [stok])
        aptok = None
        for i in range(NT):
            aptok = lnb.apply(outln_ap(i), outln_ap(i), i,
                              [rtok_b, aptok, a3dbg_tok if i == 0 else None],
                              scaled=True)
        a3_done = aptok
        kb.dbg("outln_q", q_fm[:], a3_done)
        kb.dbg("a3_lnst", lnst[:], a3_done)
        kb.flush_dbg()

    kb.barrier(dma=False)
    if "B" in phases:
        scope_b0ps.close()

    if "B" in phases:
        scope_b1 = ExitStack()
        scope_b1tr = ExitStack()
        X2 = kb.at("X2", [128, NT * D], F32, REG4)
        act_fm = kb.at("act_fm", [128, ND * T], BF16, REG3)
        xt2 = kb.at("xt2", [128, 2 * D], F32, WS)
        tmb = kb.at("tmb", [128, D], BF16, WS + 16384)
        ps_tr2 = kb.psum("ps_tr2", [128, 1024], BF16, scope_b1tr)
        dx2 = [kb.dsem("x0"), kb.dsem("x1")]
        slot_done = {0: None, 1: None}
        stok = None
        for i in range(NT):
            slot = i % 2
            xa = xt2[:, slot * D:(slot + 1) * D]

            def _(sync, i=i, xa=xa, w=slot_done[slot], ds=dx2[slot]):
                wait(sync, w)
                ds.inc(sync.dma_start(out=xa, in_=inp["L0_attn_x"][i * 128:(i + 1) * 128, :]))
            kb.q(_)
            ldtok = dx2[slot].tok()

            def _(v, i=i, xa=xa, ldtok=ldtok):
                wait(v, ldtok)
                cv.inc(v.tensor_add(X2[:, i * D:(i + 1) * D], xa, outln_ap(i)))
            kb.v(_)
            addtok = cv.tok()
            slot_done[slot] = addtok

            stok = lnb.stats(X2[:, i * D:(i + 1) * D], i, [addtok], prev=stok)
        rtok_b = lnb.rstd_batch(0, 16, [stok])
        aptok = None
        evtok = None
        trtok = None
        for i in range(NT):
            aptok = lnb.apply(X2[:, i * D:(i + 1) * D], tmb[:], i, [rtok_b, trtok])
            evtok, trtok = transpose_tile(
                kb, lambda c: tmb[:, c * 128:(c + 1) * 128], ND, ps_tr2[:],
                lambda c, i=i: act_fm[:, c * T + i * 128: c * T + (i + 1) * 128],
                idbf[:], [aptok], ps_free=evtok)
        b1_done = evtok
        kb.dbg("x2", X2[:], aptok)
        kb.dbg("n2_fm", act_fm[:], b1_done)
        kb.flush_dbg()
        kb.barrier()
        scope_b1tr.close()

        h_g = kb.at("h_g", [128, 32 * 512], BF16, REG1)
        w1tb = kb.at("w1tb", [128, 2 * 8 * 512], BF16, REG2)
        b1row = kb.at("b1row", [1, 4096], BF16, REG2 + 16384)
        b2row = kb.at("b2row", [1, 1024], BF16, REG2 + 24576)
        w2tb = kb.at("w2tb", [128, 4 * 1024], BF16, WS)
        out2g = kb.at("out2g", [128, 4 * 1024], BF16, WS + 8192)
        smB = kb.sbuf("smB", [128, 12], F32)
        drow = kb.dsem("rows")

        def _(g):
            drow.inc(g.dma_start(out=b1row[:], in_=inp["L0_ffn_b1"]))
            drow.inc(g.dma_start(out=b2row[:], in_=inp["L0_ffn_b2"]))
        kb.g(_)
        rows_tok = drow.tok()

        dw1 = [kb.dsem("tw1a"), kb.dsem("tw1b")]
        dw2 = [kb.dsem("w2a"), kb.dsem("w2b")]
        ps_h = [kb.psum("ps_h0", [128, 1024], F32, scope_b1),
                kb.psum("ps_h1", [128, 1024], F32, scope_b1)]
        ps_o2 = [kb.psum("ps_o2%d" % u, [128, 1024], F32, scope_b1) for u in range(2)]

        w1slot_tok = {0: None, 1: None}
        w2slot_tok = {0: None, 1: None}
        hbank_tok = {0: None, 1: None}
        o2_tok = {0: None, 1: None}
        hg_free = None
        out2g_free = None
        for grp in range(4):
            for p in range(8):
                slot = p % 2

                def _(sync, p=p, slot=slot, w=w1slot_tok[slot], ds=dw1[slot],
                      w2=(b0_state.get("st_tok") if (grp == 0 and p < 2) else None),
                      w3=(b0_state.get("st_tok2") if (grp == 0 and p < 2) else None)):
                    wait(sync, w, w2, w3)
                    for dc in range(8):
                        ds.inc(sync.dma_start(
                            out=w1tb[:, slot * 4096 + dc * 512:slot * 4096 + (dc + 1) * 512],
                            in_=w1t_sc[p * 512:(p + 1) * 512, dc * 128:(dc + 1) * 128],
                            transpose=True))
                kb.q(_)
                wtok = dw1[slot].tok()

                for fpr in range(2):
                    bk = fpr % 2
                    pb = ps_h[bk]

                    def _(t, pb=pb, slot=slot, fpr=fpr, p=p, grp=grp, wtok=wtok,
                          w=hbank_tok[bk]):
                        wait(t, wtok, w, rows_tok, b1_done)
                        for half in range(2):
                            fl = 2 * fpr + half
                            fc = 4 * p + fl
                            for kc in range(8):
                                t.matmul(pb[:, half * 512:(half + 1) * 512],
                                         w1tb[:, slot * 4096 + kc * 512 + fl * 128:
                                              slot * 4096 + kc * 512 + (fl + 1) * 128],
                                         act_fm[:, kc * T + grp * 512:
                                                kc * T + (grp + 1) * 512],
                                         start=(kc == 0), stop=False)
                            ins = t.matmul(pb[:, half * 512:(half + 1) * 512],
                                           b1row[0:1, fc * 128:(fc + 1) * 128],
                                           ones_b[0:1, :], start=False, stop=True)
                        ct.inc(ins)
                    kb.t(_)
                    mtok = ct.tok()

                    def _(s, pb=pb, p=p, fpr=fpr, mtok=mtok, hf=hg_free,
                          bf=(b0_state.get("st_tok") if grp == 0 and p < 2 else None),
                          bf2=(b0_state.get("st_tok2") if grp == 0 and p < 2 else None),
                          bv=(b0_state.get("vthr_tok") if grp == 0 and p < 2 else None)):
                        wait(s, mtok, hf, bf, bf2, bv)
                        fc0 = 4 * p + 2 * fpr
                        cs.inc(s.activation(h_g[:, fc0 * 512:(fc0 + 2) * 512], pb[:],
                                            AF.Gelu))
                    kb.s(_)
                    hbank_tok[bk] = cs.tok()
                w1slot_tok[slot] = ct.tok()
            h_done = cs.tok()
            if grp == 3:
                kb.dbg("h_g", h_g[:], h_done)

            ev_toks = [None] * 4
            for pair in range(2):
                for fc in range(32):
                    slot = fc % 2
                    wslice = w2tb[:, slot * 1024:(slot + 1) * 1024]
                    def _(sync, fc=fc, wslice=wslice, w=w2slot_tok[slot], ds=dw2[slot],
                          w2=(b0_state.get("st_tok") if (grp == 0 and pair == 0 and fc < 2) else None),
                          w3=(b0_state.get("st_tok2") if (grp == 0 and pair == 0 and fc < 2) else None)):
                        wait(sync, w, w2, w3)
                        ds.inc(sync.dma_start(out=wslice,
                                              in_=w2t_sc[:, fc * 128:(fc + 1) * 128],
                                              transpose=True))
                    kb.q(_)
                    w2tok = dw2[slot].tok()

                    def _(t, fc=fc, pair=pair, wslice=wslice, w2tok=w2tok,
                          h_done=h_done,
                          ot=[o2_tok[u] for u in range(2)] if fc == 0 else []):
                        wait(t, w2tok, h_done, *ot)
                        for u in range(2):
                            sub = pair * 2 + u
                            for half in range(2):
                                ins = t.matmul(
                                    ps_o2[u][:, half * 512:(half + 1) * 512],
                                    h_g[:, fc * 512 + sub * 128:fc * 512 + (sub + 1) * 128],
                                    wslice[:, half * 512:(half + 1) * 512],
                                    start=(fc == 0), stop=False)
                        if fc == 31:
                            for u in range(2):
                                for half in range(2):
                                    ins = t.matmul(
                                        ps_o2[u][:, half * 512:(half + 1) * 512],
                                        ones_b[0:1, 0:128],
                                        b2row[0:1, half * 512:(half + 1) * 512],
                                        start=False, stop=True)
                        ct.inc(ins)
                    kb.t(_)
                    w2slot_tok[slot] = ct.tok()
                o2done = ct.tok()

                for u in range(2):
                    sub = pair * 2 + u

                    def _(s, u=u, sub=sub, w=o2done, of=out2g_free):
                        wait(s, w, of)
                        cs.inc(s.activation(out2g[:, sub * D:(sub + 1) * D],
                                            ps_o2[u][:], AF.Identity))
                    kb.s(_)
                    ev_toks[sub] = cs.tok()
                    o2_tok[u] = cs.tok()

            stat_toks = []
            for sub in range(4):
                evt = ev_toks[sub]

                def _(v, sub=sub, w=evt):
                    wait(v, w)
                    cv.inc(v.tensor_reduce(smB[:, sub:sub + 1],
                                           out2g[:, sub * D:(sub + 1) * D],
                                           AX.X, ALU.add))
                kb.v(_)
                vt = cv.tok()

                def _(s, sub=sub, w=vt):
                    wait(s, w)
                    cs.inc(s.activation(sq_scr[:], out2g[:, sub * D:(sub + 1) * D],
                                        AF.Square, accum_out=smB[:, 4 + sub:5 + sub]))
                kb.s(_)
                sqt = cs.tok()

                def _(v, sub=sub, w=sqt):
                    wait(v, w)
                    v.tensor_scalar_mul(smB[:, sub:sub + 1], smB[:, sub:sub + 1], 1.0 / D)
                    cv.inc(v.tensor_scalar_mul(smB[:, 4 + sub:5 + sub],
                                               smB[:, 4 + sub:5 + sub], 1.0 / D))
                kb.v(_)
                vt2_ = cv.tok()

                def _(v, sub=sub, w=vt2_):
                    wait(v, w)
                    cv.inc(v.tensor_tensor(smB[:, 8 + sub:9 + sub],
                                           smB[:, sub:sub + 1], smB[:, sub:sub + 1],
                                           ALU.mult))
                kb.v(_)
                vt3_ = cv.tok()

                def _(v, sub=sub, w=vt3_):
                    wait(v, w)
                    cv.inc(v.tensor_sub(smB[:, 4 + sub:5 + sub],
                                        smB[:, 4 + sub:5 + sub], smB[:, 8 + sub:9 + sub]))
                kb.v(_)
                stat_toks.append(cv.tok())

            def _(s, w=list(stat_toks)):
                wait(s, *w)
                cs.inc(s.activation(smB[:, 8:12], smB[:, 4:8], AF.Ln,
                                    bias=eps_t[:, 0:1]))
            kb.s(_)
            r1 = cs.tok()

            def _(s, w=r1):
                wait(s, w)
                cs.inc(s.activation(smB[:, 8:12], smB[:, 8:12], AF.Exp, scale=-0.5))
            kb.s(_)
            r2 = cs.tok()

            ap_toks = []
            for sub in range(4):
                i = grp * 4 + sub

                def _(v, sub=sub, w=r2):
                    wait(v, w)
                    cv.inc(v.scalar_tensor_tensor(smB[:, sub:sub + 1],
                                                  smB[:, sub:sub + 1], -1.0,
                                                  smB[:, 8 + sub:9 + sub],
                                                  ALU.mult, ALU.mult))
                kb.v(_)
                bt = cv.tok()

                def _(s, sub=sub, i=i, w=bt):
                    wait(s, w)
                    cs.inc(s.activation(tmb[:], out2g[:, sub * D:(sub + 1) * D],
                                        AF.Identity, bias=smB[:, sub:sub + 1],
                                        scale=smB[:, 8 + sub:9 + sub]))
                kb.s(_)
                apt = cs.tok()

                def _(v, i=i, w=apt):
                    wait(v, w)
                    cv.inc(v.tensor_add(X2[:, i * D:(i + 1) * D],
                                        X2[:, i * D:(i + 1) * D], tmb[:]))
                kb.v(_)
                ap_toks.append(cv.tok())
            out2g_free = cv.tok()
            hg_free = o2done
        b2_done = cv.tok()
        kb.dbg("x3", X2[:], b2_done)
        kb.flush_dbg()
        kb.barrier()
        scope_b1.close()


    if "C" in phases:
        scl = kb.sbuf("scl", [1, 16], F32)
        g8s = kb.sbuf("g8s", [1, 16], F32)
        lnd_pp = kb.sbuf("lnd_pp", [128, 1], F32)
        osc_pp = kb.sbuf("osc_pp", [128, 1], F32)
        ones_f2 = kb.sbuf("ones_f2", [1, 128], F32)
        basis_b = kb.at("basis_b", [128, 8 * 256], BF16, REG1)
        cf_b = kb.at("cf_b", [128, 4 * 2 * 256], BF16, REG1 + 4096)
        basisT = kb.at("basisT", [128, 2 * 1024], BF16, REG1 + 8192)
        cfT = kb.at("cfT", [128, 4 * 2 * 256], BF16, REG1 + 12288)
        qw_b = kb.at("qw_b", [128, 8 * 256], BF16, REG1 + 16384)
        kw_b = kb.at("kw_b", [128, 8 * 256], BF16, REG1 + 20480)
        vw_b = kb.at("vw_b", [128, 8 * 256], BF16, REG1 + 24576)
        owT_b = kb.at("owT_b", [128, 2 * 1024], BF16, REG1 + 28672)
        dl1 = kb.dsem("l1a")

        def _(g):
            dl1.inc(g.dma_start(out=scl[0:1, 0:1], in_=inp["L1_attn_decay_logit"]))
            dl1.inc(g.dma_start(out=scl[0:1, 1:2], in_=inp["L1_attn_out_scale"]))
            dl1.inc(g.dma_start(out=g8s[0:1, 0:8], in_=inp["L1_ffn_smha_gates"]))
            dl1.inc(g.dma_start(out=g8s[0:1, 8:16], in_=inp["L1_ffn_moe_gates"]))
            dl1.inc(g.dma_start(
                out=basis_b[:].rearrange("p (c b) -> p c b", c=8),
                in_=inp["L1_attn_basis"].rearrange("(c p) b -> p c b", p=128)))
            for x, name in enumerate(("q", "k", "v", "o")):
                dl1.inc(g.dma_start(
                    out=cf_b[:, x * 512:(x + 1) * 512].rearrange("p (c b) -> p c b", c=2),
                    in_=inp["L1_attn_%s_coeffs" % name].rearrange("(c p) b -> p c b", p=128)))
        kb.g(_)
        l1tok = dl1.tok()

        def _(v):
            cv.inc(v.memset(ones_f2[:], 1.0))
        kb.v(_)
        vo2 = cv.tok()

        def _(s, w=l1tok):
            wait(s, w)
            s.activation(scl[0:1, 2:3], scl[0:1, 0:1], AF.Sigmoid)
            s.activation(g8s[0:1, 0:8], g8s[0:1, 0:8], AF.Sigmoid)
            cs.inc(s.activation(g8s[0:1, 8:16], g8s[0:1, 8:16], AF.Sigmoid))
        kb.s(_)
        sig_tok = cs.tok()

        def _(v, w=sig_tok):
            wait(v, w)
            cv.inc(v.tensor_scalar_max(scl[0:1, 2:3], scl[0:1, 2:3], 1e-38))
        kb.v(_)
        cl_tok = cv.tok()

        def _(s, w=cl_tok):
            wait(s, w)
            cs.inc(s.activation(scl[0:1, 3:4], scl[0:1, 2:3], AF.Ln))
        kb.s(_)
        lnd_tok = cs.tok()

        scope_c0 = ExitStack()
        ps_c0 = kb.psum("ps_c0", [128, 1024], F32, scope_c0)
        ps_c1 = kb.psum("ps_c1", [128, 1024], F32, scope_c0)
        ps_trC = kb.psum("ps_trC", [128, 1024], BF16, scope_c0)

        def _(t, w=lnd_tok, w2=vo2):
            wait(t, w, w2)
            t.matmul(ps_c0[:, 0:1], ones_f2[:], scl[0:1, 3:4], start=True, stop=True)
            ct.inc(t.matmul(ps_c0[:, 1:2], ones_f2[:], scl[0:1, 1:2], start=True, stop=True))
        kb.t(_)
        tb = ct.tok()

        def _(v, w=tb):
            wait(v, w)
            v.tensor_copy(lnd_pp[:], ps_c0[:, 0:1])
            cv.inc(v.tensor_copy(osc_pp[:], ps_c0[:, 1:2]))
        kb.v(_)
        bc_tok = cv.tok()

        tr_toks = []
        for c in range(8):
            for h in range(2):
                ev, tt = transpose_tile(
                    kb, lambda _c, c=c, h=h: basis_b[:, c * 256 + h * 128:
                                                     c * 256 + (h + 1) * 128],
                    1, ps_trC[:],
                    lambda _c, c=c, h=h: basisT[:, h * 1024 + c * 128:
                                                h * 1024 + (c + 1) * 128],
                    idbf[:], [l1tok, (tr_toks[-1] if tr_toks else None)],
                    ps_free=(tr_toks[-1] if tr_toks else None))
                tr_toks.append(ev)
        for x in range(4):
            for c in range(2):
                for h in range(2):
                    ev, tt = transpose_tile(
                        kb, lambda _c, x=x, c=c, h=h: cf_b[:, x * 512 + c * 256 +
                                                           h * 128: x * 512 + c * 256 +
                                                           (h + 1) * 128],
                        1, ps_trC[:],
                        lambda _c, x=x, c=c, h=h: cfT[:, x * 512 + h * 256 + c * 128:
                                                      x * 512 + h * 256 + (c + 1) * 128],
                        idbf[:], [l1tok, tr_toks[-1]], ps_free=tr_toks[-1])
                    tr_toks.append(ev)
        trC_done = tr_toks[-1]

        wsmall_ev = None
        for x, dstb in enumerate((qw_b, kw_b, vw_b)):
            for vc in range(8):
                def _(t, x=x, vc=vc, w=trC_done, w2=wsmall_ev):
                    wait(t, w, w2)
                    for bc in range(2):
                        ins = t.matmul(ps_c0[:, 0:256],
                                       basisT[:, bc * 1024 + vc * 128:bc * 1024 + (vc + 1) * 128],
                                       cfT[:, x * 512 + bc * 256: x * 512 + (bc + 1) * 256],
                                       start=(bc == 0), stop=(bc == 1))
                    ct.inc(ins)
                kb.t(_)
                mt = ct.tok()

                def _(v, dstb=dstb, vc=vc, mt=mt):
                    wait(v, mt)
                    cv.inc(v.tensor_copy(dstb[:, vc * 256:(vc + 1) * 256], ps_c0[:, 0:256]))
                kb.v(_)
                wsmall_ev = cv.tok()
        for chc in range(2):
            def _(t, chc=chc, w=trC_done, w2=wsmall_ev):
                wait(t, w, w2)
                for bc in range(2):
                    for half in range(2):
                        ins = t.matmul(ps_c1[:, half * 512:(half + 1) * 512],
                                       cfT[:, 3 * 512 + bc * 256 + chc * 128:
                                           3 * 512 + bc * 256 + (chc + 1) * 128],
                                       basisT[:, bc * 1024 + half * 512:
                                              bc * 1024 + (half + 1) * 512],
                                       start=(bc == 0), stop=(bc == 1))
                ct.inc(ins)
            kb.t(_)
            mt = ct.tok()

            def _(v, chc=chc, mt=mt):
                wait(v, mt)
                cv.inc(v.tensor_copy(owT_b[:, chc * 1024:(chc + 1) * 1024], ps_c1[:]))
            kb.v(_)
            wsmall_ev = cv.tok()

        G_sb = kb.at("G_sb", [128, 4096], BF16, REG1)
        m01 = kb.at("m01", [128, 4096], BF16, REG1 + 8192)
        dio = kb.at("dio", [128, 4096], F32, WS)

        def _(g, w=wsmall_ev):
            wait(g, w, trC_done)
            cg.inc(g.iota(dio[:], [[1, 4096]], base=-1920, channel_multiplier=-1,
                          allow_small_or_imprecise_dtypes=True))
        kb.g(_)
        gio = cg.tok()

        def _(v, w=gio):
            wait(v, w)
            cv.inc(v.tensor_single_scalar(m01[:], dio[:], 0.0, ALU.is_gt))
        kb.v(_)
        vm01 = cv.tok()

        def _(v, w=vm01):
            wait(v, w)
            cv.inc(v.tensor_scalar(dio[:], dio[:], 1.0, 0.0, ALU.subtract, ALU.max))
        kb.v(_)
        vexp = cv.tok()

        def _(s, w=vexp, w2=bc_tok):
            wait(s, w, w2)
            cs.inc(s.activation(G_sb[:], dio[:], AF.Exp, scale=lnd_pp[:, 0:1]))
        kb.s(_)
        sG = cs.tok()

        def _(v, w=sG):
            wait(v, w)
            cv.inc(v.tensor_mul(G_sb[:], G_sb[:], m01[:]))
        kb.v(_)
        G_done = cv.tok()
        kb.barrier()
        scope_c0.close()

        scope_c1 = ExitStack()
        tmc = kb.at("tmc", [128, D], BF16, WS + 8192)
        ps_trD = kb.psum("ps_trD", [128, 1024], BF16, scope_c1)
        stok = None
        for i in range(NT):
            stok = lnb.stats(X2[:, i * D:(i + 1) * D], i, [], prev=stok)
        rtok_b = lnb.rstd_batch(0, 16, [stok])
        aptok = None
        evtok = None
        trtok = None
        for i in range(NT):
            aptok = lnb.apply(X2[:, i * D:(i + 1) * D], tmc[:], i, [rtok_b, trtok])
            evtok, trtok = transpose_tile(
                kb, lambda c: tmc[:, c * 128:(c + 1) * 128], ND, ps_trD[:],
                lambda c, i=i: act_fm[:, c * T + i * 128: c * T + (i + 1) * 128],
                idbf[:], [aptok], ps_free=evtok)
        c1_done = evtok
        kb.dbg("n3_fm", act_fm[:], c1_done)
        kb.flush_dbg()
        kb.barrier()
        scope_c1.close()

        scope_c2 = ExitStack()
        queries_b = kb.at("queries_b", [128, 2 * T], BF16, REG2)
        keys_b = kb.at("keys_b", [128, 2 * T], BF16, REG2 + 8192)
        values_b = kb.at("values_b", [128, 2 * T], BF16, REG2 + 16384)
        values_tm = kb.at("values_tm", [128, NT * 256], BF16, REG2 + 24576)
        ps_p = [kb.psum("ps_p0", [128, 512], F32, scope_c2),
                kb.psum("ps_p1", [128, 512], F32, scope_c2)]
        pbank = {0: None, 1: None}
        for x, dstb in enumerate((qw_b, kw_b, vw_b)):
            outb = (queries_b, keys_b, values_b)[x]
            for chc in range(2):
                for tg in range(4):
                    bk = tg % 2

                    def _(t, x=x, chc=chc, tg=tg, bk=bk, dstb=dstb, w=pbank[bk]):
                        wait(t, w)
                        for vc in range(8):
                            ins = t.matmul(
                                ps_p[bk][:],
                                dstb[:, vc * 256 + chc * 128:vc * 256 + (chc + 1) * 128],
                                act_fm[:, vc * T + tg * 512: vc * T + (tg + 1) * 512],
                                start=(vc == 0), stop=(vc == 7))
                        ct.inc(ins)
                    kb.t(_)
                    mt = ct.tok()

                    def _(v, outb=outb, chc=chc, tg=tg, bk=bk, mt=mt):
                        wait(v, mt)
                        cv.inc(v.tensor_copy(
                            outb[:, chc * T + tg * 512: chc * T + (tg + 1) * 512],
                            ps_p[bk][:]))
                    kb.v(_)
                    pbank[bk] = cv.tok()
        qkv1_done = cv.tok()
        kb.dbg("queries_b", queries_b[:], qkv1_done)

        ps_trE = kb.psum("ps_trE", [128, 1024], BF16, scope_c2)
        vt_ev = None
        for j in range(NT):
            for chc in range(2):
                vt_ev, _tt = transpose_tile(
                    kb, lambda _c, j=j, chc=chc: values_b[:, chc * T + j * 128:
                                                          chc * T + (j + 1) * 128],
                    1, ps_trE[:],
                    lambda _c, j=j, chc=chc: values_tm[:, j * 256 + chc * 128:
                                                       j * 256 + (chc + 1) * 128],
                    idbf[:], [qkv1_done, vt_ev], ps_free=vt_ev)
        vtm_done = vt_ev
        kb.barrier()

        Sp_sb = kb.at("Sp_sb", [128, T], BF16, WS)
        SpT = kb.at("SpT", [128, 16 * 512], BF16, WS + 4096)
        retr_b = kb.at("retr_b", [128, 2 * 512], BF16, WS + 20480)
        a1raw = kb.at("a1raw", [128, 8 * D], BF16, REG1 + 8192)
        a1ln = kb.sbuf("a1ln", [128, D], BF16)
        ps_at2 = kb.psum("ps_at2", [128, 1024], BF16, scope_c2)
        ps_o1 = kb.psum("ps_o1", [128, 1024], F32, scope_c2)
        dx4 = kb.dsem("x4")
        sbk = {0: None, 1: None}
        at_tok = None
        spt_free = None
        sp_free = None
        retr_free = None
        o1_free = None
        a1_free = None
        stok = None
        addtok = None
        for bg in range(4):
            ts0 = 512 * bg
            retr_toks = []
            sp_done = []
            for u in range(4):
                i = 4 * bg + u
                Wd = T - ts0
                nbk = Wd // 512
                ev_toks = []
                for b in range(nbk):
                    w0c = ts0 + b * 512
                    bk = b % 2

                    def _(t, i=i, w0c=w0c, bk=bk, w=sbk[bk]):
                        wait(t, w, qkv1_done)
                        for chc in range(2):
                            ins = t.matmul(
                                ps_p[bk][:],
                                queries_b[:, chc * T + i * 128: chc * T + (i + 1) * 128],
                                keys_b[:, chc * T + w0c: chc * T + w0c + 512],
                                start=(chc == 0), stop=(chc == 1))
                        ct.inc(ins)
                    kb.t(_)
                    mt = ct.tok()

                    def _(v, i=i, w0c=w0c, bk=bk, mt=mt, G_done=G_done,
                          spf=(sp_free if b == 0 else None)):
                        wait(v, mt, G_done, spf)
                        cv.inc(v.scalar_tensor_tensor(
                            Sp_sb[:, w0c - ts0: w0c - ts0 + 512], ps_p[bk][:], 1.0,
                            G_sb[:, 1920 - 128 * i + w0c: 1920 - 128 * i + w0c + 512],
                            ALU.mult, ALU.mult))
                    kb.v(_)
                    sbk[bk] = cv.tok()
                    ev_toks.append(cv.tok())

                for grp8 in range((16 - 4 * bg + 7) // 8):
                    j0 = 4 * bg + grp8 * 8
                    j1 = min(16, j0 + 8)

                    def _(t, i=i, j0=j0, j1=j1, ts0=ts0, w=list(ev_toks),
                          pf=at_tok, sf=(spt_free if (u == 0 and grp8 == 0) else None)):
                        wait(t, pf, sf, *w)
                        for j in range(j0, j1):
                            ins = t.transpose(ps_at2[:, (j - j0) * 128:(j - j0 + 1) * 128],
                                              Sp_sb[:, j * 128 - ts0: (j + 1) * 128 - ts0],
                                              idbf[:])
                        ct.inc(ins)
                    kb.t(_)
                    tt = ct.tok()
                    sp_free = tt

                    def _(v, i=i, u=u, j0=j0, j1=j1, bg=bg, tt=tt):
                        wait(v, tt)
                        for j in range(j0, j1):
                            ins = v.tensor_copy(
                                SpT[:, (j - 4 * bg) * 512 + u * 128:
                                    (j - 4 * bg) * 512 + (u + 1) * 128],
                                ps_at2[:, (j - j0) * 128:(j - j0 + 1) * 128])
                        cv.inc(ins)
                    kb.v(_)
                    at_tok = cv.tok()
                sp_done.append(at_tok)

            for chc in range(2):
                def _(t, chc=chc, bg=bg, w=list(sp_done), rf=retr_free):
                    wait(t, vtm_done, rf, *w)
                    njc = 16 - 4 * bg
                    for jj in range(njc):
                        j = 4 * bg + jj
                        ins = t.matmul(ps_p[chc][:],
                                       values_tm[:, j * 256 + chc * 128:
                                                 j * 256 + (chc + 1) * 128],
                                       SpT[:, jj * 512:(jj + 1) * 512],
                                       start=(jj == 0), stop=(jj == njc - 1))
                    ct.inc(ins)
                kb.t(_)
                mt = ct.tok()

                def _(v, chc=chc, mt=mt):
                    wait(v, mt)
                    cv.inc(v.tensor_copy(retr_b[:, chc * 512:(chc + 1) * 512],
                                         ps_p[chc][:]))
                kb.v(_)
                retr_toks.append(cv.tok())
            spt_free = ct.tok()
            retr_done = cv.tok()
            sbk[0] = retr_done
            sbk[1] = retr_done

            for u in range(4):
                i = 4 * bg + u

                def _(t, u=u, w=retr_done, of=o1_free):
                    wait(t, w, of)
                    for chc in range(2):
                        for half in range(2):
                            ins = t.matmul(
                                ps_o1[:, half * 512:(half + 1) * 512],
                                retr_b[:, chc * 512 + u * 128: chc * 512 + (u + 1) * 128],
                                owT_b[:, chc * 1024 + half * 512:
                                      chc * 1024 + (half + 1) * 512],
                                start=(chc == 0), stop=(chc == 1))
                    ct.inc(ins)
                kb.t(_)
                otok = ct.tok()
                retr_free = otok

                def _(v, i=i, w=otok, af=a1_free):
                    wait(v, w, af)
                    cv.inc(v.tensor_copy(a1raw[:, (i % 8) * D:(i % 8 + 1) * D],
                                         ps_o1[:]))
                kb.v(_)
                o1_free = cv.tok()

                stok = lnb.stats(a1raw[:, (i % 8) * D:(i % 8 + 1) * D], i % 8,
                                 [o1_free, bc_tok], f_ap=osc_pp[:, 0:1],
                                 prev=stok)

            if bg % 2 == 1:
                rtok_b = lnb.rstd_batch(0, 8, [stok, o1_free])
                aptok = None
                for u8 in range(8):
                    i = (bg - 1) * 4 + u8
                    aptok = lnb.apply(a1raw[:, u8 * D:(u8 + 1) * D],
                                      a1ln[:], u8, [rtok_b, addtok], scaled=True)

                    def _(v, i=i, w=aptok):
                        wait(v, w)
                        cv.inc(v.tensor_add(X2[:, i * D:(i + 1) * D],
                                            X2[:, i * D:(i + 1) * D], a1ln[:]))
                    kb.v(_)
                    addtok = cv.tok()

                    def _(sync, i=i, w=addtok):
                        wait(sync, w)
                        dx4.inc(sync.dma_start(out=x4_sc[i * 128:(i + 1) * 128, :],
                                               in_=X2[:, i * D:(i + 1) * D]))
                    kb.q(_)
                a1_free = cv.tok()
                stok = None
        c4_done = cv.tok()
        kb.dbg("x4", X2[:], c4_done)
        kb.flush_dbg()
        kb.barrier()
        scope_c2.close()


    if "D" in phases:
        H128 = 1.0 / float(np.sqrt(128.0))
        sim1_b = kb.sbuf("sim1_b", [128, 64], BF16)
        sim2_b = kb.sbuf("sim2_b", [128, 64], BF16)
        simsq = kb.sbuf("simsq", [128, 64], F32)
        rn_pp = kb.sbuf("rn_pp", [128, 16], F32)
        sg_pp = kb.sbuf("sg_pp", [128, 16], F32)
        lnw_pp = kb.sbuf("lnw_pp", [128, 32], F32)
        rs_all = kb.sbuf("rs_all", [128, 16], F32)
        lgb = kb.sbuf("lgb", [128, 8], F32)
        t8b = kb.sbuf("t8b", [128, 8], F32)
        mkb_ = kb.sbuf("mkb_", [128, 24], F32)
        rw_t = kb.sbuf("rw_t", [128, 8], BF16)
        tmd = kb.sbuf("tmd", [128, D], BF16)
        idf32 = kb.sbuf("idf32", [128, 128], F32)
        sel8 = kb.sbuf("sel8", [8, 8 * 128], BF16)
        nrm8 = kb.sbuf("nrm8", [1, 16], F32)
        dl2 = kb.dsem("l1f")

        def _(g):
            cg.inc(g.iota(idf32[:], [[1, 128]], base=0, channel_multiplier=-1,
                          allow_small_or_imprecise_dtypes=True))
        kb.g(_)
        gi1 = cg.tok()

        def _(g, w=gi1):
            wait(g, w)
            cg.inc(g.tensor_single_scalar(idf32[:], idf32[:], 0.0, ALU.is_equal))
        kb.g(_)
        gid32 = cg.tok()

        def _(g, w=gid32):
            wait(g, w)
            cg.inc(g.iota(sel8[:], [[1, 8], [0, 128]], base=0, channel_multiplier=-1,
                          allow_small_or_imprecise_dtypes=True))
        kb.g(_)
        gs1 = cg.tok()

        def _(g, w=gs1):
            wait(g, w)
            cg.inc(g.tensor_single_scalar(sel8[:], sel8[:], 0.0, ALU.is_equal))
        kb.g(_)
        sel_tok = cg.tok()

        def _(g):
            dl2.inc(g.dma_start(out=sim1_b[:].rearrange("p (c e) -> p c e", c=8),
                                in_=inp["L1_ffn_smha_sim"].rearrange("(c p) e -> p c e", p=128)))
            dl2.inc(g.dma_start(out=sim2_b[:].rearrange("p (c e) -> p c e", c=8),
                                in_=inp["L1_ffn_moe_sim"].rearrange("(c p) e -> p c e", p=128)))
            for k, name in enumerate(("ln1_w", "ln1_b", "ln2_w", "ln2_b")):
                dl2.inc(g.dma_start(
                    out=lnw_pp[:, k * 8:(k + 1) * 8],
                    in_=inp["L1_ffn_" + name].rearrange("one (c p) -> p (one c)", p=128),
                    allow_slow_non_contiguous=True))
        kb.g(_)
        d0tok = dl2.tok()

        scope_d0 = ExitStack()
        ps_d0 = kb.psum("ps_d0", [128, 16], F32, scope_d0)
        nrm_toks = []
        for mi, simb in enumerate((sim1_b, sim2_b)):
            def _(v, simb=simb, w=d0tok, w2=(nrm_toks[-1] if nrm_toks else None)):
                wait(v, w, w2)
                cv.inc(v.tensor_mul(simsq[:], simb[:], simb[:]))
            kb.v(_)
            sqt = cv.tok()

            def _(v, mi=mi, w=sqt):
                wait(v, w)
                cv.inc(v.tensor_reduce(
                    rn_pp[:, 8 * mi:8 * mi + 8],
                    bass.AP(simsq, 0, [[64, 128], [1, 8], [8, 8]]),
                    AX.X, ALU.add))
            kb.v(_)
            nrm_toks.append(cv.tok())

        def _(t, w=nrm_toks[-1], w2=vo2):
            wait(t, w, w2)
            t.matmul(ps_d0[0:1, 0:8], ones_c[:], rn_pp[:, 0:8], start=True, stop=True)
            ct.inc(t.matmul(ps_d0[0:1, 8:16], ones_c[:], rn_pp[:, 8:16],
                            start=True, stop=True))
        kb.t(_)
        tnorm = ct.tok()

        def _(v, w=tnorm):
            wait(v, w)
            cv.inc(v.tensor_scalar_max(nrm8[0:1, :], ps_d0[0:1, 0:16], 1e-24))
        kb.v(_)
        vnm = cv.tok()

        def _(s, w=vnm):
            wait(s, w)
            cs.inc(s.activation(nrm8[0:1, :], nrm8[0:1, :], AF.Ln))
        kb.s(_)
        snm = cs.tok()

        def _(s, w=snm):
            wait(s, w)
            cs.inc(s.activation(nrm8[0:1, :], nrm8[0:1, :], AF.Exp, scale=-0.5))
        kb.s(_)
        snm2 = cs.tok()

        def _(t, w=snm2, w2=sig_tok):
            wait(t, w, w2)
            t.matmul(ps_d0[:, 0:16], ones_f2[:], nrm8[0:1, 0:16], start=True, stop=True)
            ct.inc(t.matmul(ps_d0[:, 0:16], ones_f2[:], nrm8[0:1, 0:16],
                            start=True, stop=True))
        kb.t(_)
        tb1 = ct.tok()

        def _(v, w=tb1):
            wait(v, w)
            cv.inc(v.tensor_copy(rn_pp[:], ps_d0[:, 0:16]))
        kb.v(_)
        vb1 = cv.tok()

        def _(t, w=vb1):
            wait(t, w)
            ct.inc(t.matmul(ps_d0[:, 0:16], ones_f2[:], g8s[0:1, 0:16],
                            start=True, stop=True))
        kb.t(_)
        tb2 = ct.tok()

        def _(v, w=tb2):
            wait(v, w)
            cv.inc(v.tensor_copy(sg_pp[:], ps_d0[:, 0:16]))
        kb.v(_)
        d0_done = cv.tok()
        kb.barrier()
        scope_d0.close()

        def d_lnpass(wcol, bcol, dbgname=None):
            scope = ExitStack()
            ps_trF = kb.psum("ps_trF_%s" % wcol, [128, 1024], BF16, scope)
            stok = None
            for i in range(NT):
                stok = lnb.stats(X2[:, i * D:(i + 1) * D], i, [], prev=stok)
            rtok_b = lnb.rstd_batch(0, 16, [stok])
            aptok = None
            evtok = None
            trtok = None
            for i in range(NT):
                aptok = lnb.apply(X2[:, i * D:(i + 1) * D], tmd[:], i,
                                  [rtok_b, trtok])
                evtok, trtok = transpose_tile(
                    kb, lambda c: tmd[:, c * 128:(c + 1) * 128], ND, ps_trF[:],
                    lambda c, i=i: act_fm[:, c * T + i * 128: c * T + (i + 1) * 128],
                    idbf[:], [aptok], ps_free=evtok)
            aff_toks = []
            for c in range(ND):
                def _(v, c=c, w=evtok, w2=d0tok, w3=(aff_toks[-1] if aff_toks else None)):
                    wait(v, w, w2, w3)
                    cv.inc(v.tensor_scalar(act_fm[:, c * T:(c + 1) * T],
                                           act_fm[:, c * T:(c + 1) * T],
                                           lnw_pp[:, wcol * 8 + c:wcol * 8 + c + 1],
                                           lnw_pp[:, bcol * 8 + c:bcol * 8 + c + 1],
                                           ALU.mult, ALU.add))
                kb.v(_)
                aff_toks.append(cv.tok())
            scope.close()
            return cv.tok()

        stok = None
        for i in range(NT):
            stok = lnb.stats(X2[:, i * D:(i + 1) * D], i, [], prev=stok)
        rtok_b = lnb.rstd_batch(0, 16, [stok])
        aptok = None
        for i in range(NT):
            aptok = lnb.apply(X2[:, i * D:(i + 1) * D], X2[:, i * D:(i + 1) * D],
                              i, [rtok_b, aptok])
        kb.barrier()
        n1f_done = d_lnpass(0, 1)
        kb.dbg("n1f_fm", act_fm[:], n1f_done)
        kb.flush_dbg()
        kb.barrier()

        rw_emaj = kb.at("rw_emaj", [8, NT * 128], BF16, WS)
        rw2_emaj = kb.at("rw2_emaj", [8, NT * 128], BF16, WS + 4096)

        lgall = kb.sbuf("lgall", [128, 128], F32)
        glst = kb.sbuf("glst", [128, 16], F32)

        def d_gating(mi, dst_emaj):
            scope = ExitStack()
            ps_nn = kb.psum("ps_nn_%d" % mi, [128, 128], F32, scope)
            ps_lg = kb.psum("ps_lg_%d" % mi, [128, 8], F32, scope)
            ps_rt = kb.psum("ps_rt_%d" % mi, [128, 128], BF16, scope)
            simb = (sim1_b, sim2_b)[mi]
            prev_ap = None
            for i in range(NT):
                def _(t, i=i, w=prev_ap):
                    wait(t, w)
                    for cc in range(ND):
                        ins = t.matmul(ps_nn[:],
                                       act_fm[:, cc * T + i * 128: cc * T + (i + 1) * 128],
                                       act_fm[:, cc * T + i * 128: cc * T + (i + 1) * 128],
                                       start=(cc == 0), stop=(cc == ND - 1))
                    ct.inc(ins)
                    for cc in range(ND):
                        ins = t.matmul(ps_lg[:],
                                       act_fm[:, cc * T + i * 128: cc * T + (i + 1) * 128],
                                       simb[:, cc * 8:(cc + 1) * 8],
                                       start=(cc == 0), stop=(cc == ND - 1))
                    ct.inc(ins)
                kb.t(_)
                mt = ct.tok()

                def _(v, i=i, w=mt, w2=gid32):
                    wait(v, w, w2)
                    v.tensor_mul(sq_scr[:, 0:128], ps_nn[:], idf32[:])
                    cv.inc(v.tensor_copy(lgall[:, i * 8:(i + 1) * 8], ps_lg[:]))
                kb.v(_)
                v1 = cv.tok()

                def _(v, i=i, w=v1):
                    wait(v, w)
                    cv.inc(v.tensor_reduce(glst[:, i:i + 1], sq_scr[:, 0:128],
                                           AX.X, ALU.add))
                kb.v(_)
                prev_ap = cv.tok()

            def _(v, w=prev_ap):
                wait(v, w)
                cv.inc(v.tensor_scalar_max(glst[:], glst[:], 1e-24))
            kb.v(_)
            g1_ = cv.tok()

            def _(s, w=g1_):
                wait(s, w)
                cs.inc(s.activation(glst[:], glst[:], AF.Ln))
            kb.s(_)
            g2_ = cs.tok()

            def _(s, w=g2_):
                wait(s, w)
                cs.inc(s.activation(glst[:], glst[:], AF.Exp, scale=-0.5))
            kb.s(_)
            rln_tok = cs.tok()

            prev_tr = None
            prev_chain = None
            for i in range(NT):
                def _(v, i=i, w=rln_tok, pc=prev_chain, mi=mi):
                    wait(v, w, pc)
                    cv.inc(v.tensor_scalar_mul(lgb[:], lgall[:, i * 8:(i + 1) * 8],
                                               glst[:, i:i + 1]))
                kb.v(_)
                v4 = cv.tok()

                def _(v, w=v4, mi=mi):
                    wait(v, w)
                    cv.inc(v.tensor_mul(lgb[:], lgb[:], rn_pp[:, 8 * mi:8 * mi + 8]))
                kb.v(_)
                v5 = cv.tok()

                def _(v, w=v5, mi=mi):
                    wait(v, w)
                    cv.inc(v.tensor_sub(lgb[:], lgb[:], sg_pp[:, 8 * mi:8 * mi + 8]))
                kb.v(_)
                v6 = cv.tok()

                def _(v, w=v6):
                    wait(v, w)
                    cv.inc(v.max(t8b[:], lgb[:]))
                kb.v(_)
                v7 = cv.tok()

                def _(v, w=v7):
                    wait(v, w)
                    v.tensor_single_scalar(mkb_[:, 16:17], t8b[:, 0:1], 0.0, ALU.is_gt)
                    cv.inc(v.tensor_single_scalar(mkb_[:, 0:8], lgb[:], 0.0, ALU.is_gt))
                kb.v(_)
                v8 = cv.tok()

                def _(v, w=v8):
                    wait(v, w)
                    cv.inc(v.tensor_scalar(mkb_[:, 8:16], lgb[:], t8b[:, 1:2], None,
                                           ALU.is_ge))
                kb.v(_)
                v9 = cv.tok()

                def _(v, w=v9):
                    wait(v, w)
                    cv.inc(v.tensor_sub(mkb_[:, 0:8], mkb_[:, 0:8], mkb_[:, 8:16]))
                kb.v(_)
                v10 = cv.tok()

                def _(v, w=v10):
                    wait(v, w)
                    cv.inc(v.scalar_tensor_tensor(mkb_[:, 0:8], mkb_[:, 0:8],
                                                  mkb_[:, 16:17], mkb_[:, 8:16],
                                                  ALU.mult, ALU.add))
                kb.v(_)
                v11 = cv.tok()

                def _(v, w=v11):
                    wait(v, w)
                    cv.inc(v.tensor_relu(lgb[:], lgb[:]))
                kb.v(_)
                v12 = cv.tok()

                def _(s, w=v12):
                    wait(s, w)
                    cs.inc(s.activation(t8b[:], lgb[:], AF.Exp))
                kb.s(_)
                s3 = cs.tok()

                def _(v, w=s3):
                    wait(v, w)
                    cv.inc(v.tensor_mul(t8b[:], t8b[:], mkb_[:, 0:8]))
                kb.v(_)
                v13 = cv.tok()

                def _(v, w=v13):
                    wait(v, w)
                    cv.inc(v.tensor_reduce(sm[:, 12:13], t8b[:], AX.X, ALU.add))
                kb.v(_)
                v14 = cv.tok()

                def _(v, w=v14):
                    wait(v, w)
                    cv.inc(v.reciprocal(sm[:, 12:13], sm[:, 12:13]))
                kb.v(_)
                v15 = cv.tok()

                def _(v, w=v15, w2=prev_tr):
                    wait(v, w, w2)
                    cv.inc(v.tensor_scalar_mul(rw_t[:], t8b[:], sm[:, 12:13]))
                kb.v(_)
                v16 = cv.tok()
                prev_chain = v16

                def _(t, i=i, w=v16, w2=prev_tr):
                    wait(t, w, w2)
                    ct.inc(t.transpose(ps_rt[0:8, 0:128], rw_t[:, 0:8], idbf[:]))
                kb.t(_)
                tt = ct.tok()

                def _(v, i=i, w=tt, dst_emaj=dst_emaj):
                    wait(v, w)
                    cv.inc(v.tensor_copy(dst_emaj[0:8, i * 128:(i + 1) * 128],
                                         ps_rt[0:8, 0:128]))
                kb.v(_)
                prev_tr = cv.tok()
            scope.close()
            return prev_tr

        rw_done = d_gating(0, rw_emaj)
        kb.dbg("rw_emaj", rw_emaj[0:8, :], rw_done)
        kb.flush_dbg()
        kb.barrier()

        scope_d3a = ExitStack()
        qp_b = kb.at("qp_b", [128, 64 * 128], BF16, REG1)
        kp_b = kb.at("kp_b", [128, 64 * 128], BF16, REG1 + 16384)
        vp_b = kb.at("vp_b", [128, 64 * 128], BF16, REG2)
        n1e = kb.at("n1e", [128, 8 * 512], BF16, REG2 + 16384)
        rwb_eg = kb.at("rwb_eg", [128, 512], BF16, REG2 + 24576)
        q_fm_d = kb.at("q_fm_d", [128, T], BF16, WS + 8192)
        k_fm_d = kb.at("k_fm_d", [128, T], BF16, WS + 12288)
        v_fm_d = kb.at("v_fm_d", [128, T], BF16, WS + 16384)
        dpq = kb.dsem("pq")

        def _(g):
            for nm, dst in (("q_proj", qp_b), ("k_proj", kp_b), ("v_proj", vp_b)):
                dpq.inc(g.dma_start(
                    out=dst[:].rearrange("p (ec h) -> p ec h", h=128),
                    in_=inp["L1_ffn_" + nm].rearrange("(ec p) h -> p ec h", p=128)))
        kb.g(_)
        pq_tok = dpq.tok()

        ps_q3 = kb.psum("ps_q3", [128, 512], F32, scope_d3a)
        ps_k3 = kb.psum("ps_k3", [128, 512], F32, scope_d3a)
        ps_v3 = kb.psum("ps_v3", [128, 512], F32, scope_d3a)
        ps_bc3 = kb.psum("ps_bc3", [128, 512], F32, scope_d3a)
        n1e_free = None
        qkv_ev = {0: None, 1: None, 2: None}
        for g_ in range(4):
            for e in range(8):
                def _(t, g_=g_, e=e, w=rw_done, w2=sel_tok):
                    wait(t, w, w2)
                    ct.inc(t.matmul(ps_bc3[:], sel8[0:8, e * 128:(e + 1) * 128],
                                    rw_emaj[0:8, g_ * 512:(g_ + 1) * 512],
                                    start=True, stop=True))
                kb.t(_)
                bct = ct.tok()

                def _(v, w=bct, nf=n1e_free):
                    wait(v, w, nf)
                    cv.inc(v.tensor_copy(rwb_eg[:], ps_bc3[:]))
                kb.v(_)
                bce = cv.tok()

                def _(v, g_=g_, w=bce):
                    wait(v, w)
                    for cc in range(ND):
                        ins = v.tensor_mul(n1e[:, cc * 512:(cc + 1) * 512],
                                           act_fm[:, cc * T + g_ * 512:
                                                  cc * T + (g_ + 1) * 512],
                                           rwb_eg[:])
                    cv.inc(ins)
                kb.v(_)
                n1etok = cv.tok()

                def _(t, g_=g_, e=e, w=n1etok, w2=pq_tok,
                      ev=[qkv_ev[x] for x in range(3)] if (g_ > 0 and e == 0) else []):
                    wait(t, w, w2, *ev)
                    for cc in range(ND):
                        fl = (e * 8 + cc) * 128
                        st_ = (e == 0 and cc == 0)
                        sp = (e == 7 and cc == ND - 1)
                        t.matmul(ps_q3[:], qp_b[:, fl:fl + 128],
                                 n1e[:, cc * 512:(cc + 1) * 512], start=st_, stop=sp)
                        t.matmul(ps_k3[:], kp_b[:, fl:fl + 128],
                                 n1e[:, cc * 512:(cc + 1) * 512], start=st_, stop=sp)
                        ins = t.matmul(ps_v3[:], vp_b[:, fl:fl + 128],
                                       n1e[:, cc * 512:(cc + 1) * 512],
                                       start=st_, stop=sp)
                    ct.inc(ins)
                kb.t(_)
                n1e_free = ct.tok()
            mtok = ct.tok()

            def _(v, g_=g_, w=mtok):
                wait(v, w)
                v.tensor_scalar_mul(q_fm_d[:, g_ * 512:(g_ + 1) * 512], ps_q3[:], H128)
                v.tensor_copy(k_fm_d[:, g_ * 512:(g_ + 1) * 512], ps_k3[:])
                cv.inc(v.tensor_copy(v_fm_d[:, g_ * 512:(g_ + 1) * 512], ps_v3[:]))
            kb.v(_)
            for x in range(3):
                qkv_ev[x] = cv.tok()
        d3a_done = cv.tok()
        kb.dbg("qf_d", q_fm_d[:], d3a_done)
        kb.flush_dbg()
        kb.barrier()
        scope_d3a.close()

        scope_d3b = ExitStack()
        v_tm_d = kb.at("v_tm_d", [128, NT * 128], BF16, REG1)
        attn_fm_d = kb.at("attn_fm_d", [128, T], BF16, REG1 + 4096)
        S2_sb = kb.at("S2_sb", [128, T], F32, REG2)
        P2_sb = kb.at("P2_sb", [128, T], BF16, REG2 + 8192)
        PT_sb = kb.at("PT_sb", [128, T], BF16, REG1 + 8192)
        ps_s3 = [kb.psum("ps_s3a", [128, 512], F32, scope_d3b),
                 kb.psum("ps_s3b", [128, 512], F32, scope_d3b)]
        ps_at3 = kb.psum("ps_at3", [128, 1024], BF16, scope_d3b)
        ps_pv = kb.psum("ps_pv", [128, 128], F32, scope_d3b)

        vt_ev = None
        for j in range(NT):
            vt_ev, _tt = transpose_tile(
                kb, lambda _c, j=j: v_fm_d[:, j * 128:(j + 1) * 128],
                1, ps_at3[:],
                lambda _c, j=j: v_tm_d[:, j * 128:(j + 1) * 128],
                idbf[:], [d3a_done, vt_ev], ps_free=vt_ev)
        vtm2_done = vt_ev

        sbk3 = {0: None, 1: None}
        at3_tok = vtm2_done
        p2_free = None
        s2_free = None
        pv_free = None
        attn_ev = None
        sexp_prev = None
        for i in range(NT):
            W = 128 * (i + 1)
            nb = (W + 511) // 512
            ev_toks = []
            for b in range(nb):
                w0c = b * 512
                w1c = min(W, w0c + 512)
                bk = b % 2

                def _(t, i=i, w0c=w0c, w1c=w1c, bk=bk, w=sbk3[bk]):
                    wait(t, w)
                    ct.inc(t.matmul(ps_s3[bk][:, 0:w1c - w0c],
                                    q_fm_d[:, i * 128:(i + 1) * 128],
                                    k_fm_d[:, w0c:w1c], start=True, stop=True))
                kb.t(_)
                mt = ct.tok()

                def _(v, w0c=w0c, w1c=w1c, bk=bk, mt=mt,
                      sf=(s2_free if b == 0 else None)):
                    wait(v, mt, sf)
                    cv.inc(v.tensor_copy(S2_sb[:, w0c:w1c], ps_s3[bk][:, 0:w1c - w0c]))
                kb.v(_)
                sbk3[bk] = cv.tok()
                ev_toks.append(cv.tok())

            def _(g, i=i, ev=ev_toks[-1]):
                wait(g, ev)
                cg.inc(g.affine_select(S2_sb[:, i * 128:(i + 1) * 128],
                                       S2_sb[:, i * 128:(i + 1) * 128],
                                       [[-1, 128]], ALU.is_ge, -1e30,
                                       base=0, channel_multiplier=1))
            kb.g(_)
            gmask = cg.tok()

            def _(v, W=W, gmask=gmask, ev=list(ev_toks), sp=sexp_prev):
                wait(v, gmask, sp, *ev)
                cv.inc(v.reduce_max(sm[:, 8:9], S2_sb[:, 0:W], AX.X, negate=True))
            kb.v(_)
            vmax = cv.tok()

            def _(s, W=W, vmax=vmax, pf=p2_free):
                wait(s, vmax, pf)
                cs.inc(s.activation(P2_sb[:, 0:W], S2_sb[:, 0:W], AF.Exp,
                                    bias=sm[:, 8:9], accum_out=ssum[:, 0:1]))
            kb.s(_)
            sexp = cs.tok()
            sexp_prev = sexp
            s2_free = sexp

            def _(v, i=i, w=sexp):
                wait(v, w)
                cv.inc(v.reciprocal(rs_all[:, i:i + 1], ssum[:, 0:1]))
            kb.v(_)

            at_toks = []
            for bg2 in range((i + 8) // 8):
                j0, j1 = bg2 * 8, min(i + 1, bg2 * 8 + 8)

                def _(t, j0=j0, j1=j1, sexp=sexp, w=at3_tok):
                    wait(t, sexp, w)
                    for j in range(j0, j1):
                        ins = t.transpose(ps_at3[:, (j - j0) * 128:(j - j0 + 1) * 128],
                                          P2_sb[:, j * 128:(j + 1) * 128], idbf[:])
                    ct.inc(ins)
                kb.t(_)
                ttok = ct.tok()
                p2_free = ttok

                def _(v, j0=j0, j1=j1, ttok=ttok):
                    wait(v, ttok)
                    cv.inc(v.tensor_copy(PT_sb[:, j0 * 128:j1 * 128],
                                         ps_at3[:, 0:(j1 - j0) * 128]))
                kb.v(_)
                at3_tok = cv.tok()
                at_toks.append(cv.tok())

            def _(t, i=i, w=list(at_toks), pvf=pv_free):
                wait(t, vtm2_done, pvf, *w)
                for j in range(i + 1):
                    ins = t.matmul(ps_pv[:], v_tm_d[:, j * 128:(j + 1) * 128],
                                   PT_sb[:, j * 128:(j + 1) * 128],
                                   start=(j == 0), stop=(j == i))
                ct.inc(ins)
            kb.t(_)
            pvt = ct.tok()

            def _(v, i=i, w=pvt, ae=attn_ev):
                wait(v, w)
                cv.inc(v.tensor_copy(attn_fm_d[:, i * 128:(i + 1) * 128], ps_pv[:]))
            kb.v(_)
            attn_ev = cv.tok()
            pv_free = attn_ev
        d3b_done = attn_ev
        kb.barrier()
        scope_d3b.close()

        op_b = kb.at("op_b", [128, 8 * 1024], BF16, REG1 + 16384)
        rwb_all = kb.at("rwb_all", [128, 8 * T], BF16, REG2)
        attn_e = kb.at("attn_e", [128, 512], BF16, WS + 20480)
        dop = kb.dsem("op")

        def _(g):
            dop.inc(g.dma_start(
                out=op_b[:].rearrange("p (e c) -> p e c", e=8),
                in_=inp["L1_ffn_o_proj"].rearrange("(e p) c -> p e c", p=128)))
        kb.g(_)
        op_tok = dop.tok()

        scope_bcc = ExitStack()
        ps_bc3 = kb.psum("ps_bcc", [128, 512], F32, scope_bcc)
        rwb_toks = []
        for e in range(8):
            for g_ in range(4):
                def _(t, e=e, g_=g_, w=rw_done,
                      w2=(rwb_toks[-1] if rwb_toks else None)):
                    wait(t, w, w2, sel_tok)
                    ct.inc(t.matmul(ps_bc3[:], sel8[0:8, e * 128:(e + 1) * 128],
                                    rw_emaj[0:8, g_ * 512:(g_ + 1) * 512],
                                    start=True, stop=True))
                kb.t(_)
                bt_ = ct.tok()

                def _(v, e=e, g_=g_, w=bt_):
                    wait(v, w)
                    cv.inc(v.tensor_copy(rwb_all[:, e * T + g_ * 512:
                                                 e * T + (g_ + 1) * 512], ps_bc3[:]))
                kb.v(_)
                rwb_toks.append(cv.tok())
        rwb_done = cv.tok()
        kb.barrier()
        scope_bcc.close()

        scope_d3c = ExitStack()
        ps_om = [kb.psum("ps_om%d" % u, [128, 1024], F32, scope_d3c) for u in range(4)]
        om_free = {u: None for u in range(4)}
        ae_free = None
        for g_ in range(4):
            add_toks = []
            for e in range(8):
                def _(v, g_=g_, e=e, w=rwb_done, w2=d3b_done, af=ae_free):
                    wait(v, w, w2, af)
                    cv.inc(v.tensor_mul(attn_e[:],
                                        attn_fm_d[:, g_ * 512:(g_ + 1) * 512],
                                        rwb_all[:, e * T + g_ * 512:e * T + (g_ + 1) * 512]))
                kb.v(_)
                aet = cv.tok()

                def _(t, g_=g_, e=e, w=aet, w2=op_tok,
                      omf=[om_free[u] for u in range(4)] if e == 0 else []):
                    wait(t, w, w2, *omf)
                    for u in range(4):
                        for half in range(2):
                            ins = t.matmul(
                                ps_om[u][:, half * 512:(half + 1) * 512],
                                attn_e[:, u * 128:(u + 1) * 128],
                                op_b[:, e * 1024 + half * 512:e * 1024 + (half + 1) * 512],
                                start=(e == 0), stop=(e == 7))
                    ct.inc(ins)
                kb.t(_)
                ae_free = ct.tok()
            omt = ct.tok()

            for u in range(4):
                i = g_ * 4 + u

                def _(v, u=u, i=i, w=omt):
                    wait(v, w)
                    cv.inc(v.scalar_tensor_tensor(X2[:, i * D:(i + 1) * D],
                                                  ps_om[u][:], rs_all[:, i:i + 1],
                                                  X2[:, i * D:(i + 1) * D],
                                                  ALU.mult, ALU.add))
                kb.v(_)
                om_free[u] = cv.tok()
                add_toks.append(cv.tok())
        d3c_done = cv.tok()
        kb.dbg("hs", X2[:], d3c_done)
        kb.flush_dbg()
        kb.barrier()
        scope_d3c.close()

        n2f_done = d_lnpass(2, 3)
        kb.dbg("n2f_fm", act_fm[:], n2f_done)
        kb.barrier()
        rw2_done = d_gating(1, rw2_emaj)
        kb.dbg("rw2_emaj", rw2_emaj[0:8, :], rw2_done)
        kb.flush_dbg()
        kb.barrier()

        scope_d5 = ExitStack()
        w1b = kb.at("w1b", [128, 2 * 8 * 1024], BF16, REG1)
        w2b = kb.at("w2b", [128, 2 * 8 * 1024], BF16, REG2)
        scaled = kb.at("scaled", [128, 8 * 512], BF16, WS + 8192)
        rw2b_eg = kb.at("rw2b_eg", [128, 512], BF16, WS + 16384)
        dmw = [kb.dsem("mw0"), kb.dsem("mw1")]
        ps_i5 = [kb.psum("ps_i5a", [128, 1024], F32, scope_d5),
                 kb.psum("ps_i5b", [128, 1024], F32, scope_d5)]
        ps_m5 = [kb.psum("ps_m5a", [128, 512], F32, scope_d5),
                 kb.psum("ps_m5b", [128, 512], F32, scope_d5)]
        ps_bc5 = kb.psum("ps_bc5", [128, 512], F32, scope_d5)
        W1f = inp["L1_ffn_w1"].rearrange("(ec p) i -> p ec i", p=128)
        W2f = inp["L1_ffn_w2"].rearrange("(ec p) i -> p ec i", p=128)
        wslot_tok = {0: None, 1: None}
        ibank_tok = {0: None, 1: None}
        mbank_tok = {0: None, 1: None}
        sc_free = None
        rwb5_free = None
        for e in range(8):
            slot = e % 2

            def _(g, e=e, slot=slot, w=wslot_tok[slot], ds=dmw[slot]):
                wait(g, w)
                ds.inc(g.dma_start(
                    out=w1b[:, slot * 8192:(slot + 1) * 8192].rearrange(
                        "p (c i) -> p c i", c=8),
                    in_=W1f[:, e * 8:(e + 1) * 8, :]))
                ds.inc(g.dma_start(
                    out=w2b[:, slot * 8192:(slot + 1) * 8192].rearrange(
                        "p (c i) -> p c i", c=8),
                    in_=W2f[:, e * 8:(e + 1) * 8, :]))
            kb.g(_)
            wtok = dmw[slot].tok()

            for g_ in range(4):
                def _(t, e=e, g_=g_, w=rw2_done, rf=rwb5_free):
                    wait(t, w, rf, sel_tok)
                    ct.inc(t.matmul(ps_bc5[:], sel8[0:8, e * 128:(e + 1) * 128],
                                    rw2_emaj[0:8, g_ * 512:(g_ + 1) * 512],
                                    start=True, stop=True))
                kb.t(_)
                bct = ct.tok()

                def _(v, w=bct, sf=sc_free):
                    wait(v, w, sf)
                    cv.inc(v.tensor_copy(rw2b_eg[:], ps_bc5[:]))
                kb.v(_)
                rwb5 = cv.tok()

                sc_toks = []
                for ipr in range(4):
                    bk = ipr % 2

                    def _(t, e=e, g_=g_, ipr=ipr, bk=bk, slot=slot, wtok=wtok,
                          w=ibank_tok[bk], w2=n2f_done):
                        wait(t, wtok, w, w2)
                        for half in range(2):
                            ic = 2 * ipr + half
                            for cc in range(ND):
                                ins = t.matmul(
                                    ps_i5[bk][:, half * 512:(half + 1) * 512],
                                    w1b[:, slot * 8192 + cc * 1024 + ic * 128:
                                        slot * 8192 + cc * 1024 + (ic + 1) * 128],
                                    act_fm[:, cc * T + g_ * 512: cc * T + (g_ + 1) * 512],
                                    start=(cc == 0), stop=(cc == ND - 1))
                        ct.inc(ins)
                    kb.t(_)
                    imt = ct.tok()

                    def _(s, ipr=ipr, bk=bk, w=imt, sf=sc_free if ipr < 2 else None):
                        wait(s, w, sf)
                        cs.inc(s.activation(scaled[:, ipr * 1024:(ipr + 1) * 1024],
                                            ps_i5[bk][:], AF.Gelu))
                    kb.s(_)
                    gt = cs.tok()
                    ibank_tok[bk] = gt

                    def _(v, ipr=ipr, w=gt, w2=rwb5):
                        wait(v, w, w2)
                        for half in range(2):
                            ic = 2 * ipr + half
                            ins = v.tensor_mul(scaled[:, ic * 512:(ic + 1) * 512],
                                               scaled[:, ic * 512:(ic + 1) * 512],
                                               rw2b_eg[:])
                        cv.inc(ins)
                    kb.v(_)
                    sc_toks.append(cv.tok())
                rwb5_free = ct.tok()

                for u in range(4):
                    i = g_ * 4 + u
                    for half in range(2):
                        bk = (2 * u + half) % 2

                        def _(t, u=u, half=half, e=e, slot=slot, bk=bk,
                              w=list(sc_toks), mf=mbank_tok[bk]):
                            wait(t, mf, *w)
                            for ic in range(ND):
                                ins = t.matmul(
                                    ps_m5[bk][:],
                                    scaled[:, ic * 512 + u * 128:ic * 512 + (u + 1) * 128],
                                    w2b[:, slot * 8192 + ic * 1024 + half * 512:
                                        slot * 8192 + ic * 1024 + (half + 1) * 512],
                                    start=(ic == 0), stop=(ic == ND - 1))
                            ct.inc(ins)
                        kb.t(_)
                        omt = ct.tok()
                        sc_free = omt

                        def _(v, i=i, half=half, bk=bk, w=omt):
                            wait(v, w)
                            dd = X2[:, i * D + half * 512: i * D + (half + 1) * 512]
                            cv.inc(v.tensor_add(dd, dd, ps_m5[bk][:]))
                        kb.v(_)
                        mbank_tok[bk] = cv.tok()
            wslot_tok[slot] = ct.tok()
        d5_done = cv.tok()
        kb.dbg("f1", X2[:], d5_done)
        kb.flush_dbg()
        kb.barrier()
        scope_d5.close()

        xt4 = kb.at("xt4", [128, 2 * D], F32, WS)
        outt = kb.at("outt", [128, 2 * D], F32, WS + 8192)
        dx4r = [kb.dsem("x4r0"), kb.dsem("x4r1")]
        douts = [kb.dsem("out0"), kb.dsem("out1")]
        stok = None
        for i in range(NT):
            stok = lnb.stats(X2[:, i * D:(i + 1) * D], i, [], prev=stok)
        rtok_b = lnb.rstd_batch(0, 16, [stok])
        slot_done6 = {0: None, 1: None}
        aptok = None
        addtok = None
        for i in range(NT):
            slot = i % 2
            xa = xt4[:, slot * D:(slot + 1) * D]
            oa = outt[:, slot * D:(slot + 1) * D]

            def _(sync, i=i, xa=xa, w=slot_done6[slot], ds=dx4r[slot]):
                wait(sync, w)
                ds.inc(sync.dma_start(out=xa, in_=x4_sc[i * 128:(i + 1) * 128, :]))
            kb.q(_)
            ldtok = dx4r[slot].tok()

            aptok = lnb.apply(X2[:, i * D:(i + 1) * D], tmd[:], i,
                              [rtok_b, aptok, addtok])

            def _(v, i=i, xa=xa, oa=oa, w=aptok, w2=ldtok, w3=slot_done6[slot]):
                wait(v, w, w2, w3)
                cv.inc(v.tensor_add(oa, xa, tmd[:]))
            kb.v(_)
            addtok = cv.tok()

            def _(sync, i=i, oa=oa, w=addtok, ds=douts[slot]):
                wait(sync, w)
                ds.inc(sync.dma_start(out=out[i * 128:(i + 1) * 128, :], in_=oa))
            kb.q(_)
            slot_done6[slot] = douts[slot].tok()

        def _(sync):
            wait(sync, douts[0].tok(), douts[1].tok())
        kb.q(_)

    if "final_stub" in DEBUG:
        def _(sync):
            for i in range(NT):
                dq.inc(sync.dma_start(out=out[i * 128:(i + 1) * 128, :],
                                      in_=outln_ap(i)))
            wait(sync, dq.tok())
        kb.q(_)

    rem = nc.sbuf_bytes_remaining
    assert rem >= 229376 - REG1, f"stack smalls overflow into manual arena: remaining={rem}"
    ctx.close()
    return nc


def shard_inputs(inputs, core):
    m = {}
    for name, shape in INPUT_SPECS:
        v = np.asarray(inputs[name])
        if name == "L0_attn_x":
            v = np.ascontiguousarray(v[core])
        m[name] = np.ascontiguousarray(
            v.astype(np.float32, copy=False).reshape(shape))
    return m


def kernel(**inputs):
    nc = build()
    in_maps = [shard_inputs(inputs, c) for c in range(8)]
    res = run_bass_kernel_spmd(nc, in_maps, core_ids=list(range(8)))
    return np.stack([res.results[c]["out"] for c in range(8)], axis=0)


# version 17
# speedup vs baseline: 1.0313x; 1.0035x over previous
import numpy as np
from contextlib import ExitStack
import concourse.bass as bass
import concourse.mybir as mybir
from concourse.bass_utils import run_bass_kernel_spmd

F32 = mybir.dt.float32
BF16 = mybir.dt.bfloat16
AF = mybir.ActivationFunctionType
ALU = mybir.AluOpType
AX = mybir.AxisListType

T = 2048
D = 1024
NT = T // 128
ND = D // 128
EPS = 1e-5

DEBUG = ()

REG1 = 40960
REG2 = 73728
REG3 = 106496
REG4 = 139264
REG5 = 172032
WS = 204800

INPUT_SPECS = [
    ("L0_attn_x", [T, D]),
    ("L0_attn_W_qkv", [D, 3 * D]),
    ("L0_attn_b_qkv", [1, 3 * D]),
    ("L0_attn_W_vconv", [D, 3]),
    ("L0_ffn_W1", [4 * D, D]),
    ("L0_ffn_b1", [1, 4 * D]),
    ("L0_ffn_W2", [D, 4 * D]),
    ("L0_ffn_b2", [1, D]),
    ("L1_attn_basis", [D, 256]),
    ("L1_attn_q_coeffs", [256, 256]),
    ("L1_attn_k_coeffs", [256, 256]),
    ("L1_attn_v_coeffs", [256, 256]),
    ("L1_attn_o_coeffs", [256, 256]),
    ("L1_attn_decay_logit", [1, 1]),
    ("L1_attn_out_scale", [1, 1]),
    ("L1_ffn_ln1_w", [1, D]),
    ("L1_ffn_ln1_b", [1, D]),
    ("L1_ffn_ln2_w", [1, D]),
    ("L1_ffn_ln2_b", [1, D]),
    ("L1_ffn_smha_sim", [D, 8]),
    ("L1_ffn_smha_gates", [1, 8]),
    ("L1_ffn_q_proj", [8 * D, 128]),
    ("L1_ffn_k_proj", [8 * D, 128]),
    ("L1_ffn_v_proj", [8 * D, 128]),
    ("L1_ffn_o_proj", [8 * 128, D]),
    ("L1_ffn_moe_sim", [D, 8]),
    ("L1_ffn_moe_gates", [1, 8]),
    ("L1_ffn_w1", [8 * D, D]),
    ("L1_ffn_w2", [8 * D, D]),
]


class Sem:
    def __init__(self, ctx, nc, name, step=1):
        self.h = ctx.enter_context(nc.semaphore(name))
        self.val = 0
        self.step = step

    def inc(self, ins, n=None):
        n = self.step if n is None else n
        ins.then_inc(self.h, n)
        self.val += n
        return (self.h, self.val)

    def tok(self):
        return (self.h, self.val)


def wait(eng, *toks):
    for tk in toks:
        if tk is None:
            continue
        h, v = tk
        if v > 0:
            eng.wait_ge(h, v)


class KB:
    def __init__(self, nc, ctx):
        self.nc = nc
        self.ctx = ctx
        self.blk = ctx.enter_context(nc.Block())
        self.dq = Sem(ctx, nc, "dq", 16)
        self.dg = Sem(ctx, nc, "dg", 16)
        self._dsems = {}
        self.ct = Sem(ctx, nc, "ct")
        self.cv = Sem(ctx, nc, "cv")
        self.cs = Sem(ctx, nc, "cs")
        self.cg = Sem(ctx, nc, "cg")
        self.dbg_outs = {}
        self.dbg_toks = []

    def dsem(self, name, step=16):
        if name not in self._dsems:
            self._dsems[name] = Sem(self.ctx, self.nc, "d_" + name, step)
        return self._dsems[name]

    def q(self, fn):
        self.blk.sync(fn)

    def g(self, fn):
        self.blk.gpsimd(fn)

    def t(self, fn):
        self.blk.tensor(fn)

    def v(self, fn):
        self.blk.vector(fn)

    def s(self, fn):
        self.blk.scalar(fn)

    def barrier(self, dma=True):
        toks = [x.tok() for x in (self.ct, self.cv, self.cs, self.cg)]
        if dma:
            toks += [x.tok() for x in (self.dq, self.dg)]
            toks += [s.tok() for s in self._dsems.values()]

        def _w(eng):
            wait(eng, *toks)
        self.q(_w)
        self.g(_w)
        self.t(_w)
        self.v(_w)
        self.s(_w)

    def sbuf(self, name, shape, dtype, scope=None):
        return (scope or self.ctx).enter_context(self.nc.sbuf_tensor(name, shape, dtype))

    def at(self, name, shape, dtype, offset):
        return self.nc.alloc_sbuf_tensor_at(name, shape, dtype, offset=offset)

    def psum(self, name, shape, dtype, scope=None):
        return (scope or self.ctx).enter_context(self.nc.psum_tensor(name, shape, dtype))

    def dbg(self, name, ap, tok):
        if name in DEBUG:
            o = self.nc.dram_tensor("dbg_" + name, list(ap.shape), ap.dtype,
                                    kind="ExternalOutput").ap()
            self.dbg_outs[name] = (o, ap)
            self.dbg_toks.append(tok)

    def flush_dbg(self):
        if not self.dbg_outs:
            return
        outs, toks, dq = self.dbg_outs, self.dbg_toks, self.dq

        def _(sync):
            wait(sync, *toks)
            for name, (o, ap) in outs.items():
                dq.inc(sync.dma_start(out=o, in_=ap))
            wait(sync, dq.tok())
        self.q(_)
        self.dbg_outs = {}
        self.dbg_toks = []


class LNB:

    def __init__(self, kb, st, sq_scr, eps_ap):
        self.kb = kb
        self.st = st
        self.sq = sq_scr
        self.eps = eps_ap

    def stats(self, src_ap, col, toks, f_ap=None, prev=None):
        kb, st = self.kb, self.st
        cv, cg = kb.cv, kb.cg
        invD = 1.0 / D
        m = st[:, col:col + 1]
        va = st[:, 16 + col:17 + col]
        r = st[:, 32 + col:33 + col]
        fcol = st[:, 48 + col:49 + col]
        nfree = len(src_ap.ap) - 1
        ax = AX.X if nfree == 1 else AX.XY
        if nfree == 1:
            sq = self.sq
        else:
            counts = [p[1] for p in src_ap.ap[1:]]
            dims = [list(self.sq.ap[0])]
            stride = 1
            rev = []
            for cdim in reversed(counts):
                rev.append([stride, cdim])
                stride *= cdim
            dims += rev[::-1]
            sq = bass.AP(self.sq.tensor, 0, dims)

        def _v1(v):
            wait(v, prev, *toks)
            ins = v.tensor_reduce(m, src_ap, ax, ALU.add)
            if f_ap is not None:
                ins = v.tensor_copy(fcol, f_ap)
            cv.inc(ins)
        kb.v(_v1)
        vt1 = cv.tok()

        def _g1(g):
            wait(g, prev, *toks)
            cg.inc(g.tensor_tensor(sq, src_ap, src_ap, ALU.mult))
        kb.g(_g1)
        gt1 = cg.tok()

        def _v1b(v):
            wait(v, gt1, vt1)
            cv.inc(v.tensor_reduce(va, sq, ax, ALU.add))
        kb.v(_v1b)
        gt2 = cv.tok()

        def _v2(v):
            wait(v, vt1, gt2)
            if f_ap is not None:
                v.tensor_scalar(m, m, fcol, invD, ALU.mult, ALU.mult)
                ins = v.tensor_mul(r, fcol, fcol)
            else:
                ins = v.tensor_scalar_mul(m, m, invD)
            cv.inc(ins)
        kb.v(_v2)
        vt2 = cv.tok()

        def _v3(v):
            wait(v, vt2)
            if f_ap is not None:
                cv.inc(v.tensor_scalar(va, va, r, invD, ALU.mult, ALU.mult))
            else:
                cv.inc(v.tensor_scalar_mul(va, va, invD))
        kb.v(_v3)
        vt3 = cv.tok()

        def _v4(v):
            wait(v, vt3)
            cv.inc(v.tensor_mul(r, m, m))
        kb.v(_v4)
        vt4 = cv.tok()

        def _v5(v):
            wait(v, vt4)
            cv.inc(v.tensor_sub(va, va, r))
        kb.v(_v5)
        return cv.tok()

    def rstd_batch(self, c0, c1, toks):
        kb, st = self.kb, self.st
        cs = kb.cs

        def _s1(s):
            wait(s, *toks)
            cs.inc(s.activation(st[:, 32 + c0:32 + c1], st[:, 16 + c0:16 + c1],
                                AF.Ln, bias=self.eps))
        kb.s(_s1)
        t1 = cs.tok()

        def _s2(s):
            wait(s, t1)
            cs.inc(s.activation(st[:, 32 + c0:32 + c1], st[:, 32 + c0:32 + c1],
                                AF.Exp, scale=-0.5))
        kb.s(_s2)
        return cs.tok()

    def apply(self, src_ap, dst_ap, col, toks, scaled=False):
        kb, st = self.kb, self.st
        cv = kb.cv
        m = st[:, col:col + 1]
        r = st[:, 32 + col:33 + col]

        def _v5(v):
            wait(v, *toks)
            cv.inc(v.scalar_tensor_tensor(m, m, -1.0, r, ALU.mult, ALU.mult))
        kb.v(_v5)
        vt5 = cv.tok()
        if scaled:
            def _v6(v):
                wait(v, vt5)
                cv.inc(v.tensor_mul(r, r, st[:, 48 + col:49 + col]))
            kb.v(_v6)
            vt5 = cv.tok()

        def _v7(v):
            wait(v, vt5)
            cv.inc(v.tensor_scalar(dst_ap, src_ap, r, m, ALU.mult, ALU.add))
        kb.v(_v7)
        return cv.tok()


def transpose_tile(kb, src_cols_fn, nchunks, ps_bf, dst_cols_fn, idbf,
                   waits, ps_free=None):
    ct, cv = kb.ct, kb.cv

    def _t(t):
        wait(t, ps_free, *waits)
        for c in range(nchunks):
            ins = t.transpose(ps_bf[:, c * 128:(c + 1) * 128], src_cols_fn(c), idbf)
        ct.inc(ins)
    kb.t(_t)
    tt = ct.tok()

    def _v(v):
        wait(v, tt)
        for c in range(nchunks):
            ins = v.tensor_copy(dst_cols_fn(c), ps_bf[:, c * 128:(c + 1) * 128])
        cv.inc(ins)
    kb.v(_v)
    return cv.tok(), tt


def build(debug=(), phases="ABCD"):
    global DEBUG
    DEBUG = tuple(debug)
    ctx = ExitStack()
    nc = bass.Bass("TRN2", target_bir_lowering=False, debug=False)
    inp = {}
    for name, shape in INPUT_SPECS:
        inp[name] = nc.dram_tensor(name, shape, F32, kind="ExternalInput").ap()
    out = nc.dram_tensor("out", [T, D], F32, kind="ExternalOutput").ap()
    w1t_sc = nc.dram_tensor("w1t_sc", [4 * D, D], BF16, kind="Internal").ap()
    w2t_sc = nc.dram_tensor("w2t_sc", [D, 4 * D], BF16, kind="Internal").ap()
    x4_sc = nc.dram_tensor("x4_sc", [T, D], F32, kind="Internal").ap()

    kb = KB(nc, ctx)
    dq, dg, ct, cv, cs, cg = kb.dq, kb.dg, kb.ct, kb.cv, kb.cs, kb.cg

    idbf = kb.sbuf("idbf", [128, 128], BF16)
    nidbf = kb.sbuf("nidbf", [128, 128], BF16)
    ones_b = kb.sbuf("ones_b", [1, 512], BF16)
    eps_t = kb.sbuf("eps_t", [128, 1], F32)
    sm = kb.sbuf("sm", [128, 16], F32)
    ssum = kb.sbuf("ssum", [128, 2], F32)
    sq_scr = kb.sbuf("sq_scr", [128, D], F32)
    bqkv_row = kb.sbuf("bqkv_row", [1, 3 * D], BF16)
    wv_pp = kb.sbuf("wv_pp", [128, 24], F32)

    dpro = kb.dsem("pro")

    def _(g):
        dpro.inc(g.dma_start(out=bqkv_row[:], in_=inp["L0_attn_b_qkv"]))
        dpro.inc(g.dma_start(out=wv_pp[:].rearrange("p (c k) -> p c k", k=3),
                           in_=inp["L0_attn_W_vconv"].rearrange("(c p) k -> p c k", p=128)))
        cg.inc(g.iota(idbf[:], [[1, 128]], base=0, channel_multiplier=-1,
                      allow_small_or_imprecise_dtypes=True))
    kb.g(_)
    g0 = cg.tok()

    def _(g):
        wait(g, g0)
        cg.inc(g.tensor_single_scalar(idbf[:], idbf[:], 0.0, ALU.is_equal))
    kb.g(_)
    g1 = cg.tok()

    def _(g):
        wait(g, g1)
        cg.inc(g.tensor_scalar_mul(nidbf[:], idbf[:], -1.0))
    kb.g(_)
    gid = cg.tok()

    def _(v):
        v.memset(ones_b[:], 1.0)
        cv.inc(v.memset(eps_t[:], 1.0 / EPS))
    kb.v(_)
    vc0 = cv.tok()

    def _(v):
        wait(v, vc0)
        cv.inc(v.reciprocal(eps_t[:], eps_t[:]))
    kb.v(_)
    vconst = cv.tok()
    pro_dg = dpro.tok()

    b0_steps = []
    b0_state = {}
    if "B" in phases:
        NW = 4096 * D
        asums = kb.sbuf("asums", [128, 64], F32)
        taus = kb.sbuf("taus", [128, 4], F32)
        tot_sb = kb.sbuf("tot_sb", [1, 2], F32)
        ones_f = kb.sbuf("ones_f", [1, 128], F32)
        ones_c = kb.sbuf("ones_c", [128, 1], F32)

        def _(v):
            v.memset(ones_f[:], 1.0)
            cv.inc(v.memset(ones_c[:], 1.0))
        kb.v(_)
        vof = cv.tok()

        W1r = inp["L0_ffn_W1"].rearrange("(c p) d -> p c d", p=128)
        W2r = inp["L0_ffn_W2"].rearrange("(c p) d -> p c d", p=128)
        NSLOT = 5
        tw = kb.at("tw", [128, NSLOT * 1024], F32, REG1)
        NSLOT1 = 2
        tw1 = kb.at("tw1", [128, NSLOT1 * 1024], F32, REG5)
        absd = kb.at("absd", [128, 1024], BF16, REG5 + NSLOT1 * 4096)
        thr = kb.at("thr", [128, 1024], F32, REG1 + 20480)
        wtb = kb.at("wtb", [128, 2 * 1024], F32, REG1 + 24576)
        dts = [kb.dsem("t%d" % k) for k in range(NSLOT)]
        dts1 = [kb.dsem("u%d" % k) for k in range(NSLOT1)]
        dst_sems = [kb.dsem("ts"), kb.dsem("ts2")]
        scope_b0ps = ExitStack()
        ps_b0 = kb.psum("ps_b0", [128, 4], F32, scope_b0ps)
        batches = []
        for b in range(32):
            batches.append((W1r[:, b:b + 1, :], b))
        for b in range(32):
            c, h = b // 4, b % 4
            batches.append((W2r[:, c:c + 1, h * 1024:(h + 1) * 1024], 32 + b))
        b0_state.update(slot_free={k: None for k in range(NSLOT)},
                        p1_free={k: None for k in range(NSLOT1)},
                        abs_prev=None, g_prev=None, st_tok=None, st_tok2=None,
                        st_par=[None, None], tau_tok=None)

        def emit_load(src_ap, col):
            slot = col % NSLOT
            twa = tw[:, slot * 1024:(slot + 1) * 1024]

            def _(sync, w=b0_state["slot_free"][slot], ds=dts[slot],
                  w2=(b0_state.get("reg1_free") if col < NSLOT else None)):
                wait(sync, w, w2)
                ds.inc(sync.dma_start(out=twa.rearrange("p (c d) -> p c d",
                                                        c=src_ap.shape[1]),
                                      in_=src_ap))
            kb.q(_)
            return twa, dts[slot].tok()

        def mk_pass1(src_ap, col):
            load = {}
            slot = col % NSLOT1

            def prefetch():
                twa = tw1[:, slot * 1024:(slot + 1) * 1024]

                def _(sync, w=b0_state["p1_free"][slot], ds=dts1[slot]):
                    wait(sync, w)
                    ds.inc(sync.dma_start(
                        out=twa.rearrange("p (c d) -> p c d", c=src_ap.shape[1]),
                        in_=src_ap))
                kb.q(_)
                load["twa"], load["tok"] = twa, dts1[slot].tok()

            def step():
                def _(s, ltok=load["tok"], twa=load["twa"],
                      pv=b0_state["abs_prev"]):
                    wait(s, ltok, pv)
                    cs.inc(s.activation(absd[:], twa, AF.Abs,
                                        accum_out=asums[:, col:col + 1]))
                kb.s(_)
                b0_state["abs_prev"] = cs.tok()
                b0_state["p1_free"][slot] = cs.tok()
            return prefetch, step

        def tau_step():
            red_all = cs.tok()

            def _(v, w=red_all):
                wait(v, w)
                v.tensor_reduce(sm[:, 9:10], asums[:, 0:32], AX.X, ALU.add)
                cv.inc(v.tensor_reduce(sm[:, 10:11], asums[:, 32:64], AX.X, ALU.add))
            kb.v(_)
            tsum_tok = cv.tok()

            def _(t, w=tsum_tok, w2=vof):
                wait(t, w, w2)
                t.matmul(ps_b0[0:1, 0:1], sm[:, 9:10], ones_c[:], start=True, stop=True)
                ct.inc(t.matmul(ps_b0[0:1, 1:2], sm[:, 10:11], ones_c[:],
                                start=True, stop=True))
            kb.t(_)
            ttot = ct.tok()

            def _(v, w=ttot):
                wait(v, w)
                cv.inc(v.tensor_scalar_mul(tot_sb[0:1, 0:2], ps_b0[0:1, 0:2], 1.0 / NW))
            kb.v(_)
            vv1 = cv.tok()

            def _(v, w=vv1):
                wait(v, w)
                cv.inc(v.tensor_scalar(tot_sb[0:1, 0:2], tot_sb[0:1, 0:2], 1e-8, 0.5,
                                       ALU.max, ALU.mult))
            kb.v(_)
            vv2 = cv.tok()

            def _(t, w=vv2):
                wait(t, w)
                ct.inc(t.matmul(ps_b0[:, 2:4], ones_f[:], tot_sb[0:1, 0:2],
                                start=True, stop=True))
            kb.t(_)
            tbc = ct.tok()

            def _(v, w=tbc):
                wait(v, w)
                v.tensor_copy(taus[:, 0:1], ps_b0[:, 2:3])
                v.tensor_copy(taus[:, 2:3], ps_b0[:, 3:4])
                cv.inc(v.tensor_scalar_mul(taus[:, 1:2], ps_b0[:, 2:3], -1.0))
            kb.v(_)
            vt_ = cv.tok()

            def _(v, w=vt_):
                wait(v, w)
                cv.inc(v.tensor_scalar_mul(taus[:, 3:4], ps_b0[:, 3:4], -1.0))
            kb.v(_)
            b0_state["tau_tok"] = cv.tok()

        def tau_switch_step():
            pass

        W1s = w1t_sc.rearrange("(c p) d -> p c d", p=128)
        W2s = w2t_sc.rearrange("(c p) d -> p c d", p=128)

        def mk_pass2(src_ap, col):
            load = {}

            def prefetch():
                load["twa"], load["tok"] = emit_load(src_ap, 64 + col)

            def step():
                m = 0 if col < 32 else 1
                slot = (64 + col) % NSLOT
                ws = col % 2
                wslice = wtb[:, ws * 1024:(ws + 1) * 1024]
                if col < 32:
                    dst = W1s[:, col:col + 1, :]
                else:
                    c, h = (col - 32) // 4, (col - 32) % 4
                    dst = W2s[:, c:c + 1, h * 1024:(h + 1) * 1024]
                twa, ltok = load["twa"], load["tok"]

                def _(v, m=m, w=ltok, pv=b0_state["g_prev"],
                      tt=b0_state["tau_tok"]):
                    wait(v, w, pv, tt)
                    cv.inc(v.tensor_scalar(thr[:], twa, taus[:, 2 * m + 1:2 * m + 2],
                                           1.0, ALU.is_ge, ALU.subtract))
                kb.v(_)
                g1 = cv.tok()

                def _(v, m=m, w=g1, st=b0_state["st_par"][ws], wslice=wslice):
                    wait(v, w, st)
                    cv.inc(v.scalar_tensor_tensor(wslice, twa,
                                                  taus[:, 2 * m:2 * m + 1],
                                                  thr[:], ALU.is_gt, ALU.add))
                kb.v(_)
                g4 = cv.tok()
                b0_state["g_prev"] = g4
                b0_state["slot_free"][slot] = g4

                def _(g, dst=dst, w=g4, wslice=wslice, ds=dst_sems[ws]):
                    wait(g, w)
                    ds.inc(g.dma_start(
                        out=dst, in_=wslice.rearrange("p (c d) -> p c d",
                                                      c=dst.shape[1])))
                kb.g(_)
                b0_state["st_par"][ws] = dst_sems[ws].tok()
                b0_state["st_tok2"] = b0_state["st_par"][1 - ws]
                b0_state["st_tok"] = b0_state["st_par"][ws]
            return prefetch, step

        _p1 = [mk_pass1(s_, c_) for s_, c_ in batches]
        _p2 = [mk_pass2(s_, c_) for s_, c_ in batches]
        b0p1_steps = []
        for k in range(NSLOT1):
            _p1[k][0]()
        for k in range(64):
            def mk(k=k):
                def step():
                    _p1[k][1]()
                    if k + NSLOT1 < 64:
                        _p1[k + NSLOT1][0]()
                return step
            b0p1_steps.append(mk())
        b0_steps.append(tau_step)

        def _pf2_initial():
            for k in range(NSLOT):
                _p2[k][0]()
        b0_steps.append(_pf2_initial)
        for k in range(64):
            def mk(k=k):
                def step():
                    _p2[k][1]()
                    if k + NSLOT < 64:
                        _p2[k + NSLOT][0]()
                return step
            b0_steps.append(mk())


    q_fm = kb.at("q_fm", [128, ND * T], BF16, REG2)
    k_fm = kb.at("k_fm", [128, ND * T], BF16, REG3)
    v_fm = kb.at("v_fm", [128, ND * T], BF16, REG4)
    C_tm = kb.at("C_tm", [128, NT * D], BF16, REG5)
    n1_fm = kb.at("n1_fm", [128, ND * T], BF16, REG1)

    lnst = kb.sbuf("lnst", [128, 64], F32)
    lnb = LNB(kb, lnst, sq_scr[:], eps_t[:, 0:1])
    xstage = q_fm
    dxa = kb.dsem("xa")

    def _(g):
        dxa.inc(g.dma_start(
            out=xstage[:].rearrange("p (i d) -> p i d", i=NT),
            in_=inp["L0_attn_x"].rearrange("(i p) d -> p i d", p=128)))
    kb.g(_)
    xload = dxa.tok()

    with ExitStack() as pha:
        n1_tm = kb.at("n1_tm", [128, D], BF16, WS + 8192)
        ps_tr = kb.psum("ps_tr", [128, 1024], BF16, pha)
        stok = None
        for i in range(NT):
            for _k in range(2):
                if b0p1_steps:
                    b0p1_steps.pop(0)()
            stok = lnb.stats(xstage[:, i * D:(i + 1) * D], i, [xload], prev=stok)
        rtok = lnb.rstd_batch(0, 16, [stok])
        evtok = None
        trtok = None
        aptok = None
        for i in range(NT):
            aptok = lnb.apply(xstage[:, i * D:(i + 1) * D], n1_tm[:], i,
                              [rtok, trtok])
            evtok, trtok = transpose_tile(
                kb, lambda c: n1_tm[:, c * 128:(c + 1) * 128], ND, ps_tr[:],
                lambda c, i=i: n1_fm[:, c * T + i * 128: c * T + (i + 1) * 128],
                idbf[:], [aptok, gid], ps_free=evtok)
        n1_done = evtok
        kb.dbg("n1_fm", n1_fm[:], n1_done)
    kb.flush_dbg()
    kb.barrier(dma=False)

    with ExitStack() as pha:
        wbuf = kb.at("wbuf", [128, 2 * 8 * 512], BF16, WS)
        ps_q = [kb.psum("ps_qkv0", [128, 512], F32, pha),
                kb.psum("ps_qkv1", [128, 512], F32, pha)]
        Wq = inp["L0_attn_W_qkv"].rearrange("(c p) f -> p c f", p=128)
        dws = [kb.dsem("w0"), kb.dsem("w1")]
        slot_tok = {0: None, 1: None}
        bank_tok = {0: None, 1: None}
        for grp in range(6):
            slot = grp % 2

            def _(g, grp=grp, slot=slot, w=slot_tok[slot], ds=dws[slot]):
                wait(g, w)
                ds.inc(g.dma_start(
                    out=wbuf[:, slot * 4096:(slot + 1) * 4096].rearrange(
                        "p (c f) -> p c f", c=8),
                    in_=Wq[:, :, grp * 512:(grp + 1) * 512]))
            kb.g(_)
            wtok = dws[slot].tok()

            for fc4 in range(4):
                if b0p1_steps:
                    b0p1_steps.pop(0)()
                if grp >= 4 and b0p1_steps:
                    b0p1_steps.pop(0)()
                fc = grp * 4 + fc4
                mat, lc = fc // 8, fc % 8
                dst = (q_fm, k_fm, v_fm)[mat]
                bk = fc4 % 2
                pb = ps_q[bk]
                for tg in range(4):
                    first = (grp == 0 and fc4 == 0 and tg == 0)
                    def _(t, pb=pb, slot=slot, fc4=fc4, tg=tg, wtok=wtok,
                          fc=fc, w=bank_tok[bk], first=first):
                        wait(t, wtok, w)
                        if first:
                            wait(t, pro_dg)
                        for kc in range(8):
                            t.matmul(pb[:],
                                     wbuf[:, slot * 4096 + kc * 512 + fc4 * 128:
                                          slot * 4096 + kc * 512 + (fc4 + 1) * 128],
                                     n1_fm[:, kc * T + tg * 512: kc * T + (tg + 1) * 512],
                                     start=(kc == 0), stop=False)
                        ct.inc(t.matmul(pb[:], bqkv_row[0:1, fc * 128:(fc + 1) * 128],
                                        ones_b[0:1, :], start=False, stop=True))
                    kb.t(_)
                    mtok = ct.tok()

                    def _(v, pb=pb, mat=mat, lc=lc, dst=dst, tg=tg, mtok=mtok):
                        wait(v, mtok)
                        dd = dst[:, lc * T + tg * 512: lc * T + (tg + 1) * 512]
                        if mat == 0:
                            cv.inc(v.tensor_scalar_mul(dd, pb[:], 1.0 / 32.0))
                        else:
                            cv.inc(v.tensor_copy(dd, pb[:]))
                    kb.v(_)
                    bank_tok[bk] = cv.tok()
            slot_tok[slot] = cv.tok()
        qkv_done = cv.tok()
        b0_state["reg1_free"] = qkv_done
        kb.dbg("q_fm", q_fm[:], qkv_done)
        kb.dbg("k_fm", k_fm[:], qkv_done)
        kb.dbg("v_fm", v_fm[:], qkv_done)
    kb.flush_dbg()
    kb.barrier(dma=False)

    with ExitStack() as pha:
        cbuf = kb.at("cbuf", [128, T], BF16, WS)
        ps_trc = kb.psum("ps_trc", [128, 1024], BF16, pha)
        ev = None
        ttk = None
        for c in range(ND):
            vchunk = v_fm[:, c * T:(c + 1) * T]
            w0 = wv_pp[:, c * 3 + 0:c * 3 + 1]
            w1 = wv_pp[:, c * 3 + 1:c * 3 + 2]
            w2 = wv_pp[:, c * 3 + 2:c * 3 + 3]

            def _(v, vchunk=vchunk, w1=w1, ev=ev):
                wait(v, ev, pro_dg, qkv_done)
                cv.inc(v.tensor_scalar_mul(cbuf[:], vchunk, w1))
            kb.v(_)
            t1 = cv.tok()

            def _(v, vchunk=vchunk, w0=w0, t1=t1):
                wait(v, t1)
                cv.inc(v.scalar_tensor_tensor(cbuf[:, 1:T], vchunk[:, 0:T - 1], w0,
                                              cbuf[:, 1:T], ALU.mult, ALU.add))
            kb.v(_)
            t2 = cv.tok()

            def _(v, vchunk=vchunk, w2=w2, t2=t2):
                wait(v, t2)
                cv.inc(v.scalar_tensor_tensor(cbuf[:, 0:T - 1], vchunk[:, 1:T], w2,
                                              cbuf[:, 0:T - 1], ALU.mult, ALU.add))
            kb.v(_)
            t3 = cv.tok()

            for half in range(2):
                ev, ttk = transpose_tile(
                    kb, lambda j, half=half: cbuf[:, half * 1024 + j * 128:
                                                  half * 1024 + (j + 1) * 128],
                    8, ps_trc[:],
                    lambda j, c=c, half=half: C_tm[:, (half * 8 + j) * D + c * 128:
                                                   (half * 8 + j) * D + (c + 1) * 128],
                    idbf[:], [t3, gid], ps_free=ev)
        c_done = ev
        kb.dbg("C_tm", C_tm[:], c_done)
    kb.flush_dbg()
    kb.barrier(dma=False)

    def outln_ap(i):
        return bass.AP(q_fm, 128 * i, [[q_fm.shape[1], 128], [T, ND], [1, 128]])

    with ExitStack() as pha:
        S_sb = kb.at("S_sb", [128, T], F32, WS)
        P_sb = kb.at("P_sb", [128, T], BF16, WS + 8192)
        AT_sb = kb.at("AT_sb", [128, T], BF16, WS + 12288)
        Dw_sb = kb.at("Dw_sb", [128, D], BF16, WS + 16384)
        D_sb = kb.at("D_sb", [128, D], BF16, WS + 18432)
        B_sb = kb.sbuf("B_sb", [128, 128], BF16, pha)
        ps_s = [kb.psum("ps_s%d" % k, [128, 512], F32, pha) for k in range(4)]
        ps_at = kb.psum("ps_at", [128, 1024], BF16, pha)
        ps_o = kb.psum("ps_o", [128, 1024], F32, pha)

        sbank_tok = {k: None for k in range(4)}
        at_ps_tok = None
        p_free = None
        s_free = None
        dw_free = None
        out_ps_free = None
        at_free = None
        stok = None
        sexp_prev = None

        def emit_scores(i):
            W = 128 * (i + 1)
            out = []
            for b in range((W + 511) // 512):
                w0c = b * 512
                w1c = min(W, w0c + 512)
                bk = b % 4

                def _(t, w0c=w0c, w1c=w1c, bk=bk, i=i, w=sbank_tok[bk]):
                    wait(t, w)
                    for kc in range(ND):
                        ins = t.matmul(
                            ps_s[bk][:, 0:w1c - w0c],
                            q_fm[:, kc * T + i * 128: kc * T + (i + 1) * 128],
                            k_fm[:, kc * T + w0c: kc * T + w1c],
                            start=(kc == 0), stop=(kc == ND - 1))
                    ct.inc(ins)
                kb.t(_)
                out.append((bk, w0c, w1c, ct.tok()))
            return out

        pend = emit_scores(0)
        for i in range(NT):
            W = 128 * (i + 1)
            ev_toks = []
            for n, (bk, w0c, w1c, mtok) in enumerate(pend):
                def _(s, bk=bk, w0c=w0c, w1c=w1c, mtok=mtok,
                      sf=(s_free if n == 0 else None)):
                    wait(s, mtok, sf)
                    cs.inc(s.activation(S_sb[:, w0c:w1c], ps_s[bk][:, 0:w1c - w0c],
                                        AF.Identity))
                kb.s(_)
                sbank_tok[bk] = cs.tok()
                ev_toks.append(cs.tok())

            def _(g, i=i, ev=ev_toks[-1]):
                wait(g, ev)
                cg.inc(g.affine_select(S_sb[:, i * 128:(i + 1) * 128],
                                       S_sb[:, i * 128:(i + 1) * 128],
                                       [[-1, 128]], ALU.is_ge, -1e30,
                                       base=0, channel_multiplier=1))
            kb.g(_)
            gmask = cg.tok()

            def _(v, W=W, gmask=gmask, ev=list(ev_toks), sp=sexp_prev):
                wait(v, gmask, sp, *ev)
                cv.inc(v.reduce_max(sm[:, 8:9], S_sb[:, 0:W], AX.X, negate=True))
            kb.v(_)
            vmax = cv.tok()

            def _(s, W=W, vmax=vmax, pf=p_free):
                wait(s, vmax, pf)
                cs.inc(s.activation(P_sb[:, 0:W], S_sb[:, 0:W], AF.Exp,
                                    bias=sm[:, 8:9], accum_out=ssum[:, 0:1]))
            kb.s(_)
            sexp = cs.tok()
            sexp_prev = sexp
            s_free = sexp

            if i + 1 < NT:
                pend = emit_scores(i + 1)
            while b0p1_steps:
                b0p1_steps.pop(0)()
            for _k in range(6 if i else 3):
                if b0_steps:
                    b0_steps.pop(0)()

            at_toks = []
            for bg in range((i + 8) // 8):
                j0, j1 = bg * 8, min(i + 1, bg * 8 + 8)

                def _(t, j0=j0, j1=j1, sexp=sexp, w=at_ps_tok,
                      af=(at_free if bg == 0 else None)):
                    wait(t, sexp, w, af)
                    for j in range(j0, j1):
                        ins = t.transpose(ps_at[:, (j - j0) * 128:(j - j0 + 1) * 128],
                                          P_sb[:, j * 128:(j + 1) * 128], idbf[:])
                    ct.inc(ins)
                kb.t(_)
                ttok = ct.tok()
                p_free = ttok

                def _(v, j0=j0, j1=j1, ttok=ttok):
                    wait(v, ttok)
                    cv.inc(v.tensor_copy(AT_sb[:, j0 * 128:j1 * 128],
                                         ps_at[:, 0:(j1 - j0) * 128]))
                kb.v(_)
                at_ps_tok = cv.tok()
                at_toks.append(cv.tok())

            def _(v, i=i, at=at_toks[-1]):
                wait(v, at)
                cv.inc(v.tensor_mul(B_sb[:], AT_sb[:, i * 128:(i + 1) * 128], nidbf[:]))
            kb.v(_)
            btok = cv.tok()

            def _(v, i=i, w=dw_free):
                wait(v, w, qkv_done)
                t0 = i * 128 + 1
                wlen = min(T, (i + 1) * 128 + 1) - t0
                for c in range(ND):
                    if wlen < 128:
                        v.memset(Dw_sb[:, c * 128 + wlen:(c + 1) * 128], 0.0)
                    ins = v.tensor_scalar_mul(Dw_sb[:, c * 128:c * 128 + wlen],
                                              v_fm[:, c * T + t0:c * T + t0 + wlen],
                                              wv_pp[:, c * 3 + 2:c * 3 + 3])
                cv.inc(ins)
            kb.v(_)
            dwtok = cv.tok()

            dtr, dw_free = transpose_tile(
                kb, lambda c: Dw_sb[:, c * 128:(c + 1) * 128], ND, ps_at[:],
                lambda c: D_sb[:, c * 128:(c + 1) * 128],
                idbf[:], [dwtok, at_free], ps_free=at_ps_tok)
            at_ps_tok = dtr

            def _(t, i=i, at=list(at_toks), btok=btok, dtr=dtr, w=out_ps_free):
                wait(t, w, btok, dtr, *at)
                for half in range(2):
                    for j in range(i + 1):
                        t.matmul(ps_o[:, half * 512:(half + 1) * 512],
                                 AT_sb[:, j * 128:(j + 1) * 128],
                                 C_tm[:, j * D + half * 512: j * D + (half + 1) * 512],
                                 start=(j == 0), stop=False)
                    ins = t.matmul(ps_o[:, half * 512:(half + 1) * 512], B_sb[:],
                                   D_sb[:, half * 512:(half + 1) * 512],
                                   start=False, stop=True)
                ct.inc(ins)
            kb.t(_)
            otok = ct.tok()
            at_free = otok

            def _(v, sexp=sexp):
                wait(v, sexp)
                cv.inc(v.reciprocal(ssum[:, 1:2], ssum[:, 0:1]))
            kb.v(_)
            rtok = cv.tok()

            def _(v, i=i, w=otok):
                wait(v, w)
                cv.inc(v.tensor_copy(outln_ap(i), ps_o[:]))
            kb.v(_)
            out_ps_free = cv.tok()

            stok = lnb.stats(outln_ap(i), i, [out_ps_free, rtok],
                             f_ap=ssum[:, 1:2], prev=stok)
            if i == NT - 1 and "a3_oraw" in DEBUG:
                oraw_sc = kb.sbuf("oraw_sc", [128, ND * 128], F32, pha)
                p_sc = kb.sbuf("p_sc", [128, T], BF16, pha)

                def _(v, i=i, w=out_ps_free, se=sexp):
                    wait(v, w, se)
                    cv.inc(v.tensor_copy(oraw_sc[:], outln_ap(i)))
                    cv.inc(v.tensor_copy(p_sc[:], P_sb[:]))
                kb.v(_)
                a3dbg_tok = cv.tok()
                kb.dbg("a3_oraw", oraw_sc[:], a3dbg_tok)
                kb.dbg("a3_p", p_sc[:], a3dbg_tok)
                kb.dbg("a3_ssum", ssum[:], a3dbg_tok)
            else:
                a3dbg_tok = None
        while b0_steps:
            b0_steps.pop(0)()
        rtok_b = lnb.rstd_batch(0, 16, [stok])
        aptok = None
        for i in range(NT):
            aptok = lnb.apply(outln_ap(i), outln_ap(i), i,
                              [rtok_b, aptok, a3dbg_tok if i == 0 else None],
                              scaled=True)
        a3_done = aptok
        kb.dbg("outln_q", q_fm[:], a3_done)
        kb.dbg("a3_lnst", lnst[:], a3_done)
        kb.flush_dbg()

    kb.barrier(dma=False)
    if "B" in phases:
        scope_b0ps.close()

    if "B" in phases:
        scope_b1 = ExitStack()
        scope_b1tr = ExitStack()
        X2 = kb.at("X2", [128, NT * D], F32, REG4)
        act_fm = kb.at("act_fm", [128, ND * T], BF16, REG3)
        xt2 = kb.at("xt2", [128, 2 * D], F32, WS)
        tmb = kb.at("tmb", [128, D], BF16, WS + 16384)
        ps_tr2 = kb.psum("ps_tr2", [128, 1024], BF16, scope_b1tr)
        dx2 = [kb.dsem("x0"), kb.dsem("x1")]
        slot_done = {0: None, 1: None}
        stok = None
        for i in range(NT):
            slot = i % 2
            xa = xt2[:, slot * D:(slot + 1) * D]

            def _(sync, i=i, xa=xa, w=slot_done[slot], ds=dx2[slot]):
                wait(sync, w)
                ds.inc(sync.dma_start(out=xa, in_=inp["L0_attn_x"][i * 128:(i + 1) * 128, :]))
            kb.q(_)
            ldtok = dx2[slot].tok()

            def _(v, i=i, xa=xa, ldtok=ldtok):
                wait(v, ldtok)
                cv.inc(v.tensor_add(X2[:, i * D:(i + 1) * D], xa, outln_ap(i)))
            kb.v(_)
            addtok = cv.tok()
            slot_done[slot] = addtok

            stok = lnb.stats(X2[:, i * D:(i + 1) * D], i, [addtok], prev=stok)
        rtok_b = lnb.rstd_batch(0, 16, [stok])
        aptok = None
        evtok = None
        trtok = None
        for i in range(NT):
            aptok = lnb.apply(X2[:, i * D:(i + 1) * D], tmb[:], i, [rtok_b, trtok])
            evtok, trtok = transpose_tile(
                kb, lambda c: tmb[:, c * 128:(c + 1) * 128], ND, ps_tr2[:],
                lambda c, i=i: act_fm[:, c * T + i * 128: c * T + (i + 1) * 128],
                idbf[:], [aptok], ps_free=evtok)
        b1_done = evtok
        kb.dbg("x2", X2[:], aptok)
        kb.dbg("n2_fm", act_fm[:], b1_done)
        kb.flush_dbg()
        kb.barrier()
        scope_b1tr.close()

        h_g = kb.at("h_g", [128, 32 * 512], BF16, REG1)
        w1tb = kb.at("w1tb", [128, 2 * 8 * 512], BF16, REG2)
        b1row = kb.at("b1row", [1, 4096], BF16, REG2 + 16384)
        b2row = kb.at("b2row", [1, 1024], BF16, REG2 + 24576)
        w2tb = kb.at("w2tb", [128, 4 * 1024], BF16, WS)
        out2g = kb.at("out2g", [128, 4 * 1024], BF16, WS + 8192)
        smB = kb.sbuf("smB", [128, 12], F32)
        drow = kb.dsem("rows")

        def _(g):
            drow.inc(g.dma_start(out=b1row[:], in_=inp["L0_ffn_b1"]))
            drow.inc(g.dma_start(out=b2row[:], in_=inp["L0_ffn_b2"]))
        kb.g(_)
        rows_tok = drow.tok()

        dw1 = [kb.dsem("tw1a"), kb.dsem("tw1b")]
        dw2 = [kb.dsem("w2a"), kb.dsem("w2b")]
        ps_h = [kb.psum("ps_h0", [128, 1024], F32, scope_b1),
                kb.psum("ps_h1", [128, 1024], F32, scope_b1)]
        ps_o2 = [kb.psum("ps_o2%d" % u, [128, 1024], F32, scope_b1) for u in range(2)]

        w1slot_tok = {0: None, 1: None}
        w2slot_tok = {0: None, 1: None}
        hbank_tok = {0: None, 1: None}
        o2_tok = {0: None, 1: None}
        hg_free = None
        out2g_free = None
        for grp in range(4):
            for p in range(8):
                slot = p % 2

                def _(sync, p=p, slot=slot, w=w1slot_tok[slot], ds=dw1[slot],
                      w2=(b0_state.get("st_tok") if (grp == 0 and p < 2) else None),
                      w3=(b0_state.get("st_tok2") if (grp == 0 and p < 2) else None)):
                    wait(sync, w, w2, w3)
                    for dc in range(8):
                        ds.inc(sync.dma_start(
                            out=w1tb[:, slot * 4096 + dc * 512:slot * 4096 + (dc + 1) * 512],
                            in_=w1t_sc[p * 512:(p + 1) * 512, dc * 128:(dc + 1) * 128],
                            transpose=True))
                kb.q(_)
                wtok = dw1[slot].tok()

                for fpr in range(2):
                    bk = fpr % 2
                    pb = ps_h[bk]

                    def _(t, pb=pb, slot=slot, fpr=fpr, p=p, grp=grp, wtok=wtok,
                          w=hbank_tok[bk]):
                        wait(t, wtok, w, rows_tok, b1_done)
                        for half in range(2):
                            fl = 2 * fpr + half
                            fc = 4 * p + fl
                            for kc in range(8):
                                t.matmul(pb[:, half * 512:(half + 1) * 512],
                                         w1tb[:, slot * 4096 + kc * 512 + fl * 128:
                                              slot * 4096 + kc * 512 + (fl + 1) * 128],
                                         act_fm[:, kc * T + grp * 512:
                                                kc * T + (grp + 1) * 512],
                                         start=(kc == 0), stop=False)
                            ins = t.matmul(pb[:, half * 512:(half + 1) * 512],
                                           b1row[0:1, fc * 128:(fc + 1) * 128],
                                           ones_b[0:1, :], start=False, stop=True)
                        ct.inc(ins)
                    kb.t(_)
                    mtok = ct.tok()

                    def _(s, pb=pb, p=p, fpr=fpr, mtok=mtok, hf=hg_free,
                          bf=(b0_state.get("st_tok") if grp == 0 and p < 2 else None),
                          bf2=(b0_state.get("st_tok2") if grp == 0 and p < 2 else None),
                          bv=(b0_state.get("vthr_tok") if grp == 0 and p < 2 else None)):
                        wait(s, mtok, hf, bf, bf2, bv)
                        fc0 = 4 * p + 2 * fpr
                        cs.inc(s.activation(h_g[:, fc0 * 512:(fc0 + 2) * 512], pb[:],
                                            AF.Gelu))
                    kb.s(_)
                    hbank_tok[bk] = cs.tok()
                w1slot_tok[slot] = ct.tok()
            h_done = cs.tok()
            if grp == 3:
                kb.dbg("h_g", h_g[:], h_done)

            ev_toks = [None] * 4
            for pair in range(2):
                for fc in range(32):
                    slot = fc % 2
                    wslice = w2tb[:, slot * 1024:(slot + 1) * 1024]
                    def _(sync, fc=fc, wslice=wslice, w=w2slot_tok[slot], ds=dw2[slot],
                          w2=(b0_state.get("st_tok") if (grp == 0 and pair == 0 and fc < 2) else None),
                          w3=(b0_state.get("st_tok2") if (grp == 0 and pair == 0 and fc < 2) else None)):
                        wait(sync, w, w2, w3)
                        ds.inc(sync.dma_start(out=wslice,
                                              in_=w2t_sc[:, fc * 128:(fc + 1) * 128],
                                              transpose=True))
                    kb.q(_)
                    w2tok = dw2[slot].tok()

                    def _(t, fc=fc, pair=pair, wslice=wslice, w2tok=w2tok,
                          h_done=h_done,
                          ot=[o2_tok[u] for u in range(2)] if fc == 0 else []):
                        wait(t, w2tok, h_done, *ot)
                        for u in range(2):
                            sub = pair * 2 + u
                            for half in range(2):
                                ins = t.matmul(
                                    ps_o2[u][:, half * 512:(half + 1) * 512],
                                    h_g[:, fc * 512 + sub * 128:fc * 512 + (sub + 1) * 128],
                                    wslice[:, half * 512:(half + 1) * 512],
                                    start=(fc == 0), stop=False)
                        if fc == 31:
                            for u in range(2):
                                for half in range(2):
                                    ins = t.matmul(
                                        ps_o2[u][:, half * 512:(half + 1) * 512],
                                        ones_b[0:1, 0:128],
                                        b2row[0:1, half * 512:(half + 1) * 512],
                                        start=False, stop=True)
                        ct.inc(ins)
                    kb.t(_)
                    w2slot_tok[slot] = ct.tok()
                o2done = ct.tok()

                for u in range(2):
                    sub = pair * 2 + u

                    def _(s, u=u, sub=sub, w=o2done, of=out2g_free):
                        wait(s, w, of)
                        cs.inc(s.activation(out2g[:, sub * D:(sub + 1) * D],
                                            ps_o2[u][:], AF.Identity))
                    kb.s(_)
                    ev_toks[sub] = cs.tok()
                    o2_tok[u] = cs.tok()

            stat_toks = []
            for sub in range(4):
                evt = ev_toks[sub]

                def _(v, sub=sub, w=evt):
                    wait(v, w)
                    cv.inc(v.tensor_reduce(smB[:, sub:sub + 1],
                                           out2g[:, sub * D:(sub + 1) * D],
                                           AX.X, ALU.add))
                kb.v(_)
                vt = cv.tok()

                def _(s, sub=sub, w=vt):
                    wait(s, w)
                    cs.inc(s.activation(sq_scr[:], out2g[:, sub * D:(sub + 1) * D],
                                        AF.Square, accum_out=smB[:, 4 + sub:5 + sub]))
                kb.s(_)
                sqt = cs.tok()

                def _(v, sub=sub, w=sqt):
                    wait(v, w)
                    v.tensor_scalar_mul(smB[:, sub:sub + 1], smB[:, sub:sub + 1], 1.0 / D)
                    cv.inc(v.tensor_scalar_mul(smB[:, 4 + sub:5 + sub],
                                               smB[:, 4 + sub:5 + sub], 1.0 / D))
                kb.v(_)
                vt2_ = cv.tok()

                def _(v, sub=sub, w=vt2_):
                    wait(v, w)
                    cv.inc(v.tensor_tensor(smB[:, 8 + sub:9 + sub],
                                           smB[:, sub:sub + 1], smB[:, sub:sub + 1],
                                           ALU.mult))
                kb.v(_)
                vt3_ = cv.tok()

                def _(v, sub=sub, w=vt3_):
                    wait(v, w)
                    cv.inc(v.tensor_sub(smB[:, 4 + sub:5 + sub],
                                        smB[:, 4 + sub:5 + sub], smB[:, 8 + sub:9 + sub]))
                kb.v(_)
                stat_toks.append(cv.tok())

            def _(s, w=list(stat_toks)):
                wait(s, *w)
                cs.inc(s.activation(smB[:, 8:12], smB[:, 4:8], AF.Ln,
                                    bias=eps_t[:, 0:1]))
            kb.s(_)
            r1 = cs.tok()

            def _(s, w=r1):
                wait(s, w)
                cs.inc(s.activation(smB[:, 8:12], smB[:, 8:12], AF.Exp, scale=-0.5))
            kb.s(_)
            r2 = cs.tok()

            ap_toks = []
            for sub in range(4):
                i = grp * 4 + sub

                def _(v, sub=sub, w=r2):
                    wait(v, w)
                    cv.inc(v.scalar_tensor_tensor(smB[:, sub:sub + 1],
                                                  smB[:, sub:sub + 1], -1.0,
                                                  smB[:, 8 + sub:9 + sub],
                                                  ALU.mult, ALU.mult))
                kb.v(_)
                bt = cv.tok()

                def _(s, sub=sub, i=i, w=bt):
                    wait(s, w)
                    cs.inc(s.activation(tmb[:], out2g[:, sub * D:(sub + 1) * D],
                                        AF.Identity, bias=smB[:, sub:sub + 1],
                                        scale=smB[:, 8 + sub:9 + sub]))
                kb.s(_)
                apt = cs.tok()

                def _(v, i=i, w=apt):
                    wait(v, w)
                    cv.inc(v.tensor_add(X2[:, i * D:(i + 1) * D],
                                        X2[:, i * D:(i + 1) * D], tmb[:]))
                kb.v(_)
                ap_toks.append(cv.tok())
            out2g_free = cv.tok()
            hg_free = o2done
        b2_done = cv.tok()
        kb.dbg("x3", X2[:], b2_done)
        kb.flush_dbg()
        kb.barrier()
        scope_b1.close()


    if "C" in phases:
        scl = kb.sbuf("scl", [1, 16], F32)
        g8s = kb.sbuf("g8s", [1, 16], F32)
        lnd_pp = kb.sbuf("lnd_pp", [128, 1], F32)
        osc_pp = kb.sbuf("osc_pp", [128, 1], F32)
        ones_f2 = kb.sbuf("ones_f2", [1, 128], F32)
        basis_b = kb.at("basis_b", [128, 8 * 256], BF16, REG1)
        cf_b = kb.at("cf_b", [128, 4 * 2 * 256], BF16, REG1 + 4096)
        basisT = kb.at("basisT", [128, 2 * 1024], BF16, REG1 + 8192)
        cfT = kb.at("cfT", [128, 4 * 2 * 256], BF16, REG1 + 12288)
        qw_b = kb.at("qw_b", [128, 8 * 256], BF16, REG1 + 16384)
        kw_b = kb.at("kw_b", [128, 8 * 256], BF16, REG1 + 20480)
        vw_b = kb.at("vw_b", [128, 8 * 256], BF16, REG1 + 24576)
        owT_b = kb.at("owT_b", [128, 2 * 1024], BF16, REG1 + 28672)
        dl1 = kb.dsem("l1a")

        def _(g):
            dl1.inc(g.dma_start(out=scl[0:1, 0:1], in_=inp["L1_attn_decay_logit"]))
            dl1.inc(g.dma_start(out=scl[0:1, 1:2], in_=inp["L1_attn_out_scale"]))
            dl1.inc(g.dma_start(out=g8s[0:1, 0:8], in_=inp["L1_ffn_smha_gates"]))
            dl1.inc(g.dma_start(out=g8s[0:1, 8:16], in_=inp["L1_ffn_moe_gates"]))
            dl1.inc(g.dma_start(
                out=basis_b[:].rearrange("p (c b) -> p c b", c=8),
                in_=inp["L1_attn_basis"].rearrange("(c p) b -> p c b", p=128)))
            for x, name in enumerate(("q", "k", "v", "o")):
                dl1.inc(g.dma_start(
                    out=cf_b[:, x * 512:(x + 1) * 512].rearrange("p (c b) -> p c b", c=2),
                    in_=inp["L1_attn_%s_coeffs" % name].rearrange("(c p) b -> p c b", p=128)))
        kb.g(_)
        l1tok = dl1.tok()

        def _(v):
            cv.inc(v.memset(ones_f2[:], 1.0))
        kb.v(_)
        vo2 = cv.tok()

        def _(s, w=l1tok):
            wait(s, w)
            s.activation(scl[0:1, 2:3], scl[0:1, 0:1], AF.Sigmoid)
            s.activation(g8s[0:1, 0:8], g8s[0:1, 0:8], AF.Sigmoid)
            cs.inc(s.activation(g8s[0:1, 8:16], g8s[0:1, 8:16], AF.Sigmoid))
        kb.s(_)
        sig_tok = cs.tok()

        def _(v, w=sig_tok):
            wait(v, w)
            cv.inc(v.tensor_scalar_max(scl[0:1, 2:3], scl[0:1, 2:3], 1e-38))
        kb.v(_)
        cl_tok = cv.tok()

        def _(s, w=cl_tok):
            wait(s, w)
            cs.inc(s.activation(scl[0:1, 3:4], scl[0:1, 2:3], AF.Ln))
        kb.s(_)
        lnd_tok = cs.tok()

        scope_c0 = ExitStack()
        ps_c0 = kb.psum("ps_c0", [128, 1024], F32, scope_c0)
        ps_c1 = kb.psum("ps_c1", [128, 1024], F32, scope_c0)
        ps_trC = kb.psum("ps_trC", [128, 1024], BF16, scope_c0)

        def _(t, w=lnd_tok, w2=vo2):
            wait(t, w, w2)
            t.matmul(ps_c0[:, 0:1], ones_f2[:], scl[0:1, 3:4], start=True, stop=True)
            ct.inc(t.matmul(ps_c0[:, 1:2], ones_f2[:], scl[0:1, 1:2], start=True, stop=True))
        kb.t(_)
        tb = ct.tok()

        def _(v, w=tb):
            wait(v, w)
            v.tensor_copy(lnd_pp[:], ps_c0[:, 0:1])
            cv.inc(v.tensor_copy(osc_pp[:], ps_c0[:, 1:2]))
        kb.v(_)
        bc_tok = cv.tok()

        tr_toks = []
        for c in range(8):
            for h in range(2):
                ev, tt = transpose_tile(
                    kb, lambda _c, c=c, h=h: basis_b[:, c * 256 + h * 128:
                                                     c * 256 + (h + 1) * 128],
                    1, ps_trC[:],
                    lambda _c, c=c, h=h: basisT[:, h * 1024 + c * 128:
                                                h * 1024 + (c + 1) * 128],
                    idbf[:], [l1tok, (tr_toks[-1] if tr_toks else None)],
                    ps_free=(tr_toks[-1] if tr_toks else None))
                tr_toks.append(ev)
        for x in range(4):
            for c in range(2):
                for h in range(2):
                    ev, tt = transpose_tile(
                        kb, lambda _c, x=x, c=c, h=h: cf_b[:, x * 512 + c * 256 +
                                                           h * 128: x * 512 + c * 256 +
                                                           (h + 1) * 128],
                        1, ps_trC[:],
                        lambda _c, x=x, c=c, h=h: cfT[:, x * 512 + h * 256 + c * 128:
                                                      x * 512 + h * 256 + (c + 1) * 128],
                        idbf[:], [l1tok, tr_toks[-1]], ps_free=tr_toks[-1])
                    tr_toks.append(ev)
        trC_done = tr_toks[-1]

        wsmall_ev = None
        for x, dstb in enumerate((qw_b, kw_b, vw_b)):
            for vc in range(8):
                def _(t, x=x, vc=vc, w=trC_done, w2=wsmall_ev):
                    wait(t, w, w2)
                    for bc in range(2):
                        ins = t.matmul(ps_c0[:, 0:256],
                                       basisT[:, bc * 1024 + vc * 128:bc * 1024 + (vc + 1) * 128],
                                       cfT[:, x * 512 + bc * 256: x * 512 + (bc + 1) * 256],
                                       start=(bc == 0), stop=(bc == 1))
                    ct.inc(ins)
                kb.t(_)
                mt = ct.tok()

                def _(v, dstb=dstb, vc=vc, mt=mt):
                    wait(v, mt)
                    cv.inc(v.tensor_copy(dstb[:, vc * 256:(vc + 1) * 256], ps_c0[:, 0:256]))
                kb.v(_)
                wsmall_ev = cv.tok()
        for chc in range(2):
            def _(t, chc=chc, w=trC_done, w2=wsmall_ev):
                wait(t, w, w2)
                for bc in range(2):
                    for half in range(2):
                        ins = t.matmul(ps_c1[:, half * 512:(half + 1) * 512],
                                       cfT[:, 3 * 512 + bc * 256 + chc * 128:
                                           3 * 512 + bc * 256 + (chc + 1) * 128],
                                       basisT[:, bc * 1024 + half * 512:
                                              bc * 1024 + (half + 1) * 512],
                                       start=(bc == 0), stop=(bc == 1))
                ct.inc(ins)
            kb.t(_)
            mt = ct.tok()

            def _(v, chc=chc, mt=mt):
                wait(v, mt)
                cv.inc(v.tensor_copy(owT_b[:, chc * 1024:(chc + 1) * 1024], ps_c1[:]))
            kb.v(_)
            wsmall_ev = cv.tok()

        G_sb = kb.at("G_sb", [128, 4096], BF16, REG1)
        m01 = kb.at("m01", [128, 4096], BF16, REG1 + 8192)
        dio = kb.at("dio", [128, 4096], F32, WS)

        def _(g, w=wsmall_ev):
            wait(g, w, trC_done)
            cg.inc(g.iota(dio[:], [[1, 4096]], base=-1920, channel_multiplier=-1,
                          allow_small_or_imprecise_dtypes=True))
        kb.g(_)
        gio = cg.tok()

        def _(v, w=gio):
            wait(v, w)
            cv.inc(v.tensor_single_scalar(m01[:], dio[:], 0.0, ALU.is_gt))
        kb.v(_)
        vm01 = cv.tok()

        def _(v, w=vm01):
            wait(v, w)
            cv.inc(v.tensor_scalar(dio[:], dio[:], 1.0, 0.0, ALU.subtract, ALU.max))
        kb.v(_)
        vexp = cv.tok()

        def _(s, w=vexp, w2=bc_tok):
            wait(s, w, w2)
            cs.inc(s.activation(G_sb[:], dio[:], AF.Exp, scale=lnd_pp[:, 0:1]))
        kb.s(_)
        sG = cs.tok()

        def _(v, w=sG):
            wait(v, w)
            cv.inc(v.tensor_mul(G_sb[:], G_sb[:], m01[:]))
        kb.v(_)
        G_done = cv.tok()
        kb.barrier()
        scope_c0.close()

        scope_c1 = ExitStack()
        tmc = kb.at("tmc", [128, D], BF16, WS + 8192)
        ps_trD = kb.psum("ps_trD", [128, 1024], BF16, scope_c1)
        stok = None
        for i in range(NT):
            stok = lnb.stats(X2[:, i * D:(i + 1) * D], i, [], prev=stok)
        rtok_b = lnb.rstd_batch(0, 16, [stok])
        aptok = None
        evtok = None
        trtok = None
        for i in range(NT):
            aptok = lnb.apply(X2[:, i * D:(i + 1) * D], tmc[:], i, [rtok_b, trtok])
            evtok, trtok = transpose_tile(
                kb, lambda c: tmc[:, c * 128:(c + 1) * 128], ND, ps_trD[:],
                lambda c, i=i: act_fm[:, c * T + i * 128: c * T + (i + 1) * 128],
                idbf[:], [aptok], ps_free=evtok)
        c1_done = evtok
        kb.dbg("n3_fm", act_fm[:], c1_done)
        kb.flush_dbg()
        kb.barrier()
        scope_c1.close()

        scope_c2 = ExitStack()
        queries_b = kb.at("queries_b", [128, 2 * T], BF16, REG2)
        keys_b = kb.at("keys_b", [128, 2 * T], BF16, REG2 + 8192)
        values_b = kb.at("values_b", [128, 2 * T], BF16, REG2 + 16384)
        values_tm = kb.at("values_tm", [128, NT * 256], BF16, REG2 + 24576)
        ps_p = [kb.psum("ps_p0", [128, 512], F32, scope_c2),
                kb.psum("ps_p1", [128, 512], F32, scope_c2)]
        pbank = {0: None, 1: None}
        for x, dstb in enumerate((qw_b, kw_b, vw_b)):
            outb = (queries_b, keys_b, values_b)[x]
            for chc in range(2):
                for tg in range(4):
                    bk = tg % 2

                    def _(t, x=x, chc=chc, tg=tg, bk=bk, dstb=dstb, w=pbank[bk]):
                        wait(t, w)
                        for vc in range(8):
                            ins = t.matmul(
                                ps_p[bk][:],
                                dstb[:, vc * 256 + chc * 128:vc * 256 + (chc + 1) * 128],
                                act_fm[:, vc * T + tg * 512: vc * T + (tg + 1) * 512],
                                start=(vc == 0), stop=(vc == 7))
                        ct.inc(ins)
                    kb.t(_)
                    mt = ct.tok()

                    def _(v, outb=outb, chc=chc, tg=tg, bk=bk, mt=mt):
                        wait(v, mt)
                        cv.inc(v.tensor_copy(
                            outb[:, chc * T + tg * 512: chc * T + (tg + 1) * 512],
                            ps_p[bk][:]))
                    kb.v(_)
                    pbank[bk] = cv.tok()
        qkv1_done = cv.tok()
        kb.dbg("queries_b", queries_b[:], qkv1_done)

        ps_trE = kb.psum("ps_trE", [128, 1024], BF16, scope_c2)
        vt_ev = None
        for j in range(NT):
            for chc in range(2):
                vt_ev, _tt = transpose_tile(
                    kb, lambda _c, j=j, chc=chc: values_b[:, chc * T + j * 128:
                                                          chc * T + (j + 1) * 128],
                    1, ps_trE[:],
                    lambda _c, j=j, chc=chc: values_tm[:, j * 256 + chc * 128:
                                                       j * 256 + (chc + 1) * 128],
                    idbf[:], [qkv1_done, vt_ev], ps_free=vt_ev)
        vtm_done = vt_ev
        kb.barrier()

        Sp_sb = kb.at("Sp_sb", [128, T], BF16, WS)
        SpT = kb.at("SpT", [128, 16 * 512], BF16, WS + 4096)
        retr_b = kb.at("retr_b", [128, 2 * 512], BF16, WS + 20480)
        a1raw = kb.at("a1raw", [128, 8 * D], BF16, REG1 + 8192)
        a1ln = kb.sbuf("a1ln", [128, D], BF16)
        ps_at2 = kb.psum("ps_at2", [128, 1024], BF16, scope_c2)
        ps_o1 = kb.psum("ps_o1", [128, 1024], F32, scope_c2)
        dx4 = kb.dsem("x4")
        sbk = {0: None, 1: None}
        at_tok = None
        spt_free = None
        sp_free = None
        retr_free = None
        o1_free = None
        a1_free = None
        stok = None
        addtok = None
        for bg in range(4):
            ts0 = 512 * bg
            retr_toks = []
            sp_done = []
            for u in range(4):
                i = 4 * bg + u
                Wd = T - ts0
                nbk = Wd // 512
                ev_toks = []
                for b in range(nbk):
                    w0c = ts0 + b * 512
                    bk = b % 2

                    def _(t, i=i, w0c=w0c, bk=bk, w=sbk[bk]):
                        wait(t, w, qkv1_done)
                        for chc in range(2):
                            ins = t.matmul(
                                ps_p[bk][:],
                                queries_b[:, chc * T + i * 128: chc * T + (i + 1) * 128],
                                keys_b[:, chc * T + w0c: chc * T + w0c + 512],
                                start=(chc == 0), stop=(chc == 1))
                        ct.inc(ins)
                    kb.t(_)
                    mt = ct.tok()

                    def _(v, i=i, w0c=w0c, bk=bk, mt=mt, G_done=G_done,
                          spf=(sp_free if b == 0 else None)):
                        wait(v, mt, G_done, spf)
                        cv.inc(v.scalar_tensor_tensor(
                            Sp_sb[:, w0c - ts0: w0c - ts0 + 512], ps_p[bk][:], 1.0,
                            G_sb[:, 1920 - 128 * i + w0c: 1920 - 128 * i + w0c + 512],
                            ALU.mult, ALU.mult))
                    kb.v(_)
                    sbk[bk] = cv.tok()
                    ev_toks.append(cv.tok())

                for grp8 in range((16 - 4 * bg + 7) // 8):
                    j0 = 4 * bg + grp8 * 8
                    j1 = min(16, j0 + 8)

                    def _(t, i=i, j0=j0, j1=j1, ts0=ts0, w=list(ev_toks),
                          pf=at_tok, sf=(spt_free if (u == 0 and grp8 == 0) else None)):
                        wait(t, pf, sf, *w)
                        for j in range(j0, j1):
                            ins = t.transpose(ps_at2[:, (j - j0) * 128:(j - j0 + 1) * 128],
                                              Sp_sb[:, j * 128 - ts0: (j + 1) * 128 - ts0],
                                              idbf[:])
                        ct.inc(ins)
                    kb.t(_)
                    tt = ct.tok()
                    sp_free = tt

                    def _(v, i=i, u=u, j0=j0, j1=j1, bg=bg, tt=tt):
                        wait(v, tt)
                        for j in range(j0, j1):
                            ins = v.tensor_copy(
                                SpT[:, (j - 4 * bg) * 512 + u * 128:
                                    (j - 4 * bg) * 512 + (u + 1) * 128],
                                ps_at2[:, (j - j0) * 128:(j - j0 + 1) * 128])
                        cv.inc(ins)
                    kb.v(_)
                    at_tok = cv.tok()
                sp_done.append(at_tok)

            for chc in range(2):
                def _(t, chc=chc, bg=bg, w=list(sp_done), rf=retr_free):
                    wait(t, vtm_done, rf, *w)
                    njc = 16 - 4 * bg
                    for jj in range(njc):
                        j = 4 * bg + jj
                        ins = t.matmul(ps_p[chc][:],
                                       values_tm[:, j * 256 + chc * 128:
                                                 j * 256 + (chc + 1) * 128],
                                       SpT[:, jj * 512:(jj + 1) * 512],
                                       start=(jj == 0), stop=(jj == njc - 1))
                    ct.inc(ins)
                kb.t(_)
                mt = ct.tok()

                def _(v, chc=chc, mt=mt):
                    wait(v, mt)
                    cv.inc(v.tensor_copy(retr_b[:, chc * 512:(chc + 1) * 512],
                                         ps_p[chc][:]))
                kb.v(_)
                retr_toks.append(cv.tok())
            spt_free = ct.tok()
            retr_done = cv.tok()
            sbk[0] = retr_done
            sbk[1] = retr_done

            for u in range(4):
                i = 4 * bg + u

                def _(t, u=u, w=retr_done, of=o1_free):
                    wait(t, w, of)
                    for chc in range(2):
                        for half in range(2):
                            ins = t.matmul(
                                ps_o1[:, half * 512:(half + 1) * 512],
                                retr_b[:, chc * 512 + u * 128: chc * 512 + (u + 1) * 128],
                                owT_b[:, chc * 1024 + half * 512:
                                      chc * 1024 + (half + 1) * 512],
                                start=(chc == 0), stop=(chc == 1))
                    ct.inc(ins)
                kb.t(_)
                otok = ct.tok()
                retr_free = otok

                def _(v, i=i, w=otok, af=a1_free):
                    wait(v, w, af)
                    cv.inc(v.tensor_copy(a1raw[:, (i % 8) * D:(i % 8 + 1) * D],
                                         ps_o1[:]))
                kb.v(_)
                o1_free = cv.tok()

                stok = lnb.stats(a1raw[:, (i % 8) * D:(i % 8 + 1) * D], i % 8,
                                 [o1_free, bc_tok], f_ap=osc_pp[:, 0:1],
                                 prev=stok)

            if bg % 2 == 1:
                rtok_b = lnb.rstd_batch(0, 8, [stok, o1_free])
                aptok = None
                for u8 in range(8):
                    i = (bg - 1) * 4 + u8
                    aptok = lnb.apply(a1raw[:, u8 * D:(u8 + 1) * D],
                                      a1ln[:], u8, [rtok_b, addtok], scaled=True)

                    def _(v, i=i, w=aptok):
                        wait(v, w)
                        cv.inc(v.tensor_add(X2[:, i * D:(i + 1) * D],
                                            X2[:, i * D:(i + 1) * D], a1ln[:]))
                    kb.v(_)
                    addtok = cv.tok()

                    def _(sync, i=i, w=addtok):
                        wait(sync, w)
                        dx4.inc(sync.dma_start(out=x4_sc[i * 128:(i + 1) * 128, :],
                                               in_=X2[:, i * D:(i + 1) * D]))
                    kb.q(_)
                a1_free = cv.tok()
                stok = None
        c4_done = cv.tok()
        kb.dbg("x4", X2[:], c4_done)
        kb.flush_dbg()
        kb.barrier()
        scope_c2.close()


    if "D" in phases:
        H128 = 1.0 / float(np.sqrt(128.0))
        sim1_b = kb.sbuf("sim1_b", [128, 64], BF16)
        sim2_b = kb.sbuf("sim2_b", [128, 64], BF16)
        simsq = kb.sbuf("simsq", [128, 64], F32)
        rn_pp = kb.sbuf("rn_pp", [128, 16], F32)
        sg_pp = kb.sbuf("sg_pp", [128, 16], F32)
        lnw_pp = kb.sbuf("lnw_pp", [128, 32], F32)
        rs_all = kb.sbuf("rs_all", [128, 16], F32)
        lgb = kb.sbuf("lgb", [128, 8], F32)
        t8b = kb.sbuf("t8b", [128, 8], F32)
        mkb_ = kb.sbuf("mkb_", [128, 24], F32)
        rw_t = kb.sbuf("rw_t", [128, 8], BF16)
        tmd = kb.sbuf("tmd", [128, D], BF16)
        idf32 = kb.sbuf("idf32", [128, 128], F32)
        sel8 = kb.sbuf("sel8", [8, 8 * 128], BF16)
        nrm8 = kb.sbuf("nrm8", [1, 16], F32)
        dl2 = kb.dsem("l1f")

        def _(g):
            cg.inc(g.iota(idf32[:], [[1, 128]], base=0, channel_multiplier=-1,
                          allow_small_or_imprecise_dtypes=True))
        kb.g(_)
        gi1 = cg.tok()

        def _(g, w=gi1):
            wait(g, w)
            cg.inc(g.tensor_single_scalar(idf32[:], idf32[:], 0.0, ALU.is_equal))
        kb.g(_)
        gid32 = cg.tok()

        def _(g, w=gid32):
            wait(g, w)
            cg.inc(g.iota(sel8[:], [[1, 8], [0, 128]], base=0, channel_multiplier=-1,
                          allow_small_or_imprecise_dtypes=True))
        kb.g(_)
        gs1 = cg.tok()

        def _(g, w=gs1):
            wait(g, w)
            cg.inc(g.tensor_single_scalar(sel8[:], sel8[:], 0.0, ALU.is_equal))
        kb.g(_)
        sel_tok = cg.tok()

        def _(g):
            dl2.inc(g.dma_start(out=sim1_b[:].rearrange("p (c e) -> p c e", c=8),
                                in_=inp["L1_ffn_smha_sim"].rearrange("(c p) e -> p c e", p=128)))
            dl2.inc(g.dma_start(out=sim2_b[:].rearrange("p (c e) -> p c e", c=8),
                                in_=inp["L1_ffn_moe_sim"].rearrange("(c p) e -> p c e", p=128)))
            for k, name in enumerate(("ln1_w", "ln1_b", "ln2_w", "ln2_b")):
                dl2.inc(g.dma_start(
                    out=lnw_pp[:, k * 8:(k + 1) * 8],
                    in_=inp["L1_ffn_" + name].rearrange("one (c p) -> p (one c)", p=128),
                    allow_slow_non_contiguous=True))
        kb.g(_)
        d0tok = dl2.tok()

        scope_d0 = ExitStack()
        ps_d0 = kb.psum("ps_d0", [128, 16], F32, scope_d0)
        nrm_toks = []
        for mi, simb in enumerate((sim1_b, sim2_b)):
            def _(v, simb=simb, w=d0tok, w2=(nrm_toks[-1] if nrm_toks else None)):
                wait(v, w, w2)
                cv.inc(v.tensor_mul(simsq[:], simb[:], simb[:]))
            kb.v(_)
            sqt = cv.tok()

            def _(v, mi=mi, w=sqt):
                wait(v, w)
                cv.inc(v.tensor_reduce(
                    rn_pp[:, 8 * mi:8 * mi + 8],
                    bass.AP(simsq, 0, [[64, 128], [1, 8], [8, 8]]),
                    AX.X, ALU.add))
            kb.v(_)
            nrm_toks.append(cv.tok())

        def _(t, w=nrm_toks[-1], w2=vo2):
            wait(t, w, w2)
            t.matmul(ps_d0[0:1, 0:8], ones_c[:], rn_pp[:, 0:8], start=True, stop=True)
            ct.inc(t.matmul(ps_d0[0:1, 8:16], ones_c[:], rn_pp[:, 8:16],
                            start=True, stop=True))
        kb.t(_)
        tnorm = ct.tok()

        def _(v, w=tnorm):
            wait(v, w)
            cv.inc(v.tensor_scalar_max(nrm8[0:1, :], ps_d0[0:1, 0:16], 1e-24))
        kb.v(_)
        vnm = cv.tok()

        def _(s, w=vnm):
            wait(s, w)
            cs.inc(s.activation(nrm8[0:1, :], nrm8[0:1, :], AF.Ln))
        kb.s(_)
        snm = cs.tok()

        def _(s, w=snm):
            wait(s, w)
            cs.inc(s.activation(nrm8[0:1, :], nrm8[0:1, :], AF.Exp, scale=-0.5))
        kb.s(_)
        snm2 = cs.tok()

        def _(t, w=snm2, w2=sig_tok):
            wait(t, w, w2)
            t.matmul(ps_d0[:, 0:16], ones_f2[:], nrm8[0:1, 0:16], start=True, stop=True)
            ct.inc(t.matmul(ps_d0[:, 0:16], ones_f2[:], nrm8[0:1, 0:16],
                            start=True, stop=True))
        kb.t(_)
        tb1 = ct.tok()

        def _(v, w=tb1):
            wait(v, w)
            cv.inc(v.tensor_copy(rn_pp[:], ps_d0[:, 0:16]))
        kb.v(_)
        vb1 = cv.tok()

        def _(t, w=vb1):
            wait(t, w)
            ct.inc(t.matmul(ps_d0[:, 0:16], ones_f2[:], g8s[0:1, 0:16],
                            start=True, stop=True))
        kb.t(_)
        tb2 = ct.tok()

        def _(v, w=tb2):
            wait(v, w)
            cv.inc(v.tensor_copy(sg_pp[:], ps_d0[:, 0:16]))
        kb.v(_)
        d0_done = cv.tok()
        kb.barrier()
        scope_d0.close()

        def d_lnpass(wcol, bcol, dbgname=None):
            scope = ExitStack()
            ps_trF = kb.psum("ps_trF_%s" % wcol, [128, 1024], BF16, scope)
            stok = None
            for i in range(NT):
                stok = lnb.stats(X2[:, i * D:(i + 1) * D], i, [], prev=stok)
            rtok_b = lnb.rstd_batch(0, 16, [stok])
            aptok = None
            evtok = None
            trtok = None
            for i in range(NT):
                aptok = lnb.apply(X2[:, i * D:(i + 1) * D], tmd[:], i,
                                  [rtok_b, trtok])
                evtok, trtok = transpose_tile(
                    kb, lambda c: tmd[:, c * 128:(c + 1) * 128], ND, ps_trF[:],
                    lambda c, i=i: act_fm[:, c * T + i * 128: c * T + (i + 1) * 128],
                    idbf[:], [aptok], ps_free=evtok)
            aff_toks = []
            for c in range(ND):
                def _(v, c=c, w=evtok, w2=d0tok, w3=(aff_toks[-1] if aff_toks else None)):
                    wait(v, w, w2, w3)
                    cv.inc(v.tensor_scalar(act_fm[:, c * T:(c + 1) * T],
                                           act_fm[:, c * T:(c + 1) * T],
                                           lnw_pp[:, wcol * 8 + c:wcol * 8 + c + 1],
                                           lnw_pp[:, bcol * 8 + c:bcol * 8 + c + 1],
                                           ALU.mult, ALU.add))
                kb.v(_)
                aff_toks.append(cv.tok())
            scope.close()
            return cv.tok()

        stok = None
        for i in range(NT):
            stok = lnb.stats(X2[:, i * D:(i + 1) * D], i, [], prev=stok)
        rtok_b = lnb.rstd_batch(0, 16, [stok])
        aptok = None
        for i in range(NT):
            aptok = lnb.apply(X2[:, i * D:(i + 1) * D], X2[:, i * D:(i + 1) * D],
                              i, [rtok_b, aptok])
        kb.barrier()
        n1f_done = d_lnpass(0, 1)
        kb.dbg("n1f_fm", act_fm[:], n1f_done)
        kb.flush_dbg()
        kb.barrier()

        rw_emaj = kb.at("rw_emaj", [8, NT * 128], BF16, WS)
        rw2_emaj = kb.at("rw2_emaj", [8, NT * 128], BF16, WS + 4096)

        lgall = kb.sbuf("lgall", [128, 128], F32)
        glst = kb.sbuf("glst", [128, 16], F32)

        def d_gating(mi, dst_emaj):
            scope = ExitStack()
            ps_nn = kb.psum("ps_nn_%d" % mi, [128, 128], F32, scope)
            ps_lg = kb.psum("ps_lg_%d" % mi, [128, 8], F32, scope)
            ps_rt = kb.psum("ps_rt_%d" % mi, [128, 128], BF16, scope)
            simb = (sim1_b, sim2_b)[mi]
            prev_ap = None
            for i in range(NT):
                def _(t, i=i, w=prev_ap):
                    wait(t, w)
                    for cc in range(ND):
                        ins = t.matmul(ps_nn[:],
                                       act_fm[:, cc * T + i * 128: cc * T + (i + 1) * 128],
                                       act_fm[:, cc * T + i * 128: cc * T + (i + 1) * 128],
                                       start=(cc == 0), stop=(cc == ND - 1))
                    ct.inc(ins)
                    for cc in range(ND):
                        ins = t.matmul(ps_lg[:],
                                       act_fm[:, cc * T + i * 128: cc * T + (i + 1) * 128],
                                       simb[:, cc * 8:(cc + 1) * 8],
                                       start=(cc == 0), stop=(cc == ND - 1))
                    ct.inc(ins)
                kb.t(_)
                mt = ct.tok()

                def _(v, i=i, w=mt, w2=gid32):
                    wait(v, w, w2)
                    v.tensor_mul(sq_scr[:, 0:128], ps_nn[:], idf32[:])
                    cv.inc(v.tensor_copy(lgall[:, i * 8:(i + 1) * 8], ps_lg[:]))
                kb.v(_)
                v1 = cv.tok()

                def _(v, i=i, w=v1):
                    wait(v, w)
                    cv.inc(v.tensor_reduce(glst[:, i:i + 1], sq_scr[:, 0:128],
                                           AX.X, ALU.add))
                kb.v(_)
                prev_ap = cv.tok()

            def _(v, w=prev_ap):
                wait(v, w)
                cv.inc(v.tensor_scalar_max(glst[:], glst[:], 1e-24))
            kb.v(_)
            g1_ = cv.tok()

            def _(s, w=g1_):
                wait(s, w)
                cs.inc(s.activation(glst[:], glst[:], AF.Ln))
            kb.s(_)
            g2_ = cs.tok()

            def _(s, w=g2_):
                wait(s, w)
                cs.inc(s.activation(glst[:], glst[:], AF.Exp, scale=-0.5))
            kb.s(_)
            rln_tok = cs.tok()

            prev_tr = None
            prev_chain = None
            for i in range(NT):
                def _(v, i=i, w=rln_tok, pc=prev_chain, mi=mi):
                    wait(v, w, pc)
                    cv.inc(v.tensor_scalar_mul(lgb[:], lgall[:, i * 8:(i + 1) * 8],
                                               glst[:, i:i + 1]))
                kb.v(_)
                v4 = cv.tok()

                def _(v, w=v4, mi=mi):
                    wait(v, w)
                    cv.inc(v.tensor_mul(lgb[:], lgb[:], rn_pp[:, 8 * mi:8 * mi + 8]))
                kb.v(_)
                v5 = cv.tok()

                def _(v, w=v5, mi=mi):
                    wait(v, w)
                    cv.inc(v.tensor_sub(lgb[:], lgb[:], sg_pp[:, 8 * mi:8 * mi + 8]))
                kb.v(_)
                v6 = cv.tok()

                def _(v, w=v6):
                    wait(v, w)
                    cv.inc(v.max(t8b[:], lgb[:]))
                kb.v(_)
                v7 = cv.tok()

                def _(v, w=v7):
                    wait(v, w)
                    v.tensor_single_scalar(mkb_[:, 16:17], t8b[:, 0:1], 0.0, ALU.is_gt)
                    cv.inc(v.tensor_single_scalar(mkb_[:, 0:8], lgb[:], 0.0, ALU.is_gt))
                kb.v(_)
                v8 = cv.tok()

                def _(v, w=v8):
                    wait(v, w)
                    cv.inc(v.tensor_scalar(mkb_[:, 8:16], lgb[:], t8b[:, 1:2], None,
                                           ALU.is_ge))
                kb.v(_)
                v9 = cv.tok()

                def _(v, w=v9):
                    wait(v, w)
                    cv.inc(v.tensor_sub(mkb_[:, 0:8], mkb_[:, 0:8], mkb_[:, 8:16]))
                kb.v(_)
                v10 = cv.tok()

                def _(v, w=v10):
                    wait(v, w)
                    cv.inc(v.scalar_tensor_tensor(mkb_[:, 0:8], mkb_[:, 0:8],
                                                  mkb_[:, 16:17], mkb_[:, 8:16],
                                                  ALU.mult, ALU.add))
                kb.v(_)
                v11 = cv.tok()

                def _(v, w=v11):
                    wait(v, w)
                    cv.inc(v.tensor_relu(lgb[:], lgb[:]))
                kb.v(_)
                v12 = cv.tok()

                def _(s, w=v12):
                    wait(s, w)
                    cs.inc(s.activation(t8b[:], lgb[:], AF.Exp))
                kb.s(_)
                s3 = cs.tok()

                def _(v, w=s3):
                    wait(v, w)
                    cv.inc(v.tensor_mul(t8b[:], t8b[:], mkb_[:, 0:8]))
                kb.v(_)
                v13 = cv.tok()

                def _(v, w=v13):
                    wait(v, w)
                    cv.inc(v.tensor_reduce(sm[:, 12:13], t8b[:], AX.X, ALU.add))
                kb.v(_)
                v14 = cv.tok()

                def _(v, w=v14):
                    wait(v, w)
                    cv.inc(v.reciprocal(sm[:, 12:13], sm[:, 12:13]))
                kb.v(_)
                v15 = cv.tok()

                def _(v, w=v15, w2=prev_tr):
                    wait(v, w, w2)
                    cv.inc(v.tensor_scalar_mul(rw_t[:], t8b[:], sm[:, 12:13]))
                kb.v(_)
                v16 = cv.tok()
                prev_chain = v16

                def _(t, i=i, w=v16, w2=prev_tr):
                    wait(t, w, w2)
                    ct.inc(t.transpose(ps_rt[0:8, 0:128], rw_t[:, 0:8], idbf[:]))
                kb.t(_)
                tt = ct.tok()

                def _(v, i=i, w=tt, dst_emaj=dst_emaj):
                    wait(v, w)
                    cv.inc(v.tensor_copy(dst_emaj[0:8, i * 128:(i + 1) * 128],
                                         ps_rt[0:8, 0:128]))
                kb.v(_)
                prev_tr = cv.tok()
            scope.close()
            return prev_tr

        rw_done = d_gating(0, rw_emaj)
        kb.dbg("rw_emaj", rw_emaj[0:8, :], rw_done)
        kb.flush_dbg()
        kb.barrier()

        scope_d3a = ExitStack()
        qp_b = kb.at("qp_b", [128, 64 * 128], BF16, REG1)
        kp_b = kb.at("kp_b", [128, 64 * 128], BF16, REG1 + 16384)
        vp_b = kb.at("vp_b", [128, 64 * 128], BF16, REG2)
        n1e = kb.at("n1e", [128, 8 * 512], BF16, REG2 + 16384)
        rwb_eg = kb.at("rwb_eg", [128, 512], BF16, REG2 + 24576)
        q_fm_d = kb.at("q_fm_d", [128, T], BF16, WS + 8192)
        k_fm_d = kb.at("k_fm_d", [128, T], BF16, WS + 12288)
        v_fm_d = kb.at("v_fm_d", [128, T], BF16, WS + 16384)
        dpq = kb.dsem("pq")

        def _(g):
            for nm, dst in (("q_proj", qp_b), ("k_proj", kp_b), ("v_proj", vp_b)):
                dpq.inc(g.dma_start(
                    out=dst[:].rearrange("p (ec h) -> p ec h", h=128),
                    in_=inp["L1_ffn_" + nm].rearrange("(ec p) h -> p ec h", p=128)))
        kb.g(_)
        pq_tok = dpq.tok()

        ps_q3 = kb.psum("ps_q3", [128, 512], F32, scope_d3a)
        ps_k3 = kb.psum("ps_k3", [128, 512], F32, scope_d3a)
        ps_v3 = kb.psum("ps_v3", [128, 512], F32, scope_d3a)
        ps_bc3 = kb.psum("ps_bc3", [128, 512], F32, scope_d3a)
        n1e_free = None
        qkv_ev = {0: None, 1: None, 2: None}
        for g_ in range(4):
            for e in range(8):
                def _(t, g_=g_, e=e, w=rw_done, w2=sel_tok):
                    wait(t, w, w2)
                    ct.inc(t.matmul(ps_bc3[:], sel8[0:8, e * 128:(e + 1) * 128],
                                    rw_emaj[0:8, g_ * 512:(g_ + 1) * 512],
                                    start=True, stop=True))
                kb.t(_)
                bct = ct.tok()

                def _(v, w=bct, nf=n1e_free):
                    wait(v, w, nf)
                    cv.inc(v.tensor_copy(rwb_eg[:], ps_bc3[:]))
                kb.v(_)
                bce = cv.tok()

                def _(v, g_=g_, w=bce):
                    wait(v, w)
                    for cc in range(ND):
                        ins = v.tensor_mul(n1e[:, cc * 512:(cc + 1) * 512],
                                           act_fm[:, cc * T + g_ * 512:
                                                  cc * T + (g_ + 1) * 512],
                                           rwb_eg[:])
                    cv.inc(ins)
                kb.v(_)
                n1etok = cv.tok()

                def _(t, g_=g_, e=e, w=n1etok, w2=pq_tok,
                      ev=[qkv_ev[x] for x in range(3)] if (g_ > 0 and e == 0) else []):
                    wait(t, w, w2, *ev)
                    for cc in range(ND):
                        fl = (e * 8 + cc) * 128
                        st_ = (e == 0 and cc == 0)
                        sp = (e == 7 and cc == ND - 1)
                        t.matmul(ps_q3[:], qp_b[:, fl:fl + 128],
                                 n1e[:, cc * 512:(cc + 1) * 512], start=st_, stop=sp)
                        t.matmul(ps_k3[:], kp_b[:, fl:fl + 128],
                                 n1e[:, cc * 512:(cc + 1) * 512], start=st_, stop=sp)
                        ins = t.matmul(ps_v3[:], vp_b[:, fl:fl + 128],
                                       n1e[:, cc * 512:(cc + 1) * 512],
                                       start=st_, stop=sp)
                    ct.inc(ins)
                kb.t(_)
                n1e_free = ct.tok()
            mtok = ct.tok()

            def _(v, g_=g_, w=mtok):
                wait(v, w)
                v.tensor_scalar_mul(q_fm_d[:, g_ * 512:(g_ + 1) * 512], ps_q3[:], H128)
                v.tensor_copy(k_fm_d[:, g_ * 512:(g_ + 1) * 512], ps_k3[:])
                cv.inc(v.tensor_copy(v_fm_d[:, g_ * 512:(g_ + 1) * 512], ps_v3[:]))
            kb.v(_)
            for x in range(3):
                qkv_ev[x] = cv.tok()
        d3a_done = cv.tok()
        kb.dbg("qf_d", q_fm_d[:], d3a_done)
        kb.flush_dbg()
        kb.barrier()
        scope_d3a.close()

        scope_d3b = ExitStack()
        v_tm_d = kb.at("v_tm_d", [128, NT * 128], BF16, REG1)
        attn_fm_d = kb.at("attn_fm_d", [128, T], BF16, REG1 + 4096)
        S2_sb = kb.at("S2_sb", [128, T], F32, REG2)
        P2_sb = kb.at("P2_sb", [128, T], BF16, REG2 + 8192)
        PT_sb = kb.at("PT_sb", [128, T], BF16, REG1 + 8192)
        ps_s3 = [kb.psum("ps_s3a", [128, 512], F32, scope_d3b),
                 kb.psum("ps_s3b", [128, 512], F32, scope_d3b)]
        ps_at3 = kb.psum("ps_at3", [128, 1024], BF16, scope_d3b)
        ps_pv = kb.psum("ps_pv", [128, 128], F32, scope_d3b)

        vt_ev = None
        for j in range(NT):
            vt_ev, _tt = transpose_tile(
                kb, lambda _c, j=j: v_fm_d[:, j * 128:(j + 1) * 128],
                1, ps_at3[:],
                lambda _c, j=j: v_tm_d[:, j * 128:(j + 1) * 128],
                idbf[:], [d3a_done, vt_ev], ps_free=vt_ev)
        vtm2_done = vt_ev

        sbk3 = {0: None, 1: None}
        at3_tok = vtm2_done
        p2_free = None
        s2_free = None
        pv_free = None
        attn_ev = None
        sexp_prev = None
        for i in range(NT):
            W = 128 * (i + 1)
            nb = (W + 511) // 512
            ev_toks = []
            for b in range(nb):
                w0c = b * 512
                w1c = min(W, w0c + 512)
                bk = b % 2

                def _(t, i=i, w0c=w0c, w1c=w1c, bk=bk, w=sbk3[bk]):
                    wait(t, w)
                    ct.inc(t.matmul(ps_s3[bk][:, 0:w1c - w0c],
                                    q_fm_d[:, i * 128:(i + 1) * 128],
                                    k_fm_d[:, w0c:w1c], start=True, stop=True))
                kb.t(_)
                mt = ct.tok()

                def _(v, w0c=w0c, w1c=w1c, bk=bk, mt=mt,
                      sf=(s2_free if b == 0 else None)):
                    wait(v, mt, sf)
                    cv.inc(v.tensor_copy(S2_sb[:, w0c:w1c], ps_s3[bk][:, 0:w1c - w0c]))
                kb.v(_)
                sbk3[bk] = cv.tok()
                ev_toks.append(cv.tok())

            def _(g, i=i, ev=ev_toks[-1]):
                wait(g, ev)
                cg.inc(g.affine_select(S2_sb[:, i * 128:(i + 1) * 128],
                                       S2_sb[:, i * 128:(i + 1) * 128],
                                       [[-1, 128]], ALU.is_ge, -1e30,
                                       base=0, channel_multiplier=1))
            kb.g(_)
            gmask = cg.tok()

            def _(v, W=W, gmask=gmask, ev=list(ev_toks), sp=sexp_prev):
                wait(v, gmask, sp, *ev)
                cv.inc(v.reduce_max(sm[:, 8:9], S2_sb[:, 0:W], AX.X, negate=True))
            kb.v(_)
            vmax = cv.tok()

            def _(s, W=W, vmax=vmax, pf=p2_free):
                wait(s, vmax, pf)
                cs.inc(s.activation(P2_sb[:, 0:W], S2_sb[:, 0:W], AF.Exp,
                                    bias=sm[:, 8:9], accum_out=ssum[:, 0:1]))
            kb.s(_)
            sexp = cs.tok()
            sexp_prev = sexp
            s2_free = sexp

            def _(v, i=i, w=sexp):
                wait(v, w)
                cv.inc(v.reciprocal(rs_all[:, i:i + 1], ssum[:, 0:1]))
            kb.v(_)

            at_toks = []
            for bg2 in range((i + 8) // 8):
                j0, j1 = bg2 * 8, min(i + 1, bg2 * 8 + 8)

                def _(t, j0=j0, j1=j1, sexp=sexp, w=at3_tok):
                    wait(t, sexp, w)
                    for j in range(j0, j1):
                        ins = t.transpose(ps_at3[:, (j - j0) * 128:(j - j0 + 1) * 128],
                                          P2_sb[:, j * 128:(j + 1) * 128], idbf[:])
                    ct.inc(ins)
                kb.t(_)
                ttok = ct.tok()
                p2_free = ttok

                def _(v, j0=j0, j1=j1, ttok=ttok):
                    wait(v, ttok)
                    cv.inc(v.tensor_copy(PT_sb[:, j0 * 128:j1 * 128],
                                         ps_at3[:, 0:(j1 - j0) * 128]))
                kb.v(_)
                at3_tok = cv.tok()
                at_toks.append(cv.tok())

            def _(t, i=i, w=list(at_toks), pvf=pv_free):
                wait(t, vtm2_done, pvf, *w)
                for j in range(i + 1):
                    ins = t.matmul(ps_pv[:], v_tm_d[:, j * 128:(j + 1) * 128],
                                   PT_sb[:, j * 128:(j + 1) * 128],
                                   start=(j == 0), stop=(j == i))
                ct.inc(ins)
            kb.t(_)
            pvt = ct.tok()

            def _(v, i=i, w=pvt, ae=attn_ev):
                wait(v, w)
                cv.inc(v.tensor_copy(attn_fm_d[:, i * 128:(i + 1) * 128], ps_pv[:]))
            kb.v(_)
            attn_ev = cv.tok()
            pv_free = attn_ev
        d3b_done = attn_ev
        kb.barrier()
        scope_d3b.close()

        op_b = kb.at("op_b", [128, 8 * 1024], BF16, REG1 + 16384)
        rwb_all = kb.at("rwb_all", [128, 8 * T], BF16, REG2)
        attn_e = kb.at("attn_e", [128, 512], BF16, WS + 20480)
        dop = kb.dsem("op")

        def _(g):
            dop.inc(g.dma_start(
                out=op_b[:].rearrange("p (e c) -> p e c", e=8),
                in_=inp["L1_ffn_o_proj"].rearrange("(e p) c -> p e c", p=128)))
        kb.g(_)
        op_tok = dop.tok()

        scope_bcc = ExitStack()
        ps_bc3 = kb.psum("ps_bcc", [128, 512], F32, scope_bcc)
        rwb_toks = []
        for e in range(8):
            for g_ in range(4):
                def _(t, e=e, g_=g_, w=rw_done,
                      w2=(rwb_toks[-1] if rwb_toks else None)):
                    wait(t, w, w2, sel_tok)
                    ct.inc(t.matmul(ps_bc3[:], sel8[0:8, e * 128:(e + 1) * 128],
                                    rw_emaj[0:8, g_ * 512:(g_ + 1) * 512],
                                    start=True, stop=True))
                kb.t(_)
                bt_ = ct.tok()

                def _(v, e=e, g_=g_, w=bt_):
                    wait(v, w)
                    cv.inc(v.tensor_copy(rwb_all[:, e * T + g_ * 512:
                                                 e * T + (g_ + 1) * 512], ps_bc3[:]))
                kb.v(_)
                rwb_toks.append(cv.tok())
        rwb_done = cv.tok()
        kb.barrier()
        scope_bcc.close()

        scope_d3c = ExitStack()
        ps_om = [kb.psum("ps_om%d" % u, [128, 1024], F32, scope_d3c) for u in range(4)]
        om_free = {u: None for u in range(4)}
        ae_free = None
        for g_ in range(4):
            add_toks = []
            for e in range(8):
                def _(v, g_=g_, e=e, w=rwb_done, w2=d3b_done, af=ae_free):
                    wait(v, w, w2, af)
                    cv.inc(v.tensor_mul(attn_e[:],
                                        attn_fm_d[:, g_ * 512:(g_ + 1) * 512],
                                        rwb_all[:, e * T + g_ * 512:e * T + (g_ + 1) * 512]))
                kb.v(_)
                aet = cv.tok()

                def _(t, g_=g_, e=e, w=aet, w2=op_tok,
                      omf=[om_free[u] for u in range(4)] if e == 0 else []):
                    wait(t, w, w2, *omf)
                    for u in range(4):
                        for half in range(2):
                            ins = t.matmul(
                                ps_om[u][:, half * 512:(half + 1) * 512],
                                attn_e[:, u * 128:(u + 1) * 128],
                                op_b[:, e * 1024 + half * 512:e * 1024 + (half + 1) * 512],
                                start=(e == 0), stop=(e == 7))
                    ct.inc(ins)
                kb.t(_)
                ae_free = ct.tok()
            omt = ct.tok()

            for u in range(4):
                i = g_ * 4 + u

                def _(v, u=u, i=i, w=omt):
                    wait(v, w)
                    cv.inc(v.scalar_tensor_tensor(X2[:, i * D:(i + 1) * D],
                                                  ps_om[u][:], rs_all[:, i:i + 1],
                                                  X2[:, i * D:(i + 1) * D],
                                                  ALU.mult, ALU.add))
                kb.v(_)
                om_free[u] = cv.tok()
                add_toks.append(cv.tok())
        d3c_done = cv.tok()
        kb.dbg("hs", X2[:], d3c_done)
        kb.flush_dbg()
        kb.barrier()
        scope_d3c.close()

        n2f_done = d_lnpass(2, 3)
        kb.dbg("n2f_fm", act_fm[:], n2f_done)
        kb.barrier()
        rw2_done = d_gating(1, rw2_emaj)
        kb.dbg("rw2_emaj", rw2_emaj[0:8, :], rw2_done)
        kb.flush_dbg()
        kb.barrier()

        scope_d5 = ExitStack()
        w1b = kb.at("w1b", [128, 2 * 8 * 1024], BF16, REG1)
        w2b = kb.at("w2b", [128, 2 * 8 * 1024], BF16, REG2)
        scaled = kb.at("scaled", [128, 8 * 512], BF16, WS + 8192)
        rw2b_eg = kb.at("rw2b_eg", [128, 512], BF16, WS + 16384)
        dmw = [kb.dsem("mw0"), kb.dsem("mw1")]
        ps_i5 = [kb.psum("ps_i5a", [128, 1024], F32, scope_d5),
                 kb.psum("ps_i5b", [128, 1024], F32, scope_d5)]
        ps_m5 = [kb.psum("ps_m5a", [128, 512], F32, scope_d5),
                 kb.psum("ps_m5b", [128, 512], F32, scope_d5)]
        ps_bc5 = kb.psum("ps_bc5", [128, 512], F32, scope_d5)
        W1f = inp["L1_ffn_w1"].rearrange("(ec p) i -> p ec i", p=128)
        W2f = inp["L1_ffn_w2"].rearrange("(ec p) i -> p ec i", p=128)
        wslot_tok = {0: None, 1: None}
        ibank_tok = {0: None, 1: None}
        mbank_tok = {0: None, 1: None}
        sc_free = None
        rwb5_free = None
        for e in range(8):
            slot = e % 2

            def _(g, e=e, slot=slot, w=wslot_tok[slot], ds=dmw[slot]):
                wait(g, w)
                ds.inc(g.dma_start(
                    out=w1b[:, slot * 8192:(slot + 1) * 8192].rearrange(
                        "p (c i) -> p c i", c=8),
                    in_=W1f[:, e * 8:(e + 1) * 8, :]))
                ds.inc(g.dma_start(
                    out=w2b[:, slot * 8192:(slot + 1) * 8192].rearrange(
                        "p (c i) -> p c i", c=8),
                    in_=W2f[:, e * 8:(e + 1) * 8, :]))
            kb.g(_)
            wtok = dmw[slot].tok()

            for g_ in range(4):
                def _(t, e=e, g_=g_, w=rw2_done, rf=rwb5_free):
                    wait(t, w, rf, sel_tok)
                    ct.inc(t.matmul(ps_bc5[:], sel8[0:8, e * 128:(e + 1) * 128],
                                    rw2_emaj[0:8, g_ * 512:(g_ + 1) * 512],
                                    start=True, stop=True))
                kb.t(_)
                bct = ct.tok()

                def _(v, w=bct, sf=sc_free):
                    wait(v, w, sf)
                    cv.inc(v.tensor_copy(rw2b_eg[:], ps_bc5[:]))
                kb.v(_)
                rwb5 = cv.tok()

                sc_toks = []
                for ipr in range(4):
                    bk = ipr % 2

                    def _(t, e=e, g_=g_, ipr=ipr, bk=bk, slot=slot, wtok=wtok,
                          w=ibank_tok[bk], w2=n2f_done):
                        wait(t, wtok, w, w2)
                        for half in range(2):
                            ic = 2 * ipr + half
                            for cc in range(ND):
                                ins = t.matmul(
                                    ps_i5[bk][:, half * 512:(half + 1) * 512],
                                    w1b[:, slot * 8192 + cc * 1024 + ic * 128:
                                        slot * 8192 + cc * 1024 + (ic + 1) * 128],
                                    act_fm[:, cc * T + g_ * 512: cc * T + (g_ + 1) * 512],
                                    start=(cc == 0), stop=(cc == ND - 1))
                        ct.inc(ins)
                    kb.t(_)
                    imt = ct.tok()

                    def _(s, ipr=ipr, bk=bk, w=imt, sf=sc_free if ipr < 2 else None):
                        wait(s, w, sf)
                        cs.inc(s.activation(scaled[:, ipr * 1024:(ipr + 1) * 1024],
                                            ps_i5[bk][:], AF.Gelu))
                    kb.s(_)
                    gt = cs.tok()
                    ibank_tok[bk] = gt

                    def _(v, ipr=ipr, w=gt, w2=rwb5):
                        wait(v, w, w2)
                        for half in range(2):
                            ic = 2 * ipr + half
                            ins = v.tensor_mul(scaled[:, ic * 512:(ic + 1) * 512],
                                               scaled[:, ic * 512:(ic + 1) * 512],
                                               rw2b_eg[:])
                        cv.inc(ins)
                    kb.v(_)
                    sc_toks.append(cv.tok())
                rwb5_free = ct.tok()

                for u in range(4):
                    i = g_ * 4 + u
                    for half in range(2):
                        bk = (2 * u + half) % 2

                        def _(t, u=u, half=half, e=e, slot=slot, bk=bk,
                              w=list(sc_toks), mf=mbank_tok[bk]):
                            wait(t, mf, *w)
                            for ic in range(ND):
                                ins = t.matmul(
                                    ps_m5[bk][:],
                                    scaled[:, ic * 512 + u * 128:ic * 512 + (u + 1) * 128],
                                    w2b[:, slot * 8192 + ic * 1024 + half * 512:
                                        slot * 8192 + ic * 1024 + (half + 1) * 512],
                                    start=(ic == 0), stop=(ic == ND - 1))
                            ct.inc(ins)
                        kb.t(_)
                        omt = ct.tok()
                        sc_free = omt

                        def _(v, i=i, half=half, bk=bk, w=omt):
                            wait(v, w)
                            dd = X2[:, i * D + half * 512: i * D + (half + 1) * 512]
                            cv.inc(v.tensor_add(dd, dd, ps_m5[bk][:]))
                        kb.v(_)
                        mbank_tok[bk] = cv.tok()
            wslot_tok[slot] = ct.tok()
        d5_done = cv.tok()
        kb.dbg("f1", X2[:], d5_done)
        kb.flush_dbg()
        kb.barrier()
        scope_d5.close()

        xt4 = kb.at("xt4", [128, 2 * D], F32, WS)
        outt = kb.at("outt", [128, 2 * D], F32, WS + 8192)
        dx4r = [kb.dsem("x4r0"), kb.dsem("x4r1")]
        douts = [kb.dsem("out0"), kb.dsem("out1")]
        stok = None
        for i in range(NT):
            stok = lnb.stats(X2[:, i * D:(i + 1) * D], i, [], prev=stok)
        rtok_b = lnb.rstd_batch(0, 16, [stok])
        slot_done6 = {0: None, 1: None}
        aptok = None
        addtok = None
        for i in range(NT):
            slot = i % 2
            xa = xt4[:, slot * D:(slot + 1) * D]
            oa = outt[:, slot * D:(slot + 1) * D]

            def _(sync, i=i, xa=xa, w=slot_done6[slot], ds=dx4r[slot]):
                wait(sync, w)
                ds.inc(sync.dma_start(out=xa, in_=x4_sc[i * 128:(i + 1) * 128, :]))
            kb.q(_)
            ldtok = dx4r[slot].tok()

            aptok = lnb.apply(X2[:, i * D:(i + 1) * D], tmd[:], i,
                              [rtok_b, aptok, addtok])

            def _(v, i=i, xa=xa, oa=oa, w=aptok, w2=ldtok, w3=slot_done6[slot]):
                wait(v, w, w2, w3)
                cv.inc(v.tensor_add(oa, xa, tmd[:]))
            kb.v(_)
            addtok = cv.tok()

            def _(sync, i=i, oa=oa, w=addtok, ds=douts[slot]):
                wait(sync, w)
                ds.inc(sync.dma_start(out=out[i * 128:(i + 1) * 128, :], in_=oa))
            kb.q(_)
            slot_done6[slot] = douts[slot].tok()

        def _(sync):
            wait(sync, douts[0].tok(), douts[1].tok())
        kb.q(_)

    if "final_stub" in DEBUG:
        def _(sync):
            for i in range(NT):
                dq.inc(sync.dma_start(out=out[i * 128:(i + 1) * 128, :],
                                      in_=outln_ap(i)))
            wait(sync, dq.tok())
        kb.q(_)

    rem = nc.sbuf_bytes_remaining
    assert rem >= 229376 - REG1, f"stack smalls overflow into manual arena: remaining={rem}"
    ctx.close()
    return nc


def shard_inputs(inputs, core):
    m = {}
    for name, shape in INPUT_SPECS:
        v = np.asarray(inputs[name])
        if name == "L0_attn_x":
            v = np.ascontiguousarray(v[core])
        m[name] = np.ascontiguousarray(
            v.astype(np.float32, copy=False).reshape(shape))
    return m


def kernel(**inputs):
    nc = build()
    in_maps = [shard_inputs(inputs, c) for c in range(8)]
    res = run_bass_kernel_spmd(nc, in_maps, core_ids=list(range(8)))
    return np.stack([res.results[c]["out"] for c in range(8)], axis=0)
